# version 05
# baseline (speedup 1.0000x reference)
import numpy as np
import ml_dtypes
from collections import defaultdict
from contextlib import ExitStack

import concourse.bass as bass
import concourse.mybir as mybir
from concourse.bass_utils import run_bass_kernel_spmd

F32 = mybir.dt.float32
BF16 = mybir.dt.bfloat16
AF = mybir.ActivationFunctionType
ALU = mybir.AluOpType
AX = mybir.AxisListType

N_CORES = 8
ENGINE_ORDER = ("sync", "gpsimd", "tensor", "scalar", "vector")


def _blocks(w, mtw=128):
    K, M = w.shape
    n_kt, n_mt = K // 128, M // mtw
    assert n_kt * 128 == K and n_mt * mtw == M
    return np.ascontiguousarray(
        w.reshape(n_kt, 128, n_mt, mtw).transpose(2, 1, 0, 3).reshape(n_mt * 128, n_kt * mtw))


def prep_inputs(inputs, n_cores=N_CORES, Bc=2, L=2048):
    g = lambda k: np.asarray(inputs[k])
    D = 2048
    T = Bc * L
    sh = {}
    sh["w_in0"] = _blocks(np.ascontiguousarray(g("L0_attn_W_in").T))
    Wg = g("L0_attn_W_gate").reshape(4, 4, 512, D).transpose(1, 0, 2, 3).reshape(8192, D)
    sh["w_gate0"] = _blocks(np.ascontiguousarray(Wg.T))
    bg = g("L0_attn_b_gate").reshape(4, 4, 512).transpose(1, 0, 2).reshape(8192)
    sh["bgate_t"] = np.ascontiguousarray(bg.reshape(64, 128).T)
    sh["w_out0"] = _blocks(np.ascontiguousarray(g("L0_attn_W_out").T))
    sh["rmsw_t"] = np.ascontiguousarray(g("L0_attn_rms_w").reshape(4, 128).T)
    sh["wr_t"] = np.ascontiguousarray(g("L0_ffn_Wr").reshape(16, 128, 8).transpose(1, 0, 2).reshape(128, 128))
    sh["br_row"] = np.ascontiguousarray(g("L0_ffn_br").reshape(1, 8))
    sh["wd_b"] = _blocks(np.ascontiguousarray(g("L0_ffn_Wd").transpose(1, 0, 2).reshape(D, 128)))
    sh["bd_col"] = np.ascontiguousarray(g("L0_ffn_bd").reshape(128, 1))
    sh["wu_b"] = _blocks(np.ascontiguousarray(g("L0_ffn_Wu").reshape(128, D)))
    sh["bu_t"] = np.ascontiguousarray(g("L0_ffn_bu"))
    sh["r16_t"] = np.ascontiguousarray(np.repeat(np.eye(8, dtype=np.float32), 16, axis=1))
    sh["w_in1"] = _blocks(np.ascontiguousarray(g("L1_attn_W_in").T))
    sh["convw_t"] = np.ascontiguousarray(g("L1_attn_conv_w").reshape(32, 128, 4).transpose(1, 0, 2).reshape(128, 128))
    sh["convb_t"] = np.ascontiguousarray(g("L1_attn_conv_b").reshape(32, 128).T)
    WxT = np.ascontiguousarray(g("L1_attn_Wx").T)
    sh["wx_ts"] = _blocks(np.ascontiguousarray(WxT[:, :128]))
    sh["wx_bc"] = _blocks(np.ascontiguousarray(WxT[:, 128:]), mtw=32)
    sh["sel32"] = np.ascontiguousarray(
        np.concatenate([np.repeat([[1.0, 0.0]], 16, 0), np.repeat([[0.0, 1.0]], 16, 0)], 0).astype(np.float32))
    sh["sel2"] = np.ascontiguousarray(sh["sel32"].T)
    sh["lnw32"] = np.ascontiguousarray(
        np.concatenate([g("L1_attn_b_ln_w"), g("L1_attn_c_ln_w")]).reshape(32, 1))
    sh["dtlnw_t"] = np.ascontiguousarray(g("L1_attn_dt_ln_w").reshape(128, 1))
    sh["wdt_b"] = _blocks(np.ascontiguousarray(g("L1_attn_Wdt").T))
    sh["dtb_t"] = np.ascontiguousarray(g("L1_attn_dt_bias").reshape(32, 128).T)
    sh["a_t"] = np.ascontiguousarray(g("L1_attn_A").reshape(32, 128, 16).transpose(1, 0, 2).reshape(128, 512))
    sh["d_t"] = np.ascontiguousarray(g("L1_attn_D").reshape(32, 128).T)
    sh["w_out1"] = _blocks(np.ascontiguousarray(g("L1_attn_Wout").T))
    sh["gatew_t"] = np.ascontiguousarray(g("L1_ffn_gate_w").reshape(16, 128, 8).transpose(1, 0, 2).reshape(128, 128))
    sh["wg_b"] = np.concatenate([_blocks(np.ascontiguousarray(g("L1_ffn_w_gate")[e])) for e in range(8)], 0)
    sh["wu_b1"] = np.concatenate([_blocks(np.ascontiguousarray(g("L1_ffn_w_up")[e])) for e in range(8)], 0)
    sh["wdn_b"] = np.concatenate([_blocks(np.ascontiguousarray(g("L1_ffn_w_down")[e])) for e in range(8)], 0)
    sh["swg_b"] = _blocks(np.ascontiguousarray(g("L1_ffn_sw_gate")))
    sh["swu_b"] = _blocks(np.ascontiguousarray(g("L1_ffn_sw_up")))
    sh["swd_b"] = _blocks(np.ascontiguousarray(g("L1_ffn_sw_down")))
    sh["eps_t"] = np.broadcast_to(np.array([1e-5, 1e-6], np.float32), (128, 2)).copy()
    sh["ones_col"] = np.ones((128, 1), np.float32)
    sh["ones_row"] = np.ones((1, 128), np.float32)
    sh["ones_row_bf"] = np.ones((1, 128), ml_dtypes.bfloat16)
    sh["ident_bf"] = np.eye(128, dtype=ml_dtypes.bfloat16)
    for k, v in sh.items():
        if v.dtype == np.float64:
            sh[k] = v.astype(np.float32)

    x = g("L0_attn_x")
    in_maps = []
    for c in range(n_cores):
        xc = x[c * Bc:(c + 1) * Bc, :L, :]
        m = dict(sh)
        m["x_fm"] = np.ascontiguousarray(xc.transpose(2, 0, 1).reshape(D, T))
        in_maps.append(m)
    return in_maps



_ESEM = {"tensor": "pe", "vector": "ve", "scalar": "se", "gpsimd": "ge", "sync": "ld"}


class _Inst:
    __slots__ = ("B", "raw", "esem")

    def __init__(self, B, raw, esem=None):
        self.B = B
        self.raw = raw
        self.esem = esem

    def then_inc(self, sem, n=1):
        self.B.C[sem] += n
        if self.raw is not None:
            self.raw.then_inc(self.B.sems[sem], n)
        return self

    def inc(self):
        return self.then_inc(self.esem, 1)


class _Eng:

    def __init__(self, B, name, raw=None):
        self._B = B
        self._name = name
        self._raw = raw

    def wait_ge(self, sem, val):
        if self._raw is not None:
            self._raw.wait_ge(self._B.sems[sem], val)

    def fence(self, sem=None):
        sem = sem or _ESEM[self._name]
        self.wait_ge(sem, self._B.C[sem])

    def sem_inc(self, sem, n=1):
        self._B.C[sem] += n
        if self._raw is not None:
            self._raw.sem_inc(self._B.sems[sem], n)

    def dma(self, out, in_, sem=None):
        assert sem is not None, "dma() needs an explicit semaphore name"
        self._B.C[sem] += 16
        if self._raw is not None:
            self._raw.dma_start(out=out, in_=in_).then_inc(self._B.sems[sem], 16)
        return self._B.C[sem]

    def __getattr__(self, m):
        B, raw = self._B, self._raw
        esem = _ESEM[self._name]

        def call(*a, **kw):
            if raw is None:
                return _Inst(B, None, esem)
            return _Inst(B, getattr(raw, m)(*a, **kw), esem)

        return call


class Bld:
    def __init__(self, nc, cfg, ctx):
        self.nc, self.cfg, self.ctx = nc, cfg, ctx
        self.C = defaultdict(int)
        self.M = {}
        self.pass1 = True
        self.uid = 0
        names = ("pe", "ve", "se", "ge", "bar",
                 "ld0", "ld1", "ld2", "ld3", "gd0", "gd1", "gd2", "gd3",
                 "st0", "st1", "st2", "st3", "wl0", "wl1", "sd0", "sd1",
                 "xa0", "xa1", "xb0", "xb1")
        self.sems = {n: ctx.enter_context(nc.semaphore(n)) for n in names}
        self.ps = [ctx.enter_context(nc.psum_tensor(f"psb{i}", [128, 512], F32)) for i in range(7)]
        self.psbf = ctx.enter_context(nc.psum_tensor("psbfx", [128, 1024], BF16))

    def mark(self, key, sem):
        if self.pass1:
            self.M[key] = self.C[sem]
        return self.M[key]

    def at(self, key):
        return self.M.get(key, 0)

    def name(self, base):
        self.uid += 1
        return f"{base}_{'a' if self.pass1 else 'b'}{self.uid}"

    def sb(self, es, base, shape, dtype):
        return es.enter_context(self.nc.sbuf_tensor(self.name(base), shape, dtype))


def _run_phases(nc, bld, phases, ctx):
    n_eng = len(ENGINE_ORDER)
    snap = {}
    for p1 in (True, False):
        bld.pass1 = p1
        bld.C.clear()
        bld.uid = 0
        if not p1:
            block = ctx.enter_context(nc.Block())
        for pidx, ph in enumerate(phases):
            es = ExitStack()
            bodies = ph(bld, es)
            for ename in ENGINE_ORDER:
                body = bodies.get(ename)
                bar_target = n_eng * pidx

                def run_body(raw_eng, body=body, ename=ename, bar_target=bar_target):
                    E = _Eng(bld, ename, raw_eng)
                    E.wait_ge("bar", bar_target)
                    if body is not None:
                        body(E)
                    bld.C["bar"] += 1
                    if raw_eng is not None:
                        raw_eng.drain().then_inc(bld.sems["bar"], 1)

                if p1:
                    run_body(None)
                else:
                    getattr(block, ename)(run_body)
                key = (pidx, ename)
                if p1:
                    snap[key] = dict(bld.C)
                else:
                    assert snap[key] == dict(bld.C), (
                        f"pass divergence at phase {pidx} engine {ename}: "
                        f"{ {k: (snap[key].get(k), bld.C.get(k)) for k in set(snap[key]) | set(bld.C) if snap[key].get(k) != bld.C.get(k)} }")
            es.close()



_DTIN = {
    "x_fm": (None, F32),
    "w_in0": ([2048, 2048], F32), "w_gate0": ([8192, 2048], F32),
    "bgate_t": ([128, 64], F32), "w_out0": ([2048, 2048], F32),
    "rmsw_t": ([128, 4], F32),
    "wr_t": ([128, 128], F32), "br_row": ([1, 8], F32),
    "wd_b": ([128, 2048], F32), "bd_col": ([128, 1], F32),
    "wu_b": ([2048, 128], F32), "bu_t": ([8, 2048], F32), "r16_t": ([8, 128], F32),
    "w_in1": ([8192, 2048], F32), "convw_t": ([128, 128], F32), "convb_t": ([128, 32], F32),
    "wx_ts": ([128, 4096], F32), "wx_bc": ([128, 1024], F32),
    "sel32": ([32, 2], F32), "sel2": ([2, 32], F32), "lnw32": ([32, 1], F32),
    "dtlnw_t": ([128, 1], F32), "wdt_b": ([4096, 128], F32), "dtb_t": ([128, 32], F32),
    "a_t": ([128, 512], F32), "d_t": ([128, 32], F32),
    "w_out1": ([2048, 4096], F32), "gatew_t": ([128, 128], F32),
    "wg_b": ([8 * 1408, 2048], F32), "wu_b1": ([8 * 1408, 2048], F32),
    "wdn_b": ([8 * 2048, 1408], F32),
    "swg_b": ([2816, 2048], F32), "swu_b": ([2816, 2048], F32), "swd_b": ([2048, 2816], F32),
    "eps_t": ([128, 2], F32),
    "ones_col": ([128, 1], F32), "ones_row": ([1, 128], F32),
    "ones_row_bf": ([1, 128], BF16), "ident_bf": ([128, 128], BF16),
}


def build_program(Bc=2, L=2048):
    D = 2048
    T = Bc * L
    cfg = dict(Bc=Bc, L=L, D=D, T=T, NS=T // 512, NCL=L // 512)

    nc = bass.Bass("TRN2", target_bir_lowering=False, debug=False)
    ctx = ExitStack()
    nc._kb_ctx = ctx

    IN = {}
    for k, (s, d) in _DTIN.items():
        if k == "x_fm":
            s = [D, T]
        IN[k] = nc.dram_tensor(k, s, d, kind="ExternalInput").ap()

    SC = {}
    for k, (s, d) in {
        "xn_s": ([D, T], BF16), "xn32_s": ([D, T], F32), "xn132_s": ([D, T], F32),
        "xp_s": ([D, T], BF16), "g0_s": ([D, T], BF16),
        "gates_s": ([8192, T], F32), "h0_s": ([D, T], BF16), "s0_s": ([D, T], BF16),
        "x1_s": ([D, T], F32), "xn1_s": ([D, T], BF16),
        "hconv_s": ([4096, T], BF16), "sg_s": ([4096, T], BF16),
        "delta_s": ([4096, T], F32), "bc_s": ([32, T], BF16),
        "y1_s": ([4096, T], BF16), "acc1_s": ([D, T], F32), "s1_s": ([D, T], BF16),
        "wfm_s": ([8, T], BF16), "whs_s": ([10 * 1408, T], BF16),
    }.items():
        SC[k] = nc.dram_tensor(k, s, d, kind="Internal").ap()
    OUT = nc.dram_tensor("y_fm", [D, T], F32, kind="ExternalOutput").ap()

    bld = Bld(nc, cfg, ctx)
    bld.IN, bld.SC, bld.OUT = IN, SC, OUT

    phases = [ph_consts, ph_ln1, ph_win0, ph_ffn0, ph_gates, ph_scan0, ph_wout0,
              ph_comb1, ph_win1, ph_ssmp, ph_moescan, ph_wout1, ph_comb2]
    if _PHASE_LIMIT is not None:
        phases = phases[:_PHASE_LIMIT]
    bld.phases = phases
    _run_phases(nc, bld, phases, ctx)
    return nc, cfg


def build_program_upto(nphases, Bc=2, L=2048):
    global _PHASE_LIMIT
    _PHASE_LIMIT = nphases
    try:
        return build_program(Bc, L)
    finally:
        _PHASE_LIMIT = None


_PHASE_LIMIT = None



_CONSTS = [
    ("bgate_t", [128, 64], F32), ("rmsw_t", [128, 4], F32),
    ("wr_t", [128, 128], F32), ("br_row", [1, 8], F32),
    ("bd_col", [128, 1], F32),
    ("bu_t", [8, 2048], BF16), ("r16_t", [8, 128], BF16),
    ("convw_t", [128, 128], F32), ("convb_t", [128, 32], F32),
    ("wx_bc", [128, 1024], BF16),
    ("sel32", [32, 2], F32), ("sel2", [2, 32], F32), ("lnw32", [32, 1], F32),
    ("dtlnw_t", [128, 1], F32),
    ("dtb_t", [128, 32], F32), ("a_t", [128, 512], F32), ("d_t", [128, 32], F32),
    ("gatew_t", [128, 128], F32),
    ("eps_t", [128, 2], F32),
    ("ones_col", [128, 1], F32), ("ones_row", [1, 128], F32),
    ("ones_row_bf", [1, 128], BF16), ("ident_bf", [128, 128], BF16),
]


def ph_consts(bld, es):
    nc = bld.nc
    if not hasattr(bld, "K"):
        bld.K = {k: bld.ctx.enter_context(nc.sbuf_tensor(k + "_sb", shape, dt))
                 for k, shape, dt in _CONSTS}
    K = bld.K

    def gpsimd(E):
        for k, shape, dt in _CONSTS:
            E.dma(K[k][:, :], bld.IN[k][:, :], sem="gd0")
        last = bld.mark("consts_done", "gd0")
        E.wait_ge("gd0", last)

    return {"gpsimd": gpsimd}



def ph_ln1(bld, es):
    return _ln_bodies(bld, es, src=bld.IN["x_fm"], dst=bld.SC["xn_s"], tag="ln1",
                      dst32=bld.SC["xn32_s"])


def _ln_bodies(bld, es, src, dst, tag, eps=1e-5, dst32=None):
    cfg = bld.cfg
    NS, D = cfg["NS"], cfg["D"]
    nc = bld.nc
    K = bld.K
    ps = bld.ps
    xs = bld.sb(es, "ln_x", [128, 2 * 16 * 512], F32)
    sq = bld.sb(es, "ln_sq", [128, 512], F32)
    row = bld.sb(es, "ln_row", [1, 6 * 512], F32)
    xn = bld.sb(es, "ln_xn", [128, 2 * 16 * 512], BF16)
    tmp = bld.sb(es, "ln_tmp", [128, 512], F32)

    def xsv(par, dt):
        return xs[:, (par * 16 + dt) * 512:(par * 16 + dt + 1) * 512]

    def xnv(par, dt):
        return xn[:, (par * 16 + dt) * 512:(par * 16 + dt + 1) * 512]

    R = lambda i: row[0:1, i * 512:(i + 1) * 512]

    def sync(E):
        for s in range(NS):
            par = s % 2
            if s >= 2:
                E.wait_ge("ve", bld.at(f"{tag}_xnd_{s-2}"))
                E.wait_ge(f"gd{par}", bld.at(f"{tag}_st_{s-2}"))
            for g4 in range(4):
                E.dma(xs[:, (par * 16 + g4 * 4) * 512:(par * 16 + g4 * 4 + 4) * 512]
                      .rearrange("p (dt c) -> p dt c", c=512),
                      src[g4 * 512:(g4 + 1) * 512, s * 512:(s + 1) * 512]
                      .rearrange("(dt p) c -> p dt c", p=128), sem=f"ld{par}")
            bld.mark(f"{tag}_ld_{s}", f"ld{par}")

    def scalar(E):
        for s in range(NS):
            par = s % 2
            E.wait_ge(f"ld{par}", bld.at(f"{tag}_ld_{s}"))
            for dt in range(16):
                E.wait_ge("pe", bld.at(f"{tag}_sqfree_{s}_{dt}"))
                E.activation(out=sq[:, :], in_=xsv(par, dt), func=AF.Square).inc()
                bld.mark(f"{tag}_sq_{s}_{dt}", "se")
            E.wait_ge("ve", bld.at(f"{tag}_var_{s}"))
            E.activation(out=R(4), in_=R(3), func=AF.Ln,
                         bias=K["eps_t"][0:1, 0:1] if eps == 1e-5 else K["eps_t"][0:1, 1:2]).inc()
            E.fence()
            E.activation(out=R(5), in_=R(4), func=AF.Exp, scale=-0.5).inc()
            bld.mark(f"{tag}_inv_{s}", "se")

    def tensor(E):
        for s in range(NS):
            par = s % 2
            E.wait_ge(f"ld{par}", bld.at(f"{tag}_ld_{s}"))
            if s >= 1:
                E.wait_ge("ve", bld.at(f"{tag}_statsread_{s-1}"))
            for dt in range(16):
                E.matmul(ps[0][0:1, :], lhsT=K["ones_col"][:, :], rhs=xsv(par, dt),
                         start=(dt == 0), stop=(dt == 15))
            for dt in range(16):
                bld.mark(f"{tag}_sqfree_{s}_{dt}", "pe")
                E.wait_ge("se", bld.at(f"{tag}_sq_{s}_{dt}"))
                E.matmul(ps[1][0:1, :], lhsT=K["ones_col"][:, :], rhs=sq[:, :],
                         start=(dt == 0), stop=(dt == 15)).then_inc("pe", 1)
            bld.mark(f"{tag}_stats_{s}", "pe")
            E.wait_ge("ve", bld.at(f"{tag}_minv_{s}"))
            if s >= 1:
                E.wait_ge("ve", bld.at(f"{tag}_xnd_{s-1}"))
            E.matmul(ps[2][:, :], lhsT=K["ones_row"][:, :], rhs=R(5), start=True, stop=True)
            E.matmul(ps[3][:, :], lhsT=K["ones_row"][:, :], rhs=R(2), start=True, stop=True).then_inc("pe", 1)
            bld.mark(f"{tag}_bcast_{s}", "pe")

    def vector(E):
        for s in range(NS):
            par = s % 2
            E.wait_ge("pe", bld.at(f"{tag}_stats_{s}"))
            E.tensor_scalar(out=R(0), in0=ps[0][0:1, :], scalar1=1.0 / D,
                            scalar2=None, op0=ALU.mult).inc()
            E.tensor_scalar(out=R(1), in0=ps[1][0:1, :], scalar1=1.0 / D,
                            scalar2=None, op0=ALU.mult).inc()
            bld.mark(f"{tag}_statsread_{s}", "ve")
            E.fence()
            E.tensor_tensor(out=tmp[0:1, :], in0=R(0), in1=R(0), op=ALU.mult).inc()
            E.fence()
            E.tensor_tensor(out=R(3), in0=R(1), in1=tmp[0:1, :], op=ALU.subtract).inc()
            bld.mark(f"{tag}_var_{s}", "ve")
            E.wait_ge("se", bld.at(f"{tag}_inv_{s}"))
            E.tensor_tensor(out=R(2), in0=R(0), in1=R(5), op=ALU.mult).inc()
            bld.mark(f"{tag}_minv_{s}", "ve")
            E.wait_ge("pe", bld.at(f"{tag}_bcast_{s}"))
            if s >= 2:
                E.wait_ge(f"gd{par}", bld.at(f"{tag}_st_{s-2}"))
            E.fence()
            for dt in range(16):
                E.tensor_tensor(out=tmp[:, :], in0=xsv(par, dt), in1=ps[2][:, :],
                                op=ALU.mult).inc()
                E.fence()
                E.tensor_tensor(out=xsv(par, dt), in0=tmp[:, :], in1=ps[3][:, :],
                                op=ALU.subtract).inc()
                E.fence()
                E.tensor_copy(out=xnv(par, dt), in_=xsv(par, dt)).inc()
                E.fence()
            bld.mark(f"{tag}_xnd_{s}", "ve")

    def gpsimd(E):
        for s in range(NS):
            par = s % 2
            E.wait_ge("ve", bld.at(f"{tag}_xnd_{s}"))
            for g4 in range(4):
                E.dma(dst[g4 * 512:(g4 + 1) * 512, s * 512:(s + 1) * 512]
                      .rearrange("(dt p) c -> p dt c", p=128),
                      xn[:, (par * 16 + g4 * 4) * 512:(par * 16 + g4 * 4 + 4) * 512]
                      .rearrange("p (dt c) -> p dt c", c=512), sem=f"gd{par}")
            if dst32 is not None:
                for g4 in range(4):
                    E.dma(dst32[g4 * 512:(g4 + 1) * 512, s * 512:(s + 1) * 512]
                          .rearrange("(dt p) c -> p dt c", p=128),
                          xs[:, (par * 16 + g4 * 4) * 512:(par * 16 + g4 * 4 + 4) * 512]
                          .rearrange("p (dt c) -> p dt c", c=512), sem=f"gd{par}")
            bld.mark(f"{tag}_st_{s}", f"gd{par}")
        for p2 in range(min(2, NS)):
            E.wait_ge(f"gd{p2}", bld.C[f"gd{p2}"])

    return {"sync": sync, "scalar": scalar, "tensor": tensor, "vector": vector,
            "gpsimd": gpsimd}



def _mm_wres(bld, es, tag, w_dram, n_kt, n_mt, src, dst, func=AF.Copy, out_dt=BF16,
             bias_col=None):
    cfg = bld.cfg
    NS = cfg["NS"]
    T = cfg["T"]
    ps = bld.ps
    NB = 4
    wsb = bld.sb(es, f"{tag}_w", [128, n_mt * n_kt * 128], BF16)
    xb = bld.sb(es, f"{tag}_x", [128, 2 * n_kt * 512], BF16)
    stage = bld.sb(es, f"{tag}_st", [128, NB * 512], out_dt)

    def wv(mt, kt):
        return wsb[:, (mt * n_kt + kt) * 128:(mt * n_kt + kt + 1) * 128]

    def xv(par, kt):
        return xb[:, (par * n_kt + kt) * 512:(par * n_kt + kt + 1) * 512]

    def gpsimd(E):
        for j in range(n_mt):
            E.dma(wsb[:, j * n_kt * 128:(j + 1) * n_kt * 128],
                  w_dram[j * 128:(j + 1) * 128, :], sem="wl0")
        bld.mark(f"{tag}_wld", "wl0")

    def sync(E):
        for s in range(NS):
            par = s % 2
            if s >= 2:
                E.wait_ge("pe", bld.at(f"{tag}_mmd_{s-2}"))
            for h in range(2):
                E.dma(xb[:, (par * n_kt + h * n_kt // 2) * 512:
                          (par * n_kt + (h + 1) * n_kt // 2) * 512]
                      .rearrange("p (kt c) -> p kt c", c=512),
                      src[h * (n_kt // 2) * 128:(h + 1) * (n_kt // 2) * 128,
                          s * 512:(s + 1) * 512]
                      .rearrange("(kt p) c -> p kt c", p=128), sem=f"ld{par}")
            bld.mark(f"{tag}_rhs_{s}", f"ld{par}")

    def tensor(E):
        E.wait_ge("wl0", bld.at(f"{tag}_wld"))
        for s in range(NS):
            par = s % 2
            E.wait_ge(f"ld{par}", bld.at(f"{tag}_rhs_{s}"))
            for mt in range(n_mt):
                g = s * n_mt + mt
                bank = g % NB
                if g >= NB:
                    E.wait_ge("se", bld.at(f"{tag}_ev_{g-NB}"))
                for kt in range(n_kt):
                    mm = E.matmul(ps[bank][:, :], lhsT=wv(mt, kt), rhs=xv(par, kt),
                                  start=(kt == 0), stop=(kt == n_kt - 1))
                mm.inc()
                bld.mark(f"{tag}_mm_{g}", "pe")
            bld.mark(f"{tag}_mmd_{s}", "pe")

    def scalar(E):
        for s in range(NS):
            for mt in range(n_mt):
                g = s * n_mt + mt
                bank = g % NB
                E.wait_ge("pe", bld.at(f"{tag}_mm_{g}"))
                if g >= NB:
                    E.wait_ge(f"st{bank}", bld.at(f"{tag}_sst_{g-NB}"))
                kw = {}
                fn = func
                if bias_col is not None:
                    kw["bias"] = bias_col(mt)
                    if fn == AF.Copy:
                        fn = AF.Identity
                E.activation(out=stage[:, bank * 512:(bank + 1) * 512],
                             in_=ps[bank][:, :], func=fn, **kw).inc()
                bld.mark(f"{tag}_ev_{g}", "se")
                E.fence("se")
                E.dma(dst[mt * 128:(mt + 1) * 128, s * 512:(s + 1) * 512],
                      stage[:, bank * 512:(bank + 1) * 512], sem=f"st{bank}")
                bld.mark(f"{tag}_sst_{g}", f"st{bank}")
        for b2 in range(NB):
            E.wait_ge(f"st{b2}", bld.C[f"st{b2}"])

    return {"gpsimd": gpsimd, "sync": sync, "tensor": tensor, "scalar": scalar}


def _mm_rhsres(bld, es, tag, w_dram, n_kt, n_mt, src, dst, windows=None,
               func=AF.Copy, out_dt=BF16, bias_col=None):
    cfg = bld.cfg
    if windows is None:
        windows = [(0, cfg["T"])]
    WMAX = max(c1 - c0 for c0, c1 in windows)
    ps = bld.ps
    NB = 4
    xr = bld.sb(es, f"{tag}_x", [128, n_kt * WMAX], BF16)
    wsl = bld.sb(es, f"{tag}_w", [128, 2 * n_kt * 128], BF16)
    stage = bld.sb(es, f"{tag}_st", [128, NB * 512], out_dt)

    def sync(E):
        for w, (c0, c1) in enumerate(windows):
            if w >= 1:
                E.wait_ge("pe", bld.at(f"{tag}_alldone_{w-1}"))
            for h in range(n_kt):
                E.dma(xr[:, h * WMAX:h * WMAX + (c1 - c0)],
                      src[h * 128:(h + 1) * 128, c0:c1], sem="ld0")
            bld.mark(f"{tag}_rhs_{w}", "ld0")

    def gpsimd(E):
        for w, _ in enumerate(windows):
            for mt in range(n_mt):
                if (w, mt) != (0, 0):
                    pw, pmt = (w, mt - 2) if mt >= 2 else (w - 1, n_mt + mt - 2)
                    if pmt >= 0:
                        E.wait_ge("pe", bld.at(f"{tag}_wfree_{pw}_{pmt}"))
                E.dma(wsl[:, (mt % 2) * n_kt * 128:((mt % 2) + 1) * n_kt * 128],
                      w_dram[mt * 128:(mt + 1) * 128, :], sem=f"wl{mt % 2}")
                bld.mark(f"{tag}_wld_{w}_{mt}", f"wl{mt % 2}")

    def tensor(E):
        g = 0
        for w, (c0, c1) in enumerate(windows):
            NSL = (c1 - c0) // 512
            E.wait_ge("ld0", bld.at(f"{tag}_rhs_{w}"))
            for mt in range(n_mt):
                par = mt % 2
                E.wait_ge(f"wl{par}", bld.at(f"{tag}_wld_{w}_{mt}"))
                for s in range(NSL):
                    bank = g % NB
                    if g >= NB:
                        E.wait_ge("se", bld.at(f"{tag}_ev_{g-NB}"))
                    for kt in range(n_kt):
                        mm = E.matmul(ps[bank][:, :],
                                      lhsT=wsl[:, (par * n_kt + kt) * 128:(par * n_kt + kt + 1) * 128],
                                      rhs=xr[:, kt * WMAX + s * 512:kt * WMAX + (s + 1) * 512],
                                      start=(kt == 0), stop=(kt == n_kt - 1))
                    mm.inc()
                    bld.mark(f"{tag}_mm_{g}", "pe")
                    g += 1
                bld.mark(f"{tag}_wfree_{w}_{mt}", "pe")
            bld.mark(f"{tag}_alldone_{w}", "pe")

    def scalar(E):
        g = 0
        for w, (c0, c1) in enumerate(windows):
            NSL = (c1 - c0) // 512
            for mt in range(n_mt):
                for s in range(NSL):
                    bank = g % NB
                    E.wait_ge("pe", bld.at(f"{tag}_mm_{g}"))
                    if g >= NB:
                        E.wait_ge(f"st{bank}", bld.at(f"{tag}_sst_{g-NB}"))
                    kw = {}
                    fn = func
                    if bias_col is not None:
                        kw["bias"] = bias_col(mt)
                        if fn == AF.Copy:
                            fn = AF.Identity
                    E.activation(out=stage[:, bank * 512:(bank + 1) * 512],
                                 in_=ps[bank][:, :], func=fn, **kw).inc()
                    bld.mark(f"{tag}_ev_{g}", "se")
                    E.fence("se")
                    E.dma(dst[mt * 128:(mt + 1) * 128, c0 + s * 512:c0 + (s + 1) * 512],
                          stage[:, bank * 512:(bank + 1) * 512], sem=f"st{bank}")
                    bld.mark(f"{tag}_sst_{g}", f"st{bank}")
                    g += 1
        for b2 in range(NB):
            E.wait_ge(f"st{b2}", bld.C[f"st{b2}"])

    return {"gpsimd": gpsimd, "sync": sync, "tensor": tensor, "scalar": scalar}



def ph_win0(bld, es):
    return _mm_wres(bld, es, "win0", bld.IN["w_in0"], 16, 16,
                    bld.SC["xn_s"], bld.SC["xp_s"])


def ph_gates(bld, es):
    return _mm_rhsres(bld, es, "gat", bld.IN["w_gate0"], 16, 64,
                      bld.SC["xp_s"], bld.SC["gates_s"],
                      out_dt=F32, bias_col=lambda mt: bld.K["bgate_t"][:, mt:mt + 1])


def ph_ffn0(bld, es):
    cfg = bld.cfg
    NS, T = cfg["NS"], cfg["T"]
    nc = bld.nc
    K = bld.K
    ps, psbf = bld.ps, bld.psbf
    xb = bld.sb(es, "f0_x", [128, 2 * 16 * 512], BF16)
    xb32 = bld.sb(es, "f0_x32", [128, 2 * 16 * 512], F32)
    wdl = bld.sb(es, "f0_wd", [128, 2048], BF16)
    wul = bld.sb(es, "f0_wu", [128, 2048], BF16)
    gex = bld.sb(es, "f0_gex", [128, 32], F32)
    gsum = bld.sb(es, "f0_gsum", [128, 4], F32)
    gsm = bld.sb(es, "f0_gsm", [128, 32], F32)
    mx = bld.sb(es, "f0_mx", [128, 32], F32)
    m1 = bld.sb(es, "f0_m1", [128, 32], F32)
    m2 = bld.sb(es, "f0_m2", [128, 32], F32)
    t1 = bld.sb(es, "f0_t1", [128, 32], F32)
    t2 = bld.sb(es, "f0_t2", [128, 32], F32)
    rs = bld.sb(es, "f0_rs", [128, 12], F32)
    wtm = bld.sb(es, "f0_wtm", [128, 32], BF16)
    wfm = bld.sb(es, "f0_wfm", [8, T], BF16)
    hsb = bld.sb(es, "f0_hsb", [128, 512], F32)
    whb = bld.sb(es, "f0_whb", [128, 512], BF16)
    stage = bld.sb(es, "f0_stg", [128, 4 * 512], BF16)

    def xv(par, kt):
        return xb[:, (par * 16 + kt) * 512:(par * 16 + kt + 1) * 512]

    def c8(buf, tt):
        return buf[:, tt * 8:(tt + 1) * 8]

    def gpsimd(E):
        E.dma(wdl[:, :], bld.IN["wd_b"][:, :], sem="wl0")
        E.dma(wul.ap().rearrange("p (bk m) -> p bk m", m=128),
              bld.IN["wu_b"].rearrange("(bk p) m -> p bk m", p=128), sem="wl0")
        bld.mark("f0_wld", "wl0")

    def sync(E):
        for s in range(NS):
            par = s % 2
            if s >= 2:
                E.wait_ge("pe", bld.at(f"f0_done_{s-2}"))
            for h in range(2):
                E.dma(xb[:, (par * 16 + h * 8) * 512:(par * 16 + (h + 1) * 8) * 512]
                      .rearrange("p (kt c) -> p kt c", c=512),
                      bld.SC["xn_s"][h * 8 * 128:(h + 1) * 8 * 128, s * 512:(s + 1) * 512]
                      .rearrange("(kt p) c -> p kt c", p=128), sem=f"ld{par}")
            for h in range(4):
                E.dma(xb32[:, (par * 16 + h * 4) * 512:(par * 16 + (h + 1) * 4) * 512]
                      .rearrange("p (kt c) -> p kt c", c=512),
                      bld.SC["xn32_s"][h * 4 * 128:(h + 1) * 4 * 128, s * 512:(s + 1) * 512]
                      .rearrange("(kt p) c -> p kt c", p=128), sem=f"ld{par}")
            bld.mark(f"f0_rhs_{s}", f"ld{par}")

    def tensor(E):
        E.wait_ge("wl0", bld.at("f0_wld"))
        for s in range(NS):
            par = s % 2
            E.wait_ge(f"ld{par}", bld.at(f"f0_rhs_{s}"))
            for tt in range(4):
                if s >= 1:
                    E.wait_ge("se", bld.at(f"f0_allev_{s-1}"))
                for kt in range(16):
                    E.matmul(ps[tt][:, 0:8],
                             lhsT=xb32[:, (par * 16 + kt) * 512 + tt * 128:
                                       (par * 16 + kt) * 512 + (tt + 1) * 128],
                             rhs=K["wr_t"][:, kt * 8:(kt + 1) * 8],
                             start=(kt == 0), stop=False)
                E.matmul(ps[tt][:, 0:8], lhsT=K["ones_row"][0:1, :],
                         rhs=K["br_row"][0:1, :], start=False, stop=True).inc()
                bld.mark(f"f0_rmm_{s}_{tt}", "pe")
            if s >= 1:
                E.wait_ge("se", bld.at(f"f0_wfm_{s-1}"))
            for tt in range(4):
                E.wait_ge("ve", bld.at(f"f0_wtm_{s}_{tt}"))
                E.transpose(psbf[0:8, tt * 128:(tt + 1) * 128], in_=c8(wtm, tt),
                            identity=K["ident_bf"][:, :]).inc()
                bld.mark(f"f0_tr_{s}_{tt}", "pe")
            if s >= 1:
                E.wait_ge("se", bld.at(f"f0_hev_{s-1}"))
            for kt in range(16):
                mm = E.matmul(ps[4][:, :], lhsT=wdl[:, kt * 128:(kt + 1) * 128],
                              rhs=xv(par, kt), start=(kt == 0), stop=(kt == 15))
            mm.inc()
            bld.mark(f"f0_hmm_{s}", "pe")
            E.wait_ge("se", bld.at(f"f0_wfm_{s}"))
            if s >= 1:
                E.wait_ge("ve", bld.at(f"f0_whb_{s-1}"))
            E.matmul(ps[5][:, :], lhsT=K["r16_t"][:, :],
                     rhs=wfm[0:8, s * 512:(s + 1) * 512], start=True, stop=True).inc()
            bld.mark(f"f0_we_{s}", "pe")
            E.wait_ge("ve", bld.at(f"f0_whb_{s}"))
            for dt in range(16):
                bank = dt % 4
                E.wait_ge("se", bld.at(f"f0_rexp_{s}_{bank}"))
                if dt >= 4:
                    E.wait_ge("se", bld.at(f"f0_g0ev_{s}_{dt-4}"))
                E.matmul(ps[bank][:, :], lhsT=wul[:, dt * 128:(dt + 1) * 128],
                         rhs=whb[:, :], start=True, stop=False)
                E.matmul(ps[bank][:, :], lhsT=K["bu_t"][0:8, dt * 128:(dt + 1) * 128],
                         rhs=wfm[0:8, s * 512:(s + 1) * 512],
                         start=False, stop=True).inc()
                bld.mark(f"f0_dmm_{s}_{dt}", "pe")
            bld.mark(f"f0_done_{s}", "pe")

    def scalar(E):
        for s in range(NS):
            if s >= 1:
                E.wait_ge("ve", bld.at(f"f0_wtm_{s-1}_3"))
            for tt in range(4):
                E.wait_ge("pe", bld.at(f"f0_rmm_{s}_{tt}"))
                E.activation(out=c8(gex, tt), in_=ps[tt][:, 0:8], func=AF.Exp,
                             accum_out=gsum[:, tt:tt + 1]).inc()
                bld.mark(f"f0_rexp_{s}_{tt}", "se")
            for tt in range(4):
                E.wait_ge("pe", bld.at(f"f0_tr_{s}_{tt}"))
                E.activation(out=wfm[0:8, s * 512 + tt * 128:s * 512 + (tt + 1) * 128],
                             in_=psbf[0:8, tt * 128:(tt + 1) * 128], func=AF.Copy).inc()
            bld.mark(f"f0_wfm_{s}", "se")
            E.wait_ge("pe", bld.at(f"f0_hmm_{s}"))
            if s >= 1:
                E.wait_ge("ve", bld.at(f"f0_whb_{s-1}"))
            E.activation(out=hsb[:, :], in_=ps[4][:, :], func=AF.Relu,
                         bias=K["bd_col"][:, :]).inc()
            bld.mark(f"f0_hev_{s}", "se")
            for dt in range(16):
                bank = dt % 4
                g = s * 16 + dt
                E.wait_ge("pe", bld.at(f"f0_dmm_{s}_{dt}"))
                if g >= 4:
                    E.wait_ge(f"st{bank}", bld.at(f"f0_g0st_{g-4}"))
                E.activation(out=stage[:, bank * 512:(bank + 1) * 512],
                             in_=ps[bank][:, :], func=AF.Silu).inc()
                bld.mark(f"f0_g0ev_{s}_{dt}", "se")
                E.fence("se")
                E.dma(bld.SC["g0_s"][dt * 128:(dt + 1) * 128, s * 512:(s + 1) * 512],
                      stage[:, bank * 512:(bank + 1) * 512], sem=f"st{bank}")
                bld.mark(f"f0_g0st_{g}", f"st{bank}")
            bld.mark(f"f0_allev_{s}", "se")
        for b2 in range(4):
            E.wait_ge(f"st{b2}", bld.C[f"st{b2}"])

    def vector(E):
        for s in range(NS):
            for tt in range(4):
                E.wait_ge("se", bld.at(f"f0_rexp_{s}_{tt}"))
                if s >= 1:
                    E.wait_ge("pe", bld.at(f"f0_tr_{s-1}_{tt}"))
                E.reciprocal(rs[:, tt:tt + 1], gsum[:, tt:tt + 1]).inc()
                E.fence()
                E.tensor_scalar(out=c8(gsm, tt), in0=c8(gex, tt),
                                scalar1=rs[:, tt:tt + 1], scalar2=None,
                                op0=ALU.mult).inc()
                E.fence()
                E.max(out=c8(mx, tt), in_=c8(gsm, tt)).inc()
                E.fence()
                E.tensor_tensor(out=rs[:, 4 + tt:5 + tt], in0=mx[:, tt * 8:tt * 8 + 1],
                                in1=mx[:, tt * 8 + 1:tt * 8 + 2], op=ALU.add).inc()
                E.fence()
                E.reciprocal(rs[:, 8 + tt:9 + tt], rs[:, 4 + tt:5 + tt]).inc()
                E.tensor_scalar(out=c8(m1, tt), in0=c8(gsm, tt),
                                scalar1=mx[:, tt * 8:tt * 8 + 1], scalar2=None,
                                op0=ALU.is_equal).inc()
                E.tensor_scalar(out=c8(m2, tt), in0=c8(gsm, tt),
                                scalar1=mx[:, tt * 8 + 1:tt * 8 + 2], scalar2=None,
                                op0=ALU.is_equal).inc()
                E.fence()
                E.tensor_scalar(out=c8(t1, tt), in0=c8(m1, tt),
                                scalar1=mx[:, tt * 8:tt * 8 + 1],
                                scalar2=rs[:, 8 + tt:9 + tt],
                                op0=ALU.mult, op1=ALU.mult).inc()
                E.tensor_scalar(out=c8(t2, tt), in0=c8(m2, tt),
                                scalar1=mx[:, tt * 8 + 1:tt * 8 + 2],
                                scalar2=rs[:, 8 + tt:9 + tt],
                                op0=ALU.mult, op1=ALU.mult).inc()
                E.fence()
                E.tensor_tensor(out=c8(wtm, tt), in0=c8(t1, tt), in1=c8(t2, tt),
                                op=ALU.add).inc()
                bld.mark(f"f0_wtm_{s}_{tt}", "ve")
            E.wait_ge("pe", bld.at(f"f0_we_{s}"))
            E.wait_ge("se", bld.at(f"f0_hev_{s}"))
            if s >= 1:
                E.wait_ge("pe", bld.at(f"f0_done_{s-1}"))
            E.tensor_tensor(out=whb[:, :], in0=hsb[:, :], in1=ps[5][:, :],
                            op=ALU.mult).inc()
            bld.mark(f"f0_whb_{s}", "ve")

    return {"sync": sync, "gpsimd": gpsimd, "tensor": tensor, "scalar": scalar, "vector": vector}

def ph_scan0(bld, es):
    cfg = bld.cfg
    Bc, L, NCL = cfg["Bc"], cfg["L"], cfg["NCL"]
    K = bld.K
    ps = bld.ps
    GS = bld.SC["gates_s"]
    trip = bld.sb(es, "s0_trip", [128, 3 * L], F32)
    oin = bld.sb(es, "s0_oin", [128, 4 * L], F32)
    ctr = bld.sb(es, "s0_ctr", [128, 4 * L], F32)
    us = bld.sb(es, "s0_us", [128, L], F32)
    sq = bld.sb(es, "s0_sq", [128, L], F32)
    cn = bld.sb(es, "s0_cn", [128, L], F32)
    th = bld.sb(es, "s0_th", [128, L], F32)
    invr = bld.sb(es, "s0_inv", [1, L], F32)
    hst = bld.sb(es, "s0_h", [128, 4 * L], BF16)
    NBH = Bc * 4

    def rows(gt, h, q):
        r0 = gt * 2048 + (h * 4 + q) * 128
        return (r0, r0 + 128)

    def sync(E):
        for bh in range(NBH):
            b, h = divmod(bh, 4)
            for q in range(4):
                if bh > 0 or q > 0:
                    pb, pq = (bh, q - 1) if q > 0 else (bh - 1, 3)
                    E.wait_ge("ve", bld.at(f"s0_tripfree_{pb}_{pq}"))
                for slot, gt in ((0, 0), (1, 1), (2, 3)):
                    r0, r1 = rows(gt, h, q)
                    E.dma(trip[:, slot * L:(slot + 1) * L],
                          GS[r0:r1, b * L:(b + 1) * L], sem="ld0")
                bld.mark(f"s0_trip_{bh}_{q}", "ld0")
            if bh > 0:
                E.wait_ge("ve", bld.at(f"s0_hdone_{bh-1}"))
            for q in range(4):
                r0, r1 = rows(2, h, q)
                E.dma(oin[:, q * L:(q + 1) * L], GS[r0:r1, b * L:(b + 1) * L], sem="ld1")
            bld.mark(f"s0_oin_{bh}", "ld1")

    def scalar(E):
        for bh in range(NBH):
            for q in range(4):
                E.wait_ge("ld0", bld.at(f"s0_trip_{bh}_{q}"))
                E.activation(out=trip[:, L:2 * L], in_=trip[:, L:2 * L],
                             func=AF.Sigmoid).inc()
                E.activation(out=trip[:, 2 * L:3 * L], in_=trip[:, 2 * L:3 * L],
                             func=AF.Tanh).inc()
                E.activation(out=trip[:, 0:L], in_=trip[:, 0:L], func=AF.Exp).inc()
                bld.mark(f"s0_acts_{bh}_{q}", "se")
                E.wait_ge("ve", bld.at(f"s0_scan_{bh}_{q}"))
                if q > 0:
                    E.wait_ge("pe", bld.at(f"s0_sqmm_{bh}_{q-1}"))
                elif bh > 0:
                    E.wait_ge("pe", bld.at(f"s0_sqmm_{bh-1}_3"))
                E.activation(out=sq[:, :], in_=ctr[:, q * L:(q + 1) * L],
                             func=AF.Square).inc()
                bld.mark(f"s0_csq_{bh}_{q}", "se")
            E.wait_ge("ld1", bld.at(f"s0_oin_{bh}"))
            E.activation(out=oin[:, :], in_=oin[:, :], func=AF.Sigmoid).inc()
            bld.mark(f"s0_osig_{bh}", "se")
            E.wait_ge("pe", bld.at(f"s0_stats_{bh}"))
            for lc in range(NCL):
                E.activation(out=invr[0:1, lc * 512:(lc + 1) * 512],
                             in_=ps[lc][0:1, :], func=AF.Ln, scale=1.0 / 512,
                             bias=K["eps_t"][0:1, 1:2]).inc()
            bld.mark(f"s0_lnread_{bh}", "se")
            E.fence("se")
            E.activation(out=invr[0:1, :], in_=invr[0:1, :], func=AF.Exp,
                         scale=-0.5).inc()
            bld.mark(f"s0_inv_{bh}", "se")
            for q in range(4):
                E.wait_ge("ve", bld.at(f"s0_cn_{bh}_{q}"))
                if q > 0:
                    E.wait_ge("ve", bld.at(f"s0_hq_{bh}_{q-1}"))
                elif bh > 0:
                    E.wait_ge("ve", bld.at(f"s0_hq_{bh-1}_3"))
                E.activation(out=th[:, :], in_=cn[:, :], func=AF.Tanh,
                             scale=K["rmsw_t"][:, q:q + 1]).inc()
                bld.mark(f"s0_th_{bh}_{q}", "se")

    def vector(E):
        for bh in range(NBH):
            for q in range(4):
                E.wait_ge("se", bld.at(f"s0_acts_{bh}_{q}"))
                if bh > 0:
                    E.wait_ge("se", bld.at(f"s0_csq_{bh-1}_{q}"))
                E.fence()
                E.tensor_tensor(out=us[:, :], in0=trip[:, 0:L], in1=trip[:, 2 * L:3 * L],
                                op=ALU.mult).inc()
                E.fence()
                E.tensor_tensor_scan(out=ctr[:, q * L:(q + 1) * L],
                                     data0=trip[:, L:2 * L], data1=us[:, :],
                                     initial=0.0, op0=ALU.mult, op1=ALU.add).inc()
                bld.mark(f"s0_scan_{bh}_{q}", "ve")
                bld.mark(f"s0_tripfree_{bh}_{q}", "ve")
            for q in range(4):
                E.wait_ge("pe", bld.at(f"s0_bcast_{bh}"))
                E.fence()
                for lc in range(NCL):
                    E.tensor_tensor(out=cn[:, lc * 512:(lc + 1) * 512],
                                    in0=ctr[:, q * L + lc * 512:q * L + (lc + 1) * 512],
                                    in1=ps[lc][:, :], op=ALU.mult).inc()
                bld.mark(f"s0_cn_{bh}_{q}", "ve")
                E.wait_ge("se", bld.at(f"s0_th_{bh}_{q}"))
                E.wait_ge("se", bld.at(f"s0_osig_{bh}"))
                if q == 0 and bh > 0:
                    E.wait_ge("gd0", bld.at(f"s0_hst_{bh-1}"))
                E.fence()
                E.tensor_tensor(out=hst[:, q * L:(q + 1) * L],
                                in0=oin[:, q * L:(q + 1) * L], in1=th[:, :],
                                op=ALU.mult).inc()
                bld.mark(f"s0_hq_{bh}_{q}", "ve")
            bld.mark(f"s0_hdone_{bh}", "ve")

    def tensor(E):
        for bh in range(NBH):
            for q in range(4):
                E.wait_ge("se", bld.at(f"s0_csq_{bh}_{q}"))
                if q == 0 and bh > 0:
                    E.wait_ge("ve", bld.at(f"s0_cn_{bh-1}_3"))
                for lc in range(NCL):
                    mm = E.matmul(ps[lc][0:1, :], lhsT=K["ones_col"][:, :],
                                  rhs=sq[:, lc * 512:(lc + 1) * 512],
                                  start=(q == 0), stop=(q == 3))
                mm.inc()
                bld.mark(f"s0_sqmm_{bh}_{q}", "pe")
            bld.mark(f"s0_stats_{bh}", "pe")
            E.wait_ge("se", bld.at(f"s0_inv_{bh}"))
            for lc in range(NCL):
                mm = E.matmul(ps[lc][:, :], lhsT=K["ones_row"][:, :],
                              rhs=invr[0:1, lc * 512:(lc + 1) * 512],
                              start=True, stop=True)
            mm.inc()
            bld.mark(f"s0_bcast_{bh}", "pe")

    def gpsimd(E):
        for bh in range(NBH):
            b, h = divmod(bh, 4)
            for q in range(4):
                E.wait_ge("ve", bld.at(f"s0_hq_{bh}_{q}"))
                E.dma(bld.SC["h0_s"][(h * 4 + q) * 128:(h * 4 + q + 1) * 128,
                                     b * L:(b + 1) * L],
                      hst[:, q * L:(q + 1) * L], sem="gd0")
            bld.mark(f"s0_hst_{bh}", "gd0")
        E.wait_ge("gd0", bld.C["gd0"])

    return {"sync": sync, "scalar": scalar, "vector": vector, "tensor": tensor,
            "gpsimd": gpsimd}

def ph_wout0(bld, es):
    return _mm_wres(bld, es, "wo0", bld.IN["w_out0"], 16, 16,
                    bld.SC["h0_s"], bld.SC["s0_s"])

def ph_comb1(bld, es):
    cfg = bld.cfg
    NS, D = cfg["NS"], cfg["D"]
    K = bld.K
    ps = bld.ps
    tag = "cb1"
    eps = 1e-5
    xs = bld.sb(es, "cb_x", [128, 2 * 16 * 512], F32)
    s0b = bld.sb(es, "cb_s0", [128, 16 * 512], BF16)
    g0b = bld.sb(es, "cb_g0", [128, 16 * 512], BF16)
    xf = bld.sb(es, "cb_xf", [128, 16 * 512], F32)
    sq = bld.sb(es, "cb_sq", [128, 512], F32)
    row = bld.sb(es, "cb_row", [1, 6 * 512], F32)
    xn = bld.sb(es, "cb_xn", [128, 16 * 512], BF16)
    tmp = bld.sb(es, "cb_tmp", [128, 512], F32)

    def xsv(par, dt):
        return xs[:, (par * 16 + dt) * 512:(par * 16 + dt + 1) * 512]

    def xnv(par, dt):
        return xn[:, dt * 512:(dt + 1) * 512]

    def v16(buf, dt):
        return buf[:, dt * 512:(dt + 1) * 512]

    R = lambda i: row[0:1, i * 512:(i + 1) * 512]
    src, s0s, g0s = bld.IN["x_fm"], bld.SC["s0_s"], bld.SC["g0_s"]
    dx1, dxn, dxn32 = bld.SC["x1_s"], bld.SC["xn1_s"], bld.SC["xn132_s"]
    gex = bld.sb(es, "cb_gex", [128, 32], F32)
    gsum = bld.sb(es, "cb_gsum", [128, 4], F32)
    gsm = bld.sb(es, "cb_gsm", [128, 32], F32)
    mx = bld.sb(es, "cb_mx", [128, 32], F32)
    m1 = bld.sb(es, "cb_m1", [128, 32], F32)
    m2 = bld.sb(es, "cb_m2", [128, 32], F32)
    t1 = bld.sb(es, "cb_t1", [128, 32], F32)
    t2 = bld.sb(es, "cb_t2", [128, 32], F32)
    rs = bld.sb(es, "cb_rs", [128, 4], F32)
    wtm = bld.sb(es, "cb_wtm", [128, 32], BF16)
    wstg = bld.sb(es, "cb_wst", [8, 512], BF16)
    psbf = bld.psbf

    def c8(buf, tt):
        return buf[:, tt * 8:(tt + 1) * 8]

    def sync(E):
        for s in range(NS):
            par = s % 2
            if s >= 2:
                E.wait_ge("ve", bld.at(f"{tag}_xnd_{s-2}"))
                E.wait_ge(f"gd{par}", bld.at(f"{tag}_stx_{s-2}"))
            for g4 in range(4):
                E.dma(xs[:, (par * 16 + g4 * 4) * 512:(par * 16 + g4 * 4 + 4) * 512]
                      .rearrange("p (dt c) -> p dt c", c=512),
                      src[g4 * 512:(g4 + 1) * 512, s * 512:(s + 1) * 512]
                      .rearrange("(dt p) c -> p dt c", p=128), sem=f"ld{par}")
            bld.mark(f"{tag}_ldx_{s}", f"ld{par}")
            if s >= 1:
                E.wait_ge("ve", bld.at(f"{tag}_x1d_{s-1}"))
            for g8, dst_b, src_b in ((0, s0b, s0s), (1, g0b, g0s)):
                for h in range(2):
                    E.dma(dst_b[:, h * 8 * 512:(h + 1) * 8 * 512]
                          .rearrange("p (dt c) -> p dt c", c=512),
                          src_b[h * 8 * 128:(h + 1) * 8 * 128, s * 512:(s + 1) * 512]
                          .rearrange("(dt p) c -> p dt c", p=128), sem="ld2")
            bld.mark(f"{tag}_ldsg_{s}", "ld2")

    def scalar(E):
        for s in range(NS):
            par = s % 2
            for dt in range(16):
                E.wait_ge("ve", bld.at(f"{tag}_x1_{s}_{dt}"))
                E.wait_ge("pe", bld.at(f"{tag}_sqfree_{s}_{dt}"))
                E.activation(out=sq[:, :], in_=xsv(par, dt), func=AF.Square).inc()
                bld.mark(f"{tag}_sq_{s}_{dt}", "se")
            E.wait_ge("ve", bld.at(f"{tag}_var_{s}"))
            E.activation(out=R(4), in_=R(3), func=AF.Ln,
                         bias=K["eps_t"][0:1, 0:1]).inc()
            E.fence()
            E.activation(out=R(5), in_=R(4), func=AF.Exp, scale=-0.5).inc()
            bld.mark(f"{tag}_inv_{s}", "se")
            if s >= 1:
                E.wait_ge("ve", bld.at(f"{tag}_wtm_{s-1}_3"))
            for tt in range(4):
                E.wait_ge("pe", bld.at(f"{tag}_rmm_{s}_{tt}"))
                E.activation(out=c8(gex, tt), in_=ps[4 + tt % 2][:, 0:8], func=AF.Exp,
                             accum_out=gsum[:, tt:tt + 1]).inc()
                bld.mark(f"{tag}_rexp_{s}_{tt}", "se")
            for tt in range(4):
                E.wait_ge("pe", bld.at(f"{tag}_tr_{s}_{tt}"))
                if s >= 1 and tt == 0:
                    E.wait_ge("sd0", bld.at(f"{tag}_wst_{s-1}"))
                E.activation(out=wstg[0:8, tt * 128:(tt + 1) * 128],
                             in_=psbf[0:8, tt * 128:(tt + 1) * 128], func=AF.Copy).inc()
                bld.mark(f"{tag}_wcp_{s}_{tt}", "se")
            E.fence("se")
            E.dma(bld.SC["wfm_s"][0:8, s * 512:(s + 1) * 512], wstg[0:8, :], sem="sd0")
            bld.mark(f"{tag}_wst_{s}", "sd0")

    def tensor(E):
        for s in range(NS):
            par = s % 2
            if s >= 1:
                E.wait_ge("ve", bld.at(f"{tag}_statsread_{s-1}"))
            for dt in range(16):
                E.wait_ge("ve", bld.at(f"{tag}_x1_{s}_{dt}"))
                E.matmul(ps[0][0:1, :], lhsT=K["ones_col"][:, :], rhs=xsv(par, dt),
                         start=(dt == 0), stop=(dt == 15))
            for dt in range(16):
                bld.mark(f"{tag}_sqfree_{s}_{dt}", "pe")
                E.wait_ge("se", bld.at(f"{tag}_sq_{s}_{dt}"))
                E.matmul(ps[1][0:1, :], lhsT=K["ones_col"][:, :], rhs=sq[:, :],
                         start=(dt == 0), stop=(dt == 15)).then_inc("pe", 1)
            bld.mark(f"{tag}_stats_{s}", "pe")
            E.wait_ge("ve", bld.at(f"{tag}_minv_{s}"))
            if s >= 1:
                E.wait_ge("ve", bld.at(f"{tag}_xnd_{s-1}"))
            E.matmul(ps[2][:, :], lhsT=K["ones_row"][:, :], rhs=R(5), start=True, stop=True)
            E.matmul(ps[3][:, :], lhsT=K["ones_row"][:, :], rhs=R(2), start=True,
                     stop=True).then_inc("pe", 1)
            bld.mark(f"{tag}_bcast_{s}", "pe")
            E.wait_ge("ve", bld.at(f"{tag}_xnd_{s}"))
            for tt in range(4):
                if tt >= 2:
                    E.wait_ge("se", bld.at(f"{tag}_rexp_{s}_{tt-2}"))
                elif s >= 1:
                    E.wait_ge("se", bld.at(f"{tag}_rexp_{s-1}_{tt+2}"))
                for kt in range(16):
                    mm = E.matmul(ps[4 + tt % 2][:, 0:8],
                                  lhsT=xf[:, kt * 512 + tt * 128:kt * 512 + (tt + 1) * 128],
                                  rhs=K["gatew_t"][:, kt * 8:(kt + 1) * 8],
                                  start=(kt == 0), stop=(kt == 15))
                mm.inc()
                bld.mark(f"{tag}_rmm_{s}_{tt}", "pe")
            if s >= 1:
                E.wait_ge("se", bld.at(f"{tag}_wcp_{s-1}_3"))
            for tt in range(4):
                E.wait_ge("ve", bld.at(f"{tag}_wtm_{s}_{tt}"))
                E.transpose(psbf[0:8, tt * 128:(tt + 1) * 128], in_=c8(wtm, tt),
                            identity=K["ident_bf"][:, :]).inc()
                bld.mark(f"{tag}_tr_{s}_{tt}", "pe")

    def vector(E):
        for s in range(NS):
            par = s % 2
            E.wait_ge(f"ld{par}", bld.at(f"{tag}_ldx_{s}"))
            E.wait_ge("ld2", bld.at(f"{tag}_ldsg_{s}"))
            for dt in range(16):
                E.fence()
                E.tensor_tensor(out=tmp[:, :], in0=v16(s0b, dt), in1=v16(g0b, dt),
                                op=ALU.mult).inc()
                E.fence()
                E.tensor_tensor(out=xsv(par, dt), in0=xsv(par, dt), in1=tmp[:, :],
                                op=ALU.add).inc()
                bld.mark(f"{tag}_x1_{s}_{dt}", "ve")
            bld.mark(f"{tag}_x1d_{s}", "ve")
            E.wait_ge("pe", bld.at(f"{tag}_stats_{s}"))
            E.tensor_scalar(out=R(0), in0=ps[0][0:1, :], scalar1=1.0 / D,
                            scalar2=None, op0=ALU.mult).inc()
            E.tensor_scalar(out=R(1), in0=ps[1][0:1, :], scalar1=1.0 / D,
                            scalar2=None, op0=ALU.mult).inc()
            bld.mark(f"{tag}_statsread_{s}", "ve")
            E.fence()
            E.tensor_tensor(out=tmp[0:1, :], in0=R(0), in1=R(0), op=ALU.mult).inc()
            E.fence()
            E.tensor_tensor(out=R(3), in0=R(1), in1=tmp[0:1, :], op=ALU.subtract).inc()
            bld.mark(f"{tag}_var_{s}", "ve")
            E.wait_ge("se", bld.at(f"{tag}_inv_{s}"))
            E.tensor_tensor(out=R(2), in0=R(0), in1=R(5), op=ALU.mult).inc()
            bld.mark(f"{tag}_minv_{s}", "ve")
            E.wait_ge("pe", bld.at(f"{tag}_bcast_{s}"))
            if s >= 1:
                E.wait_ge(f"gd{1-par}", bld.at(f"{tag}_stn_{s-1}"))
            if s >= 2:
                E.wait_ge(f"gd{par}", bld.at(f"{tag}_stn_{s-2}"))
            E.fence()
            for dt in range(16):
                E.tensor_tensor(out=tmp[:, :], in0=xsv(par, dt), in1=ps[2][:, :],
                                op=ALU.mult).inc()
                E.fence()
                E.tensor_tensor(out=v16(xf, dt), in0=tmp[:, :], in1=ps[3][:, :],
                                op=ALU.subtract).inc()
                E.fence()
                E.tensor_copy(out=xnv(par, dt), in_=v16(xf, dt)).inc()
                E.fence()
            bld.mark(f"{tag}_xnd_{s}", "ve")
            for tt in range(4):
                E.wait_ge("se", bld.at(f"{tag}_rexp_{s}_{tt}"))
                if s >= 1:
                    E.wait_ge("pe", bld.at(f"{tag}_tr_{s-1}_{tt}"))
                E.fence()
                E.reciprocal(rs[:, tt:tt + 1], gsum[:, tt:tt + 1]).inc()
                E.fence()
                E.tensor_scalar(out=c8(gsm, tt), in0=c8(gex, tt),
                                scalar1=rs[:, tt:tt + 1], scalar2=None,
                                op0=ALU.mult).inc()
                E.fence()
                E.max(out=c8(mx, tt), in_=c8(gsm, tt)).inc()
                E.fence()
                E.tensor_scalar(out=c8(m1, tt), in0=c8(gsm, tt),
                                scalar1=mx[:, tt * 8:tt * 8 + 1], scalar2=None,
                                op0=ALU.is_equal).inc()
                E.tensor_scalar(out=c8(m2, tt), in0=c8(gsm, tt),
                                scalar1=mx[:, tt * 8 + 1:tt * 8 + 2], scalar2=None,
                                op0=ALU.is_equal).inc()
                E.fence()
                E.tensor_scalar(out=c8(t1, tt), in0=c8(m1, tt),
                                scalar1=mx[:, tt * 8:tt * 8 + 1], scalar2=None,
                                op0=ALU.mult).inc()
                E.tensor_scalar(out=c8(t2, tt), in0=c8(m2, tt),
                                scalar1=mx[:, tt * 8 + 1:tt * 8 + 2], scalar2=None,
                                op0=ALU.mult).inc()
                E.fence()
                E.tensor_tensor(out=c8(wtm, tt), in0=c8(t1, tt), in1=c8(t2, tt),
                                op=ALU.add).inc()
                bld.mark(f"{tag}_wtm_{s}_{tt}", "ve")

    def gpsimd(E):
        for s in range(NS):
            par = s % 2
            E.wait_ge("ve", bld.at(f"{tag}_x1d_{s}"))
            for g4 in range(4):
                E.dma(dx1[g4 * 512:(g4 + 1) * 512, s * 512:(s + 1) * 512]
                      .rearrange("(dt p) c -> p dt c", p=128),
                      xs[:, (par * 16 + g4 * 4) * 512:(par * 16 + g4 * 4 + 4) * 512]
                      .rearrange("p (dt c) -> p dt c", c=512), sem=f"gd{par}")
            bld.mark(f"{tag}_stx_{s}", f"gd{par}")
            E.wait_ge("ve", bld.at(f"{tag}_xnd_{s}"))
            for g4 in range(4):
                E.dma(dxn[g4 * 512:(g4 + 1) * 512, s * 512:(s + 1) * 512]
                      .rearrange("(dt p) c -> p dt c", p=128),
                      xn[:, g4 * 4 * 512:(g4 * 4 + 4) * 512]
                      .rearrange("p (dt c) -> p dt c", c=512), sem=f"gd{par}")
                E.dma(dxn32[g4 * 512:(g4 + 1) * 512, s * 512:(s + 1) * 512]
                      .rearrange("(dt p) c -> p dt c", p=128),
                      xf[:, g4 * 4 * 512:(g4 * 4 + 4) * 512]
                      .rearrange("p (dt c) -> p dt c", c=512), sem=f"gd{par}")
            bld.mark(f"{tag}_stn_{s}", f"gd{par}")
        for p2 in range(min(2, NS)):
            E.wait_ge(f"gd{p2}", bld.C[f"gd{p2}"])

    return {"sync": sync, "scalar": scalar, "tensor": tensor, "vector": vector,
            "gpsimd": gpsimd}

def ph_win1(bld, es):
    cfg = bld.cfg
    Bc, L, NCL, T = cfg["Bc"], cfg["L"], cfg["NCL"], cfg["T"]
    K = bld.K
    ps = bld.ps
    n_kt, n_mt = 16, 64
    xr = bld.sb(es, "w1_x", [128, n_kt * T], BF16)
    wsl = bld.sb(es, "w1_w", [128, 2 * n_kt * 128], BF16)
    cvb = bld.sb(es, "w1_cv", [128, 3 + L], F32)
    acc = bld.sb(es, "w1_acc", [128, L], F32)
    hstg = bld.sb(es, "w1_hst", [128, 2 * L], BF16)
    gstg = bld.sb(es, "w1_gst", [128, 4 * 512], BF16)
    src = bld.SC["xn1_s"]
    dsth, dstg = bld.SC["hconv_s"], bld.SC["sg_s"]

    def sync(E):
        for h in range(n_kt):
            E.dma(xr[:, h * T:(h + 1) * T], src[h * 128:(h + 1) * 128, :], sem="ld0")
        bld.mark("w1_rhs", "ld0")

    def gpsimd(E):
        for mt in range(n_mt):
            if mt >= 2:
                E.wait_ge("pe", bld.at(f"w1_wfree_{mt-2}"))
            E.dma(wsl[:, (mt % 2) * n_kt * 128:((mt % 2) + 1) * n_kt * 128],
                  bld.IN["w_in1"][mt * 128:(mt + 1) * 128, :], sem=f"wl{mt % 2}")
            bld.mark(f"w1_wld_{mt}", f"wl{mt % 2}")

    def tensor(E):
        E.wait_ge("ld0", bld.at("w1_rhs"))
        g = 0
        for mt in range(n_mt):
            par = mt % 2
            E.wait_ge(f"wl{par}", bld.at(f"w1_wld_{mt}"))
            for b in range(Bc):
                for lc in range(NCL):
                    bank = g % 4
                    if g >= 4:
                        E.wait_ge("se", bld.at(f"w1_ev_{g-4}"))
                    for kt in range(n_kt):
                        mm = E.matmul(ps[bank][:, :],
                                      lhsT=wsl[:, (par * n_kt + kt) * 128:
                                              (par * n_kt + kt + 1) * 128],
                                      rhs=xr[:, kt * T + b * L + lc * 512:
                                             kt * T + b * L + (lc + 1) * 512],
                                      start=(kt == 0), stop=(kt == n_kt - 1))
                    mm.inc()
                    bld.mark(f"w1_mm_{g}", "pe")
                    g += 1
            bld.mark(f"w1_wfree_{mt}", "pe")

    def scalar(E):
        g = 0
        for mt in range(n_mt):
            for b in range(Bc):
                ob = (mt * Bc + b) % 2
                for lc in range(NCL):
                    bank = g % 4
                    E.wait_ge("pe", bld.at(f"w1_mm_{g}"))
                    if mt < 32:
                        if mt + b > 0:
                            E.wait_ge("ve", bld.at(f"w1_acc_{(mt * Bc + b) - 1}"))
                        E.activation(out=cvb[:, 3 + lc * 512:3 + (lc + 1) * 512],
                                     in_=ps[bank][:, :], func=AF.Copy).inc()
                    else:
                        gg = (mt - 32) * Bc * NCL + b * NCL + lc
                        if gg >= 4:
                            E.wait_ge(f"st{bank}", bld.at(f"w1_gst_{gg-4}"))
                        E.activation(out=gstg[:, bank * 512:(bank + 1) * 512],
                                     in_=ps[bank][:, :], func=AF.Silu).inc()
                        E.fence("se")
                        E.dma(dstg[(mt - 32) * 128:(mt - 31) * 128,
                                   b * L + lc * 512:b * L + (lc + 1) * 512],
                              gstg[:, bank * 512:(bank + 1) * 512], sem=f"st{bank}")
                        bld.mark(f"w1_gst_{gg}", f"st{bank}")
                    bld.mark(f"w1_ev_{g}", "se")
                    g += 1
                if mt < 32:
                    bld.mark(f"w1_cvb_{mt * Bc + b}", "se")
                    E.wait_ge("ve", bld.at(f"w1_conv_{mt * Bc + b}"))
                    if mt * Bc + b >= 2:
                        E.wait_ge(f"sd{ob}", bld.at(f"w1_hstd_{mt * Bc + b - 2}"))
                    E.activation(out=hstg[:, ob * L:(ob + 1) * L], in_=acc[:, :],
                                 func=AF.Silu, bias=K["convb_t"][:, mt:mt + 1]).inc()
                    bld.mark(f"w1_hsig_{mt * Bc + b}", "se")
                    E.fence("se")
                    E.dma(dsth[mt * 128:(mt + 1) * 128, b * L:(b + 1) * L],
                          hstg[:, ob * L:(ob + 1) * L], sem=f"sd{ob}")
                    bld.mark(f"w1_hstd_{mt * Bc + b}", f"sd{ob}")
        for b2 in range(4):
            E.wait_ge(f"st{b2}", bld.C[f"st{b2}"])
        for b2 in range(2):
            E.wait_ge(f"sd{b2}", bld.C[f"sd{b2}"])

    def vector(E):
        E.memset(cvb[:, 0:3], 0.0).inc()
        for ob in range(32 * Bc):
            mt = ob // Bc
            E.wait_ge("se", bld.at(f"w1_cvb_{ob}"))
            if ob >= 1:
                E.wait_ge("se", bld.at(f"w1_hsig_{ob-1}"))
            E.fence()
            E.tensor_scalar(out=acc[:, :], in0=cvb[:, 0:L],
                            scalar1=K["convw_t"][:, mt * 4:mt * 4 + 1],
                            scalar2=None, op0=ALU.mult).inc()
            for k in range(1, 4):
                E.fence()
                E.scalar_tensor_tensor(out=acc[:, :], in0=cvb[:, k:k + L],
                                       scalar=K["convw_t"][:, mt * 4 + k:mt * 4 + k + 1],
                                       in1=acc[:, :], op0=ALU.mult, op1=ALU.add).inc()
            bld.mark(f"w1_conv_{ob}", "ve")

    return {"sync": sync, "gpsimd": gpsimd, "tensor": tensor, "scalar": scalar,
            "vector": vector}

def ph_ssmp(bld, es):
    cfg = bld.cfg
    Bc, L, NCL = cfg["Bc"], cfg["L"], cfg["NCL"]
    K = bld.K
    ps = bld.ps
    rhsb = bld.sb(es, "sp_x", [128, 2 * 32 * 512], BF16)
    wxl = bld.sb(es, "sp_wx", [128, 4096], BF16)
    wdtl = bld.sb(es, "sp_wdt", [128, 4096], BF16)
    tssq = bld.sb(es, "sp_tssq", [128, 512], F32)
    ts_sb = bld.sb(es, "sp_ts", [128, 512], F32)
    tsn = bld.sb(es, "sp_tsn", [128, 512], BF16)
    bcsq = bld.sb(es, "sp_bcsq", [32, 512], F32)
    bc_sb = bld.sb(es, "sp_bc", [32, 512], F32)
    bcn = bld.sb(es, "sp_bcn", [32, 512], BF16)
    tmp2 = bld.sb(es, "sp_tmp2", [32, 512], F32)
    invr = bld.sb(es, "sp_inv", [1, 512], F32)
    inv2 = bld.sb(es, "sp_inv2", [2, 512], F32)
    spa = bld.sb(es, "sp_a", [128, 512], F32)
    spr = bld.sb(es, "sp_r", [128, 512], F32)
    dstage = bld.sb(es, "sp_dst", [128, 4 * 512], F32)
    tmp = bld.sb(es, "sp_tmp", [128, 512], F32)
    src = bld.SC["hconv_s"]
    NCH = Bc * NCL

    def gpsimd(E):
        E.dma(wxl[:, :], bld.IN["wx_ts"][:, :], sem="wl0")
        E.dma(wdtl.ap().rearrange("p (bk m) -> p bk m", m=128),
              bld.IN["wdt_b"].rearrange("(bk p) m -> p bk m", p=128), sem="wl0")
        bld.mark("sp_wld", "wl0")
        for c in range(NCH):
            b, lc = divmod(c, NCL)
            E.wait_ge("ve", bld.at(f"sp_bcn_{c}"))
            E.dma(bld.SC["bc_s"][0:32, b * L + lc * 512:b * L + (lc + 1) * 512],
                  bcn[0:32, :], sem="gd0")
            bld.mark(f"sp_bcst_{c}", "gd0")
        E.wait_ge("gd0", bld.C["gd0"])

    def rv(par, kt):
        return rhsb[:, (par * 32 + kt) * 512:(par * 32 + kt + 1) * 512]

    def tensor(E):
        E.wait_ge("wl0", bld.at("sp_wld"))
        for c in range(NCH):
            par = c % 2
            E.wait_ge(f"ld{par}", bld.at(f"sp_rhs_{c}"))
            if c >= 1:
                E.wait_ge("se", bld.at(f"sp_psfree_{c-1}"))
            for kt in range(32):
                mm = E.matmul(ps[0][:, :], lhsT=wxl[:, kt * 128:(kt + 1) * 128],
                              rhs=rv(par, kt), start=(kt == 0), stop=(kt == 31))
            mm.inc()
            bld.mark(f"sp_tsmm_{c}", "pe")
            for kt in range(32):
                mm = E.matmul(ps[1][0:32, :], lhsT=K["wx_bc"][:, kt * 32:(kt + 1) * 32],
                              rhs=rv(par, kt), start=(kt == 0), stop=(kt == 31))
            mm.inc()
            bld.mark(f"sp_bcmm_{c}", "pe")
            E.wait_ge("se", bld.at(f"sp_tssq_{c}"))
            E.matmul(ps[2][0:1, :], lhsT=K["ones_col"][:, :], rhs=tssq[:, :],
                     start=True, stop=True).inc()
            bld.mark(f"sp_st1_{c}", "pe")
            E.wait_ge("se", bld.at(f"sp_bcsq_{c}"))
            if c >= 1:
                E.wait_ge("se", bld.at(f"sp_spl_{c-1}_31"))
            E.matmul(ps[6][0:2, :], lhsT=K["sel32"][0:32, :], rhs=bcsq[0:32, :],
                     start=True, stop=True).inc()
            bld.mark(f"sp_st2_{c}", "pe")
            E.wait_ge("se", bld.at(f"sp_inv_{c}"))
            if c >= 1:
                E.wait_ge("ve", bld.at(f"sp_tsn_{c-1}"))
            E.matmul(ps[3][:, :], lhsT=K["ones_row"][:, :], rhs=invr[0:1, :],
                     start=True, stop=True).inc()
            bld.mark(f"sp_bc1_{c}", "pe")
            E.wait_ge("se", bld.at(f"sp_inv2_{c}"))
            if c >= 1:
                E.wait_ge("ve", bld.at(f"sp_bcn_{c-1}"))
            E.matmul(ps[4][0:32, :], lhsT=K["sel2"][0:2, :], rhs=inv2[0:2, :],
                     start=True, stop=True).inc()
            bld.mark(f"sp_bc2_{c}", "pe")
            E.wait_ge("ve", bld.at(f"sp_tsn_{c}"))
            E.wait_ge("se", bld.at(f"sp_inv2r_{c}"))
            for it in range(32):
                bank = 5 + it % 2
                if it >= 2:
                    E.wait_ge("se", bld.at(f"sp_spl_{c}_{it-2}"))
                elif c >= 1:
                    E.wait_ge("se", bld.at(f"sp_spl_{c-1}_{30 + it}"))
                E.matmul(ps[bank][:, :], lhsT=wdtl[:, it * 128:(it + 1) * 128],
                         rhs=tsn[:, :], start=True, stop=True).inc()
                bld.mark(f"sp_dt_{c}_{it}", "pe")
            bld.mark(f"sp_done_{c}", "pe")

    def scalar(E):
        for c in range(NCH):
            E.wait_ge("pe", bld.at(f"sp_tsmm_{c}"))
            if c >= 1:
                E.wait_ge("pe", bld.at(f"sp_st1_{c-1}"))
            E.activation(out=tssq[:, :], in_=ps[0][:, :], func=AF.Square).inc()
            bld.mark(f"sp_tssq_{c}", "se")
            E.wait_ge("ve", bld.at(f"sp_tsn1_{c-1}") if c >= 1 else 0)
            E.activation(out=ts_sb[:, :], in_=ps[0][:, :], func=AF.Copy).inc()
            bld.mark(f"sp_tscp_{c}", "se")
            E.wait_ge("pe", bld.at(f"sp_bcmm_{c}"))
            if c >= 1:
                E.wait_ge("pe", bld.at(f"sp_st2_{c-1}"))
            E.activation(out=bcsq[0:32, :], in_=ps[1][0:32, :], func=AF.Square).inc()
            bld.mark(f"sp_bcsq_{c}", "se")
            E.wait_ge("ve", bld.at(f"sp_bcn1_{c-1}") if c >= 1 else 0)
            E.activation(out=bc_sb[0:32, :], in_=ps[1][0:32, :], func=AF.Copy).inc()
            bld.mark(f"sp_bccp_{c}", "se")
            bld.mark(f"sp_psfree_{c}", "se")
            E.wait_ge("pe", bld.at(f"sp_st1_{c}"))
            if c >= 1:
                E.wait_ge("pe", bld.at(f"sp_bc1_{c-1}"))
            E.activation(out=invr[0:1, :], in_=ps[2][0:1, :], func=AF.Ln,
                         scale=1.0 / 128, bias=K["eps_t"][0:1, 0:1]).inc()
            E.fence("se")
            E.activation(out=invr[0:1, :], in_=invr[0:1, :], func=AF.Exp,
                         scale=-0.5).inc()
            bld.mark(f"sp_inv_{c}", "se")
            E.wait_ge("pe", bld.at(f"sp_st2_{c}"))
            if c >= 1:
                E.wait_ge("pe", bld.at(f"sp_bc2_{c-1}"))
            E.activation(out=inv2[0:2, :], in_=ps[6][0:2, :], func=AF.Ln,
                         scale=1.0 / 16, bias=K["eps_t"][0:2, 0:1]).inc()
            E.fence("se")
            E.activation(out=inv2[0:2, :], in_=inv2[0:2, :], func=AF.Exp,
                         scale=-0.5).inc()
            bld.mark(f"sp_inv2_{c}", "se")
            bld.mark(f"sp_inv2r_{c}", "se")
            for it in range(32):
                bank = 5 + it % 2
                E.wait_ge("pe", bld.at(f"sp_dt_{c}_{it}"))
                if it >= 1:
                    E.wait_ge("ve", bld.at(f"sp_del_{c}_{it-1}"))
                elif c >= 1:
                    E.wait_ge("ve", bld.at(f"sp_del_{c-1}_31"))
                E.activation(out=spa[:, :], in_=ps[bank][:, :], func=AF.Abs,
                             bias=K["dtb_t"][:, it:it + 1]).inc()
                E.activation(out=spr[:, :], in_=ps[bank][:, :], func=AF.Relu,
                             bias=K["dtb_t"][:, it:it + 1]).inc()
                bld.mark(f"sp_spl0_{c}_{it}", "se")
                E.fence("se")
                E.activation(out=spa[:, :], in_=spa[:, :], func=AF.Exp,
                             scale=-1.0).inc()
                E.fence("se")
                E.activation(out=spa[:, :], in_=spa[:, :], func=AF.Ln,
                             bias=1.0).inc()
                bld.mark(f"sp_spl_{c}_{it}", "se")

    def vector(E):
        for c in range(NCH):
            b, lc = divmod(c, NCL)
            E.wait_ge("pe", bld.at(f"sp_bc1_{c}"))
            E.wait_ge("se", bld.at(f"sp_tscp_{c}"))
            E.tensor_tensor(out=tmp[:, :], in0=ts_sb[:, :], in1=ps[3][:, :],
                            op=ALU.mult).inc()
            bld.mark(f"sp_tsn1_{c}", "ve")
            E.fence()
            E.tensor_scalar(out=tsn[:, :], in0=tmp[:, :], scalar1=K["dtlnw_t"][:, :],
                            scalar2=None, op0=ALU.mult).inc()
            bld.mark(f"sp_tsn_{c}", "ve")
            E.wait_ge("pe", bld.at(f"sp_bc2_{c}"))
            E.wait_ge("se", bld.at(f"sp_bccp_{c}"))
            if c >= 1:
                E.wait_ge("gd0", bld.at(f"sp_bcst_{c-1}"))
            E.fence()
            E.tensor_tensor(out=tmp2[0:32, :], in0=bc_sb[0:32, :], in1=ps[4][0:32, :],
                            op=ALU.mult).inc()
            bld.mark(f"sp_bcn1_{c}", "ve")
            E.fence()
            E.tensor_scalar(out=bcn[0:32, :], in0=tmp2[0:32, :],
                            scalar1=K["lnw32"][0:32, :], scalar2=None,
                            op0=ALU.mult).inc()
            bld.mark(f"sp_bcn_{c}", "ve")
            for it in range(32):
                slot = it % 4
                E.wait_ge("se", bld.at(f"sp_spl_{c}_{it}"))
                if it % 4 == 0:
                    it4p = it // 4 - 1
                    if it4p >= 0:
                        E.wait_ge(f"st{it4p % 2}", bld.at(f"sp_dst_{c}_{it4p}"))
                    elif c >= 1:
                        E.wait_ge("st1", bld.at(f"sp_dst_{c-1}_7"))
                E.fence()
                E.tensor_tensor(out=dstage[:, slot * 512:(slot + 1) * 512],
                                in0=spa[:, :], in1=spr[:, :], op=ALU.add).inc()
                bld.mark(f"sp_del_{c}_{it}", "ve")

    def sync_all(E):
        for c in range(NCH + 1):
            if c < NCH:
                b, lc = divmod(c, NCL)
                par = c % 2
                if c >= 2:
                    E.wait_ge("pe", bld.at(f"sp_done_{c-2}"))
                for h in range(4):
                    E.dma(rhsb[:, (par * 32 + h * 8) * 512:(par * 32 + (h + 1) * 8) * 512]
                          .rearrange("p (kt x) -> p kt x", x=512),
                          src[h * 8 * 128:(h + 1) * 8 * 128,
                              b * L + lc * 512:b * L + (lc + 1) * 512]
                          .rearrange("(kt p) x -> p kt x", p=128), sem=f"ld{par}")
                bld.mark(f"sp_rhs_{c}", f"ld{par}")
            if c >= 1:
                cc = c - 1
                b, lc = divmod(cc, NCL)
                for it4 in range(8):
                    E.wait_ge("ve", bld.at(f"sp_del_{cc}_{it4 * 4 + 3}"))
                    E.dma(bld.SC["delta_s"][it4 * 4 * 128:(it4 + 1) * 4 * 128,
                                            b * L + lc * 512:b * L + (lc + 1) * 512]
                          .rearrange("(q p) x -> p q x", p=128),
                          dstage.ap().rearrange("p (q x) -> p q x", x=512),
                          sem=f"st{it4 % 2}")
                    bld.mark(f"sp_dst_{cc}_{it4}", f"st{it4 % 2}")
        for b2 in range(2):
            E.wait_ge(f"st{b2}", bld.C[f"st{b2}"])

    return {"sync": sync_all, "gpsimd": gpsimd, "tensor": tensor, "scalar": scalar,
            "vector": vector}

def ph_scan1(bld, es):
    cfg = bld.cfg
    Bc, L, NCL = cfg["Bc"], cfg["L"], cfg["NCL"]
    K = bld.K
    ps = bld.ps
    G = 2
    NIT = 32 // G
    delta2 = bld.sb(es, "s1_d", [128, G * L], F32)
    u2 = bld.sb(es, "s1_u", [128, G * L], BF16)
    sg2 = bld.sb(es, "s1_sg", [128, G * L], BF16)
    du2 = bld.sb(es, "s1_du", [128, G * L], BF16)
    dA2 = bld.sb(es, "s1_dA", [128, G * L], F32)
    dbu_v = bld.sb(es, "s1_dbv", [128, L], BF16)
    dbu_g = bld.sb(es, "s1_dbg", [128, L], BF16)
    s_v = bld.sb(es, "s1_sv", [128, L], BF16)
    s_g = bld.sb(es, "s1_sgg", [128, L], BF16)
    t_v = bld.sb(es, "s1_tv", [128, L], BF16)
    t_g = bld.sb(es, "s1_tg", [128, L], BF16)
    yac = bld.sb(es, "s1_y", [128, G * L], F32)
    bcrow = bld.sb(es, "s1_bcr", [1, 2 * 2 * L], BF16)
    Bb = bld.sb(es, "s1_Bb", [128, 2 * L], BF16)
    Cb = bld.sb(es, "s1_Cb", [128, 2 * L], BF16)
    ystg = du2
    NBG = Bc * NIT

    def dv(i):
        return delta2[:, i * L:(i + 1) * L]

    def uv(i):
        return u2[:, i * L:(i + 1) * L]

    def duv(i):
        return du2[:, i * L:(i + 1) * L]

    def yv(i):
        return yac[:, i * L:(i + 1) * L]

    def Bv(n):
        return Bb[:, (n % 2) * L:(n % 2 + 1) * L]

    def Cv(n):
        return Cb[:, (n % 2) * L:(n % 2 + 1) * L]

    def sync(E):
        for bg in range(NBG):
            b, g = divmod(bg, NIT)
            if bg >= 1:
                E.wait_ge("ve", bld.at(f"s1_grpdone_{bg-1}"))
                E.wait_ge("ge", bld.at(f"s1_gdone_{bg-1}"))
                E.wait_ge("gd0", bld.at(f"s1_yst_{bg-1}"))
            r0 = g * G * 128
            E.dma(delta2.ap().rearrange("p (q x) -> p q x", x=L),
                  bld.SC["delta_s"][r0:r0 + G * 128, b * L:(b + 1) * L]
                  .rearrange("(q p) x -> p q x", p=128), sem="ld0")
            E.dma(u2.ap().rearrange("p (q x) -> p q x", x=L),
                  bld.SC["hconv_s"][r0:r0 + G * 128, b * L:(b + 1) * L]
                  .rearrange("(q p) x -> p q x", p=128), sem="ld0")
            E.dma(sg2.ap().rearrange("p (q x) -> p q x", x=L),
                  bld.SC["sg_s"][r0:r0 + G * 128, b * L:(b + 1) * L]
                  .rearrange("(q p) x -> p q x", p=128), sem="ld0")
            bld.mark(f"s1_ld_{bg}", "ld0")
            for n in range(16):
                par = n % 2
                if bg + n > 0:
                    pk = (f"s1_bcmm_{bg}_{n-2}" if n >= 2 else
                          (f"s1_bcmm_{bg}_{n-1}" if n == 1 else f"s1_bcmm_{bg-1}_15"))
                    if n >= 2:
                        E.wait_ge("pe", bld.at(f"s1_bcmm_{bg}_{n-2}"))
                    elif bg > 0:
                        E.wait_ge("pe", bld.at(f"s1_bcmm_{bg-1}_{14 + n}"))
                sem = "ld1" if par == 0 else "ld3"
                E.dma(bcrow[0:1, par * 2 * L:par * 2 * L + L],
                      bld.SC["bc_s"][n:n + 1, b * L:(b + 1) * L], sem=sem)
                E.dma(bcrow[0:1, par * 2 * L + L:(par + 1) * 2 * L],
                      bld.SC["bc_s"][16 + n:17 + n, b * L:(b + 1) * L], sem=sem)
                bld.mark(f"s1_bcld_{bg}_{n}", sem)

    def tensor(E):
        for bg in range(NBG):
            for n in range(16):
                par = n % 2
                E.wait_ge("ld1" if par == 0 else "ld3", bld.at(f"s1_bcld_{bg}_{n}"))
                if n >= 2:
                    E.wait_ge("ve", bld.at(f"s1_p_{bg}_{n-2}_0"))
                    E.wait_ge("ge", bld.at(f"s1_gp_{bg}_{n-2}"))
                elif bg > 0:
                    E.wait_ge("ve", bld.at(f"s1_p_{bg-1}_{14 + n}_0"))
                    E.wait_ge("ge", bld.at(f"s1_gp_{bg-1}_{14 + n}"))
                for j in range(2):
                    for lc in range(NCL):
                        u = j * NCL + lc
                        bank = 4 + u % 2
                        if u >= 2:
                            E.wait_ge("se", bld.at(f"s1_bcc_{bg}_{n}_{u-2}"))
                        elif bg + n > 0:
                            pu = 2 * NCL - 2 + u
                            E.wait_ge("se", bld.at(f"s1_bcc_{bg}_{n-1}_{pu}" if n > 0
                                                   else f"s1_bcc_{bg-1}_15_{pu}"))
                        E.matmul(ps[bank][:, :512],
                                 lhsT=K["ones_row_bf"][:, :],
                                 rhs=bcrow[0:1, par * 2 * L + j * L + lc * 512:
                                           par * 2 * L + j * L + (lc + 1) * 512],
                                 start=True, stop=True).inc()
                        bld.mark(f"s1_bcb_{bg}_{n}_{u}", "pe")
                bld.mark(f"s1_bcmm_{bg}_{n}", "pe")

    def scalar(E):
        for bg in range(NBG):
            b, g = divmod(bg, NIT)
            E.wait_ge("ld0", bld.at(f"s1_ld_{bg}"))
            for n in range(16):
                for j in range(2):
                    buf = Bb if j == 0 else Cb
                    for lc in range(NCL):
                        u = j * NCL + lc
                        bank = 4 + u % 2
                        E.wait_ge("pe", bld.at(f"s1_bcb_{bg}_{n}_{u}"))
                        E.activation(out=buf[:, (n % 2) * L + lc * 512:
                                             (n % 2) * L + (lc + 1) * 512],
                                     in_=ps[bank][:, :512], func=AF.Copy).inc()
                        bld.mark(f"s1_bcc_{bg}_{n}_{u}", "se")
                for i in range(G):
                    it = g * G + i
                    if n > 0:
                        E.wait_ge("ve" if i == 0 else "ve",
                                  bld.at(f"s1_scan_{bg}_{n-1}_{i}"))
                    elif bg > 0:
                        E.wait_ge("ve", bld.at(f"s1_scan_{bg-1}_15_{i}"))
                    E.activation(out=dA2[:, i * L:(i + 1) * L], in_=dv(i),
                                 func=AF.Exp,
                                 scale=K["a_t"][:, it * 16 + n:it * 16 + n + 1]).inc()
                    bld.mark(f"s1_dA_{bg}_{n}_{i}", "se")

    def vector(E):
        for bg in range(NBG):
            b, g = divmod(bg, NIT)
            E.wait_ge("ld0", bld.at(f"s1_ld_{bg}"))
            if bg >= 1:
                E.wait_ge("ge", bld.at(f"s1_gdone_{bg-1}"))
                E.wait_ge("gd0", bld.at(f"s1_yst_{bg-1}"))
            E.fence()
            for i in range(G):
                E.tensor_tensor(out=duv(i), in0=dv(i), in1=uv(i), op=ALU.mult).inc()
            bld.mark(f"s1_dud_{bg}", "ve")
            E.fence()
            for n in range(16):
                E.wait_ge("se", bld.at(f"s1_bcc_{bg}_{n}_{NCL-1}"))
                E.fence()
                E.tensor_tensor(out=dbu_v[:, :], in0=duv(0), in1=Bv(n),
                                op=ALU.mult).inc()
                bld.mark(f"s1_dbu_{bg}_{n}_0", "ve")
                E.wait_ge("se", bld.at(f"s1_dA_{bg}_{n}_0"))
                E.fence()
                E.tensor_tensor_scan(out=s_v[:, :], data0=dA2[:, 0:L],
                                     data1=dbu_v[:, :], initial=0.0,
                                     op0=ALU.mult, op1=ALU.add).inc()
                bld.mark(f"s1_scan_{bg}_{n}_0", "ve")
                E.wait_ge("se", bld.at(f"s1_bcc_{bg}_{n}_{2*NCL-1}"))
                E.fence()
                if n == 0:
                    E.tensor_tensor(out=yv(0), in0=s_v[:, :], in1=Cv(n),
                                    op=ALU.mult).inc()
                else:
                    E.tensor_tensor(out=t_v[:, :], in0=s_v[:, :], in1=Cv(n),
                                    op=ALU.mult).inc()
                    E.fence()
                    E.tensor_tensor(out=yv(0), in0=yv(0), in1=t_v[:, :],
                                    op=ALU.add).inc()
                bld.mark(f"s1_p_{bg}_{n}_0", "ve")
                E.wait_ge("ge", bld.at(f"s1_gdbu_{bg}_{n}"))
                E.wait_ge("se", bld.at(f"s1_dA_{bg}_{n}_1"))
                if n > 0:
                    E.wait_ge("ge", bld.at(f"s1_gp_{bg}_{n-1}"))
                elif bg > 0:
                    E.wait_ge("ge", bld.at(f"s1_gp_{bg-1}_15"))
                E.fence()
                E.tensor_tensor_scan(out=s_g[:, :], data0=dA2[:, L:2 * L],
                                     data1=dbu_g[:, :], initial=0.0,
                                     op0=ALU.mult, op1=ALU.add).inc()
                bld.mark(f"s1_scan_{bg}_{n}_1", "ve")
            E.wait_ge("ge", bld.at(f"s1_gnd_{bg}"))
            for i in range(G):
                it = g * G + i
                E.fence()
                E.scalar_tensor_tensor(out=t_v[:, :], in0=uv(i),
                                       scalar=K["d_t"][:, it:it + 1],
                                       in1=yv(i), op0=ALU.mult, op1=ALU.add).inc()
                E.fence()
                E.tensor_tensor(out=ystg[:, i * L:(i + 1) * L], in0=t_v[:, :],
                                in1=sg2[:, i * L:(i + 1) * L], op=ALU.mult).inc()
            bld.mark(f"s1_tail_{bg}", "ve")
            bld.mark(f"s1_grpdone_{bg}", "ve")

    def gpsimd(E):
        for bg in range(NBG):
            b, g = divmod(bg, NIT)
            E.wait_ge("ve", bld.at(f"s1_dud_{bg}"))
            for n in range(16):
                E.wait_ge("se", bld.at(f"s1_bcc_{bg}_{n}_{NCL-1}"))
                E.fence()
                E.tensor_tensor(out=dbu_g[:, :], in0=duv(1), in1=Bv(n),
                                op=ALU.mult).inc()
                bld.mark(f"s1_gdbu_{bg}_{n}", "ge")
                E.wait_ge("ve", bld.at(f"s1_scan_{bg}_{n}_1"))
                E.wait_ge("se", bld.at(f"s1_bcc_{bg}_{n}_{2*NCL-1}"))
                E.fence()
                if n == 0:
                    E.tensor_tensor(out=yv(1), in0=s_g[:, :], in1=Cv(n),
                                    op=ALU.mult).inc()
                else:
                    E.tensor_tensor(out=t_g[:, :], in0=s_g[:, :], in1=Cv(n),
                                    op=ALU.mult).inc()
                    E.fence()
                    E.tensor_tensor(out=yv(1), in0=yv(1), in1=t_g[:, :],
                                    op=ALU.add).inc()
                bld.mark(f"s1_gp_{bg}_{n}", "ge")
            bld.mark(f"s1_gnd_{bg}", "ge")
            bld.mark(f"s1_gdone_{bg}", "ge")
            E.wait_ge("ve", bld.at(f"s1_tail_{bg}"))
            E.dma(bld.SC["y1_s"][g * G * 128:(g + 1) * G * 128, b * L:(b + 1) * L]
                  .rearrange("(q p) x -> p q x", p=128),
                  ystg.ap().rearrange("p (q x) -> p q x", x=L), sem="gd0")
            bld.mark(f"s1_yst_{bg}", "gd0")
        E.wait_ge("gd0", bld.C["gd0"])

    return {"sync": sync, "tensor": tensor, "scalar": scalar, "vector": vector,
            "gpsimd": gpsimd}


def ph_moe(bld, es):
    cfg = bld.cfg
    NS, T = cfg["NS"], cfg["T"]
    K = bld.K
    ps = bld.ps
    NMT = 11
    wgr = bld.sb(es, "mo_wg", [128, NMT * 2048], BF16)
    wur = bld.sb(es, "mo_wu", [128, NMT * 2048], BF16)
    wdr = bld.sb(es, "mo_wd", [128, 16 * NMT * 128], BF16)
    xb = bld.sb(es, "mo_x", [128, 16 * 512], BF16)
    wrow = bld.sb(es, "mo_wr", [1, T], BF16)
    gtmp = bld.sb(es, "mo_gt", [128, 512], F32)
    wh = bld.sb(es, "mo_wh", [128, NMT * 512], BF16)
    dstage = bld.sb(es, "mo_ds", [128, 4 * 512], F32)
    tmp = bld.sb(es, "mo_tmp", [128, 512], F32)
    acc = bld.SC["acc1_s"]

    def wsrc(e):
        if e < 8:
            return (bld.IN["wg_b"][e * 1408:(e + 1) * 1408, :],
                    bld.IN["wu_b1"][e * 1408:(e + 1) * 1408, :],
                    lambda j: bld.IN["wdn_b"][e * 2048 + j * 128:e * 2048 + (j + 1) * 128, :])
        half = e - 8
        return (bld.IN["swg_b"][half * 1408:(half + 1) * 1408, :],
                bld.IN["swu_b"][half * 1408:(half + 1) * 1408, :],
                lambda j: bld.IN["swd_b"][j * 128:(j + 1) * 128,
                                          half * 1408:(half + 1) * 1408])

    def gpsimd(E):
        for e in range(10):
            gd, ud, ddf = wsrc(e)
            if e >= 1:
                E.wait_ge("pe", bld.at(f"mo_edone_{e-1}"))
            for j in range(NMT):
                E.dma(wgr[:, j * 2048:(j + 1) * 2048], gd[j * 128:(j + 1) * 128, :],
                      sem="wl0")
                E.dma(wur[:, j * 2048:(j + 1) * 2048], ud[j * 128:(j + 1) * 128, :],
                      sem="wl0")
            for j in range(16):
                E.dma(wdr[:, j * NMT * 128:(j + 1) * NMT * 128], ddf(j), sem="wl0")
            bld.mark(f"mo_wld_{e}", "wl0")
            for s in range(NS):
                for d4 in range(4):
                    E.wait_ge("se", bld.at(f"mo_dcp_{e}_{s}_{d4 * 4 + 3}"))
                    if e == 0:
                        E.dma(acc[d4 * 4 * 128:(d4 + 1) * 4 * 128,
                                  s * 512:(s + 1) * 512]
                              .rearrange("(q p) x -> p q x", p=128),
                              dstage.ap().rearrange("p (q x) -> p q x", x=512),
                              sem="gd0")
                    else:
                        bld.C["gd0"] += 16
                        if not bld.pass1:
                            E._raw.dma_start(
                                out=acc[d4 * 4 * 128:(d4 + 1) * 4 * 128,
                                        s * 512:(s + 1) * 512]
                                .rearrange("(q p) x -> p q x", p=128),
                                in_=dstage.ap().rearrange("p (q x) -> p q x", x=512),
                                accum_op=ALU.add,
                            ).then_inc(bld.sems["gd0"], 16)
                    bld.mark(f"mo_dst_{e}_{s}_{d4}", "gd0")
        E.wait_ge("gd0", bld.C["gd0"])

    def sync(E):
        for e in range(10):
            if e < 8:
                if e >= 1:
                    E.wait_ge("pe", bld.at(f"mo_cb_{e-1}_{bld.cfg['NS']-1}"))
                E.dma(wrow[0:1, :], bld.SC["wfm_s"][e:e + 1, :], sem="ld2")
                bld.mark(f"mo_wrow_{e}", "ld2")
            for s in range(NS):
                if e + s > 0:
                    E.wait_ge("pe", bld.at(f"mo_gu_{e}_{s-1}" if s > 0
                                           else f"mo_gu_{e-1}_{NS-1}"))
                for h in range(2):
                    E.dma(xb[:, h * 8 * 512:(h + 1) * 8 * 512]
                          .rearrange("p (kt x) -> p kt x", x=512),
                          bld.SC["xn1_s"][h * 8 * 128:(h + 1) * 8 * 128,
                                          s * 512:(s + 1) * 512]
                          .rearrange("(kt p) x -> p kt x", p=128), sem="ld0")
                bld.mark(f"mo_rhs_{e}_{s}", "ld0")

    def tensor(E):
        for e in range(10):
            E.wait_ge("wl0", bld.at(f"mo_wld_{e}"))
            if e < 8:
                E.wait_ge("ld2", bld.at(f"mo_wrow_{e}"))
            for s in range(NS):
                E.wait_ge("ld0", bld.at(f"mo_rhs_{e}_{s}"))
                if e < 8:
                    if e + s > 0:
                        E.wait_ge("ve", bld.at(f"mo_wh_{e}_{s-1}_{NMT-1}" if s > 0
                                               else f"mo_wh_{e-1}_{NS-1}_{NMT-1}"))
                    E.matmul(ps[4][:, :], lhsT=K["ones_row_bf"][0:1, :],
                             rhs=wrow[0:1, s * 512:(s + 1) * 512],
                             start=True, stop=True).inc()
                    bld.mark(f"mo_cb_{e}_{s}", "pe")
                for mt in range(NMT):
                    bg, bu = mt % 2, 2 + mt % 2
                    if mt >= 2:
                        E.wait_ge("se", bld.at(f"mo_sil_{e}_{s}_{mt-2}"))
                        E.wait_ge("ve", bld.at(f"mo_wh1_{e}_{s}_{mt-2}"))
                    elif e + s > 0:
                        pk = (f"{e}_{s-1}" if s > 0 else f"{e-1}_{NS-1}")
                        E.wait_ge("se", bld.at(f"mo_sil_{pk}_{NMT + mt - 2}"))
                        E.wait_ge("ve", bld.at(f"mo_wh1_{pk}_{NMT + mt - 2}"))
                    for kt in range(16):
                        mm = E.matmul(ps[bg][:, :],
                                      lhsT=wgr[:, mt * 2048 + kt * 128:mt * 2048 + (kt + 1) * 128],
                                      rhs=xb[:, kt * 512:(kt + 1) * 512],
                                      start=(kt == 0), stop=(kt == 15))
                    mm.inc()
                    bld.mark(f"mo_gmm_{e}_{s}_{mt}", "pe")
                    for kt in range(16):
                        mm = E.matmul(ps[bu][:, :],
                                      lhsT=wur[:, mt * 2048 + kt * 128:mt * 2048 + (kt + 1) * 128],
                                      rhs=xb[:, kt * 512:(kt + 1) * 512],
                                      start=(kt == 0), stop=(kt == 15))
                    mm.inc()
                    bld.mark(f"mo_umm_{e}_{s}_{mt}", "pe")
                bld.mark(f"mo_gu_{e}_{s}", "pe")
                E.wait_ge("ve", bld.at(f"mo_wh_{e}_{s}_{NMT-1}"))
                for dt in range(16):
                    bank = 5 + dt % 2
                    if dt >= 2:
                        E.wait_ge("se", bld.at(f"mo_dcp_{e}_{s}_{dt-2}"))
                    elif e + s > 0:
                        pk = (f"{e}_{s-1}" if s > 0 else f"{e-1}_{NS-1}")
                        E.wait_ge("se", bld.at(f"mo_dcp_{pk}_{14 + dt}"))
                    for j in range(NMT):
                        mm = E.matmul(ps[bank][:, :],
                                      lhsT=wdr[:, (dt * NMT + j) * 128:(dt * NMT + j + 1) * 128],
                                      rhs=wh[:, j * 512:(j + 1) * 512],
                                      start=(j == 0), stop=(j == NMT - 1))
                    mm.inc()
                    bld.mark(f"mo_dmm_{e}_{s}_{dt}", "pe")
            bld.mark(f"mo_edone_{e}", "pe")

    def scalar(E):
        for e in range(10):
            for s in range(NS):
                for mt in range(NMT):
                    bg = mt % 2
                    E.wait_ge("pe", bld.at(f"mo_gmm_{e}_{s}_{mt}"))
                    idx = (e * NS + s) * NMT + mt
                    if idx > 0:
                        pidx = idx - 1
                        pe_, ps_, pm_ = pidx // (NS * NMT), (pidx // NMT) % NS, pidx % NMT
                        E.wait_ge("ve", bld.at(f"mo_wh1_{pe_}_{ps_}_{pm_}"))
                    E.activation(out=gtmp[:, :], in_=ps[bg][:, :], func=AF.Silu).inc()
                    bld.mark(f"mo_sil_{e}_{s}_{mt}", "se")
                for dt in range(16):
                    bank = 5 + dt % 2
                    E.wait_ge("pe", bld.at(f"mo_dmm_{e}_{s}_{dt}"))
                    if dt % 4 == 0:
                        d4p = dt // 4 - 1
                        if d4p >= 0:
                            E.wait_ge("gd0", bld.at(f"mo_dst_{e}_{s}_{d4p}"))
                        elif e + s > 0:
                            pk = (f"{e}_{s-1}" if s > 0 else f"{e-1}_{NS-1}")
                            E.wait_ge("gd0", bld.at(f"mo_dst_{pk}_3"))
                    E.activation(out=dstage[:, (dt % 4) * 512:(dt % 4 + 1) * 512],
                                 in_=ps[bank][:, :], func=AF.Copy).inc()
                    bld.mark(f"mo_dcp_{e}_{s}_{dt}", "se")

    def vector(E):
        for e in range(10):
            for s in range(NS):
                for mt in range(NMT):
                    bu = 2 + mt % 2
                    E.wait_ge("se", bld.at(f"mo_sil_{e}_{s}_{mt}"))
                    E.wait_ge("pe", bld.at(f"mo_umm_{e}_{s}_{mt}"))
                    if s == 0 and mt == 0 and e > 0:
                        E.wait_ge("pe", bld.at(f"mo_edone_{e-1}"))
                    elif mt == 0 and s > 0:
                        E.wait_ge("pe", bld.at(f"mo_dmm_{e}_{s-1}_15"))
                    E.fence()
                    if e < 8:
                        E.wait_ge("pe", bld.at(f"mo_cb_{e}_{s}"))
                        E.tensor_tensor(out=tmp[:, :], in0=gtmp[:, :],
                                        in1=ps[bu][:, :], op=ALU.mult).inc()
                        bld.mark(f"mo_wh1_{e}_{s}_{mt}", "ve")
                        E.fence()
                        E.tensor_tensor(out=wh[:, mt * 512:(mt + 1) * 512],
                                        in0=tmp[:, :], in1=ps[4][:, :],
                                        op=ALU.mult).inc()
                    else:
                        E.tensor_tensor(out=wh[:, mt * 512:(mt + 1) * 512],
                                        in0=gtmp[:, :], in1=ps[bu][:, :],
                                        op=ALU.mult).inc()
                        bld.mark(f"mo_wh1_{e}_{s}_{mt}", "ve")
                    bld.mark(f"mo_wh_{e}_{s}_{mt}", "ve")

    return {"gpsimd": gpsimd, "sync": sync, "tensor": tensor, "scalar": scalar,
            "vector": vector}

MOESCAN = r"""Merged MoE + selective-scan phase: round-robin unit schedule so
TensorE runs the MoE while VectorE runs the scan. GpSimd does the MoE
elementwise (from SBUF evacuations; Pool cannot touch PSUM)."""


def ph_moescan(bld, es):
    cfg = bld.cfg
    Bc, L, NCL, NS, T = cfg["Bc"], cfg["L"], cfg["NCL"], cfg["NS"], cfg["T"]
    K = bld.K
    ps = bld.ps
    G = 2
    NIT = 32 // G
    NBG = Bc * NIT
    delta2 = bld.sb(es, "ms_d", [128, G * L], F32)
    u2 = bld.sb(es, "ms_u", [128, G * L], BF16)
    du2 = bld.sb(es, "ms_du", [128, G * L], BF16)
    dA2 = bld.sb(es, "ms_dA", [128, G * L], F32)
    dbu_v = bld.sb(es, "ms_dbv", [128, L], BF16)
    s_v = bld.sb(es, "ms_sv", [128, L], F32)
    t_v = bld.sb(es, "ms_tv", [128, L], F32)
    yac = bld.sb(es, "ms_y", [128, G * L], F32)
    Bb = bld.sb(es, "ms_Bb", [128, 2 * L], BF16)
    Cb = bld.sb(es, "ms_Cb", [128, 2 * L], BF16)
    bcrow = bld.sb(es, "ms_bcr", [1, L], BF16)
    ystg = du2
    NMT = 11
    arena = bld.sb(es, "ms_ar", [128, 25088], BF16)
    xb = bld.sb(es, "ms_x", [128, 16 * 512], BF16)
    whl = bld.sb(es, "ms_whl", [128, NMT * 512], BF16)
    gtmp = bld.sb(es, "ms_gt", [128, 512], F32)
    usb = bld.sb(es, "ms_us", [128, 512], F32)
    comb_sb = bld.sb(es, "ms_cmb", [128, 512], F32)
    whst = bld.sb(es, "ms_whst", [128, 2 * 512], BF16)
    wrow_sl = bld.sb(es, "ms_wr", [1, 512], BF16)
    dstage = bld.sb(es, "ms_ds", [128, 2 * 512], F32)
    acc = bld.SC["acc1_s"]
    whs = bld.SC["whs_s"]

    def dv(i):
        return delta2[:, i * L:(i + 1) * L]

    def uv(i):
        return u2[:, i * L:(i + 1) * L]

    def duv(i):
        return du2[:, i * L:(i + 1) * L]

    def yv(i):
        return yac[:, i * L:(i + 1) * L]

    def Bv(n):
        return Bb[:, (n % 2) * L:(n % 2 + 1) * L]

    def Cv(n):
        return Cb[:, (n % 2) * L:(n % 2 + 1) * L]


    def su_sync(E, bg, n):
        b, g = divmod(bg, NIT)
        if n == 0:
            if bg >= 1:
                E.wait_ge("ve", bld.at(f"ms_grpdone_{bg-1}"))
                E.wait_ge("gd1", bld.at(f"ms_yst_{bg-1}"))
            r0 = g * G * 128
            E.dma(delta2.ap().rearrange("p (q x) -> p q x", x=L),
                  bld.SC["delta_s"][r0:r0 + G * 128, b * L:(b + 1) * L]
                  .rearrange("(q p) x -> p q x", p=128), sem="ld2")
            E.dma(u2.ap().rearrange("p (q x) -> p q x", x=L),
                  bld.SC["hconv_s"][r0:r0 + G * 128, b * L:(b + 1) * L]
                  .rearrange("(q p) x -> p q x", p=128), sem="ld2")
            bld.mark(f"ms_ld_{bg}", "ld2")
        if bg + n > 0:
            E.wait_ge("pe", bld.at(f"ms_bcmm_{bg}_{n-1}" if n > 0
                                   else f"ms_bcmm_{bg-1}_15"))
        E.dma(bcrow[0:1, 0:L],
              bld.SC["bc_s"][n:n + 1, b * L:(b + 1) * L], sem="ld1")
        bld.mark(f"ms_bcldB_{bg}_{n}", "ld1")
        E.wait_ge("pe", bld.at(f"ms_bcbB_{bg}_{n}"))
        E.dma(bcrow[0:1, 0:L],
              bld.SC["bc_s"][16 + n:17 + n, b * L:(b + 1) * L], sem="ld1")
        bld.mark(f"ms_bcldC_{bg}_{n}", "ld1")

    def su_tensor(E, bg, n):
        for j in range(2):
            E.wait_ge("ld1", bld.at(f"ms_bcldB_{bg}_{n}" if j == 0
                                    else f"ms_bcldC_{bg}_{n}"))
            for lc in range(NCL):
                u = j * NCL + lc
                bank = 4 if u % 2 == 0 else 6
                if u >= 2:
                    E.wait_ge("se", bld.at(f"ms_bcc_{bg}_{n}_{u-2}"))
                elif bg + n > 0:
                    pu = 2 * NCL - 2 + u
                    E.wait_ge("se", bld.at(f"ms_bcc_{bg}_{n-1}_{pu}" if n > 0
                                           else f"ms_bcc_{bg-1}_15_{pu}"))
                E.matmul(ps[bank][:, :512], lhsT=K["ones_row_bf"][:, :],
                         rhs=bcrow[0:1, lc * 512:(lc + 1) * 512],
                         start=True, stop=True).inc()
                bld.mark(f"ms_bcb_{bg}_{n}_{u}", "pe")
            if j == 0:
                bld.mark(f"ms_bcbB_{bg}_{n}", "pe")
        bld.mark(f"ms_bcmm_{bg}_{n}", "pe")

    def su_scalar(E, bg, n):
        b, g = divmod(bg, NIT)
        if n == 0:
            E.wait_ge("ld2", bld.at(f"ms_ld_{bg}"))
        for j in range(2):
            buf = Bb if j == 0 else Cb
            for lc in range(NCL):
                u = j * NCL + lc
                bank = 4 if u % 2 == 0 else 6
                E.wait_ge("pe", bld.at(f"ms_bcb_{bg}_{n}_{u}"))
                if j == 0:
                    if bg + n > 1:
                        E.wait_ge("ve", bld.at(f"ms_dbu_{bg}_{n-2}_1" if n >= 2 else
                                               (f"ms_dbu_{bg-1}_{14+n}_1")))
                else:
                    if bg + n > 1:
                        E.wait_ge("ve", bld.at(f"ms_p_{bg}_{n-2}_1" if n >= 2 else
                                               (f"ms_p_{bg-1}_{14+n}_1")))
                E.activation(out=buf[:, (n % 2) * L + lc * 512:
                                     (n % 2) * L + (lc + 1) * 512],
                             in_=ps[bank][:, :512], func=AF.Copy).inc()
                bld.mark(f"ms_bcc_{bg}_{n}_{u}", "se")
        for i in range(G):
            it = g * G + i
            if n > 0:
                E.wait_ge("ve", bld.at(f"ms_scan_{bg}_{n-1}_{i}"))
            elif bg > 0:
                E.wait_ge("ve", bld.at(f"ms_scan_{bg-1}_15_{i}"))
            E.activation(out=dA2[:, i * L:(i + 1) * L], in_=dv(i), func=AF.Exp,
                         scale=K["a_t"][:, it * 16 + n:it * 16 + n + 1]).inc()
            bld.mark(f"ms_dA_{bg}_{n}_{i}", "se")

    def su_vector(E, bg, n):
        b, g = divmod(bg, NIT)
        if n == 0:
            E.wait_ge("ld2", bld.at(f"ms_ld_{bg}"))
            if bg >= 1:
                E.wait_ge("gd1", bld.at(f"ms_yst_{bg-1}"))
            E.fence()
            for i in range(G):
                E.tensor_tensor(out=duv(i), in0=dv(i), in1=uv(i), op=ALU.mult).inc()
            bld.mark(f"ms_dud_{bg}", "ve")
        for i in range(G):
            E.wait_ge("se", bld.at(f"ms_bcc_{bg}_{n}_{NCL-1}"))
            E.fence()
            E.tensor_tensor(out=dbu_v[:, :], in0=duv(i), in1=Bv(n), op=ALU.mult).inc()
            bld.mark(f"ms_dbu_{bg}_{n}_{i}", "ve")
            E.wait_ge("se", bld.at(f"ms_dA_{bg}_{n}_{i}"))
            E.fence()
            E.tensor_tensor_scan(out=s_v[:, :], data0=dA2[:, i * L:(i + 1) * L],
                                 data1=dbu_v[:, :], initial=0.0,
                                 op0=ALU.mult, op1=ALU.add).inc()
            bld.mark(f"ms_scan_{bg}_{n}_{i}", "ve")
            E.wait_ge("se", bld.at(f"ms_bcc_{bg}_{n}_{2*NCL-1}"))
            E.fence()
            if n == 0:
                E.tensor_tensor(out=yv(i), in0=s_v[:, :], in1=Cv(n), op=ALU.mult).inc()
            else:
                E.tensor_tensor(out=t_v[:, :], in0=s_v[:, :], in1=Cv(n),
                                op=ALU.mult).inc()
                E.fence()
                E.tensor_tensor(out=yv(i), in0=yv(i), in1=t_v[:, :], op=ALU.add).inc()
            bld.mark(f"ms_p_{bg}_{n}_{i}", "ve")
        if n == 15:
            for i in range(G):
                it = g * G + i
                E.wait_ge("xa1", bld.at(f"ms_sgld_{bg}_{i}"))
                E.fence()
                E.scalar_tensor_tensor(out=t_v[:, :], in0=uv(i),
                                       scalar=K["d_t"][:, it:it + 1],
                                       in1=yv(i), op0=ALU.mult, op1=ALU.add).inc()
                E.fence()
                E.tensor_tensor(out=ystg[:, i * L:(i + 1) * L], in0=t_v[:, :],
                                in1=dbu_v[:, :], op=ALU.mult).inc()
                bld.mark(f"ms_tail_{bg}_{i}", "ve")
            bld.mark(f"ms_grpdone_{bg}", "ve")

    def su_sync_tail(E, bg):
        b, g = divmod(bg, NIT)
        r0 = g * G * 128
        for i in range(G):
            E.wait_ge("ve", bld.at(f"ms_scan_{bg}_15_{G-1}" if i == 0
                                   else f"ms_tail_{bg}_{i-1}"))
            E.dma(dbu_v[:, :], bld.SC["sg_s"][r0 + i * 128:r0 + (i + 1) * 128,
                                              b * L:(b + 1) * L], sem="xa1")
            bld.mark(f"ms_sgld_{bg}_{i}", "xa1")

    def su_gpsimd(E, bg):
        b, g = divmod(bg, NIT)
        E.wait_ge("ve", bld.at(f"ms_grpdone_{bg}"))
        E.dma(bld.SC["y1_s"][g * G * 128:(g + 1) * G * 128, b * L:(b + 1) * L]
              .rearrange("(q p) x -> p q x", p=128),
              ystg.ap().rearrange("p (q x) -> p q x", x=L), sem="gd1")
        bld.mark(f"ms_yst_{bg}", "gd1")

    passes = []
    for e in range(10):
        for h in range(2):
            mts = list(range(h * 6, min(NMT, h * 6 + 6)))
            passes.append((e, h, mts))

    def wsrc(e):
        if e < 8:
            return (bld.IN["wg_b"][e * 1408:(e + 1) * 1408, :],
                    bld.IN["wu_b1"][e * 1408:(e + 1) * 1408, :],
                    lambda j: bld.IN["wdn_b"][e * 2048 + j * 128:e * 2048 + (j + 1) * 128, :])
        half = e - 8
        return (bld.IN["swg_b"][half * 1408:(half + 1) * 1408, :],
                bld.IN["swu_b"][half * 1408:(half + 1) * 1408, :],
                lambda j: bld.IN["swd_b"][j * 128:(j + 1) * 128,
                                          half * 1408:(half + 1) * 1408])

    ps5 = {"mark": None}
    kkbase = {}
    _acc = 0
    for _pi, (_e, _h, _mts) in enumerate(passes):
        for _s in range(NS):
            kkbase[(_pi, _s)] = _acc
            _acc += len(_mts)
    KKTOT = _acc

    def up_gpsimd(E, pi, s):
        e, h, mts = passes[pi]
        if s == 0:
            gd, ud, _ = wsrc(e)
            if pi >= 1:
                E.wait_ge("pe", bld.at(f"ms_passdone_{pi-1}"))
            if h == 0 and e > 0:
                E.wait_ge("pe", bld.at(f"ms_dmm_{(e - 1) * NS + NS - 1}_15"))
            for k, mt in enumerate(mts):
                E.dma(arena[:, k * 2048:(k + 1) * 2048],
                      gd[mt * 128:(mt + 1) * 128, :], sem="wl0")
                E.dma(arena[:, (6 + k) * 2048:(7 + k) * 2048],
                      ud[mt * 128:(mt + 1) * 128, :], sem="wl0")
            bld.mark(f"ms_wld_{pi}", "wl0")

    def up_sync(E, pi, s):
        e, h, mts = passes[pi]
        g = pi * NS + s
        if g >= 1:
            E.wait_ge("pe", bld.at(f"ms_gu_{g-1}"))
        for hh in range(2):
            E.dma(xb[:, hh * 8 * 512:(hh + 1) * 8 * 512]
                  .rearrange("p (kt x) -> p kt x", x=512),
                  bld.SC["xn1_s"][hh * 8 * 128:(hh + 1) * 8 * 128,
                                  s * 512:(s + 1) * 512]
                  .rearrange("(kt p) x -> p kt x", p=128), sem="ld0")
        bld.mark(f"ms_rhs_{g}", "ld0")
        if e < 8:
            E.wait_ge("se", bld.at(f"ms_cmbcp_{pi}_{s-1}" if s > 0
                                   else (f"ms_cmbcp_{pi-1}_{NS-1}" if pi > 0 else "zz")))
            E.dma(wrow_sl[0:1, :], bld.SC["wfm_s"][e:e + 1, s * 512:(s + 1) * 512],
                  sem="xa0")
            bld.mark(f"ms_wrow_{pi}_{s}", "xa0")

    def up_tensor(E, pi, s):
        e, h, mts = passes[pi]
        g = pi * NS + s
        E.wait_ge("wl0", bld.at(f"ms_wld_{pi}"))
        E.wait_ge("ld0", bld.at(f"ms_rhs_{g}"))
        if e < 8:
            E.wait_ge("xa0", bld.at(f"ms_wrow_{pi}_{s}"))
            if ps5["mark"] is not None:
                E.wait_ge("se", bld.at(ps5["mark"]))
            E.matmul(ps[5][:, :], lhsT=K["ones_row_bf"][0:1, :],
                     rhs=wrow_sl[0:1, :], start=True, stop=True).inc()
            bld.mark(f"ms_cmb_{pi}_{s}", "pe")
            ps5["mark"] = f"ms_cmbcp_{pi}_{s}"
        for k, mt in enumerate(mts):
            bg_, bu_ = k % 2, 2 + k % 2
            kk = kkbase[(pi, s)] + k
            if kk >= 2:
                E.wait_ge("se", bld.at(f"ms_sil_{kk-2}"))
                E.wait_ge("se", bld.at(f"ms_ucp_{kk-2}"))
            for kt in range(16):
                mm = E.matmul(ps[bg_][:, :], lhsT=arena[:, k * 2048 + kt * 128:
                                                        k * 2048 + (kt + 1) * 128],
                              rhs=xb[:, kt * 512:(kt + 1) * 512],
                              start=(kt == 0), stop=(kt == 15))
            mm.inc()
            bld.mark(f"ms_gmm_{kk}", "pe")
            for kt in range(16):
                mm = E.matmul(ps[bu_][:, :], lhsT=arena[:, (6 + k) * 2048 + kt * 128:
                                                        (6 + k) * 2048 + (kt + 1) * 128],
                              rhs=xb[:, kt * 512:(kt + 1) * 512],
                              start=(kt == 0), stop=(kt == 15))
            mm.inc()
            bld.mark(f"ms_umm_{kk}", "pe")
        bld.mark(f"ms_gu_{g}", "pe")

    def up_scalar(E, pi, s):
        e, h, mts = passes[pi]
        g = pi * NS + s
        if e < 8:
            E.wait_ge("pe", bld.at(f"ms_cmb_{pi}_{s}"))
            if g >= 1:
                E.wait_ge("ge", bld.at(f"ms_wht_{kkbase[(pi, s)] - 1}"))
            E.activation(out=comb_sb[:, :], in_=ps[5][:, :], func=AF.Copy).inc()
            bld.mark(f"ms_cmbcp_{pi}_{s}", "se")
        for k, mt in enumerate(mts):
            bg_, bu_ = k % 2, 2 + k % 2
            kk = kkbase[(pi, s)] + k
            E.wait_ge("pe", bld.at(f"ms_gmm_{kk}"))
            if kk >= 1:
                E.wait_ge("ge", bld.at(f"ms_wht_{kk-1}"))
            E.activation(out=gtmp[:, :], in_=ps[bg_][:, :], func=AF.Silu).inc()
            bld.mark(f"ms_sil_{kk}", "se")
            E.wait_ge("pe", bld.at(f"ms_umm_{kk}"))
            E.activation(out=usb[:, :], in_=ps[bu_][:, :], func=AF.Copy).inc()
            bld.mark(f"ms_ucp_{kk}", "se")
            E.wait_ge("ge", bld.at(f"ms_wht_{kk}"))
            E.fence("se")
            E.dma(whs[(e * 1408 if e < 8 else (8 * 1408 + (e - 8) * 1408)) +
                      mt * 128:(e * 1408 if e < 8 else (8 * 1408 + (e - 8) * 1408)) +
                      (mt + 1) * 128, s * 512:(s + 1) * 512],
                  whst[:, (kk % 2) * 512:(kk % 2 + 1) * 512], sem=f"sd{kk % 2}")
            bld.mark(f"ms_whstd_{kk}", f"sd{kk % 2}")

    def up_gpsimd2(E, pi, s):
        e, h, mts = passes[pi]
        g = pi * NS + s
        for k, mt in enumerate(mts):
            kk = kkbase[(pi, s)] + k
            E.wait_ge("se", bld.at(f"ms_sil_{kk}"))
            E.wait_ge("se", bld.at(f"ms_ucp_{kk}"))
            if kk >= 2:
                E.wait_ge(f"sd{kk % 2}", bld.at(f"ms_whstd_{kk-2}"))
            E.fence()
            if e < 8:
                E.tensor_tensor(out=usb[:, :], in0=gtmp[:, :], in1=usb[:, :],
                                op=ALU.mult).inc()
                E.fence()
                E.tensor_tensor(out=whst[:, (kk % 2) * 512:(kk % 2 + 1) * 512],
                                in0=usb[:, :], in1=comb_sb[:, :], op=ALU.mult).inc()
            else:
                E.tensor_tensor(out=whst[:, (kk % 2) * 512:(kk % 2 + 1) * 512],
                                in0=gtmp[:, :], in1=usb[:, :], op=ALU.mult).inc()
            bld.mark(f"ms_wht_{kk}", "ge")


    def dn_sync(E, e, s):
        d = e * NS + s
        if d >= 1:
            E.wait_ge("pe", bld.at(f"ms_dmm_{d-1}_15"))
        for b2 in range(2):
            E.wait_ge(f"sd{b2}", bld.at(f"ms_upallst_{e}_{b2}"))
        base = e * 1408 if e < 8 else 8 * 1408 + (e - 8) * 1408
        E.dma(whl.ap().rearrange("p (q x) -> p q x", x=512),
              whs[base:base + 1408, s * 512:(s + 1) * 512]
              .rearrange("(q p) x -> p q x", p=128), sem="ld3")
        bld.mark(f"ms_whl_{d}", "ld3")

    def dn_tensor(E, e, s):
        d = e * NS + s
        E.wait_ge("ld3", bld.at(f"ms_whl_{d}"))
        if s == 0:
            E.wait_ge("wl1", bld.at(f"ms_dwld_{e}"))
        for dt in range(16):
            if ps5["mark"] is not None:
                E.wait_ge("se", bld.at(ps5["mark"]))
            for j in range(NMT):
                mm = E.matmul(ps[5][:, :], lhsT=arena[:, (dt * NMT + j) * 128:
                                                      (dt * NMT + j + 1) * 128],
                              rhs=whl[:, j * 512:(j + 1) * 512],
                              start=(j == 0), stop=(j == NMT - 1))
            mm.inc()
            bld.mark(f"ms_dmm_{d}_{dt}", "pe")
            ps5["mark"] = f"ms_dcp_{d}_{dt}"

    def dn_gpsimd(E, e, s):
        d = e * NS + s
        if s == 0:
            _, _, ddf = wsrc(e)
            E.wait_ge("pe", bld.at(f"ms_gu_{(2 * e + 2) * NS - 1}"))
            for j in range(16):
                E.dma(arena[:, j * 1408:(j + 1) * 1408], ddf(j), sem="wl1")
            bld.mark(f"ms_dwld_{e}", "wl1")
        for d4 in range(8):
            E.wait_ge("se", bld.at(f"ms_dcp_{d}_{d4 * 2 + 1}"))
            if e == 0:
                E.dma(acc[d4 * 2 * 128:(d4 + 1) * 2 * 128, s * 512:(s + 1) * 512]
                      .rearrange("(q p) x -> p q x", p=128),
                      dstage.ap().rearrange("p (q x) -> p q x", x=512), sem="gd0")
            else:
                bld.C["gd0"] += 16
                if not bld.pass1:
                    E._raw.dma_start(
                        out=acc[d4 * 2 * 128:(d4 + 1) * 2 * 128,
                                s * 512:(s + 1) * 512]
                        .rearrange("(q p) x -> p q x", p=128),
                        in_=dstage.ap().rearrange("p (q x) -> p q x", x=512),
                        accum_op=ALU.add,
                    ).then_inc(bld.sems["gd0"], 16)
            bld.mark(f"ms_dst_{d}_{d4}", "gd0")

    def dn_scalar(E, e, s):
        d = e * NS + s
        for dt in range(16):
            E.wait_ge("pe", bld.at(f"ms_dmm_{d}_{dt}"))
            if dt % 2 == 0:
                d4p = dt // 2 - 1
                if d4p >= 0:
                    E.wait_ge("gd0", bld.at(f"ms_dst_{d}_{d4p}"))
                elif d >= 1:
                    E.wait_ge("gd0", bld.at(f"ms_dst_{d-1}_7"))
            E.activation(out=dstage[:, (dt % 2) * 512:(dt % 2 + 1) * 512],
                         in_=ps[5][:, :], func=AF.Copy).inc()
            bld.mark(f"ms_dcp_{d}_{dt}", "se")

    scan_units = [("s", bg, n) for bg in range(NBG) for n in range(16)]
    moe_units = []
    for e in range(10):
        for h in range(2):
            pi = e * 2 + h
            for s in range(NS):
                moe_units.append(("u", pi, s))
        for s in range(NS):
            moe_units.append(("d", e, s))
    sched = []
    mi = 0
    for k, su in enumerate(scan_units):
        sched.append(su)
        want = (k + 1) * len(moe_units) // len(scan_units)
        while mi < want:
            sched.append(moe_units[mi])
            mi += 1
    while mi < len(moe_units):
        sched.append(moe_units[mi])
        mi += 1

    def mark_upall(E, e):
        for b2 in range(2):
            bld.mark(f"ms_upallst_{e}_{b2}", f"sd{b2}")

    def body(engine):
        def run(E):
            seen_upend = set()
            for unit in sched:
                if unit[0] == "s":
                    _, bg, n = unit
                    if engine == "sync":
                        su_sync(E, bg, n)
                        if n == 15:
                            su_sync_tail(E, bg)
                    elif engine == "tensor":
                        su_tensor(E, bg, n)
                    elif engine == "scalar":
                        su_scalar(E, bg, n)
                    elif engine == "vector":
                        su_vector(E, bg, n)
                    elif engine == "gpsimd" and n == 15:
                        su_gpsimd(E, bg)
                elif unit[0] == "u":
                    _, pi, s = unit
                    e, h, mts = passes[pi]
                    if engine == "sync":
                        up_sync(E, pi, s)
                    elif engine == "gpsimd":
                        up_gpsimd(E, pi, s)
                        up_gpsimd2(E, pi, s)
                    elif engine == "tensor":
                        up_tensor(E, pi, s)
                        if s == NS - 1:
                            bld.mark(f"ms_passdone_{pi}", "pe")
                    elif engine == "scalar":
                        up_scalar(E, pi, s)
                        if s == NS - 1 and h == 1 and engine == "scalar":
                            mark_upall(E, e)
                else:
                    _, e, s = unit
                    if engine == "sync":
                        dn_sync(E, e, s)
                    elif engine == "tensor":
                        dn_tensor(E, e, s)
                    elif engine == "gpsimd":
                        dn_gpsimd(E, e, s)
                    elif engine == "scalar":
                        dn_scalar(E, e, s)
            if engine == "gpsimd":
                E.wait_ge("gd0", bld.C["gd0"])
                E.wait_ge("gd1", bld.C["gd1"])
        return run

    return {e: body(e) for e in ("sync", "gpsimd", "tensor", "scalar", "vector")}


def ph_wout1(bld, es):
    cfg = bld.cfg
    wins = [(b * cfg["L"], (b + 1) * cfg["L"]) for b in range(cfg["Bc"])]
    return _mm_rhsres(bld, es, "wo1", bld.IN["w_out1"], 32, 16,
                      bld.SC["y1_s"], bld.SC["s1_s"], windows=wins)

def ph_comb2(bld, es):
    cfg = bld.cfg
    NS = cfg["NS"]
    ps = bld.ps
    x1b = bld.sb(es, "c2_x", [128, 2 * 16 * 512], F32)
    a1b = bld.sb(es, "c2_a", [128, 2 * 16 * 512], F32)
    s1b = bld.sb(es, "c2_s", [128, 2 * 16 * 512], BF16)
    g1t = bld.sb(es, "c2_g", [128, 2 * 512], BF16)
    tmp = bld.sb(es, "c2_t", [128, 512], F32)
    ystg = bld.sb(es, "c2_y", [128, 16 * 512], F32)
    x1s, a1s, s1s = bld.SC["x1_s"], bld.SC["acc1_s"], bld.SC["s1_s"]
    OUT = bld.OUT

    def sync(E):
        for s in range(NS):
            par = s % 2
            if s >= 2:
                E.wait_ge("ve", bld.at(f"c2_yd_{s-2}"))
            for g4 in range(4):
                E.dma(x1b[:, (par * 16 + g4 * 4) * 512:(par * 16 + g4 * 4 + 4) * 512]
                      .rearrange("p (dt c) -> p dt c", c=512),
                      x1s[g4 * 512:(g4 + 1) * 512, s * 512:(s + 1) * 512]
                      .rearrange("(dt p) c -> p dt c", p=128), sem=f"ld{par}")
                E.dma(a1b[:, (par * 16 + g4 * 4) * 512:(par * 16 + g4 * 4 + 4) * 512]
                      .rearrange("p (dt c) -> p dt c", c=512),
                      a1s[g4 * 512:(g4 + 1) * 512, s * 512:(s + 1) * 512]
                      .rearrange("(dt p) c -> p dt c", p=128), sem=f"ld{par + 2}")
            bld.mark(f"c2_lda_{s}", f"ld{par + 2}")
            for h in range(2):
                E.dma(s1b[:, (par * 16 + h * 8) * 512:(par * 16 + (h + 1) * 8) * 512]
                      .rearrange("p (dt c) -> p dt c", c=512),
                      s1s[h * 8 * 128:(h + 1) * 8 * 128, s * 512:(s + 1) * 512]
                      .rearrange("(dt p) c -> p dt c", p=128), sem=f"xa{par}")
            bld.mark(f"c2_lds_{s}", f"xa{par}")
            bld.mark(f"c2_ld_{s}", f"ld{par}")

    def scalar(E):
        for s in range(NS):
            par = s % 2
            E.wait_ge(f"ld{par + 2}", bld.at(f"c2_lda_{s}"))
            for dt in range(16):
                slot = dt % 2
                if s + dt > 0:
                    pk = f"{s}_{dt-2}" if dt >= 2 else f"{s-1}_{14 + dt}"
                    E.wait_ge("ve", bld.at(f"c2_t_{pk}"))
                E.activation(out=g1t[:, slot * 512:(slot + 1) * 512],
                             in_=a1b[:, (par * 16 + dt) * 512:(par * 16 + dt + 1) * 512],
                             func=AF.Silu).inc()
                bld.mark(f"c2_g_{s}_{dt}", "se")

    def vector(E):
        for s in range(NS):
            par = s % 2
            E.wait_ge(f"ld{par}", bld.at(f"c2_ld_{s}"))
            E.wait_ge(f"xa{par}", bld.at(f"c2_lds_{s}"))
            if s >= 1:
                E.wait_ge(f"gd{1 - par}", bld.at(f"c2_st_{s-1}"))
            for dt in range(16):
                slot = dt % 2
                E.wait_ge("se", bld.at(f"c2_g_{s}_{dt}"))
                E.fence()
                E.tensor_tensor(out=tmp[:, :],
                                in0=s1b[:, (par * 16 + dt) * 512:(par * 16 + dt + 1) * 512],
                                in1=g1t[:, slot * 512:(slot + 1) * 512],
                                op=ALU.mult).inc()
                bld.mark(f"c2_t_{s}_{dt}", "ve")
                E.fence()
                E.tensor_tensor(out=ystg[:, dt * 512:(dt + 1) * 512],
                                in0=x1b[:, (par * 16 + dt) * 512:(par * 16 + dt + 1) * 512],
                                in1=tmp[:, :], op=ALU.add).inc()
            bld.mark(f"c2_yd_{s}", "ve")

    def gpsimd(E):
        for s in range(NS):
            par = s % 2
            E.wait_ge("ve", bld.at(f"c2_yd_{s}"))
            for g4 in range(4):
                E.dma(OUT[g4 * 512:(g4 + 1) * 512, s * 512:(s + 1) * 512]
                      .rearrange("(dt p) c -> p dt c", p=128),
                      ystg[:, g4 * 4 * 512:(g4 * 4 + 4) * 512]
                      .rearrange("p (dt c) -> p dt c", c=512), sem=f"gd{par}")
            bld.mark(f"c2_st_{s}", f"gd{par}")
        for p2 in range(min(2, NS)):
            E.wait_ge(f"gd{p2}", bld.C[f"gd{p2}"])

    return {"sync": sync, "scalar": scalar, "vector": vector, "gpsimd": gpsimd}



def kernel(**inputs):
    Bc, L = 2, 2048
    in_maps = prep_inputs(inputs, N_CORES, Bc, L)
    nc, cfg = build_program(Bc, L)
    res = run_bass_kernel_spmd(nc, in_maps, core_ids=list(range(N_CORES)))
    D, T = cfg["D"], cfg["T"]
    outs = []
    for c in range(N_CORES):
        y = np.asarray(res.results[c]["y_fm"]).reshape(D, Bc, L)
        outs.append(y.transpose(1, 2, 0))
    return np.ascontiguousarray(np.concatenate(outs, 0), dtype=np.float32)


# version 06
# speedup vs baseline: 1.1838x; 1.1838x over previous
import numpy as np
import ml_dtypes
from collections import defaultdict
from contextlib import ExitStack

import concourse.bass as bass
import concourse.mybir as mybir
from concourse.bass_utils import run_bass_kernel_spmd

F32 = mybir.dt.float32
BF16 = mybir.dt.bfloat16
AF = mybir.ActivationFunctionType
ALU = mybir.AluOpType
AX = mybir.AxisListType

N_CORES = 8
ENGINE_ORDER = ("sync", "gpsimd", "tensor", "scalar", "vector")


def _blocks(w, mtw=128):
    K, M = w.shape
    n_kt, n_mt = K // 128, M // mtw
    assert n_kt * 128 == K and n_mt * mtw == M
    return np.ascontiguousarray(
        w.reshape(n_kt, 128, n_mt, mtw).transpose(2, 1, 0, 3).reshape(n_mt * 128, n_kt * mtw))


def prep_inputs(inputs, n_cores=N_CORES, Bc=2, L=2048):
    g = lambda k: np.asarray(inputs[k])
    D = 2048
    T = Bc * L
    sh = {}
    sh["w_in0"] = _blocks(np.ascontiguousarray(g("L0_attn_W_in").T))
    Wg = g("L0_attn_W_gate").reshape(4, 4, 512, D).transpose(1, 0, 2, 3).reshape(8192, D)
    sh["w_gate0"] = _blocks(np.ascontiguousarray(Wg.T))
    bg = g("L0_attn_b_gate").reshape(4, 4, 512).transpose(1, 0, 2).reshape(8192)
    sh["bgate_t"] = np.ascontiguousarray(bg.reshape(64, 128).T)
    sh["w_out0"] = _blocks(np.ascontiguousarray(g("L0_attn_W_out").T))
    sh["rmsw_t"] = np.ascontiguousarray(g("L0_attn_rms_w").reshape(4, 128).T)
    sh["wr_t"] = np.ascontiguousarray(g("L0_ffn_Wr").reshape(16, 128, 8).transpose(1, 0, 2).reshape(128, 128))
    sh["br_row"] = np.ascontiguousarray(g("L0_ffn_br").reshape(1, 8))
    sh["wd_b"] = _blocks(np.ascontiguousarray(g("L0_ffn_Wd").transpose(1, 0, 2).reshape(D, 128)))
    sh["bd_col"] = np.ascontiguousarray(g("L0_ffn_bd").reshape(128, 1))
    sh["wu_b"] = _blocks(np.ascontiguousarray(g("L0_ffn_Wu").reshape(128, D)))
    sh["bu_t"] = np.ascontiguousarray(g("L0_ffn_bu"))
    sh["r16_t"] = np.ascontiguousarray(np.repeat(np.eye(8, dtype=np.float32), 16, axis=1))
    sh["w_in1"] = _blocks(np.ascontiguousarray(g("L1_attn_W_in").T))
    sh["convw_t"] = np.ascontiguousarray(g("L1_attn_conv_w").reshape(32, 128, 4).transpose(1, 0, 2).reshape(128, 128))
    sh["convb_t"] = np.ascontiguousarray(g("L1_attn_conv_b").reshape(32, 128).T)
    WxT = np.ascontiguousarray(g("L1_attn_Wx").T)
    sh["wx_ts"] = _blocks(np.ascontiguousarray(WxT[:, :128]))
    sh["wx_bc"] = _blocks(np.ascontiguousarray(WxT[:, 128:]), mtw=32)
    sh["sel32"] = np.ascontiguousarray(
        np.concatenate([np.repeat([[1.0, 0.0]], 16, 0), np.repeat([[0.0, 1.0]], 16, 0)], 0).astype(np.float32))
    sh["sel2"] = np.ascontiguousarray(sh["sel32"].T)
    sh["lnw32"] = np.ascontiguousarray(
        np.concatenate([g("L1_attn_b_ln_w"), g("L1_attn_c_ln_w")]).reshape(32, 1))
    sh["dtlnw_t"] = np.ascontiguousarray(g("L1_attn_dt_ln_w").reshape(128, 1))
    sh["wdt_b"] = _blocks(np.ascontiguousarray(g("L1_attn_Wdt").T))
    sh["dtb_t"] = np.ascontiguousarray(g("L1_attn_dt_bias").reshape(32, 128).T)
    sh["a_t"] = np.ascontiguousarray(g("L1_attn_A").reshape(32, 128, 16).transpose(1, 0, 2).reshape(128, 512))
    sh["d_t"] = np.ascontiguousarray(g("L1_attn_D").reshape(32, 128).T)
    sh["w_out1"] = _blocks(np.ascontiguousarray(g("L1_attn_Wout").T))
    sh["gatew_t"] = np.ascontiguousarray(g("L1_ffn_gate_w").reshape(16, 128, 8).transpose(1, 0, 2).reshape(128, 128))
    sh["wg_b"] = np.concatenate([_blocks(np.ascontiguousarray(g("L1_ffn_w_gate")[e])) for e in range(8)], 0)
    sh["wu_b1"] = np.concatenate([_blocks(np.ascontiguousarray(g("L1_ffn_w_up")[e])) for e in range(8)], 0)
    sh["wdn_b"] = np.concatenate([_blocks(np.ascontiguousarray(g("L1_ffn_w_down")[e])) for e in range(8)], 0)
    sh["swg_b"] = _blocks(np.ascontiguousarray(g("L1_ffn_sw_gate")))
    sh["swu_b"] = _blocks(np.ascontiguousarray(g("L1_ffn_sw_up")))
    sh["swd_b"] = _blocks(np.ascontiguousarray(g("L1_ffn_sw_down")))
    sh["eps_t"] = np.broadcast_to(np.array([1e-5, 1e-6], np.float32), (128, 2)).copy()
    sh["ones_col"] = np.ones((128, 1), np.float32)
    sh["ones_row"] = np.ones((1, 128), np.float32)
    sh["ones_row_bf"] = np.ones((1, 128), ml_dtypes.bfloat16)
    sh["ident_bf"] = np.eye(128, dtype=ml_dtypes.bfloat16)
    for k, v in sh.items():
        if v.dtype == np.float64:
            sh[k] = v.astype(np.float32)

    x = g("L0_attn_x")
    in_maps = []
    for c in range(n_cores):
        xc = x[c * Bc:(c + 1) * Bc, :L, :]
        m = dict(sh)
        m["x_fm"] = np.ascontiguousarray(xc.transpose(2, 0, 1).reshape(D, T))
        in_maps.append(m)
    return in_maps



_ESEM = {"tensor": "pe", "vector": "ve", "scalar": "se", "gpsimd": "ge", "sync": "ld"}


class _Inst:
    __slots__ = ("B", "raw", "esem")

    def __init__(self, B, raw, esem=None):
        self.B = B
        self.raw = raw
        self.esem = esem

    def then_inc(self, sem, n=1):
        self.B.C[sem] += n
        if self.raw is not None:
            self.raw.then_inc(self.B.sems[sem], n)
        return self

    def inc(self):
        return self.then_inc(self.esem, 1)


class _Eng:

    def __init__(self, B, name, raw=None):
        self._B = B
        self._name = name
        self._raw = raw

    def wait_ge(self, sem, val):
        if self._raw is not None:
            self._raw.wait_ge(self._B.sems[sem], val)

    def fence(self, sem=None):
        sem = sem or _ESEM[self._name]
        self.wait_ge(sem, self._B.C[sem])

    def sem_inc(self, sem, n=1):
        self._B.C[sem] += n
        if self._raw is not None:
            self._raw.sem_inc(self._B.sems[sem], n)

    def dma(self, out, in_, sem=None):
        assert sem is not None, "dma() needs an explicit semaphore name"
        self._B.C[sem] += 16
        if self._raw is not None:
            self._raw.dma_start(out=out, in_=in_).then_inc(self._B.sems[sem], 16)
        return self._B.C[sem]

    def __getattr__(self, m):
        B, raw = self._B, self._raw
        esem = _ESEM[self._name]

        def call(*a, **kw):
            if raw is None:
                return _Inst(B, None, esem)
            return _Inst(B, getattr(raw, m)(*a, **kw), esem)

        return call


class Bld:
    def __init__(self, nc, cfg, ctx):
        self.nc, self.cfg, self.ctx = nc, cfg, ctx
        self.C = defaultdict(int)
        self.M = {}
        self.pass1 = True
        self.uid = 0
        names = ("pe", "ve", "se", "ge", "bar",
                 "ld0", "ld1", "ld2", "ld3", "gd0", "gd1", "gd2", "gd3",
                 "st0", "st1", "st2", "st3", "wl0", "wl1", "sd0", "sd1",
                 "xa0", "xa1", "xb0", "xb1")
        self.sems = {n: ctx.enter_context(nc.semaphore(n)) for n in names}
        self.ps = [ctx.enter_context(nc.psum_tensor(f"psb{i}", [128, 512], F32)) for i in range(7)]
        self.psbf = ctx.enter_context(nc.psum_tensor("psbfx", [128, 1024], BF16))

    def mark(self, key, sem):
        if self.pass1:
            self.M[key] = self.C[sem]
        return self.M[key]

    def at(self, key):
        return self.M.get(key, 0)

    def name(self, base):
        self.uid += 1
        return f"{base}_{'a' if self.pass1 else 'b'}{self.uid}"

    def sb(self, es, base, shape, dtype):
        return es.enter_context(self.nc.sbuf_tensor(self.name(base), shape, dtype))


def _run_phases(nc, bld, phases, ctx):
    n_eng = len(ENGINE_ORDER)
    snap = {}
    for p1 in (True, False):
        bld.pass1 = p1
        bld.C.clear()
        bld.uid = 0
        if not p1:
            block = ctx.enter_context(nc.Block())
        for pidx, ph in enumerate(phases):
            es = ExitStack()
            bodies = ph(bld, es)
            for ename in ENGINE_ORDER:
                body = bodies.get(ename)
                bar_target = n_eng * pidx

                def run_body(raw_eng, body=body, ename=ename, bar_target=bar_target):
                    E = _Eng(bld, ename, raw_eng)
                    E.wait_ge("bar", bar_target)
                    if body is not None:
                        body(E)
                    bld.C["bar"] += 1
                    if raw_eng is not None:
                        raw_eng.drain().then_inc(bld.sems["bar"], 1)

                if p1:
                    run_body(None)
                else:
                    getattr(block, ename)(run_body)
                key = (pidx, ename)
                if p1:
                    snap[key] = dict(bld.C)
                else:
                    assert snap[key] == dict(bld.C), (
                        f"pass divergence at phase {pidx} engine {ename}: "
                        f"{ {k: (snap[key].get(k), bld.C.get(k)) for k in set(snap[key]) | set(bld.C) if snap[key].get(k) != bld.C.get(k)} }")
            es.close()



_DTIN = {
    "x_fm": (None, F32),
    "w_in0": ([2048, 2048], F32), "w_gate0": ([8192, 2048], F32),
    "bgate_t": ([128, 64], F32), "w_out0": ([2048, 2048], F32),
    "rmsw_t": ([128, 4], F32),
    "wr_t": ([128, 128], F32), "br_row": ([1, 8], F32),
    "wd_b": ([128, 2048], F32), "bd_col": ([128, 1], F32),
    "wu_b": ([2048, 128], F32), "bu_t": ([8, 2048], F32), "r16_t": ([8, 128], F32),
    "w_in1": ([8192, 2048], F32), "convw_t": ([128, 128], F32), "convb_t": ([128, 32], F32),
    "wx_ts": ([128, 4096], F32), "wx_bc": ([128, 1024], F32),
    "sel32": ([32, 2], F32), "sel2": ([2, 32], F32), "lnw32": ([32, 1], F32),
    "dtlnw_t": ([128, 1], F32), "wdt_b": ([4096, 128], F32), "dtb_t": ([128, 32], F32),
    "a_t": ([128, 512], F32), "d_t": ([128, 32], F32),
    "w_out1": ([2048, 4096], F32), "gatew_t": ([128, 128], F32),
    "wg_b": ([8 * 1408, 2048], F32), "wu_b1": ([8 * 1408, 2048], F32),
    "wdn_b": ([8 * 2048, 1408], F32),
    "swg_b": ([2816, 2048], F32), "swu_b": ([2816, 2048], F32), "swd_b": ([2048, 2816], F32),
    "eps_t": ([128, 2], F32),
    "ones_col": ([128, 1], F32), "ones_row": ([1, 128], F32),
    "ones_row_bf": ([1, 128], BF16), "ident_bf": ([128, 128], BF16),
}


def build_program(Bc=2, L=2048):
    D = 2048
    T = Bc * L
    cfg = dict(Bc=Bc, L=L, D=D, T=T, NS=T // 512, NCL=L // 512)

    nc = bass.Bass("TRN2", target_bir_lowering=False, debug=False)
    ctx = ExitStack()
    nc._kb_ctx = ctx

    IN = {}
    for k, (s, d) in _DTIN.items():
        if k == "x_fm":
            s = [D, T]
        IN[k] = nc.dram_tensor(k, s, d, kind="ExternalInput").ap()

    SC = {}
    for k, (s, d) in {
        "xn_s": ([D, T], BF16), "xn32_s": ([D, T], F32), "xn132_s": ([D, T], F32),
        "xp_s": ([D, T], BF16), "g0_s": ([D, T], BF16),
        "gates_s": ([8192, T], F32), "h0_s": ([D, T], BF16), "s0_s": ([D, T], BF16),
        "x1_s": ([D, T], F32), "xn1_s": ([D, T], BF16),
        "hconv_s": ([4096, T], BF16), "sg_s": ([4096, T], BF16),
        "delta_s": ([4096, T], F32), "bc_s": ([32, T], BF16),
        "y1_s": ([4096, T], BF16), "acc1_s": ([D, T], F32), "s1_s": ([D, T], BF16),
        "wfm_s": ([8, T], BF16), "whs_s": ([10 * 1408, T], BF16),
    }.items():
        SC[k] = nc.dram_tensor(k, s, d, kind="Internal").ap()
    OUT = nc.dram_tensor("y_fm", [D, T], F32, kind="ExternalOutput").ap()

    bld = Bld(nc, cfg, ctx)
    bld.IN, bld.SC, bld.OUT = IN, SC, OUT

    phases = [ph_consts, ph_ln1, ph_win0, ph_ffn0, ph_gates, ph_scan0, ph_wout0,
              ph_comb1, ph_win1, ph_ssmp, ph_moescan, ph_wout1, ph_comb2]
    if _PHASE_LIMIT is not None:
        phases = phases[:_PHASE_LIMIT]
    bld.phases = phases
    _run_phases(nc, bld, phases, ctx)
    return nc, cfg


def build_program_upto(nphases, Bc=2, L=2048):
    global _PHASE_LIMIT
    _PHASE_LIMIT = nphases
    try:
        return build_program(Bc, L)
    finally:
        _PHASE_LIMIT = None


_PHASE_LIMIT = None



_CONSTS = [
    ("bgate_t", [128, 64], F32), ("rmsw_t", [128, 4], F32),
    ("wr_t", [128, 128], F32), ("br_row", [1, 8], F32),
    ("bd_col", [128, 1], F32),
    ("bu_t", [8, 2048], BF16), ("r16_t", [8, 128], BF16),
    ("convw_t", [128, 128], F32), ("convb_t", [128, 32], F32),
    ("wx_bc", [128, 1024], BF16),
    ("sel32", [32, 2], F32), ("sel2", [2, 32], F32), ("lnw32", [32, 1], F32),
    ("dtlnw_t", [128, 1], F32),
    ("dtb_t", [128, 32], F32), ("a_t", [128, 512], F32), ("d_t", [128, 32], F32),
    ("gatew_t", [128, 128], F32),
    ("eps_t", [128, 2], F32),
    ("ones_col", [128, 1], F32), ("ones_row", [1, 128], F32),
    ("ones_row_bf", [1, 128], BF16), ("ident_bf", [128, 128], BF16),
]


def ph_consts(bld, es):
    nc = bld.nc
    if not hasattr(bld, "K"):
        bld.K = {k: bld.ctx.enter_context(nc.sbuf_tensor(k + "_sb", shape, dt))
                 for k, shape, dt in _CONSTS}
    K = bld.K

    def gpsimd(E):
        for k, shape, dt in _CONSTS:
            E.dma(K[k][:, :], bld.IN[k][:, :], sem="gd0")
        last = bld.mark("consts_done", "gd0")
        E.wait_ge("gd0", last)

    return {"gpsimd": gpsimd}



def ph_ln1(bld, es):
    return _ln_bodies(bld, es, src=bld.IN["x_fm"], dst=bld.SC["xn_s"], tag="ln1",
                      dst32=bld.SC["xn32_s"])


def _ln_bodies(bld, es, src, dst, tag, eps=1e-5, dst32=None):
    cfg = bld.cfg
    NS, D = cfg["NS"], cfg["D"]
    nc = bld.nc
    K = bld.K
    ps = bld.ps
    xs = bld.sb(es, "ln_x", [128, 2 * 16 * 512], F32)
    sq = bld.sb(es, "ln_sq", [128, 512], F32)
    row = bld.sb(es, "ln_row", [1, 6 * 512], F32)
    xn = bld.sb(es, "ln_xn", [128, 2 * 16 * 512], BF16)
    tmp = bld.sb(es, "ln_tmp", [128, 512], F32)

    def xsv(par, dt):
        return xs[:, (par * 16 + dt) * 512:(par * 16 + dt + 1) * 512]

    def xnv(par, dt):
        return xn[:, (par * 16 + dt) * 512:(par * 16 + dt + 1) * 512]

    R = lambda i: row[0:1, i * 512:(i + 1) * 512]

    def sync(E):
        for s in range(NS):
            par = s % 2
            if s >= 2:
                E.wait_ge("ve", bld.at(f"{tag}_xnd_{s-2}"))
                E.wait_ge(f"gd{par}", bld.at(f"{tag}_st_{s-2}"))
            for g4 in range(4):
                E.dma(xs[:, (par * 16 + g4 * 4) * 512:(par * 16 + g4 * 4 + 4) * 512]
                      .rearrange("p (dt c) -> p dt c", c=512),
                      src[g4 * 512:(g4 + 1) * 512, s * 512:(s + 1) * 512]
                      .rearrange("(dt p) c -> p dt c", p=128), sem=f"ld{par}")
            bld.mark(f"{tag}_ld_{s}", f"ld{par}")

    def scalar(E):
        for s in range(NS):
            par = s % 2
            E.wait_ge(f"ld{par}", bld.at(f"{tag}_ld_{s}"))
            for dt in range(16):
                E.wait_ge("pe", bld.at(f"{tag}_sqfree_{s}_{dt}"))
                E.activation(out=sq[:, :], in_=xsv(par, dt), func=AF.Square).inc()
                bld.mark(f"{tag}_sq_{s}_{dt}", "se")
            E.wait_ge("ve", bld.at(f"{tag}_var_{s}"))
            E.activation(out=R(4), in_=R(3), func=AF.Ln,
                         bias=K["eps_t"][0:1, 0:1] if eps == 1e-5 else K["eps_t"][0:1, 1:2]).inc()
            E.fence()
            E.activation(out=R(5), in_=R(4), func=AF.Exp, scale=-0.5).inc()
            bld.mark(f"{tag}_inv_{s}", "se")

    def tensor(E):
        for s in range(NS):
            par = s % 2
            E.wait_ge(f"ld{par}", bld.at(f"{tag}_ld_{s}"))
            if s >= 1:
                E.wait_ge("ve", bld.at(f"{tag}_statsread_{s-1}"))
            for dt in range(16):
                E.matmul(ps[0][0:1, :], lhsT=K["ones_col"][:, :], rhs=xsv(par, dt),
                         start=(dt == 0), stop=(dt == 15))
            for dt in range(16):
                bld.mark(f"{tag}_sqfree_{s}_{dt}", "pe")
                E.wait_ge("se", bld.at(f"{tag}_sq_{s}_{dt}"))
                E.matmul(ps[1][0:1, :], lhsT=K["ones_col"][:, :], rhs=sq[:, :],
                         start=(dt == 0), stop=(dt == 15)).then_inc("pe", 1)
            bld.mark(f"{tag}_stats_{s}", "pe")
            E.wait_ge("ve", bld.at(f"{tag}_minv_{s}"))
            if s >= 1:
                E.wait_ge("ve", bld.at(f"{tag}_xnd_{s-1}"))
            E.matmul(ps[2][:, :], lhsT=K["ones_row"][:, :], rhs=R(5), start=True, stop=True)
            E.matmul(ps[3][:, :], lhsT=K["ones_row"][:, :], rhs=R(2), start=True, stop=True).then_inc("pe", 1)
            bld.mark(f"{tag}_bcast_{s}", "pe")

    def vector(E):
        for s in range(NS):
            par = s % 2
            E.wait_ge("pe", bld.at(f"{tag}_stats_{s}"))
            E.tensor_scalar(out=R(0), in0=ps[0][0:1, :], scalar1=1.0 / D,
                            scalar2=None, op0=ALU.mult).inc()
            E.tensor_scalar(out=R(1), in0=ps[1][0:1, :], scalar1=1.0 / D,
                            scalar2=None, op0=ALU.mult).inc()
            bld.mark(f"{tag}_statsread_{s}", "ve")
            E.fence()
            E.tensor_tensor(out=tmp[0:1, :], in0=R(0), in1=R(0), op=ALU.mult).inc()
            E.fence()
            E.tensor_tensor(out=R(3), in0=R(1), in1=tmp[0:1, :], op=ALU.subtract).inc()
            bld.mark(f"{tag}_var_{s}", "ve")
            E.wait_ge("se", bld.at(f"{tag}_inv_{s}"))
            E.tensor_tensor(out=R(2), in0=R(0), in1=R(5), op=ALU.mult).inc()
            bld.mark(f"{tag}_minv_{s}", "ve")
            E.wait_ge("pe", bld.at(f"{tag}_bcast_{s}"))
            if s >= 2:
                E.wait_ge(f"gd{par}", bld.at(f"{tag}_st_{s-2}"))
            E.fence()
            for dt in range(16):
                E.tensor_tensor(out=tmp[:, :], in0=xsv(par, dt), in1=ps[2][:, :],
                                op=ALU.mult).inc()
                E.fence()
                E.tensor_tensor(out=xsv(par, dt), in0=tmp[:, :], in1=ps[3][:, :],
                                op=ALU.subtract).inc()
                E.fence()
                E.tensor_copy(out=xnv(par, dt), in_=xsv(par, dt)).inc()
                E.fence()
            bld.mark(f"{tag}_xnd_{s}", "ve")

    def gpsimd(E):
        for s in range(NS):
            par = s % 2
            E.wait_ge("ve", bld.at(f"{tag}_xnd_{s}"))
            for g4 in range(4):
                E.dma(dst[g4 * 512:(g4 + 1) * 512, s * 512:(s + 1) * 512]
                      .rearrange("(dt p) c -> p dt c", p=128),
                      xn[:, (par * 16 + g4 * 4) * 512:(par * 16 + g4 * 4 + 4) * 512]
                      .rearrange("p (dt c) -> p dt c", c=512), sem=f"gd{par}")
            if dst32 is not None:
                for g4 in range(4):
                    E.dma(dst32[g4 * 512:(g4 + 1) * 512, s * 512:(s + 1) * 512]
                          .rearrange("(dt p) c -> p dt c", p=128),
                          xs[:, (par * 16 + g4 * 4) * 512:(par * 16 + g4 * 4 + 4) * 512]
                          .rearrange("p (dt c) -> p dt c", c=512), sem=f"gd{par}")
            bld.mark(f"{tag}_st_{s}", f"gd{par}")
        for p2 in range(min(2, NS)):
            E.wait_ge(f"gd{p2}", bld.C[f"gd{p2}"])

    return {"sync": sync, "scalar": scalar, "tensor": tensor, "vector": vector,
            "gpsimd": gpsimd}



def _mm_wres(bld, es, tag, w_dram, n_kt, n_mt, src, dst, func=AF.Copy, out_dt=BF16,
             bias_col=None):
    cfg = bld.cfg
    NS = cfg["NS"]
    T = cfg["T"]
    ps = bld.ps
    NB = 4
    wsb = bld.sb(es, f"{tag}_w", [128, n_mt * n_kt * 128], BF16)
    xb = bld.sb(es, f"{tag}_x", [128, 2 * n_kt * 512], BF16)
    stage = bld.sb(es, f"{tag}_st", [128, NB * 512], out_dt)

    def wv(mt, kt):
        return wsb[:, (mt * n_kt + kt) * 128:(mt * n_kt + kt + 1) * 128]

    def xv(par, kt):
        return xb[:, (par * n_kt + kt) * 512:(par * n_kt + kt + 1) * 512]

    def gpsimd(E):
        for j in range(n_mt):
            E.dma(wsb[:, j * n_kt * 128:(j + 1) * n_kt * 128],
                  w_dram[j * 128:(j + 1) * 128, :], sem="wl0")
        bld.mark(f"{tag}_wld", "wl0")

    def sync(E):
        for s in range(NS):
            par = s % 2
            if s >= 2:
                E.wait_ge("pe", bld.at(f"{tag}_mmd_{s-2}"))
            for h in range(2):
                E.dma(xb[:, (par * n_kt + h * n_kt // 2) * 512:
                          (par * n_kt + (h + 1) * n_kt // 2) * 512]
                      .rearrange("p (kt c) -> p kt c", c=512),
                      src[h * (n_kt // 2) * 128:(h + 1) * (n_kt // 2) * 128,
                          s * 512:(s + 1) * 512]
                      .rearrange("(kt p) c -> p kt c", p=128), sem=f"ld{par}")
            bld.mark(f"{tag}_rhs_{s}", f"ld{par}")

    def tensor(E):
        E.wait_ge("wl0", bld.at(f"{tag}_wld"))
        for s in range(NS):
            par = s % 2
            E.wait_ge(f"ld{par}", bld.at(f"{tag}_rhs_{s}"))
            for mt in range(n_mt):
                g = s * n_mt + mt
                bank = g % NB
                if g >= NB:
                    E.wait_ge("se", bld.at(f"{tag}_ev_{g-NB}"))
                for kt in range(n_kt):
                    mm = E.matmul(ps[bank][:, :], lhsT=wv(mt, kt), rhs=xv(par, kt),
                                  start=(kt == 0), stop=(kt == n_kt - 1))
                mm.inc()
                bld.mark(f"{tag}_mm_{g}", "pe")
            bld.mark(f"{tag}_mmd_{s}", "pe")

    def scalar(E):
        for s in range(NS):
            for mt in range(n_mt):
                g = s * n_mt + mt
                bank = g % NB
                E.wait_ge("pe", bld.at(f"{tag}_mm_{g}"))
                if g >= NB:
                    E.wait_ge(f"st{bank}", bld.at(f"{tag}_sst_{g-NB}"))
                kw = {}
                fn = func
                if bias_col is not None:
                    kw["bias"] = bias_col(mt)
                    if fn == AF.Copy:
                        fn = AF.Identity
                E.activation(out=stage[:, bank * 512:(bank + 1) * 512],
                             in_=ps[bank][:, :], func=fn, **kw).inc()
                bld.mark(f"{tag}_ev_{g}", "se")
                E.fence("se")
                E.dma(dst[mt * 128:(mt + 1) * 128, s * 512:(s + 1) * 512],
                      stage[:, bank * 512:(bank + 1) * 512], sem=f"st{bank}")
                bld.mark(f"{tag}_sst_{g}", f"st{bank}")
        for b2 in range(NB):
            E.wait_ge(f"st{b2}", bld.C[f"st{b2}"])

    return {"gpsimd": gpsimd, "sync": sync, "tensor": tensor, "scalar": scalar}


def _mm_rhsres(bld, es, tag, w_dram, n_kt, n_mt, src, dst, windows=None,
               func=AF.Copy, out_dt=BF16, bias_col=None):
    cfg = bld.cfg
    if windows is None:
        windows = [(0, cfg["T"])]
    WMAX = max(c1 - c0 for c0, c1 in windows)
    ps = bld.ps
    NB = 4
    xr = bld.sb(es, f"{tag}_x", [128, n_kt * WMAX], BF16)
    wsl = bld.sb(es, f"{tag}_w", [128, 2 * n_kt * 128], BF16)
    stage = bld.sb(es, f"{tag}_st", [128, NB * 512], out_dt)

    def sync(E):
        for w, (c0, c1) in enumerate(windows):
            if w >= 1:
                E.wait_ge("pe", bld.at(f"{tag}_alldone_{w-1}"))
            for h in range(n_kt):
                E.dma(xr[:, h * WMAX:h * WMAX + (c1 - c0)],
                      src[h * 128:(h + 1) * 128, c0:c1], sem="ld0")
            bld.mark(f"{tag}_rhs_{w}", "ld0")

    def gpsimd(E):
        for w, _ in enumerate(windows):
            for mt in range(n_mt):
                if (w, mt) != (0, 0):
                    pw, pmt = (w, mt - 2) if mt >= 2 else (w - 1, n_mt + mt - 2)
                    if pmt >= 0:
                        E.wait_ge("pe", bld.at(f"{tag}_wfree_{pw}_{pmt}"))
                E.dma(wsl[:, (mt % 2) * n_kt * 128:((mt % 2) + 1) * n_kt * 128],
                      w_dram[mt * 128:(mt + 1) * 128, :], sem=f"wl{mt % 2}")
                bld.mark(f"{tag}_wld_{w}_{mt}", f"wl{mt % 2}")

    def tensor(E):
        g = 0
        for w, (c0, c1) in enumerate(windows):
            NSL = (c1 - c0) // 512
            E.wait_ge("ld0", bld.at(f"{tag}_rhs_{w}"))
            for mt in range(n_mt):
                par = mt % 2
                E.wait_ge(f"wl{par}", bld.at(f"{tag}_wld_{w}_{mt}"))
                for s in range(NSL):
                    bank = g % NB
                    if g >= NB:
                        E.wait_ge("se", bld.at(f"{tag}_ev_{g-NB}"))
                    for kt in range(n_kt):
                        mm = E.matmul(ps[bank][:, :],
                                      lhsT=wsl[:, (par * n_kt + kt) * 128:(par * n_kt + kt + 1) * 128],
                                      rhs=xr[:, kt * WMAX + s * 512:kt * WMAX + (s + 1) * 512],
                                      start=(kt == 0), stop=(kt == n_kt - 1))
                    mm.inc()
                    bld.mark(f"{tag}_mm_{g}", "pe")
                    g += 1
                bld.mark(f"{tag}_wfree_{w}_{mt}", "pe")
            bld.mark(f"{tag}_alldone_{w}", "pe")

    def scalar(E):
        g = 0
        for w, (c0, c1) in enumerate(windows):
            NSL = (c1 - c0) // 512
            for mt in range(n_mt):
                for s in range(NSL):
                    bank = g % NB
                    E.wait_ge("pe", bld.at(f"{tag}_mm_{g}"))
                    if g >= NB:
                        E.wait_ge(f"st{bank}", bld.at(f"{tag}_sst_{g-NB}"))
                    kw = {}
                    fn = func
                    if bias_col is not None:
                        kw["bias"] = bias_col(mt)
                        if fn == AF.Copy:
                            fn = AF.Identity
                    E.activation(out=stage[:, bank * 512:(bank + 1) * 512],
                                 in_=ps[bank][:, :], func=fn, **kw).inc()
                    bld.mark(f"{tag}_ev_{g}", "se")
                    E.fence("se")
                    E.dma(dst[mt * 128:(mt + 1) * 128, c0 + s * 512:c0 + (s + 1) * 512],
                          stage[:, bank * 512:(bank + 1) * 512], sem=f"st{bank}")
                    bld.mark(f"{tag}_sst_{g}", f"st{bank}")
                    g += 1
        for b2 in range(NB):
            E.wait_ge(f"st{b2}", bld.C[f"st{b2}"])

    return {"gpsimd": gpsimd, "sync": sync, "tensor": tensor, "scalar": scalar}



def ph_win0(bld, es):
    return _mm_wres(bld, es, "win0", bld.IN["w_in0"], 16, 16,
                    bld.SC["xn_s"], bld.SC["xp_s"])


def ph_gates(bld, es):
    return _mm_rhsres(bld, es, "gat", bld.IN["w_gate0"], 16, 64,
                      bld.SC["xp_s"], bld.SC["gates_s"],
                      out_dt=F32, bias_col=lambda mt: bld.K["bgate_t"][:, mt:mt + 1])


def ph_ffn0(bld, es):
    cfg = bld.cfg
    NS, T = cfg["NS"], cfg["T"]
    nc = bld.nc
    K = bld.K
    ps, psbf = bld.ps, bld.psbf
    xb = bld.sb(es, "f0_x", [128, 2 * 16 * 512], BF16)
    xb32 = bld.sb(es, "f0_x32", [128, 2 * 16 * 512], F32)
    wdl = bld.sb(es, "f0_wd", [128, 2048], BF16)
    wul = bld.sb(es, "f0_wu", [128, 2048], BF16)
    gex = bld.sb(es, "f0_gex", [128, 32], F32)
    gsum = bld.sb(es, "f0_gsum", [128, 4], F32)
    gsm = bld.sb(es, "f0_gsm", [128, 32], F32)
    mx = bld.sb(es, "f0_mx", [128, 32], F32)
    m1 = bld.sb(es, "f0_m1", [128, 32], F32)
    m2 = bld.sb(es, "f0_m2", [128, 32], F32)
    t1 = bld.sb(es, "f0_t1", [128, 32], F32)
    t2 = bld.sb(es, "f0_t2", [128, 32], F32)
    rs = bld.sb(es, "f0_rs", [128, 12], F32)
    wtm = bld.sb(es, "f0_wtm", [128, 32], BF16)
    wfm = bld.sb(es, "f0_wfm", [8, T], BF16)
    hsb = bld.sb(es, "f0_hsb", [128, 512], F32)
    whb = bld.sb(es, "f0_whb", [128, 512], BF16)
    stage = bld.sb(es, "f0_stg", [128, 4 * 512], BF16)

    def xv(par, kt):
        return xb[:, (par * 16 + kt) * 512:(par * 16 + kt + 1) * 512]

    def c8(buf, tt):
        return buf[:, tt * 8:(tt + 1) * 8]

    def gpsimd(E):
        E.dma(wdl[:, :], bld.IN["wd_b"][:, :], sem="wl0")
        E.dma(wul.ap().rearrange("p (bk m) -> p bk m", m=128),
              bld.IN["wu_b"].rearrange("(bk p) m -> p bk m", p=128), sem="wl0")
        bld.mark("f0_wld", "wl0")

    def sync(E):
        for s in range(NS):
            par = s % 2
            if s >= 2:
                E.wait_ge("pe", bld.at(f"f0_done_{s-2}"))
            for h in range(2):
                E.dma(xb[:, (par * 16 + h * 8) * 512:(par * 16 + (h + 1) * 8) * 512]
                      .rearrange("p (kt c) -> p kt c", c=512),
                      bld.SC["xn_s"][h * 8 * 128:(h + 1) * 8 * 128, s * 512:(s + 1) * 512]
                      .rearrange("(kt p) c -> p kt c", p=128), sem=f"ld{par}")
            for h in range(4):
                E.dma(xb32[:, (par * 16 + h * 4) * 512:(par * 16 + (h + 1) * 4) * 512]
                      .rearrange("p (kt c) -> p kt c", c=512),
                      bld.SC["xn32_s"][h * 4 * 128:(h + 1) * 4 * 128, s * 512:(s + 1) * 512]
                      .rearrange("(kt p) c -> p kt c", p=128), sem=f"ld{par}")
            bld.mark(f"f0_rhs_{s}", f"ld{par}")

    def tensor(E):
        E.wait_ge("wl0", bld.at("f0_wld"))
        for s in range(NS):
            par = s % 2
            E.wait_ge(f"ld{par}", bld.at(f"f0_rhs_{s}"))
            for tt in range(4):
                if s >= 1:
                    E.wait_ge("se", bld.at(f"f0_allev_{s-1}"))
                for kt in range(16):
                    E.matmul(ps[tt][:, 0:8],
                             lhsT=xb32[:, (par * 16 + kt) * 512 + tt * 128:
                                       (par * 16 + kt) * 512 + (tt + 1) * 128],
                             rhs=K["wr_t"][:, kt * 8:(kt + 1) * 8],
                             start=(kt == 0), stop=False)
                E.matmul(ps[tt][:, 0:8], lhsT=K["ones_row"][0:1, :],
                         rhs=K["br_row"][0:1, :], start=False, stop=True).inc()
                bld.mark(f"f0_rmm_{s}_{tt}", "pe")
            if s >= 1:
                E.wait_ge("se", bld.at(f"f0_wfm_{s-1}"))
            for tt in range(4):
                E.wait_ge("ve", bld.at(f"f0_wtm_{s}_{tt}"))
                E.transpose(psbf[0:8, tt * 128:(tt + 1) * 128], in_=c8(wtm, tt),
                            identity=K["ident_bf"][:, :]).inc()
                bld.mark(f"f0_tr_{s}_{tt}", "pe")
            if s >= 1:
                E.wait_ge("se", bld.at(f"f0_hev_{s-1}"))
            for kt in range(16):
                mm = E.matmul(ps[4][:, :], lhsT=wdl[:, kt * 128:(kt + 1) * 128],
                              rhs=xv(par, kt), start=(kt == 0), stop=(kt == 15))
            mm.inc()
            bld.mark(f"f0_hmm_{s}", "pe")
            E.wait_ge("se", bld.at(f"f0_wfm_{s}"))
            if s >= 1:
                E.wait_ge("ve", bld.at(f"f0_whb_{s-1}"))
            E.matmul(ps[5][:, :], lhsT=K["r16_t"][:, :],
                     rhs=wfm[0:8, s * 512:(s + 1) * 512], start=True, stop=True).inc()
            bld.mark(f"f0_we_{s}", "pe")
            E.wait_ge("ve", bld.at(f"f0_whb_{s}"))
            for dt in range(16):
                bank = dt % 4
                E.wait_ge("se", bld.at(f"f0_rexp_{s}_{bank}"))
                if dt >= 4:
                    E.wait_ge("se", bld.at(f"f0_g0ev_{s}_{dt-4}"))
                E.matmul(ps[bank][:, :], lhsT=wul[:, dt * 128:(dt + 1) * 128],
                         rhs=whb[:, :], start=True, stop=False)
                E.matmul(ps[bank][:, :], lhsT=K["bu_t"][0:8, dt * 128:(dt + 1) * 128],
                         rhs=wfm[0:8, s * 512:(s + 1) * 512],
                         start=False, stop=True).inc()
                bld.mark(f"f0_dmm_{s}_{dt}", "pe")
            bld.mark(f"f0_done_{s}", "pe")

    def scalar(E):
        for s in range(NS):
            if s >= 1:
                E.wait_ge("ve", bld.at(f"f0_wtm_{s-1}_3"))
            for tt in range(4):
                E.wait_ge("pe", bld.at(f"f0_rmm_{s}_{tt}"))
                E.activation(out=c8(gex, tt), in_=ps[tt][:, 0:8], func=AF.Exp,
                             accum_out=gsum[:, tt:tt + 1]).inc()
                bld.mark(f"f0_rexp_{s}_{tt}", "se")
            for tt in range(4):
                E.wait_ge("pe", bld.at(f"f0_tr_{s}_{tt}"))
                E.activation(out=wfm[0:8, s * 512 + tt * 128:s * 512 + (tt + 1) * 128],
                             in_=psbf[0:8, tt * 128:(tt + 1) * 128], func=AF.Copy).inc()
            bld.mark(f"f0_wfm_{s}", "se")
            E.wait_ge("pe", bld.at(f"f0_hmm_{s}"))
            if s >= 1:
                E.wait_ge("ve", bld.at(f"f0_whb_{s-1}"))
            E.activation(out=hsb[:, :], in_=ps[4][:, :], func=AF.Relu,
                         bias=K["bd_col"][:, :]).inc()
            bld.mark(f"f0_hev_{s}", "se")
            for dt in range(16):
                bank = dt % 4
                g = s * 16 + dt
                E.wait_ge("pe", bld.at(f"f0_dmm_{s}_{dt}"))
                if g >= 4:
                    E.wait_ge(f"st{bank}", bld.at(f"f0_g0st_{g-4}"))
                E.activation(out=stage[:, bank * 512:(bank + 1) * 512],
                             in_=ps[bank][:, :], func=AF.Silu).inc()
                bld.mark(f"f0_g0ev_{s}_{dt}", "se")
                E.fence("se")
                E.dma(bld.SC["g0_s"][dt * 128:(dt + 1) * 128, s * 512:(s + 1) * 512],
                      stage[:, bank * 512:(bank + 1) * 512], sem=f"st{bank}")
                bld.mark(f"f0_g0st_{g}", f"st{bank}")
            bld.mark(f"f0_allev_{s}", "se")
        for b2 in range(4):
            E.wait_ge(f"st{b2}", bld.C[f"st{b2}"])

    def vector(E):
        for s in range(NS):
            for tt in range(4):
                E.wait_ge("se", bld.at(f"f0_rexp_{s}_{tt}"))
                if s >= 1:
                    E.wait_ge("pe", bld.at(f"f0_tr_{s-1}_{tt}"))
                E.reciprocal(rs[:, tt:tt + 1], gsum[:, tt:tt + 1]).inc()
                E.fence()
                E.tensor_scalar(out=c8(gsm, tt), in0=c8(gex, tt),
                                scalar1=rs[:, tt:tt + 1], scalar2=None,
                                op0=ALU.mult).inc()
                E.fence()
                E.max(out=c8(mx, tt), in_=c8(gsm, tt)).inc()
                E.fence()
                E.tensor_tensor(out=rs[:, 4 + tt:5 + tt], in0=mx[:, tt * 8:tt * 8 + 1],
                                in1=mx[:, tt * 8 + 1:tt * 8 + 2], op=ALU.add).inc()
                E.fence()
                E.reciprocal(rs[:, 8 + tt:9 + tt], rs[:, 4 + tt:5 + tt]).inc()
                E.tensor_scalar(out=c8(m1, tt), in0=c8(gsm, tt),
                                scalar1=mx[:, tt * 8:tt * 8 + 1], scalar2=None,
                                op0=ALU.is_equal).inc()
                E.tensor_scalar(out=c8(m2, tt), in0=c8(gsm, tt),
                                scalar1=mx[:, tt * 8 + 1:tt * 8 + 2], scalar2=None,
                                op0=ALU.is_equal).inc()
                E.fence()
                E.tensor_scalar(out=c8(t1, tt), in0=c8(m1, tt),
                                scalar1=mx[:, tt * 8:tt * 8 + 1],
                                scalar2=rs[:, 8 + tt:9 + tt],
                                op0=ALU.mult, op1=ALU.mult).inc()
                E.tensor_scalar(out=c8(t2, tt), in0=c8(m2, tt),
                                scalar1=mx[:, tt * 8 + 1:tt * 8 + 2],
                                scalar2=rs[:, 8 + tt:9 + tt],
                                op0=ALU.mult, op1=ALU.mult).inc()
                E.fence()
                E.tensor_tensor(out=c8(wtm, tt), in0=c8(t1, tt), in1=c8(t2, tt),
                                op=ALU.add).inc()
                bld.mark(f"f0_wtm_{s}_{tt}", "ve")
            E.wait_ge("pe", bld.at(f"f0_we_{s}"))
            E.wait_ge("se", bld.at(f"f0_hev_{s}"))
            if s >= 1:
                E.wait_ge("pe", bld.at(f"f0_done_{s-1}"))
            E.tensor_tensor(out=whb[:, :], in0=hsb[:, :], in1=ps[5][:, :],
                            op=ALU.mult).inc()
            bld.mark(f"f0_whb_{s}", "ve")

    return {"sync": sync, "gpsimd": gpsimd, "tensor": tensor, "scalar": scalar, "vector": vector}

def ph_scan0(bld, es):
    cfg = bld.cfg
    Bc, L, NCL = cfg["Bc"], cfg["L"], cfg["NCL"]
    K = bld.K
    ps = bld.ps
    GS = bld.SC["gates_s"]
    trip = bld.sb(es, "s0_trip", [128, 3 * L], F32)
    oin = bld.sb(es, "s0_oin", [128, 4 * L], F32)
    ctr = bld.sb(es, "s0_ctr", [128, 4 * L], F32)
    us = bld.sb(es, "s0_us", [128, L], F32)
    sq = bld.sb(es, "s0_sq", [128, L], F32)
    cn = bld.sb(es, "s0_cn", [128, L], F32)
    th = bld.sb(es, "s0_th", [128, L], F32)
    invr = bld.sb(es, "s0_inv", [1, L], F32)
    hst = bld.sb(es, "s0_h", [128, 4 * L], BF16)
    NBH = Bc * 4

    def rows(gt, h, q):
        r0 = gt * 2048 + (h * 4 + q) * 128
        return (r0, r0 + 128)

    def sync(E):
        for bh in range(NBH):
            b, h = divmod(bh, 4)
            for q in range(4):
                if bh > 0 or q > 0:
                    pb, pq = (bh, q - 1) if q > 0 else (bh - 1, 3)
                    E.wait_ge("ve", bld.at(f"s0_tripfree_{pb}_{pq}"))
                for slot, gt in ((0, 0), (1, 1), (2, 3)):
                    r0, r1 = rows(gt, h, q)
                    E.dma(trip[:, slot * L:(slot + 1) * L],
                          GS[r0:r1, b * L:(b + 1) * L], sem="ld0")
                bld.mark(f"s0_trip_{bh}_{q}", "ld0")
            if bh > 0:
                E.wait_ge("ve", bld.at(f"s0_hdone_{bh-1}"))
            for q in range(4):
                r0, r1 = rows(2, h, q)
                E.dma(oin[:, q * L:(q + 1) * L], GS[r0:r1, b * L:(b + 1) * L], sem="ld1")
            bld.mark(f"s0_oin_{bh}", "ld1")

    def scalar(E):
        for bh in range(NBH):
            for q in range(4):
                E.wait_ge("ld0", bld.at(f"s0_trip_{bh}_{q}"))
                E.activation(out=trip[:, L:2 * L], in_=trip[:, L:2 * L],
                             func=AF.Sigmoid).inc()
                E.activation(out=trip[:, 2 * L:3 * L], in_=trip[:, 2 * L:3 * L],
                             func=AF.Tanh).inc()
                E.activation(out=trip[:, 0:L], in_=trip[:, 0:L], func=AF.Exp).inc()
                bld.mark(f"s0_acts_{bh}_{q}", "se")
                E.wait_ge("ve", bld.at(f"s0_scan_{bh}_{q}"))
                if q > 0:
                    E.wait_ge("pe", bld.at(f"s0_sqmm_{bh}_{q-1}"))
                elif bh > 0:
                    E.wait_ge("pe", bld.at(f"s0_sqmm_{bh-1}_3"))
                E.activation(out=sq[:, :], in_=ctr[:, q * L:(q + 1) * L],
                             func=AF.Square).inc()
                bld.mark(f"s0_csq_{bh}_{q}", "se")
            E.wait_ge("ld1", bld.at(f"s0_oin_{bh}"))
            E.activation(out=oin[:, :], in_=oin[:, :], func=AF.Sigmoid).inc()
            bld.mark(f"s0_osig_{bh}", "se")
            E.wait_ge("pe", bld.at(f"s0_stats_{bh}"))
            for lc in range(NCL):
                E.activation(out=invr[0:1, lc * 512:(lc + 1) * 512],
                             in_=ps[lc][0:1, :], func=AF.Ln, scale=1.0 / 512,
                             bias=K["eps_t"][0:1, 1:2]).inc()
            bld.mark(f"s0_lnread_{bh}", "se")
            E.fence("se")
            E.activation(out=invr[0:1, :], in_=invr[0:1, :], func=AF.Exp,
                         scale=-0.5).inc()
            bld.mark(f"s0_inv_{bh}", "se")
            for q in range(4):
                E.wait_ge("ve", bld.at(f"s0_cn_{bh}_{q}"))
                if q > 0:
                    E.wait_ge("ve", bld.at(f"s0_hq_{bh}_{q-1}"))
                elif bh > 0:
                    E.wait_ge("ve", bld.at(f"s0_hq_{bh-1}_3"))
                E.activation(out=th[:, :], in_=cn[:, :], func=AF.Tanh,
                             scale=K["rmsw_t"][:, q:q + 1]).inc()
                bld.mark(f"s0_th_{bh}_{q}", "se")

    def vector(E):
        for bh in range(NBH):
            for q in range(4):
                E.wait_ge("se", bld.at(f"s0_acts_{bh}_{q}"))
                if bh > 0:
                    E.wait_ge("se", bld.at(f"s0_csq_{bh-1}_{q}"))
                E.fence()
                E.tensor_tensor(out=us[:, :], in0=trip[:, 0:L], in1=trip[:, 2 * L:3 * L],
                                op=ALU.mult).inc()
                E.fence()
                E.tensor_tensor_scan(out=ctr[:, q * L:(q + 1) * L],
                                     data0=trip[:, L:2 * L], data1=us[:, :],
                                     initial=0.0, op0=ALU.mult, op1=ALU.add).inc()
                bld.mark(f"s0_scan_{bh}_{q}", "ve")
                bld.mark(f"s0_tripfree_{bh}_{q}", "ve")
            for q in range(4):
                E.wait_ge("pe", bld.at(f"s0_bcast_{bh}"))
                E.fence()
                for lc in range(NCL):
                    E.tensor_tensor(out=cn[:, lc * 512:(lc + 1) * 512],
                                    in0=ctr[:, q * L + lc * 512:q * L + (lc + 1) * 512],
                                    in1=ps[lc][:, :], op=ALU.mult).inc()
                bld.mark(f"s0_cn_{bh}_{q}", "ve")
                E.wait_ge("se", bld.at(f"s0_th_{bh}_{q}"))
                E.wait_ge("se", bld.at(f"s0_osig_{bh}"))
                if q == 0 and bh > 0:
                    E.wait_ge("gd0", bld.at(f"s0_hst_{bh-1}"))
                E.fence()
                E.tensor_tensor(out=hst[:, q * L:(q + 1) * L],
                                in0=oin[:, q * L:(q + 1) * L], in1=th[:, :],
                                op=ALU.mult).inc()
                bld.mark(f"s0_hq_{bh}_{q}", "ve")
            bld.mark(f"s0_hdone_{bh}", "ve")

    def tensor(E):
        for bh in range(NBH):
            for q in range(4):
                E.wait_ge("se", bld.at(f"s0_csq_{bh}_{q}"))
                if q == 0 and bh > 0:
                    E.wait_ge("ve", bld.at(f"s0_cn_{bh-1}_3"))
                for lc in range(NCL):
                    mm = E.matmul(ps[lc][0:1, :], lhsT=K["ones_col"][:, :],
                                  rhs=sq[:, lc * 512:(lc + 1) * 512],
                                  start=(q == 0), stop=(q == 3))
                mm.inc()
                bld.mark(f"s0_sqmm_{bh}_{q}", "pe")
            bld.mark(f"s0_stats_{bh}", "pe")
            E.wait_ge("se", bld.at(f"s0_inv_{bh}"))
            for lc in range(NCL):
                mm = E.matmul(ps[lc][:, :], lhsT=K["ones_row"][:, :],
                              rhs=invr[0:1, lc * 512:(lc + 1) * 512],
                              start=True, stop=True)
            mm.inc()
            bld.mark(f"s0_bcast_{bh}", "pe")

    def gpsimd(E):
        for bh in range(NBH):
            b, h = divmod(bh, 4)
            for q in range(4):
                E.wait_ge("ve", bld.at(f"s0_hq_{bh}_{q}"))
                E.dma(bld.SC["h0_s"][(h * 4 + q) * 128:(h * 4 + q + 1) * 128,
                                     b * L:(b + 1) * L],
                      hst[:, q * L:(q + 1) * L], sem="gd0")
            bld.mark(f"s0_hst_{bh}", "gd0")
        E.wait_ge("gd0", bld.C["gd0"])

    return {"sync": sync, "scalar": scalar, "vector": vector, "tensor": tensor,
            "gpsimd": gpsimd}

def ph_wout0(bld, es):
    return _mm_wres(bld, es, "wo0", bld.IN["w_out0"], 16, 16,
                    bld.SC["h0_s"], bld.SC["s0_s"])

def ph_comb1(bld, es):
    cfg = bld.cfg
    NS, D = cfg["NS"], cfg["D"]
    K = bld.K
    ps = bld.ps
    tag = "cb1"
    eps = 1e-5
    xs = bld.sb(es, "cb_x", [128, 2 * 16 * 512], F32)
    s0b = bld.sb(es, "cb_s0", [128, 16 * 512], BF16)
    g0b = bld.sb(es, "cb_g0", [128, 16 * 512], BF16)
    xf = bld.sb(es, "cb_xf", [128, 16 * 512], F32)
    sq = bld.sb(es, "cb_sq", [128, 512], F32)
    row = bld.sb(es, "cb_row", [1, 6 * 512], F32)
    xn = bld.sb(es, "cb_xn", [128, 16 * 512], BF16)
    tmp = bld.sb(es, "cb_tmp", [128, 512], F32)

    def xsv(par, dt):
        return xs[:, (par * 16 + dt) * 512:(par * 16 + dt + 1) * 512]

    def xnv(par, dt):
        return xn[:, dt * 512:(dt + 1) * 512]

    def v16(buf, dt):
        return buf[:, dt * 512:(dt + 1) * 512]

    R = lambda i: row[0:1, i * 512:(i + 1) * 512]
    src, s0s, g0s = bld.IN["x_fm"], bld.SC["s0_s"], bld.SC["g0_s"]
    dx1, dxn, dxn32 = bld.SC["x1_s"], bld.SC["xn1_s"], bld.SC["xn132_s"]
    gex = bld.sb(es, "cb_gex", [128, 32], F32)
    gsum = bld.sb(es, "cb_gsum", [128, 4], F32)
    gsm = bld.sb(es, "cb_gsm", [128, 32], F32)
    mx = bld.sb(es, "cb_mx", [128, 32], F32)
    m1 = bld.sb(es, "cb_m1", [128, 32], F32)
    m2 = bld.sb(es, "cb_m2", [128, 32], F32)
    t1 = bld.sb(es, "cb_t1", [128, 32], F32)
    t2 = bld.sb(es, "cb_t2", [128, 32], F32)
    rs = bld.sb(es, "cb_rs", [128, 4], F32)
    wtm = bld.sb(es, "cb_wtm", [128, 32], BF16)
    wstg = bld.sb(es, "cb_wst", [8, 512], BF16)
    psbf = bld.psbf

    def c8(buf, tt):
        return buf[:, tt * 8:(tt + 1) * 8]

    def sync(E):
        for s in range(NS):
            par = s % 2
            if s >= 2:
                E.wait_ge("ve", bld.at(f"{tag}_xnd_{s-2}"))
                E.wait_ge(f"gd{par}", bld.at(f"{tag}_stx_{s-2}"))
            for g4 in range(4):
                E.dma(xs[:, (par * 16 + g4 * 4) * 512:(par * 16 + g4 * 4 + 4) * 512]
                      .rearrange("p (dt c) -> p dt c", c=512),
                      src[g4 * 512:(g4 + 1) * 512, s * 512:(s + 1) * 512]
                      .rearrange("(dt p) c -> p dt c", p=128), sem=f"ld{par}")
            bld.mark(f"{tag}_ldx_{s}", f"ld{par}")
            if s >= 1:
                E.wait_ge("ve", bld.at(f"{tag}_x1d_{s-1}"))
            for g8, dst_b, src_b in ((0, s0b, s0s), (1, g0b, g0s)):
                for h in range(2):
                    E.dma(dst_b[:, h * 8 * 512:(h + 1) * 8 * 512]
                          .rearrange("p (dt c) -> p dt c", c=512),
                          src_b[h * 8 * 128:(h + 1) * 8 * 128, s * 512:(s + 1) * 512]
                          .rearrange("(dt p) c -> p dt c", p=128), sem="ld2")
            bld.mark(f"{tag}_ldsg_{s}", "ld2")

    def scalar(E):
        for s in range(NS):
            par = s % 2
            for dt in range(16):
                E.wait_ge("ve", bld.at(f"{tag}_x1_{s}_{dt}"))
                E.wait_ge("pe", bld.at(f"{tag}_sqfree_{s}_{dt}"))
                E.activation(out=sq[:, :], in_=xsv(par, dt), func=AF.Square).inc()
                bld.mark(f"{tag}_sq_{s}_{dt}", "se")
            E.wait_ge("ve", bld.at(f"{tag}_var_{s}"))
            E.activation(out=R(4), in_=R(3), func=AF.Ln,
                         bias=K["eps_t"][0:1, 0:1]).inc()
            E.fence()
            E.activation(out=R(5), in_=R(4), func=AF.Exp, scale=-0.5).inc()
            bld.mark(f"{tag}_inv_{s}", "se")
            if s >= 1:
                E.wait_ge("ve", bld.at(f"{tag}_wtm_{s-1}_3"))
            for tt in range(4):
                E.wait_ge("pe", bld.at(f"{tag}_rmm_{s}_{tt}"))
                E.activation(out=c8(gex, tt), in_=ps[4 + tt % 2][:, 0:8], func=AF.Exp,
                             accum_out=gsum[:, tt:tt + 1]).inc()
                bld.mark(f"{tag}_rexp_{s}_{tt}", "se")
            for tt in range(4):
                E.wait_ge("pe", bld.at(f"{tag}_tr_{s}_{tt}"))
                if s >= 1 and tt == 0:
                    E.wait_ge("sd0", bld.at(f"{tag}_wst_{s-1}"))
                E.activation(out=wstg[0:8, tt * 128:(tt + 1) * 128],
                             in_=psbf[0:8, tt * 128:(tt + 1) * 128], func=AF.Copy).inc()
                bld.mark(f"{tag}_wcp_{s}_{tt}", "se")
            E.fence("se")
            E.dma(bld.SC["wfm_s"][0:8, s * 512:(s + 1) * 512], wstg[0:8, :], sem="sd0")
            bld.mark(f"{tag}_wst_{s}", "sd0")

    def tensor(E):
        for s in range(NS):
            par = s % 2
            if s >= 1:
                E.wait_ge("ve", bld.at(f"{tag}_statsread_{s-1}"))
            for dt in range(16):
                E.wait_ge("ve", bld.at(f"{tag}_x1_{s}_{dt}"))
                E.matmul(ps[0][0:1, :], lhsT=K["ones_col"][:, :], rhs=xsv(par, dt),
                         start=(dt == 0), stop=(dt == 15))
            for dt in range(16):
                bld.mark(f"{tag}_sqfree_{s}_{dt}", "pe")
                E.wait_ge("se", bld.at(f"{tag}_sq_{s}_{dt}"))
                E.matmul(ps[1][0:1, :], lhsT=K["ones_col"][:, :], rhs=sq[:, :],
                         start=(dt == 0), stop=(dt == 15)).then_inc("pe", 1)
            bld.mark(f"{tag}_stats_{s}", "pe")
            E.wait_ge("ve", bld.at(f"{tag}_minv_{s}"))
            if s >= 1:
                E.wait_ge("ve", bld.at(f"{tag}_xnd_{s-1}"))
            E.matmul(ps[2][:, :], lhsT=K["ones_row"][:, :], rhs=R(5), start=True, stop=True)
            E.matmul(ps[3][:, :], lhsT=K["ones_row"][:, :], rhs=R(2), start=True,
                     stop=True).then_inc("pe", 1)
            bld.mark(f"{tag}_bcast_{s}", "pe")
            E.wait_ge("ve", bld.at(f"{tag}_xnd_{s}"))
            for tt in range(4):
                if tt >= 2:
                    E.wait_ge("se", bld.at(f"{tag}_rexp_{s}_{tt-2}"))
                elif s >= 1:
                    E.wait_ge("se", bld.at(f"{tag}_rexp_{s-1}_{tt+2}"))
                for kt in range(16):
                    mm = E.matmul(ps[4 + tt % 2][:, 0:8],
                                  lhsT=xf[:, kt * 512 + tt * 128:kt * 512 + (tt + 1) * 128],
                                  rhs=K["gatew_t"][:, kt * 8:(kt + 1) * 8],
                                  start=(kt == 0), stop=(kt == 15))
                mm.inc()
                bld.mark(f"{tag}_rmm_{s}_{tt}", "pe")
            if s >= 1:
                E.wait_ge("se", bld.at(f"{tag}_wcp_{s-1}_3"))
            for tt in range(4):
                E.wait_ge("ve", bld.at(f"{tag}_wtm_{s}_{tt}"))
                E.transpose(psbf[0:8, tt * 128:(tt + 1) * 128], in_=c8(wtm, tt),
                            identity=K["ident_bf"][:, :]).inc()
                bld.mark(f"{tag}_tr_{s}_{tt}", "pe")

    def vector(E):
        for s in range(NS):
            par = s % 2
            E.wait_ge(f"ld{par}", bld.at(f"{tag}_ldx_{s}"))
            E.wait_ge("ld2", bld.at(f"{tag}_ldsg_{s}"))
            for dt in range(16):
                E.fence()
                E.tensor_tensor(out=tmp[:, :], in0=v16(s0b, dt), in1=v16(g0b, dt),
                                op=ALU.mult).inc()
                E.fence()
                E.tensor_tensor(out=xsv(par, dt), in0=xsv(par, dt), in1=tmp[:, :],
                                op=ALU.add).inc()
                bld.mark(f"{tag}_x1_{s}_{dt}", "ve")
            bld.mark(f"{tag}_x1d_{s}", "ve")
            E.wait_ge("pe", bld.at(f"{tag}_stats_{s}"))
            E.tensor_scalar(out=R(0), in0=ps[0][0:1, :], scalar1=1.0 / D,
                            scalar2=None, op0=ALU.mult).inc()
            E.tensor_scalar(out=R(1), in0=ps[1][0:1, :], scalar1=1.0 / D,
                            scalar2=None, op0=ALU.mult).inc()
            bld.mark(f"{tag}_statsread_{s}", "ve")
            E.fence()
            E.tensor_tensor(out=tmp[0:1, :], in0=R(0), in1=R(0), op=ALU.mult).inc()
            E.fence()
            E.tensor_tensor(out=R(3), in0=R(1), in1=tmp[0:1, :], op=ALU.subtract).inc()
            bld.mark(f"{tag}_var_{s}", "ve")
            E.wait_ge("se", bld.at(f"{tag}_inv_{s}"))
            E.tensor_tensor(out=R(2), in0=R(0), in1=R(5), op=ALU.mult).inc()
            bld.mark(f"{tag}_minv_{s}", "ve")
            E.wait_ge("pe", bld.at(f"{tag}_bcast_{s}"))
            if s >= 1:
                E.wait_ge(f"gd{1-par}", bld.at(f"{tag}_stn_{s-1}"))
            if s >= 2:
                E.wait_ge(f"gd{par}", bld.at(f"{tag}_stn_{s-2}"))
            E.fence()
            for dt in range(16):
                E.tensor_tensor(out=tmp[:, :], in0=xsv(par, dt), in1=ps[2][:, :],
                                op=ALU.mult).inc()
                E.fence()
                E.tensor_tensor(out=v16(xf, dt), in0=tmp[:, :], in1=ps[3][:, :],
                                op=ALU.subtract).inc()
                E.fence()
                E.tensor_copy(out=xnv(par, dt), in_=v16(xf, dt)).inc()
                E.fence()
            bld.mark(f"{tag}_xnd_{s}", "ve")
            for tt in range(4):
                E.wait_ge("se", bld.at(f"{tag}_rexp_{s}_{tt}"))
                if s >= 1:
                    E.wait_ge("pe", bld.at(f"{tag}_tr_{s-1}_{tt}"))
                E.fence()
                E.reciprocal(rs[:, tt:tt + 1], gsum[:, tt:tt + 1]).inc()
                E.fence()
                E.tensor_scalar(out=c8(gsm, tt), in0=c8(gex, tt),
                                scalar1=rs[:, tt:tt + 1], scalar2=None,
                                op0=ALU.mult).inc()
                E.fence()
                E.max(out=c8(mx, tt), in_=c8(gsm, tt)).inc()
                E.fence()
                E.tensor_scalar(out=c8(m1, tt), in0=c8(gsm, tt),
                                scalar1=mx[:, tt * 8:tt * 8 + 1], scalar2=None,
                                op0=ALU.is_equal).inc()
                E.tensor_scalar(out=c8(m2, tt), in0=c8(gsm, tt),
                                scalar1=mx[:, tt * 8 + 1:tt * 8 + 2], scalar2=None,
                                op0=ALU.is_equal).inc()
                E.fence()
                E.tensor_scalar(out=c8(t1, tt), in0=c8(m1, tt),
                                scalar1=mx[:, tt * 8:tt * 8 + 1], scalar2=None,
                                op0=ALU.mult).inc()
                E.tensor_scalar(out=c8(t2, tt), in0=c8(m2, tt),
                                scalar1=mx[:, tt * 8 + 1:tt * 8 + 2], scalar2=None,
                                op0=ALU.mult).inc()
                E.fence()
                E.tensor_tensor(out=c8(wtm, tt), in0=c8(t1, tt), in1=c8(t2, tt),
                                op=ALU.add).inc()
                bld.mark(f"{tag}_wtm_{s}_{tt}", "ve")

    def gpsimd(E):
        for s in range(NS):
            par = s % 2
            E.wait_ge("ve", bld.at(f"{tag}_x1d_{s}"))
            for g4 in range(4):
                E.dma(dx1[g4 * 512:(g4 + 1) * 512, s * 512:(s + 1) * 512]
                      .rearrange("(dt p) c -> p dt c", p=128),
                      xs[:, (par * 16 + g4 * 4) * 512:(par * 16 + g4 * 4 + 4) * 512]
                      .rearrange("p (dt c) -> p dt c", c=512), sem=f"gd{par}")
            bld.mark(f"{tag}_stx_{s}", f"gd{par}")
            E.wait_ge("ve", bld.at(f"{tag}_xnd_{s}"))
            for g4 in range(4):
                E.dma(dxn[g4 * 512:(g4 + 1) * 512, s * 512:(s + 1) * 512]
                      .rearrange("(dt p) c -> p dt c", p=128),
                      xn[:, g4 * 4 * 512:(g4 * 4 + 4) * 512]
                      .rearrange("p (dt c) -> p dt c", c=512), sem=f"gd{par}")
                E.dma(dxn32[g4 * 512:(g4 + 1) * 512, s * 512:(s + 1) * 512]
                      .rearrange("(dt p) c -> p dt c", p=128),
                      xf[:, g4 * 4 * 512:(g4 * 4 + 4) * 512]
                      .rearrange("p (dt c) -> p dt c", c=512), sem=f"gd{par}")
            bld.mark(f"{tag}_stn_{s}", f"gd{par}")
        for p2 in range(min(2, NS)):
            E.wait_ge(f"gd{p2}", bld.C[f"gd{p2}"])

    return {"sync": sync, "scalar": scalar, "tensor": tensor, "vector": vector,
            "gpsimd": gpsimd}

def ph_win1(bld, es):
    cfg = bld.cfg
    Bc, L, NCL, T = cfg["Bc"], cfg["L"], cfg["NCL"], cfg["T"]
    K = bld.K
    ps = bld.ps
    n_kt, n_mt = 16, 64
    xr = bld.sb(es, "w1_x", [128, n_kt * T], BF16)
    wsl = bld.sb(es, "w1_w", [128, 2 * n_kt * 128], BF16)
    cvb = bld.sb(es, "w1_cv", [128, 3 + L], F32)
    acc = bld.sb(es, "w1_acc", [128, L], F32)
    hstg = bld.sb(es, "w1_hst", [128, 2 * L], BF16)
    gstg = bld.sb(es, "w1_gst", [128, 4 * 512], BF16)
    src = bld.SC["xn1_s"]
    dsth, dstg = bld.SC["hconv_s"], bld.SC["sg_s"]

    def sync(E):
        for h in range(n_kt):
            E.dma(xr[:, h * T:(h + 1) * T], src[h * 128:(h + 1) * 128, :], sem="ld0")
        bld.mark("w1_rhs", "ld0")

    def gpsimd(E):
        for mt in range(n_mt):
            if mt >= 2:
                E.wait_ge("pe", bld.at(f"w1_wfree_{mt-2}"))
            E.dma(wsl[:, (mt % 2) * n_kt * 128:((mt % 2) + 1) * n_kt * 128],
                  bld.IN["w_in1"][mt * 128:(mt + 1) * 128, :], sem=f"wl{mt % 2}")
            bld.mark(f"w1_wld_{mt}", f"wl{mt % 2}")

    def tensor(E):
        E.wait_ge("ld0", bld.at("w1_rhs"))
        g = 0
        for mt in range(n_mt):
            par = mt % 2
            E.wait_ge(f"wl{par}", bld.at(f"w1_wld_{mt}"))
            for b in range(Bc):
                for lc in range(NCL):
                    bank = g % 4
                    if g >= 4:
                        E.wait_ge("se", bld.at(f"w1_ev_{g-4}"))
                    for kt in range(n_kt):
                        mm = E.matmul(ps[bank][:, :],
                                      lhsT=wsl[:, (par * n_kt + kt) * 128:
                                              (par * n_kt + kt + 1) * 128],
                                      rhs=xr[:, kt * T + b * L + lc * 512:
                                             kt * T + b * L + (lc + 1) * 512],
                                      start=(kt == 0), stop=(kt == n_kt - 1))
                    mm.inc()
                    bld.mark(f"w1_mm_{g}", "pe")
                    g += 1
            bld.mark(f"w1_wfree_{mt}", "pe")

    def scalar(E):
        g = 0
        for mt in range(n_mt):
            for b in range(Bc):
                ob = (mt * Bc + b) % 2
                for lc in range(NCL):
                    bank = g % 4
                    E.wait_ge("pe", bld.at(f"w1_mm_{g}"))
                    if mt < 32:
                        if mt + b > 0:
                            E.wait_ge("ve", bld.at(f"w1_acc_{(mt * Bc + b) - 1}"))
                        E.activation(out=cvb[:, 3 + lc * 512:3 + (lc + 1) * 512],
                                     in_=ps[bank][:, :], func=AF.Copy).inc()
                    else:
                        gg = (mt - 32) * Bc * NCL + b * NCL + lc
                        if gg >= 4:
                            E.wait_ge(f"st{bank}", bld.at(f"w1_gst_{gg-4}"))
                        E.activation(out=gstg[:, bank * 512:(bank + 1) * 512],
                                     in_=ps[bank][:, :], func=AF.Silu).inc()
                        E.fence("se")
                        E.dma(dstg[(mt - 32) * 128:(mt - 31) * 128,
                                   b * L + lc * 512:b * L + (lc + 1) * 512],
                              gstg[:, bank * 512:(bank + 1) * 512], sem=f"st{bank}")
                        bld.mark(f"w1_gst_{gg}", f"st{bank}")
                    bld.mark(f"w1_ev_{g}", "se")
                    g += 1
                if mt < 32:
                    bld.mark(f"w1_cvb_{mt * Bc + b}", "se")
                    E.wait_ge("ve", bld.at(f"w1_conv_{mt * Bc + b}"))
                    if mt * Bc + b >= 2:
                        E.wait_ge(f"sd{ob}", bld.at(f"w1_hstd_{mt * Bc + b - 2}"))
                    E.activation(out=hstg[:, ob * L:(ob + 1) * L], in_=acc[:, :],
                                 func=AF.Silu, bias=K["convb_t"][:, mt:mt + 1]).inc()
                    bld.mark(f"w1_hsig_{mt * Bc + b}", "se")
                    E.fence("se")
                    E.dma(dsth[mt * 128:(mt + 1) * 128, b * L:(b + 1) * L],
                          hstg[:, ob * L:(ob + 1) * L], sem=f"sd{ob}")
                    bld.mark(f"w1_hstd_{mt * Bc + b}", f"sd{ob}")
        for b2 in range(4):
            E.wait_ge(f"st{b2}", bld.C[f"st{b2}"])
        for b2 in range(2):
            E.wait_ge(f"sd{b2}", bld.C[f"sd{b2}"])

    def vector(E):
        E.memset(cvb[:, 0:3], 0.0).inc()
        for ob in range(32 * Bc):
            mt = ob // Bc
            E.wait_ge("se", bld.at(f"w1_cvb_{ob}"))
            if ob >= 1:
                E.wait_ge("se", bld.at(f"w1_hsig_{ob-1}"))
            E.fence()
            E.tensor_scalar(out=acc[:, :], in0=cvb[:, 0:L],
                            scalar1=K["convw_t"][:, mt * 4:mt * 4 + 1],
                            scalar2=None, op0=ALU.mult).inc()
            for k in range(1, 4):
                E.fence()
                E.scalar_tensor_tensor(out=acc[:, :], in0=cvb[:, k:k + L],
                                       scalar=K["convw_t"][:, mt * 4 + k:mt * 4 + k + 1],
                                       in1=acc[:, :], op0=ALU.mult, op1=ALU.add).inc()
            bld.mark(f"w1_conv_{ob}", "ve")

    return {"sync": sync, "gpsimd": gpsimd, "tensor": tensor, "scalar": scalar,
            "vector": vector}

def ph_ssmp(bld, es):
    cfg = bld.cfg
    Bc, L, NCL = cfg["Bc"], cfg["L"], cfg["NCL"]
    K = bld.K
    ps = bld.ps
    rhsb = bld.sb(es, "sp_x", [128, 2 * 32 * 512], BF16)
    wxl = bld.sb(es, "sp_wx", [128, 4096], BF16)
    wdtl = bld.sb(es, "sp_wdt", [128, 4096], BF16)
    tssq = bld.sb(es, "sp_tssq", [128, 512], F32)
    ts_sb = bld.sb(es, "sp_ts", [128, 512], F32)
    tsn = bld.sb(es, "sp_tsn", [128, 512], BF16)
    bcsq = bld.sb(es, "sp_bcsq", [32, 512], F32)
    bc_sb = bld.sb(es, "sp_bc", [32, 512], F32)
    bcn = bld.sb(es, "sp_bcn", [32, 512], BF16)
    tmp2 = bld.sb(es, "sp_tmp2", [32, 512], F32)
    invr = bld.sb(es, "sp_inv", [1, 512], F32)
    inv2 = bld.sb(es, "sp_inv2", [2, 512], F32)
    spa = bld.sb(es, "sp_a", [128, 512], F32)
    spr = bld.sb(es, "sp_r", [128, 512], F32)
    dstage = bld.sb(es, "sp_dst", [128, 4 * 512], F32)
    tmp = bld.sb(es, "sp_tmp", [128, 512], F32)
    src = bld.SC["hconv_s"]
    NCH = Bc * NCL

    def gpsimd(E):
        E.dma(wxl[:, :], bld.IN["wx_ts"][:, :], sem="wl0")
        E.dma(wdtl.ap().rearrange("p (bk m) -> p bk m", m=128),
              bld.IN["wdt_b"].rearrange("(bk p) m -> p bk m", p=128), sem="wl0")
        bld.mark("sp_wld", "wl0")
        for c in range(NCH):
            b, lc = divmod(c, NCL)
            E.wait_ge("ve", bld.at(f"sp_bcn_{c}"))
            E.dma(bld.SC["bc_s"][0:32, b * L + lc * 512:b * L + (lc + 1) * 512],
                  bcn[0:32, :], sem="gd0")
            bld.mark(f"sp_bcst_{c}", "gd0")
        E.wait_ge("gd0", bld.C["gd0"])

    def rv(par, kt):
        return rhsb[:, (par * 32 + kt) * 512:(par * 32 + kt + 1) * 512]

    def tensor(E):
        E.wait_ge("wl0", bld.at("sp_wld"))
        for c in range(NCH):
            par = c % 2
            E.wait_ge(f"ld{par}", bld.at(f"sp_rhs_{c}"))
            if c >= 1:
                E.wait_ge("se", bld.at(f"sp_psfree_{c-1}"))
            for kt in range(32):
                mm = E.matmul(ps[0][:, :], lhsT=wxl[:, kt * 128:(kt + 1) * 128],
                              rhs=rv(par, kt), start=(kt == 0), stop=(kt == 31))
            mm.inc()
            bld.mark(f"sp_tsmm_{c}", "pe")
            for kt in range(32):
                mm = E.matmul(ps[1][0:32, :], lhsT=K["wx_bc"][:, kt * 32:(kt + 1) * 32],
                              rhs=rv(par, kt), start=(kt == 0), stop=(kt == 31))
            mm.inc()
            bld.mark(f"sp_bcmm_{c}", "pe")
            E.wait_ge("se", bld.at(f"sp_tssq_{c}"))
            E.matmul(ps[2][0:1, :], lhsT=K["ones_col"][:, :], rhs=tssq[:, :],
                     start=True, stop=True).inc()
            bld.mark(f"sp_st1_{c}", "pe")
            E.wait_ge("se", bld.at(f"sp_bcsq_{c}"))
            if c >= 1:
                E.wait_ge("se", bld.at(f"sp_spl_{c-1}_31"))
            E.matmul(ps[6][0:2, :], lhsT=K["sel32"][0:32, :], rhs=bcsq[0:32, :],
                     start=True, stop=True).inc()
            bld.mark(f"sp_st2_{c}", "pe")
            E.wait_ge("se", bld.at(f"sp_inv_{c}"))
            if c >= 1:
                E.wait_ge("ve", bld.at(f"sp_tsn_{c-1}"))
            E.matmul(ps[3][:, :], lhsT=K["ones_row"][:, :], rhs=invr[0:1, :],
                     start=True, stop=True).inc()
            bld.mark(f"sp_bc1_{c}", "pe")
            E.wait_ge("se", bld.at(f"sp_inv2_{c}"))
            if c >= 1:
                E.wait_ge("ve", bld.at(f"sp_bcn_{c-1}"))
            E.matmul(ps[4][0:32, :], lhsT=K["sel2"][0:2, :], rhs=inv2[0:2, :],
                     start=True, stop=True).inc()
            bld.mark(f"sp_bc2_{c}", "pe")
            E.wait_ge("ve", bld.at(f"sp_tsn_{c}"))
            E.wait_ge("se", bld.at(f"sp_inv2r_{c}"))
            for it in range(32):
                bank = 5 + it % 2
                if it >= 2:
                    E.wait_ge("se", bld.at(f"sp_spl_{c}_{it-2}"))
                elif c >= 1:
                    E.wait_ge("se", bld.at(f"sp_spl_{c-1}_{30 + it}"))
                E.matmul(ps[bank][:, :], lhsT=wdtl[:, it * 128:(it + 1) * 128],
                         rhs=tsn[:, :], start=True, stop=True).inc()
                bld.mark(f"sp_dt_{c}_{it}", "pe")
            bld.mark(f"sp_done_{c}", "pe")

    def scalar(E):
        for c in range(NCH):
            E.wait_ge("pe", bld.at(f"sp_tsmm_{c}"))
            if c >= 1:
                E.wait_ge("pe", bld.at(f"sp_st1_{c-1}"))
            E.activation(out=tssq[:, :], in_=ps[0][:, :], func=AF.Square).inc()
            bld.mark(f"sp_tssq_{c}", "se")
            E.wait_ge("ve", bld.at(f"sp_tsn1_{c-1}") if c >= 1 else 0)
            E.activation(out=ts_sb[:, :], in_=ps[0][:, :], func=AF.Copy).inc()
            bld.mark(f"sp_tscp_{c}", "se")
            E.wait_ge("pe", bld.at(f"sp_bcmm_{c}"))
            if c >= 1:
                E.wait_ge("pe", bld.at(f"sp_st2_{c-1}"))
            E.activation(out=bcsq[0:32, :], in_=ps[1][0:32, :], func=AF.Square).inc()
            bld.mark(f"sp_bcsq_{c}", "se")
            E.wait_ge("ve", bld.at(f"sp_bcn1_{c-1}") if c >= 1 else 0)
            E.activation(out=bc_sb[0:32, :], in_=ps[1][0:32, :], func=AF.Copy).inc()
            bld.mark(f"sp_bccp_{c}", "se")
            bld.mark(f"sp_psfree_{c}", "se")
            E.wait_ge("pe", bld.at(f"sp_st1_{c}"))
            if c >= 1:
                E.wait_ge("pe", bld.at(f"sp_bc1_{c-1}"))
            E.activation(out=invr[0:1, :], in_=ps[2][0:1, :], func=AF.Ln,
                         scale=1.0 / 128, bias=K["eps_t"][0:1, 0:1]).inc()
            E.fence("se")
            E.activation(out=invr[0:1, :], in_=invr[0:1, :], func=AF.Exp,
                         scale=-0.5).inc()
            bld.mark(f"sp_inv_{c}", "se")
            E.wait_ge("pe", bld.at(f"sp_st2_{c}"))
            if c >= 1:
                E.wait_ge("pe", bld.at(f"sp_bc2_{c-1}"))
            E.activation(out=inv2[0:2, :], in_=ps[6][0:2, :], func=AF.Ln,
                         scale=1.0 / 16, bias=K["eps_t"][0:2, 0:1]).inc()
            E.fence("se")
            E.activation(out=inv2[0:2, :], in_=inv2[0:2, :], func=AF.Exp,
                         scale=-0.5).inc()
            bld.mark(f"sp_inv2_{c}", "se")
            bld.mark(f"sp_inv2r_{c}", "se")
            for it in range(32):
                bank = 5 + it % 2
                E.wait_ge("pe", bld.at(f"sp_dt_{c}_{it}"))
                if it >= 1:
                    E.wait_ge("ve", bld.at(f"sp_del_{c}_{it-1}"))
                elif c >= 1:
                    E.wait_ge("ve", bld.at(f"sp_del_{c-1}_31"))
                E.activation(out=spa[:, :], in_=ps[bank][:, :], func=AF.Abs,
                             bias=K["dtb_t"][:, it:it + 1]).inc()
                E.activation(out=spr[:, :], in_=ps[bank][:, :], func=AF.Relu,
                             bias=K["dtb_t"][:, it:it + 1]).inc()
                bld.mark(f"sp_spl0_{c}_{it}", "se")
                E.fence("se")
                E.activation(out=spa[:, :], in_=spa[:, :], func=AF.Exp,
                             scale=-1.0).inc()
                E.fence("se")
                E.activation(out=spa[:, :], in_=spa[:, :], func=AF.Ln,
                             bias=1.0).inc()
                bld.mark(f"sp_spl_{c}_{it}", "se")

    def vector(E):
        for c in range(NCH):
            b, lc = divmod(c, NCL)
            E.wait_ge("pe", bld.at(f"sp_bc1_{c}"))
            E.wait_ge("se", bld.at(f"sp_tscp_{c}"))
            E.tensor_tensor(out=tmp[:, :], in0=ts_sb[:, :], in1=ps[3][:, :],
                            op=ALU.mult).inc()
            bld.mark(f"sp_tsn1_{c}", "ve")
            E.fence()
            E.tensor_scalar(out=tsn[:, :], in0=tmp[:, :], scalar1=K["dtlnw_t"][:, :],
                            scalar2=None, op0=ALU.mult).inc()
            bld.mark(f"sp_tsn_{c}", "ve")
            E.wait_ge("pe", bld.at(f"sp_bc2_{c}"))
            E.wait_ge("se", bld.at(f"sp_bccp_{c}"))
            if c >= 1:
                E.wait_ge("gd0", bld.at(f"sp_bcst_{c-1}"))
            E.fence()
            E.tensor_tensor(out=tmp2[0:32, :], in0=bc_sb[0:32, :], in1=ps[4][0:32, :],
                            op=ALU.mult).inc()
            bld.mark(f"sp_bcn1_{c}", "ve")
            E.fence()
            E.tensor_scalar(out=bcn[0:32, :], in0=tmp2[0:32, :],
                            scalar1=K["lnw32"][0:32, :], scalar2=None,
                            op0=ALU.mult).inc()
            bld.mark(f"sp_bcn_{c}", "ve")
            for it in range(32):
                slot = it % 4
                E.wait_ge("se", bld.at(f"sp_spl_{c}_{it}"))
                if it % 4 == 0:
                    it4p = it // 4 - 1
                    if it4p >= 0:
                        E.wait_ge(f"st{it4p % 2}", bld.at(f"sp_dst_{c}_{it4p}"))
                    elif c >= 1:
                        E.wait_ge("st1", bld.at(f"sp_dst_{c-1}_7"))
                E.fence()
                E.tensor_tensor(out=dstage[:, slot * 512:(slot + 1) * 512],
                                in0=spa[:, :], in1=spr[:, :], op=ALU.add).inc()
                bld.mark(f"sp_del_{c}_{it}", "ve")

    def sync_all(E):
        for c in range(NCH + 1):
            if c < NCH:
                b, lc = divmod(c, NCL)
                par = c % 2
                if c >= 2:
                    E.wait_ge("pe", bld.at(f"sp_done_{c-2}"))
                for h in range(4):
                    E.dma(rhsb[:, (par * 32 + h * 8) * 512:(par * 32 + (h + 1) * 8) * 512]
                          .rearrange("p (kt x) -> p kt x", x=512),
                          src[h * 8 * 128:(h + 1) * 8 * 128,
                              b * L + lc * 512:b * L + (lc + 1) * 512]
                          .rearrange("(kt p) x -> p kt x", p=128), sem=f"ld{par}")
                bld.mark(f"sp_rhs_{c}", f"ld{par}")
            if c >= 1:
                cc = c - 1
                b, lc = divmod(cc, NCL)
                for it4 in range(8):
                    E.wait_ge("ve", bld.at(f"sp_del_{cc}_{it4 * 4 + 3}"))
                    E.dma(bld.SC["delta_s"][it4 * 4 * 128:(it4 + 1) * 4 * 128,
                                            b * L + lc * 512:b * L + (lc + 1) * 512]
                          .rearrange("(q p) x -> p q x", p=128),
                          dstage.ap().rearrange("p (q x) -> p q x", x=512),
                          sem=f"st{it4 % 2}")
                    bld.mark(f"sp_dst_{cc}_{it4}", f"st{it4 % 2}")
        for b2 in range(2):
            E.wait_ge(f"st{b2}", bld.C[f"st{b2}"])

    return {"sync": sync_all, "gpsimd": gpsimd, "tensor": tensor, "scalar": scalar,
            "vector": vector}

def ph_scan1(bld, es):
    cfg = bld.cfg
    Bc, L, NCL = cfg["Bc"], cfg["L"], cfg["NCL"]
    K = bld.K
    ps = bld.ps
    G = 2
    NIT = 32 // G
    delta2 = bld.sb(es, "s1_d", [128, G * L], F32)
    u2 = bld.sb(es, "s1_u", [128, G * L], BF16)
    sg2 = bld.sb(es, "s1_sg", [128, G * L], BF16)
    du2 = bld.sb(es, "s1_du", [128, G * L], BF16)
    dA2 = bld.sb(es, "s1_dA", [128, G * L], F32)
    dbu_v = bld.sb(es, "s1_dbv", [128, L], BF16)
    dbu_g = bld.sb(es, "s1_dbg", [128, L], BF16)
    s_v = bld.sb(es, "s1_sv", [128, L], BF16)
    s_g = bld.sb(es, "s1_sgg", [128, L], BF16)
    t_v = bld.sb(es, "s1_tv", [128, L], BF16)
    t_g = bld.sb(es, "s1_tg", [128, L], BF16)
    yac = bld.sb(es, "s1_y", [128, G * L], F32)
    bcrow = bld.sb(es, "s1_bcr", [1, 2 * 2 * L], BF16)
    Bb = bld.sb(es, "s1_Bb", [128, 2 * L], BF16)
    Cb = bld.sb(es, "s1_Cb", [128, 2 * L], BF16)
    ystg = du2
    NBG = Bc * NIT

    def dv(i):
        return delta2[:, i * L:(i + 1) * L]

    def uv(i):
        return u2[:, i * L:(i + 1) * L]

    def duv(i):
        return du2[:, i * L:(i + 1) * L]

    def yv(i):
        return yac[:, i * L:(i + 1) * L]

    def Bv(n):
        return Bb[:, (n % 2) * L:(n % 2 + 1) * L]

    def Cv(n):
        return Cb[:, (n % 2) * L:(n % 2 + 1) * L]

    def sync(E):
        for bg in range(NBG):
            b, g = divmod(bg, NIT)
            if bg >= 1:
                E.wait_ge("ve", bld.at(f"s1_grpdone_{bg-1}"))
                E.wait_ge("ge", bld.at(f"s1_gdone_{bg-1}"))
                E.wait_ge("gd0", bld.at(f"s1_yst_{bg-1}"))
            r0 = g * G * 128
            E.dma(delta2.ap().rearrange("p (q x) -> p q x", x=L),
                  bld.SC["delta_s"][r0:r0 + G * 128, b * L:(b + 1) * L]
                  .rearrange("(q p) x -> p q x", p=128), sem="ld0")
            E.dma(u2.ap().rearrange("p (q x) -> p q x", x=L),
                  bld.SC["hconv_s"][r0:r0 + G * 128, b * L:(b + 1) * L]
                  .rearrange("(q p) x -> p q x", p=128), sem="ld0")
            E.dma(sg2.ap().rearrange("p (q x) -> p q x", x=L),
                  bld.SC["sg_s"][r0:r0 + G * 128, b * L:(b + 1) * L]
                  .rearrange("(q p) x -> p q x", p=128), sem="ld0")
            bld.mark(f"s1_ld_{bg}", "ld0")
            for n in range(16):
                par = n % 2
                if bg + n > 0:
                    pk = (f"s1_bcmm_{bg}_{n-2}" if n >= 2 else
                          (f"s1_bcmm_{bg}_{n-1}" if n == 1 else f"s1_bcmm_{bg-1}_15"))
                    if n >= 2:
                        E.wait_ge("pe", bld.at(f"s1_bcmm_{bg}_{n-2}"))
                    elif bg > 0:
                        E.wait_ge("pe", bld.at(f"s1_bcmm_{bg-1}_{14 + n}"))
                sem = "ld1" if par == 0 else "ld3"
                E.dma(bcrow[0:1, par * 2 * L:par * 2 * L + L],
                      bld.SC["bc_s"][n:n + 1, b * L:(b + 1) * L], sem=sem)
                E.dma(bcrow[0:1, par * 2 * L + L:(par + 1) * 2 * L],
                      bld.SC["bc_s"][16 + n:17 + n, b * L:(b + 1) * L], sem=sem)
                bld.mark(f"s1_bcld_{bg}_{n}", sem)

    def tensor(E):
        for bg in range(NBG):
            for n in range(16):
                par = n % 2
                E.wait_ge("ld1" if par == 0 else "ld3", bld.at(f"s1_bcld_{bg}_{n}"))
                if n >= 2:
                    E.wait_ge("ve", bld.at(f"s1_p_{bg}_{n-2}_0"))
                    E.wait_ge("ge", bld.at(f"s1_gp_{bg}_{n-2}"))
                elif bg > 0:
                    E.wait_ge("ve", bld.at(f"s1_p_{bg-1}_{14 + n}_0"))
                    E.wait_ge("ge", bld.at(f"s1_gp_{bg-1}_{14 + n}"))
                for j in range(2):
                    for lc in range(NCL):
                        u = j * NCL + lc
                        bank = 4 + u % 2
                        if u >= 2:
                            E.wait_ge("se", bld.at(f"s1_bcc_{bg}_{n}_{u-2}"))
                        elif bg + n > 0:
                            pu = 2 * NCL - 2 + u
                            E.wait_ge("se", bld.at(f"s1_bcc_{bg}_{n-1}_{pu}" if n > 0
                                                   else f"s1_bcc_{bg-1}_15_{pu}"))
                        E.matmul(ps[bank][:, :512],
                                 lhsT=K["ones_row_bf"][:, :],
                                 rhs=bcrow[0:1, par * 2 * L + j * L + lc * 512:
                                           par * 2 * L + j * L + (lc + 1) * 512],
                                 start=True, stop=True).inc()
                        bld.mark(f"s1_bcb_{bg}_{n}_{u}", "pe")
                bld.mark(f"s1_bcmm_{bg}_{n}", "pe")

    def scalar(E):
        for bg in range(NBG):
            b, g = divmod(bg, NIT)
            E.wait_ge("ld0", bld.at(f"s1_ld_{bg}"))
            for n in range(16):
                for j in range(2):
                    buf = Bb if j == 0 else Cb
                    for lc in range(NCL):
                        u = j * NCL + lc
                        bank = 4 + u % 2
                        E.wait_ge("pe", bld.at(f"s1_bcb_{bg}_{n}_{u}"))
                        E.activation(out=buf[:, (n % 2) * L + lc * 512:
                                             (n % 2) * L + (lc + 1) * 512],
                                     in_=ps[bank][:, :512], func=AF.Copy).inc()
                        bld.mark(f"s1_bcc_{bg}_{n}_{u}", "se")
                for i in range(G):
                    it = g * G + i
                    if n > 0:
                        E.wait_ge("ve" if i == 0 else "ve",
                                  bld.at(f"s1_scan_{bg}_{n-1}_{i}"))
                    elif bg > 0:
                        E.wait_ge("ve", bld.at(f"s1_scan_{bg-1}_15_{i}"))
                    E.activation(out=dA2[:, i * L:(i + 1) * L], in_=dv(i),
                                 func=AF.Exp,
                                 scale=K["a_t"][:, it * 16 + n:it * 16 + n + 1]).inc()
                    bld.mark(f"s1_dA_{bg}_{n}_{i}", "se")

    def vector(E):
        for bg in range(NBG):
            b, g = divmod(bg, NIT)
            E.wait_ge("ld0", bld.at(f"s1_ld_{bg}"))
            if bg >= 1:
                E.wait_ge("ge", bld.at(f"s1_gdone_{bg-1}"))
                E.wait_ge("gd0", bld.at(f"s1_yst_{bg-1}"))
            E.fence()
            for i in range(G):
                E.tensor_tensor(out=duv(i), in0=dv(i), in1=uv(i), op=ALU.mult).inc()
            bld.mark(f"s1_dud_{bg}", "ve")
            E.fence()
            for n in range(16):
                E.wait_ge("se", bld.at(f"s1_bcc_{bg}_{n}_{NCL-1}"))
                E.fence()
                E.tensor_tensor(out=dbu_v[:, :], in0=duv(0), in1=Bv(n),
                                op=ALU.mult).inc()
                bld.mark(f"s1_dbu_{bg}_{n}_0", "ve")
                E.wait_ge("se", bld.at(f"s1_dA_{bg}_{n}_0"))
                E.fence()
                E.tensor_tensor_scan(out=s_v[:, :], data0=dA2[:, 0:L],
                                     data1=dbu_v[:, :], initial=0.0,
                                     op0=ALU.mult, op1=ALU.add).inc()
                bld.mark(f"s1_scan_{bg}_{n}_0", "ve")
                E.wait_ge("se", bld.at(f"s1_bcc_{bg}_{n}_{2*NCL-1}"))
                E.fence()
                if n == 0:
                    E.tensor_tensor(out=yv(0), in0=s_v[:, :], in1=Cv(n),
                                    op=ALU.mult).inc()
                else:
                    E.tensor_tensor(out=t_v[:, :], in0=s_v[:, :], in1=Cv(n),
                                    op=ALU.mult).inc()
                    E.fence()
                    E.tensor_tensor(out=yv(0), in0=yv(0), in1=t_v[:, :],
                                    op=ALU.add).inc()
                bld.mark(f"s1_p_{bg}_{n}_0", "ve")
                E.wait_ge("ge", bld.at(f"s1_gdbu_{bg}_{n}"))
                E.wait_ge("se", bld.at(f"s1_dA_{bg}_{n}_1"))
                if n > 0:
                    E.wait_ge("ge", bld.at(f"s1_gp_{bg}_{n-1}"))
                elif bg > 0:
                    E.wait_ge("ge", bld.at(f"s1_gp_{bg-1}_15"))
                E.fence()
                E.tensor_tensor_scan(out=s_g[:, :], data0=dA2[:, L:2 * L],
                                     data1=dbu_g[:, :], initial=0.0,
                                     op0=ALU.mult, op1=ALU.add).inc()
                bld.mark(f"s1_scan_{bg}_{n}_1", "ve")
            E.wait_ge("ge", bld.at(f"s1_gnd_{bg}"))
            for i in range(G):
                it = g * G + i
                E.fence()
                E.scalar_tensor_tensor(out=t_v[:, :], in0=uv(i),
                                       scalar=K["d_t"][:, it:it + 1],
                                       in1=yv(i), op0=ALU.mult, op1=ALU.add).inc()
                E.fence()
                E.tensor_tensor(out=ystg[:, i * L:(i + 1) * L], in0=t_v[:, :],
                                in1=sg2[:, i * L:(i + 1) * L], op=ALU.mult).inc()
            bld.mark(f"s1_tail_{bg}", "ve")
            bld.mark(f"s1_grpdone_{bg}", "ve")

    def gpsimd(E):
        for bg in range(NBG):
            b, g = divmod(bg, NIT)
            E.wait_ge("ve", bld.at(f"s1_dud_{bg}"))
            for n in range(16):
                E.wait_ge("se", bld.at(f"s1_bcc_{bg}_{n}_{NCL-1}"))
                E.fence()
                E.tensor_tensor(out=dbu_g[:, :], in0=duv(1), in1=Bv(n),
                                op=ALU.mult).inc()
                bld.mark(f"s1_gdbu_{bg}_{n}", "ge")
                E.wait_ge("ve", bld.at(f"s1_scan_{bg}_{n}_1"))
                E.wait_ge("se", bld.at(f"s1_bcc_{bg}_{n}_{2*NCL-1}"))
                E.fence()
                if n == 0:
                    E.tensor_tensor(out=yv(1), in0=s_g[:, :], in1=Cv(n),
                                    op=ALU.mult).inc()
                else:
                    E.tensor_tensor(out=t_g[:, :], in0=s_g[:, :], in1=Cv(n),
                                    op=ALU.mult).inc()
                    E.fence()
                    E.tensor_tensor(out=yv(1), in0=yv(1), in1=t_g[:, :],
                                    op=ALU.add).inc()
                bld.mark(f"s1_gp_{bg}_{n}", "ge")
            bld.mark(f"s1_gnd_{bg}", "ge")
            bld.mark(f"s1_gdone_{bg}", "ge")
            E.wait_ge("ve", bld.at(f"s1_tail_{bg}"))
            E.dma(bld.SC["y1_s"][g * G * 128:(g + 1) * G * 128, b * L:(b + 1) * L]
                  .rearrange("(q p) x -> p q x", p=128),
                  ystg.ap().rearrange("p (q x) -> p q x", x=L), sem="gd0")
            bld.mark(f"s1_yst_{bg}", "gd0")
        E.wait_ge("gd0", bld.C["gd0"])

    return {"sync": sync, "tensor": tensor, "scalar": scalar, "vector": vector,
            "gpsimd": gpsimd}


def ph_moe(bld, es):
    cfg = bld.cfg
    NS, T = cfg["NS"], cfg["T"]
    K = bld.K
    ps = bld.ps
    NMT = 11
    wgr = bld.sb(es, "mo_wg", [128, NMT * 2048], BF16)
    wur = bld.sb(es, "mo_wu", [128, NMT * 2048], BF16)
    wdr = bld.sb(es, "mo_wd", [128, 16 * NMT * 128], BF16)
    xb = bld.sb(es, "mo_x", [128, 16 * 512], BF16)
    wrow = bld.sb(es, "mo_wr", [1, T], BF16)
    gtmp = bld.sb(es, "mo_gt", [128, 512], F32)
    wh = bld.sb(es, "mo_wh", [128, NMT * 512], BF16)
    dstage = bld.sb(es, "mo_ds", [128, 4 * 512], F32)
    tmp = bld.sb(es, "mo_tmp", [128, 512], F32)
    acc = bld.SC["acc1_s"]

    def wsrc(e):
        if e < 8:
            return (bld.IN["wg_b"][e * 1408:(e + 1) * 1408, :],
                    bld.IN["wu_b1"][e * 1408:(e + 1) * 1408, :],
                    lambda j: bld.IN["wdn_b"][e * 2048 + j * 128:e * 2048 + (j + 1) * 128, :])
        half = e - 8
        return (bld.IN["swg_b"][half * 1408:(half + 1) * 1408, :],
                bld.IN["swu_b"][half * 1408:(half + 1) * 1408, :],
                lambda j: bld.IN["swd_b"][j * 128:(j + 1) * 128,
                                          half * 1408:(half + 1) * 1408])

    def gpsimd(E):
        for e in range(10):
            gd, ud, ddf = wsrc(e)
            if e >= 1:
                E.wait_ge("pe", bld.at(f"mo_edone_{e-1}"))
            for j in range(NMT):
                E.dma(wgr[:, j * 2048:(j + 1) * 2048], gd[j * 128:(j + 1) * 128, :],
                      sem="wl0")
                E.dma(wur[:, j * 2048:(j + 1) * 2048], ud[j * 128:(j + 1) * 128, :],
                      sem="wl0")
            for j in range(16):
                E.dma(wdr[:, j * NMT * 128:(j + 1) * NMT * 128], ddf(j), sem="wl0")
            bld.mark(f"mo_wld_{e}", "wl0")
            for s in range(NS):
                for d4 in range(4):
                    E.wait_ge("se", bld.at(f"mo_dcp_{e}_{s}_{d4 * 4 + 3}"))
                    if e == 0:
                        E.dma(acc[d4 * 4 * 128:(d4 + 1) * 4 * 128,
                                  s * 512:(s + 1) * 512]
                              .rearrange("(q p) x -> p q x", p=128),
                              dstage.ap().rearrange("p (q x) -> p q x", x=512),
                              sem="gd0")
                    else:
                        bld.C["gd0"] += 16
                        if not bld.pass1:
                            E._raw.dma_start(
                                out=acc[d4 * 4 * 128:(d4 + 1) * 4 * 128,
                                        s * 512:(s + 1) * 512]
                                .rearrange("(q p) x -> p q x", p=128),
                                in_=dstage.ap().rearrange("p (q x) -> p q x", x=512),
                                accum_op=ALU.add,
                            ).then_inc(bld.sems["gd0"], 16)
                    bld.mark(f"mo_dst_{e}_{s}_{d4}", "gd0")
        E.wait_ge("gd0", bld.C["gd0"])

    def sync(E):
        for e in range(10):
            if e < 8:
                if e >= 1:
                    E.wait_ge("pe", bld.at(f"mo_cb_{e-1}_{bld.cfg['NS']-1}"))
                E.dma(wrow[0:1, :], bld.SC["wfm_s"][e:e + 1, :], sem="ld2")
                bld.mark(f"mo_wrow_{e}", "ld2")
            for s in range(NS):
                if e + s > 0:
                    E.wait_ge("pe", bld.at(f"mo_gu_{e}_{s-1}" if s > 0
                                           else f"mo_gu_{e-1}_{NS-1}"))
                for h in range(2):
                    E.dma(xb[:, h * 8 * 512:(h + 1) * 8 * 512]
                          .rearrange("p (kt x) -> p kt x", x=512),
                          bld.SC["xn1_s"][h * 8 * 128:(h + 1) * 8 * 128,
                                          s * 512:(s + 1) * 512]
                          .rearrange("(kt p) x -> p kt x", p=128), sem="ld0")
                bld.mark(f"mo_rhs_{e}_{s}", "ld0")

    def tensor(E):
        for e in range(10):
            E.wait_ge("wl0", bld.at(f"mo_wld_{e}"))
            if e < 8:
                E.wait_ge("ld2", bld.at(f"mo_wrow_{e}"))
            for s in range(NS):
                E.wait_ge("ld0", bld.at(f"mo_rhs_{e}_{s}"))
                if e < 8:
                    if e + s > 0:
                        E.wait_ge("ve", bld.at(f"mo_wh_{e}_{s-1}_{NMT-1}" if s > 0
                                               else f"mo_wh_{e-1}_{NS-1}_{NMT-1}"))
                    E.matmul(ps[4][:, :], lhsT=K["ones_row_bf"][0:1, :],
                             rhs=wrow[0:1, s * 512:(s + 1) * 512],
                             start=True, stop=True).inc()
                    bld.mark(f"mo_cb_{e}_{s}", "pe")
                for mt in range(NMT):
                    bg, bu = mt % 2, 2 + mt % 2
                    if mt >= 2:
                        E.wait_ge("se", bld.at(f"mo_sil_{e}_{s}_{mt-2}"))
                        E.wait_ge("ve", bld.at(f"mo_wh1_{e}_{s}_{mt-2}"))
                    elif e + s > 0:
                        pk = (f"{e}_{s-1}" if s > 0 else f"{e-1}_{NS-1}")
                        E.wait_ge("se", bld.at(f"mo_sil_{pk}_{NMT + mt - 2}"))
                        E.wait_ge("ve", bld.at(f"mo_wh1_{pk}_{NMT + mt - 2}"))
                    for kt in range(16):
                        mm = E.matmul(ps[bg][:, :],
                                      lhsT=wgr[:, mt * 2048 + kt * 128:mt * 2048 + (kt + 1) * 128],
                                      rhs=xb[:, kt * 512:(kt + 1) * 512],
                                      start=(kt == 0), stop=(kt == 15))
                    mm.inc()
                    bld.mark(f"mo_gmm_{e}_{s}_{mt}", "pe")
                    for kt in range(16):
                        mm = E.matmul(ps[bu][:, :],
                                      lhsT=wur[:, mt * 2048 + kt * 128:mt * 2048 + (kt + 1) * 128],
                                      rhs=xb[:, kt * 512:(kt + 1) * 512],
                                      start=(kt == 0), stop=(kt == 15))
                    mm.inc()
                    bld.mark(f"mo_umm_{e}_{s}_{mt}", "pe")
                bld.mark(f"mo_gu_{e}_{s}", "pe")
                E.wait_ge("ve", bld.at(f"mo_wh_{e}_{s}_{NMT-1}"))
                for dt in range(16):
                    bank = 5 + dt % 2
                    if dt >= 2:
                        E.wait_ge("se", bld.at(f"mo_dcp_{e}_{s}_{dt-2}"))
                    elif e + s > 0:
                        pk = (f"{e}_{s-1}" if s > 0 else f"{e-1}_{NS-1}")
                        E.wait_ge("se", bld.at(f"mo_dcp_{pk}_{14 + dt}"))
                    for j in range(NMT):
                        mm = E.matmul(ps[bank][:, :],
                                      lhsT=wdr[:, (dt * NMT + j) * 128:(dt * NMT + j + 1) * 128],
                                      rhs=wh[:, j * 512:(j + 1) * 512],
                                      start=(j == 0), stop=(j == NMT - 1))
                    mm.inc()
                    bld.mark(f"mo_dmm_{e}_{s}_{dt}", "pe")
            bld.mark(f"mo_edone_{e}", "pe")

    def scalar(E):
        for e in range(10):
            for s in range(NS):
                for mt in range(NMT):
                    bg = mt % 2
                    E.wait_ge("pe", bld.at(f"mo_gmm_{e}_{s}_{mt}"))
                    idx = (e * NS + s) * NMT + mt
                    if idx > 0:
                        pidx = idx - 1
                        pe_, ps_, pm_ = pidx // (NS * NMT), (pidx // NMT) % NS, pidx % NMT
                        E.wait_ge("ve", bld.at(f"mo_wh1_{pe_}_{ps_}_{pm_}"))
                    E.activation(out=gtmp[:, :], in_=ps[bg][:, :], func=AF.Silu).inc()
                    bld.mark(f"mo_sil_{e}_{s}_{mt}", "se")
                for dt in range(16):
                    bank = 5 + dt % 2
                    E.wait_ge("pe", bld.at(f"mo_dmm_{e}_{s}_{dt}"))
                    if dt % 4 == 0:
                        d4p = dt // 4 - 1
                        if d4p >= 0:
                            E.wait_ge("gd0", bld.at(f"mo_dst_{e}_{s}_{d4p}"))
                        elif e + s > 0:
                            pk = (f"{e}_{s-1}" if s > 0 else f"{e-1}_{NS-1}")
                            E.wait_ge("gd0", bld.at(f"mo_dst_{pk}_3"))
                    E.activation(out=dstage[:, (dt % 4) * 512:(dt % 4 + 1) * 512],
                                 in_=ps[bank][:, :], func=AF.Copy).inc()
                    bld.mark(f"mo_dcp_{e}_{s}_{dt}", "se")

    def vector(E):
        for e in range(10):
            for s in range(NS):
                for mt in range(NMT):
                    bu = 2 + mt % 2
                    E.wait_ge("se", bld.at(f"mo_sil_{e}_{s}_{mt}"))
                    E.wait_ge("pe", bld.at(f"mo_umm_{e}_{s}_{mt}"))
                    if s == 0 and mt == 0 and e > 0:
                        E.wait_ge("pe", bld.at(f"mo_edone_{e-1}"))
                    elif mt == 0 and s > 0:
                        E.wait_ge("pe", bld.at(f"mo_dmm_{e}_{s-1}_15"))
                    E.fence()
                    if e < 8:
                        E.wait_ge("pe", bld.at(f"mo_cb_{e}_{s}"))
                        E.tensor_tensor(out=tmp[:, :], in0=gtmp[:, :],
                                        in1=ps[bu][:, :], op=ALU.mult).inc()
                        bld.mark(f"mo_wh1_{e}_{s}_{mt}", "ve")
                        E.fence()
                        E.tensor_tensor(out=wh[:, mt * 512:(mt + 1) * 512],
                                        in0=tmp[:, :], in1=ps[4][:, :],
                                        op=ALU.mult).inc()
                    else:
                        E.tensor_tensor(out=wh[:, mt * 512:(mt + 1) * 512],
                                        in0=gtmp[:, :], in1=ps[bu][:, :],
                                        op=ALU.mult).inc()
                        bld.mark(f"mo_wh1_{e}_{s}_{mt}", "ve")
                    bld.mark(f"mo_wh_{e}_{s}_{mt}", "ve")

    return {"gpsimd": gpsimd, "sync": sync, "tensor": tensor, "scalar": scalar,
            "vector": vector}

MOESCAN = r"""Merged MoE + selective-scan phase: round-robin unit schedule so
TensorE runs the MoE while VectorE runs the scan. GpSimd does the MoE
elementwise (from SBUF evacuations; Pool cannot touch PSUM)."""


def ph_moescan(bld, es):
    cfg = bld.cfg
    Bc, L, NCL, NS, T = cfg["Bc"], cfg["L"], cfg["NCL"], cfg["NS"], cfg["T"]
    K = bld.K
    ps = bld.ps
    NIT = 32
    NBG = Bc * NIT
    delta1 = bld.sb(es, "ms_d", [128, L], F32)
    u1 = bld.sb(es, "ms_u", [128, L], BF16)
    du1 = bld.sb(es, "ms_du", [128, L], BF16)
    dA2 = bld.sb(es, "ms_dA", [128, 2 * L], F32)
    dbu_v = bld.sb(es, "ms_dbv", [128, L], BF16)
    s_v = bld.sb(es, "ms_sv", [128, L], F32)
    t_v = bld.sb(es, "ms_tv", [128, L], F32)
    yac = bld.sb(es, "ms_y", [128, L], F32)
    Bb = bld.sb(es, "ms_Bb", [128, 2 * L], BF16)
    Cb = bld.sb(es, "ms_Cb", [128, 2 * L], BF16)
    ystg = du1
    NMT = 11
    arena = bld.sb(es, "ms_ar", [128, 24576], BF16)
    xb = bld.sb(es, "ms_x", [128, 2 * 16 * 512], BF16)
    whl = bld.sb(es, "ms_whl", [128, NMT * 512], BF16)
    gtmp = bld.sb(es, "ms_gt", [128, 2 * 512], F32)
    usb = bld.sb(es, "ms_us", [128, 2 * 512], F32)
    comb_sb = bld.sb(es, "ms_cmb", [128, 2 * 512], BF16)
    whst = bld.sb(es, "ms_whst", [128, 2 * 512], BF16)
    dstage = bld.sb(es, "ms_ds", [128, 2 * 512], F32)
    acc = bld.SC["acc1_s"]
    whs = bld.SC["whs_s"]

    def Bv(n):
        return Bb[:, (n % 2) * L:(n % 2 + 1) * L]

    def Cv(n):
        return Cb[:, (n % 2) * L:(n % 2 + 1) * L]

    def dAv(n):
        return dA2[:, (n % 2) * L:(n % 2 + 1) * L]


    def su_sync(E, bg, n):
        b, it = divmod(bg, NIT)
        if n == 0:
            if bg >= 1:
                E.wait_ge("ve", bld.at(f"ms_grpdone_{bg-1}"))
                E.wait_ge("se", bld.at(f"ms_dA_{bg-1}_15"))
                E.wait_ge("gd1", bld.at(f"ms_yst_{bg-1}"))
            r0 = it * 128
            E.dma(delta1[:, :],
                  bld.SC["delta_s"][r0:r0 + 128, b * L:(b + 1) * L], sem="ld2")
            E.dma(u1[:, :],
                  bld.SC["hconv_s"][r0:r0 + 128, b * L:(b + 1) * L], sem="ld2")
            bld.mark(f"ms_ld_{bg}", "ld2")
        if n >= 2:
            E.wait_ge("ve", bld.at(f"ms_dbu_{bg}_{n-2}"))
        elif bg > 0:
            E.wait_ge("ve", bld.at(f"ms_dbu_{bg-1}_{14+n}"))
        E.dma(Bv(n), bld.SC["bc_s"][n:n + 1, b * L:(b + 1) * L]
              .partition_broadcast(128), sem="xb0")
        bld.mark(f"ms_bcldB_{bg}_{n}", "xb0")
        if n >= 2:
            E.wait_ge("ve", bld.at(f"ms_p_{bg}_{n-2}"))
        elif bg > 0:
            E.wait_ge("ve", bld.at(f"ms_p_{bg-1}_{14+n}"))
        E.dma(Cv(n), bld.SC["bc_s"][16 + n:17 + n, b * L:(b + 1) * L]
              .partition_broadcast(128), sem="xb1")
        bld.mark(f"ms_bcldC_{bg}_{n}", "xb1")

    def su_sync_tail(E, bg):
        b, it = divmod(bg, NIT)
        E.wait_ge("ve", bld.at(f"ms_scan_{bg}_15"))
        E.dma(dbu_v[:, :], bld.SC["sg_s"][it * 128:(it + 1) * 128,
                                          b * L:(b + 1) * L], sem="xa1")
        bld.mark(f"ms_sgld_{bg}", "xa1")

    def su_scalar(E, bg, n):
        b, it = divmod(bg, NIT)
        if n == 0:
            E.wait_ge("ld2", bld.at(f"ms_ld_{bg}"))
        if n >= 2:
            E.wait_ge("ve", bld.at(f"ms_scan_{bg}_{n-2}"))
        elif bg > 0:
            E.wait_ge("ve", bld.at(f"ms_scan_{bg-1}_{14+n}"))
        E.activation(out=dAv(n), in_=delta1[:, :], func=AF.Exp,
                     scale=K["a_t"][:, it * 16 + n:it * 16 + n + 1]).inc()
        bld.mark(f"ms_dA_{bg}_{n}", "se")

    def su_vector(E, bg, n):
        b, it = divmod(bg, NIT)
        if n == 0:
            E.wait_ge("ld2", bld.at(f"ms_ld_{bg}"))
            if bg >= 1:
                E.wait_ge("gd1", bld.at(f"ms_yst_{bg-1}"))
            E.fence()
            E.tensor_tensor(out=du1[:, :], in0=delta1[:, :], in1=u1[:, :],
                            op=ALU.mult).inc()
            bld.mark(f"ms_dud_{bg}", "ve")
        E.wait_ge("xb0", bld.at(f"ms_bcldB_{bg}_{n}"))
        E.fence()
        E.tensor_tensor(out=dbu_v[:, :], in0=du1[:, :], in1=Bv(n),
                        op=ALU.mult).inc()
        bld.mark(f"ms_dbu_{bg}_{n}", "ve")
        E.wait_ge("se", bld.at(f"ms_dA_{bg}_{n}"))
        E.fence()
        E.tensor_tensor_scan(out=s_v[:, :], data0=dAv(n), data1=dbu_v[:, :],
                             initial=0.0, op0=ALU.mult, op1=ALU.add).inc()
        bld.mark(f"ms_scan_{bg}_{n}", "ve")
        E.wait_ge("xb1", bld.at(f"ms_bcldC_{bg}_{n}"))
        E.fence()
        if n == 0:
            E.tensor_tensor(out=yac[:, :], in0=s_v[:, :], in1=Cv(n),
                            op=ALU.mult).inc()
        else:
            E.tensor_tensor(out=t_v[:, :], in0=s_v[:, :], in1=Cv(n),
                            op=ALU.mult).inc()
            E.fence()
            E.tensor_tensor(out=yac[:, :], in0=yac[:, :], in1=t_v[:, :],
                            op=ALU.add).inc()
        bld.mark(f"ms_p_{bg}_{n}", "ve")
        if n == 15:
            E.wait_ge("xa1", bld.at(f"ms_sgld_{bg}"))
            E.fence()
            E.scalar_tensor_tensor(out=t_v[:, :], in0=u1[:, :],
                                   scalar=K["d_t"][:, it:it + 1],
                                   in1=yac[:, :], op0=ALU.mult, op1=ALU.add).inc()
            E.fence()
            E.tensor_tensor(out=ystg[:, :], in0=t_v[:, :], in1=dbu_v[:, :],
                            op=ALU.mult).inc()
            bld.mark(f"ms_tail_{bg}", "ve")
            bld.mark(f"ms_grpdone_{bg}", "ve")

    def su_gpsimd(E, bg):
        b, it = divmod(bg, NIT)
        E.wait_ge("ve", bld.at(f"ms_grpdone_{bg}"))
        E.dma(bld.SC["y1_s"][it * 128:(it + 1) * 128, b * L:(b + 1) * L],
              ystg[:, :], sem="gd1")
        bld.mark(f"ms_yst_{bg}", "gd1")

    passes = []
    for e in range(10):
        for h in range(2):
            mts = list(range(h * 6, min(NMT, h * 6 + 6)))
            passes.append((e, h, mts))

    def wsrc(e):
        if e < 8:
            return (bld.IN["wg_b"][e * 1408:(e + 1) * 1408, :],
                    bld.IN["wu_b1"][e * 1408:(e + 1) * 1408, :],
                    lambda j: bld.IN["wdn_b"][e * 2048 + j * 128:e * 2048 + (j + 1) * 128, :])
        half = e - 8
        return (bld.IN["swg_b"][half * 1408:(half + 1) * 1408, :],
                bld.IN["swu_b"][half * 1408:(half + 1) * 1408, :],
                lambda j: bld.IN["swd_b"][j * 128:(j + 1) * 128,
                                          half * 1408:(half + 1) * 1408])

    kkbase = {}
    _acc = 0
    for _pi, (_e, _h, _mts) in enumerate(passes):
        for _s in range(NS):
            kkbase[(_pi, _s)] = _acc
            _acc += len(_mts)

    def xv(g, kt):
        return xb[:, ((g % 2) * 16 + kt) * 512:((g % 2) * 16 + kt + 1) * 512]

    def up_gpsimd(E, pi, s):
        e, h, mts = passes[pi]
        if s == 0:
            gd, ud, _ = wsrc(e)
            if pi >= 1:
                E.wait_ge("pe", bld.at(f"ms_passdone_{pi-1}"))
            if h == 0 and e > 0:
                E.wait_ge("pe", bld.at(f"ms_dmm_{(e - 1) * NS + NS - 1}_15"))
            for k, mt in enumerate(mts):
                E.dma(arena[:, k * 2048:(k + 1) * 2048],
                      gd[mt * 128:(mt + 1) * 128, :], sem="wl0")
                E.dma(arena[:, (6 + k) * 2048:(7 + k) * 2048],
                      ud[mt * 128:(mt + 1) * 128, :], sem="wl0")
            bld.mark(f"ms_wld_{pi}", "wl0")
        if e < 8:
            E.dma(comb_sb[:, (s % 2) * 512:(s % 2 + 1) * 512],
                  bld.SC["wfm_s"][e:e + 1, s * 512:(s + 1) * 512]
                  .partition_broadcast(128), sem="xa0")
            bld.mark(f"ms_comb_{pi}_{s}", "xa0")

    def up_sync(E, pi, s):
        g = pi * NS + s
        if g >= 2:
            E.wait_ge("pe", bld.at(f"ms_gu_{g-2}"))
        for hh in range(2):
            E.dma(xb[:, ((g % 2) * 16 + hh * 8) * 512:
                     ((g % 2) * 16 + (hh + 1) * 8) * 512]
                  .rearrange("p (kt x) -> p kt x", x=512),
                  bld.SC["xn1_s"][hh * 8 * 128:(hh + 1) * 8 * 128,
                                  s * 512:(s + 1) * 512]
                  .rearrange("(kt p) x -> p kt x", p=128), sem="ld0")
        bld.mark(f"ms_rhs_{g}", "ld0")

    def up_tensor(E, pi, s):
        e, h, mts = passes[pi]
        g = pi * NS + s
        E.wait_ge("wl0", bld.at(f"ms_wld_{pi}"))
        E.wait_ge("ld0", bld.at(f"ms_rhs_{g}"))
        for k, mt in enumerate(mts):
            bg_, bu_ = k % 2, 2 + k % 2
            kk = kkbase[(pi, s)] + k
            if kk >= 2:
                E.wait_ge("se", bld.at(f"ms_sil_{kk-2}"))
            for kt in range(16):
                mm = E.matmul(ps[bg_][:, :], lhsT=arena[:, k * 2048 + kt * 128:
                                                        k * 2048 + (kt + 1) * 128],
                              rhs=xv(g, kt), start=(kt == 0), stop=(kt == 15))
            mm.inc()
            bld.mark(f"ms_gmm_{kk}", "pe")
            if kk >= 2:
                E.wait_ge("se", bld.at(f"ms_ucp_{kk-2}"))
            for kt in range(16):
                mm = E.matmul(ps[bu_][:, :], lhsT=arena[:, (6 + k) * 2048 + kt * 128:
                                                        (6 + k) * 2048 + (kt + 1) * 128],
                              rhs=xv(g, kt), start=(kt == 0), stop=(kt == 15))
            mm.inc()
            bld.mark(f"ms_umm_{kk}", "pe")
        bld.mark(f"ms_gu_{g}", "pe")

    def up_scalar(E, pi, s):
        e, h, mts = passes[pi]
        for k, mt in enumerate(mts):
            bg_, bu_ = k % 2, 2 + k % 2
            kk = kkbase[(pi, s)] + k
            sl = kk % 2
            E.wait_ge("pe", bld.at(f"ms_gmm_{kk}"))
            if kk >= 2:
                E.wait_ge("ge", bld.at(f"ms_wht_{kk-2}"))
            E.activation(out=gtmp[:, sl * 512:(sl + 1) * 512],
                         in_=ps[bg_][:, :], func=AF.Silu).inc()
            bld.mark(f"ms_sil_{kk}", "se")
            E.wait_ge("pe", bld.at(f"ms_umm_{kk}"))
            E.activation(out=usb[:, sl * 512:(sl + 1) * 512],
                         in_=ps[bu_][:, :], func=AF.Copy).inc()
            bld.mark(f"ms_ucp_{kk}", "se")
            E.wait_ge("ge", bld.at(f"ms_wht_{kk}"))
            E.fence("se")
            E.dma(whs[(e * 1408 if e < 8 else (8 * 1408 + (e - 8) * 1408)) +
                      mt * 128:(e * 1408 if e < 8 else (8 * 1408 + (e - 8) * 1408)) +
                      (mt + 1) * 128, s * 512:(s + 1) * 512],
                  whst[:, (kk % 2) * 512:(kk % 2 + 1) * 512], sem=f"sd{kk % 2}")
            bld.mark(f"ms_whstd_{kk}", f"sd{kk % 2}")

    def up_gpsimd2(E, pi, s):
        e, h, mts = passes[pi]
        for k, mt in enumerate(mts):
            kk = kkbase[(pi, s)] + k
            sl = kk % 2
            E.wait_ge("se", bld.at(f"ms_sil_{kk}"))
            E.wait_ge("se", bld.at(f"ms_ucp_{kk}"))
            if kk >= 2:
                E.wait_ge(f"sd{kk % 2}", bld.at(f"ms_whstd_{kk-2}"))
            if e < 8:
                E.wait_ge("xa0", bld.at(f"ms_comb_{pi}_{s}"))
            E.fence()
            if e < 8:
                E.tensor_tensor(out=usb[:, sl * 512:(sl + 1) * 512],
                                in0=gtmp[:, sl * 512:(sl + 1) * 512],
                                in1=usb[:, sl * 512:(sl + 1) * 512],
                                op=ALU.mult).inc()
                E.fence()
                E.tensor_tensor(out=whst[:, (kk % 2) * 512:(kk % 2 + 1) * 512],
                                in0=usb[:, sl * 512:(sl + 1) * 512],
                                in1=comb_sb[:, (s % 2) * 512:(s % 2 + 1) * 512],
                                op=ALU.mult).inc()
            else:
                E.tensor_tensor(out=whst[:, (kk % 2) * 512:(kk % 2 + 1) * 512],
                                in0=gtmp[:, sl * 512:(sl + 1) * 512],
                                in1=usb[:, sl * 512:(sl + 1) * 512],
                                op=ALU.mult).inc()
            bld.mark(f"ms_wht_{kk}", "ge")

    def dn_sync(E, e, s):
        d = e * NS + s
        if d >= 1:
            E.wait_ge("pe", bld.at(f"ms_dmm_{d-1}_15"))
        for b2 in range(2):
            E.wait_ge(f"sd{b2}", bld.at(f"ms_upallst_{e}_{b2}"))
        base = e * 1408 if e < 8 else 8 * 1408 + (e - 8) * 1408
        E.dma(whl.ap().rearrange("p (q x) -> p q x", x=512),
              whs[base:base + 1408, s * 512:(s + 1) * 512]
              .rearrange("(q p) x -> p q x", p=128), sem="ld3")
        bld.mark(f"ms_whl_{d}", "ld3")

    def dn_tensor(E, e, s):
        d = e * NS + s
        E.wait_ge("ld3", bld.at(f"ms_whl_{d}"))
        if s == 0:
            E.wait_ge("wl1", bld.at(f"ms_dwld_{e}"))
        for dt in range(16):
            idx = d * 16 + dt
            bank = 4 + idx % 3
            if idx >= 3:
                E.wait_ge("se", bld.at(f"ms_dcp_i{idx-3}"))
            for j in range(NMT):
                mm = E.matmul(ps[bank][:, :], lhsT=arena[:, (dt * NMT + j) * 128:
                                                        (dt * NMT + j + 1) * 128],
                              rhs=whl[:, j * 512:(j + 1) * 512],
                              start=(j == 0), stop=(j == NMT - 1))
            mm.inc()
            bld.mark(f"ms_dmm_{d}_{dt}", "pe")

    def dn_gpsimd(E, e, s):
        d = e * NS + s
        if s == 0:
            _, _, ddf = wsrc(e)
            E.wait_ge("pe", bld.at(f"ms_gu_{(2 * e + 2) * NS - 1}"))
            for j in range(16):
                E.dma(arena[:, j * 1408:(j + 1) * 1408], ddf(j), sem="wl1")
            bld.mark(f"ms_dwld_{e}", "wl1")
        for d4 in range(8):
            E.wait_ge("se", bld.at(f"ms_dcp_{d}_{d4 * 2 + 1}"))
            if e == 0:
                E.dma(acc[d4 * 2 * 128:(d4 + 1) * 2 * 128, s * 512:(s + 1) * 512]
                      .rearrange("(q p) x -> p q x", p=128),
                      dstage.ap().rearrange("p (q x) -> p q x", x=512), sem="gd0")
            else:
                bld.C["gd0"] += 16
                if not bld.pass1:
                    E._raw.dma_start(
                        out=acc[d4 * 2 * 128:(d4 + 1) * 2 * 128,
                                s * 512:(s + 1) * 512]
                        .rearrange("(q p) x -> p q x", p=128),
                        in_=dstage.ap().rearrange("p (q x) -> p q x", x=512),
                        accum_op=ALU.add,
                    ).then_inc(bld.sems["gd0"], 16)
            bld.mark(f"ms_dst_{d}_{d4}", "gd0")

    def dn_scalar(E, e, s):
        d = e * NS + s
        for dt in range(16):
            idx = d * 16 + dt
            bank = 4 + idx % 3
            E.wait_ge("pe", bld.at(f"ms_dmm_{d}_{dt}"))
            if dt % 2 == 0:
                d4p = dt // 2 - 1
                if d4p >= 0:
                    E.wait_ge("gd0", bld.at(f"ms_dst_{d}_{d4p}"))
                elif d >= 1:
                    E.wait_ge("gd0", bld.at(f"ms_dst_{d-1}_7"))
            E.activation(out=dstage[:, (dt % 2) * 512:(dt % 2 + 1) * 512],
                         in_=ps[bank][:, :], func=AF.Copy).inc()
            bld.mark(f"ms_dcp_i{idx}", "se")
            bld.mark(f"ms_dcp_{d}_{dt}", "se")

    scan_units = [("s", bg, n) for bg in range(NBG) for n in range(16)]
    moe_units = []
    for e in range(10):
        for h in range(2):
            pi = e * 2 + h
            for s in range(NS):
                moe_units.append(("u", pi, s))
        for s in range(NS):
            moe_units.append(("d", e, s))
    sched = []
    mi = 0
    for k, su in enumerate(scan_units):
        sched.append(su)
        want = (k + 1) * len(moe_units) // len(scan_units)
        while mi < want:
            sched.append(moe_units[mi])
            mi += 1
    while mi < len(moe_units):
        sched.append(moe_units[mi])
        mi += 1

    def mark_upall(E, e):
        for b2 in range(2):
            bld.mark(f"ms_upallst_{e}_{b2}", f"sd{b2}")

    def body(engine):
        def run(E):
            for unit in sched:
                if unit[0] == "s":
                    _, bg, n = unit
                    if engine == "sync":
                        su_sync(E, bg, n)
                        if n == 15:
                            su_sync_tail(E, bg)
                    elif engine == "scalar":
                        su_scalar(E, bg, n)
                    elif engine == "vector":
                        su_vector(E, bg, n)
                    elif engine == "gpsimd" and n == 15:
                        su_gpsimd(E, bg)
                elif unit[0] == "u":
                    _, pi, s = unit
                    e, h, mts = passes[pi]
                    if engine == "sync":
                        up_sync(E, pi, s)
                    elif engine == "gpsimd":
                        up_gpsimd(E, pi, s)
                        up_gpsimd2(E, pi, s)
                    elif engine == "tensor":
                        up_tensor(E, pi, s)
                        if s == NS - 1:
                            bld.mark(f"ms_passdone_{pi}", "pe")
                    elif engine == "scalar":
                        up_scalar(E, pi, s)
                        if s == NS - 1 and h == 1:
                            mark_upall(E, e)
                else:
                    _, e, s = unit
                    if engine == "sync":
                        dn_sync(E, e, s)
                    elif engine == "tensor":
                        dn_tensor(E, e, s)
                    elif engine == "gpsimd":
                        dn_gpsimd(E, e, s)
                    elif engine == "scalar":
                        dn_scalar(E, e, s)
            if engine == "gpsimd":
                E.wait_ge("gd0", bld.C["gd0"])
                E.wait_ge("gd1", bld.C["gd1"])
        return run

    return {e: body(e) for e in ("sync", "gpsimd", "tensor", "scalar", "vector")}


def ph_wout1(bld, es):
    cfg = bld.cfg
    wins = [(b * cfg["L"], (b + 1) * cfg["L"]) for b in range(cfg["Bc"])]
    return _mm_rhsres(bld, es, "wo1", bld.IN["w_out1"], 32, 16,
                      bld.SC["y1_s"], bld.SC["s1_s"], windows=wins)

def ph_comb2(bld, es):
    cfg = bld.cfg
    NS = cfg["NS"]
    ps = bld.ps
    x1b = bld.sb(es, "c2_x", [128, 2 * 16 * 512], F32)
    a1b = bld.sb(es, "c2_a", [128, 2 * 16 * 512], F32)
    s1b = bld.sb(es, "c2_s", [128, 2 * 16 * 512], BF16)
    g1t = bld.sb(es, "c2_g", [128, 2 * 512], BF16)
    tmp = bld.sb(es, "c2_t", [128, 512], F32)
    ystg = bld.sb(es, "c2_y", [128, 16 * 512], F32)
    x1s, a1s, s1s = bld.SC["x1_s"], bld.SC["acc1_s"], bld.SC["s1_s"]
    OUT = bld.OUT

    def sync(E):
        for s in range(NS):
            par = s % 2
            if s >= 2:
                E.wait_ge("ve", bld.at(f"c2_yd_{s-2}"))
            for g4 in range(4):
                E.dma(x1b[:, (par * 16 + g4 * 4) * 512:(par * 16 + g4 * 4 + 4) * 512]
                      .rearrange("p (dt c) -> p dt c", c=512),
                      x1s[g4 * 512:(g4 + 1) * 512, s * 512:(s + 1) * 512]
                      .rearrange("(dt p) c -> p dt c", p=128), sem=f"ld{par}")
                E.dma(a1b[:, (par * 16 + g4 * 4) * 512:(par * 16 + g4 * 4 + 4) * 512]
                      .rearrange("p (dt c) -> p dt c", c=512),
                      a1s[g4 * 512:(g4 + 1) * 512, s * 512:(s + 1) * 512]
                      .rearrange("(dt p) c -> p dt c", p=128), sem=f"ld{par + 2}")
            bld.mark(f"c2_lda_{s}", f"ld{par + 2}")
            for h in range(2):
                E.dma(s1b[:, (par * 16 + h * 8) * 512:(par * 16 + (h + 1) * 8) * 512]
                      .rearrange("p (dt c) -> p dt c", c=512),
                      s1s[h * 8 * 128:(h + 1) * 8 * 128, s * 512:(s + 1) * 512]
                      .rearrange("(dt p) c -> p dt c", p=128), sem=f"xa{par}")
            bld.mark(f"c2_lds_{s}", f"xa{par}")
            bld.mark(f"c2_ld_{s}", f"ld{par}")

    def scalar(E):
        for s in range(NS):
            par = s % 2
            E.wait_ge(f"ld{par + 2}", bld.at(f"c2_lda_{s}"))
            for dt in range(16):
                slot = dt % 2
                if s + dt > 0:
                    pk = f"{s}_{dt-2}" if dt >= 2 else f"{s-1}_{14 + dt}"
                    E.wait_ge("ve", bld.at(f"c2_t_{pk}"))
                E.activation(out=g1t[:, slot * 512:(slot + 1) * 512],
                             in_=a1b[:, (par * 16 + dt) * 512:(par * 16 + dt + 1) * 512],
                             func=AF.Silu).inc()
                bld.mark(f"c2_g_{s}_{dt}", "se")

    def vector(E):
        for s in range(NS):
            par = s % 2
            E.wait_ge(f"ld{par}", bld.at(f"c2_ld_{s}"))
            E.wait_ge(f"xa{par}", bld.at(f"c2_lds_{s}"))
            if s >= 1:
                E.wait_ge(f"gd{1 - par}", bld.at(f"c2_st_{s-1}"))
            for dt in range(16):
                slot = dt % 2
                E.wait_ge("se", bld.at(f"c2_g_{s}_{dt}"))
                E.fence()
                E.tensor_tensor(out=tmp[:, :],
                                in0=s1b[:, (par * 16 + dt) * 512:(par * 16 + dt + 1) * 512],
                                in1=g1t[:, slot * 512:(slot + 1) * 512],
                                op=ALU.mult).inc()
                bld.mark(f"c2_t_{s}_{dt}", "ve")
                E.fence()
                E.tensor_tensor(out=ystg[:, dt * 512:(dt + 1) * 512],
                                in0=x1b[:, (par * 16 + dt) * 512:(par * 16 + dt + 1) * 512],
                                in1=tmp[:, :], op=ALU.add).inc()
            bld.mark(f"c2_yd_{s}", "ve")

    def gpsimd(E):
        for s in range(NS):
            par = s % 2
            E.wait_ge("ve", bld.at(f"c2_yd_{s}"))
            for g4 in range(4):
                E.dma(OUT[g4 * 512:(g4 + 1) * 512, s * 512:(s + 1) * 512]
                      .rearrange("(dt p) c -> p dt c", p=128),
                      ystg[:, g4 * 4 * 512:(g4 * 4 + 4) * 512]
                      .rearrange("p (dt c) -> p dt c", c=512), sem=f"gd{par}")
            bld.mark(f"c2_st_{s}", f"gd{par}")
        for p2 in range(min(2, NS)):
            E.wait_ge(f"gd{p2}", bld.C[f"gd{p2}"])

    return {"sync": sync, "scalar": scalar, "vector": vector, "gpsimd": gpsimd}



def kernel(**inputs):
    Bc, L = 2, 2048
    in_maps = prep_inputs(inputs, N_CORES, Bc, L)
    nc, cfg = build_program(Bc, L)
    res = run_bass_kernel_spmd(nc, in_maps, core_ids=list(range(N_CORES)))
    D, T = cfg["D"], cfg["T"]
    outs = []
    for c in range(N_CORES):
        y = np.asarray(res.results[c]["y_fm"]).reshape(D, Bc, L)
        outs.append(y.transpose(1, 2, 0))
    return np.ascontiguousarray(np.concatenate(outs, 0), dtype=np.float32)


# version 07
# speedup vs baseline: 1.1962x; 1.0105x over previous
import numpy as np
import ml_dtypes
from collections import defaultdict
from contextlib import ExitStack

import concourse.bass as bass
import concourse.mybir as mybir
from concourse.bass_utils import run_bass_kernel_spmd

F32 = mybir.dt.float32
BF16 = mybir.dt.bfloat16
AF = mybir.ActivationFunctionType
ALU = mybir.AluOpType
AX = mybir.AxisListType

N_CORES = 8
ENGINE_ORDER = ("sync", "gpsimd", "tensor", "scalar", "vector")


def _blocks(w, mtw=128):
    K, M = w.shape
    n_kt, n_mt = K // 128, M // mtw
    assert n_kt * 128 == K and n_mt * mtw == M
    return np.ascontiguousarray(
        w.reshape(n_kt, 128, n_mt, mtw).transpose(2, 1, 0, 3).reshape(n_mt * 128, n_kt * mtw))


def prep_inputs(inputs, n_cores=N_CORES, Bc=2, L=2048):
    g = lambda k: np.asarray(inputs[k])
    D = 2048
    T = Bc * L
    sh = {}
    sh["w_in0"] = _blocks(np.ascontiguousarray(g("L0_attn_W_in").T))
    Wg = g("L0_attn_W_gate").reshape(4, 4, 512, D).transpose(1, 0, 2, 3).reshape(8192, D)
    sh["w_gate0"] = _blocks(np.ascontiguousarray(Wg.T))
    bg = g("L0_attn_b_gate").reshape(4, 4, 512).transpose(1, 0, 2).reshape(8192)
    sh["bgate_t"] = np.ascontiguousarray(bg.reshape(64, 128).T)
    sh["w_out0"] = _blocks(np.ascontiguousarray(g("L0_attn_W_out").T))
    sh["rmsw_t"] = np.ascontiguousarray(g("L0_attn_rms_w").reshape(4, 128).T)
    sh["wr_t"] = np.ascontiguousarray(g("L0_ffn_Wr").reshape(16, 128, 8).transpose(1, 0, 2).reshape(128, 128))
    sh["br_row"] = np.ascontiguousarray(g("L0_ffn_br").reshape(1, 8))
    sh["wd_b"] = _blocks(np.ascontiguousarray(g("L0_ffn_Wd").transpose(1, 0, 2).reshape(D, 128)))
    sh["bd_col"] = np.ascontiguousarray(g("L0_ffn_bd").reshape(128, 1))
    sh["wu_b"] = _blocks(np.ascontiguousarray(g("L0_ffn_Wu").reshape(128, D)))
    sh["bu_t"] = np.ascontiguousarray(g("L0_ffn_bu"))
    sh["r16_t"] = np.ascontiguousarray(np.repeat(np.eye(8, dtype=np.float32), 16, axis=1))
    sh["w_in1"] = _blocks(np.ascontiguousarray(g("L1_attn_W_in").T))
    sh["convw_t"] = np.ascontiguousarray(g("L1_attn_conv_w").reshape(32, 128, 4).transpose(1, 0, 2).reshape(128, 128))
    sh["convb_t"] = np.ascontiguousarray(g("L1_attn_conv_b").reshape(32, 128).T)
    WxT = np.ascontiguousarray(g("L1_attn_Wx").T)
    sh["wx_ts"] = _blocks(np.ascontiguousarray(WxT[:, :128]))
    sh["wx_bc"] = _blocks(np.ascontiguousarray(WxT[:, 128:]), mtw=32)
    sh["sel32"] = np.ascontiguousarray(
        np.concatenate([np.repeat([[1.0, 0.0]], 16, 0), np.repeat([[0.0, 1.0]], 16, 0)], 0).astype(np.float32))
    sh["sel2"] = np.ascontiguousarray(sh["sel32"].T)
    sh["lnw32"] = np.ascontiguousarray(
        np.concatenate([g("L1_attn_b_ln_w"), g("L1_attn_c_ln_w")]).reshape(32, 1))
    sh["dtlnw_t"] = np.ascontiguousarray(g("L1_attn_dt_ln_w").reshape(128, 1))
    sh["wdt_b"] = _blocks(np.ascontiguousarray(g("L1_attn_Wdt").T))
    sh["dtb_t"] = np.ascontiguousarray(g("L1_attn_dt_bias").reshape(32, 128).T)
    sh["a_t"] = np.ascontiguousarray(g("L1_attn_A").reshape(32, 128, 16).transpose(1, 0, 2).reshape(128, 512))
    sh["d_t"] = np.ascontiguousarray(g("L1_attn_D").reshape(32, 128).T)
    sh["w_out1"] = _blocks(np.ascontiguousarray(g("L1_attn_Wout").T))
    sh["gatew_t"] = np.ascontiguousarray(g("L1_ffn_gate_w").reshape(16, 128, 8).transpose(1, 0, 2).reshape(128, 128))
    sh["wg_b"] = np.concatenate([_blocks(np.ascontiguousarray(g("L1_ffn_w_gate")[e])) for e in range(8)], 0)
    sh["wu_b1"] = np.concatenate([_blocks(np.ascontiguousarray(g("L1_ffn_w_up")[e])) for e in range(8)], 0)
    sh["wdn_b"] = np.concatenate([_blocks(np.ascontiguousarray(g("L1_ffn_w_down")[e])) for e in range(8)], 0)
    sh["swg_b"] = _blocks(np.ascontiguousarray(g("L1_ffn_sw_gate")))
    sh["swu_b"] = _blocks(np.ascontiguousarray(g("L1_ffn_sw_up")))
    sh["swd_b"] = _blocks(np.ascontiguousarray(g("L1_ffn_sw_down")))
    sh["eps_t"] = np.broadcast_to(np.array([1e-5, 1e-6], np.float32), (128, 2)).copy()
    sh["ones_col"] = np.ones((128, 1), np.float32)
    sh["ones_row"] = np.ones((1, 128), np.float32)
    sh["ones_row_bf"] = np.ones((1, 128), ml_dtypes.bfloat16)
    sh["ident_bf"] = np.eye(128, dtype=ml_dtypes.bfloat16)
    for k, v in sh.items():
        if v.dtype == np.float64:
            sh[k] = v.astype(np.float32)

    x = g("L0_attn_x")
    in_maps = []
    for c in range(n_cores):
        xc = x[c * Bc:(c + 1) * Bc, :L, :]
        m = dict(sh)
        m["x_fm"] = np.ascontiguousarray(xc.transpose(2, 0, 1).reshape(D, T))
        in_maps.append(m)
    return in_maps



_ESEM = {"tensor": "pe", "vector": "ve", "scalar": "se", "gpsimd": "ge", "sync": "ld"}


class _Inst:
    __slots__ = ("B", "raw", "esem")

    def __init__(self, B, raw, esem=None):
        self.B = B
        self.raw = raw
        self.esem = esem

    def then_inc(self, sem, n=1):
        self.B.C[sem] += n
        if self.raw is not None:
            self.raw.then_inc(self.B.sems[sem], n)
        return self

    def inc(self):
        return self.then_inc(self.esem, 1)


class _Eng:

    def __init__(self, B, name, raw=None):
        self._B = B
        self._name = name
        self._raw = raw

    def wait_ge(self, sem, val):
        if self._raw is not None:
            self._raw.wait_ge(self._B.sems[sem], val)

    def fence(self, sem=None):
        sem = sem or _ESEM[self._name]
        self.wait_ge(sem, self._B.C[sem])

    def sem_inc(self, sem, n=1):
        self._B.C[sem] += n
        if self._raw is not None:
            self._raw.sem_inc(self._B.sems[sem], n)

    def dma(self, out, in_, sem=None):
        assert sem is not None, "dma() needs an explicit semaphore name"
        self._B.C[sem] += 16
        if self._raw is not None:
            self._raw.dma_start(out=out, in_=in_).then_inc(self._B.sems[sem], 16)
        return self._B.C[sem]

    def __getattr__(self, m):
        B, raw = self._B, self._raw
        esem = _ESEM[self._name]

        def call(*a, **kw):
            if raw is None:
                return _Inst(B, None, esem)
            return _Inst(B, getattr(raw, m)(*a, **kw), esem)

        return call


class Bld:
    def __init__(self, nc, cfg, ctx):
        self.nc, self.cfg, self.ctx = nc, cfg, ctx
        self.C = defaultdict(int)
        self.M = {}
        self.pass1 = True
        self.uid = 0
        names = ("pe", "ve", "se", "ge", "bar",
                 "ld0", "ld1", "ld2", "ld3", "gd0", "gd1", "gd2", "gd3",
                 "st0", "st1", "st2", "st3", "wl0", "wl1", "sd0", "sd1",
                 "xa0", "xa1", "xb0", "xb1")
        self.sems = {n: ctx.enter_context(nc.semaphore(n)) for n in names}
        self.ps = [ctx.enter_context(nc.psum_tensor(f"psb{i}", [128, 512], F32)) for i in range(7)]
        self.psbf = ctx.enter_context(nc.psum_tensor("psbfx", [128, 1024], BF16))

    def mark(self, key, sem):
        if self.pass1:
            self.M[key] = self.C[sem]
        return self.M[key]

    def at(self, key):
        return self.M.get(key, 0)

    def name(self, base):
        self.uid += 1
        return f"{base}_{'a' if self.pass1 else 'b'}{self.uid}"

    def sb(self, es, base, shape, dtype):
        return es.enter_context(self.nc.sbuf_tensor(self.name(base), shape, dtype))


def _run_phases(nc, bld, phases, ctx):
    n_eng = len(ENGINE_ORDER)
    snap = {}
    for p1 in (True, False):
        bld.pass1 = p1
        bld.C.clear()
        bld.uid = 0
        if not p1:
            block = ctx.enter_context(nc.Block())
        for pidx, ph in enumerate(phases):
            es = ExitStack()
            bodies = ph(bld, es)
            for ename in ENGINE_ORDER:
                body = bodies.get(ename)
                bar_target = n_eng * pidx

                def run_body(raw_eng, body=body, ename=ename, bar_target=bar_target):
                    E = _Eng(bld, ename, raw_eng)
                    E.wait_ge("bar", bar_target)
                    if body is not None:
                        body(E)
                    bld.C["bar"] += 1
                    if raw_eng is not None:
                        raw_eng.drain().then_inc(bld.sems["bar"], 1)

                if p1:
                    run_body(None)
                else:
                    getattr(block, ename)(run_body)
                key = (pidx, ename)
                if p1:
                    snap[key] = dict(bld.C)
                else:
                    assert snap[key] == dict(bld.C), (
                        f"pass divergence at phase {pidx} engine {ename}: "
                        f"{ {k: (snap[key].get(k), bld.C.get(k)) for k in set(snap[key]) | set(bld.C) if snap[key].get(k) != bld.C.get(k)} }")
            es.close()



_DTIN = {
    "x_fm": (None, F32),
    "w_in0": ([2048, 2048], F32), "w_gate0": ([8192, 2048], F32),
    "bgate_t": ([128, 64], F32), "w_out0": ([2048, 2048], F32),
    "rmsw_t": ([128, 4], F32),
    "wr_t": ([128, 128], F32), "br_row": ([1, 8], F32),
    "wd_b": ([128, 2048], F32), "bd_col": ([128, 1], F32),
    "wu_b": ([2048, 128], F32), "bu_t": ([8, 2048], F32), "r16_t": ([8, 128], F32),
    "w_in1": ([8192, 2048], F32), "convw_t": ([128, 128], F32), "convb_t": ([128, 32], F32),
    "wx_ts": ([128, 4096], F32), "wx_bc": ([128, 1024], F32),
    "sel32": ([32, 2], F32), "sel2": ([2, 32], F32), "lnw32": ([32, 1], F32),
    "dtlnw_t": ([128, 1], F32), "wdt_b": ([4096, 128], F32), "dtb_t": ([128, 32], F32),
    "a_t": ([128, 512], F32), "d_t": ([128, 32], F32),
    "w_out1": ([2048, 4096], F32), "gatew_t": ([128, 128], F32),
    "wg_b": ([8 * 1408, 2048], F32), "wu_b1": ([8 * 1408, 2048], F32),
    "wdn_b": ([8 * 2048, 1408], F32),
    "swg_b": ([2816, 2048], F32), "swu_b": ([2816, 2048], F32), "swd_b": ([2048, 2816], F32),
    "eps_t": ([128, 2], F32),
    "ones_col": ([128, 1], F32), "ones_row": ([1, 128], F32),
    "ones_row_bf": ([1, 128], BF16), "ident_bf": ([128, 128], BF16),
}


def build_program(Bc=2, L=2048):
    D = 2048
    T = Bc * L
    cfg = dict(Bc=Bc, L=L, D=D, T=T, NS=T // 512, NCL=L // 512)

    nc = bass.Bass("TRN2", target_bir_lowering=False, debug=False)
    ctx = ExitStack()
    nc._kb_ctx = ctx

    IN = {}
    for k, (s, d) in _DTIN.items():
        if k == "x_fm":
            s = [D, T]
        IN[k] = nc.dram_tensor(k, s, d, kind="ExternalInput").ap()

    SC = {}
    for k, (s, d) in {
        "xn_s": ([D, T], BF16), "xn32_s": ([D, T], F32), "xn132_s": ([D, T], F32),
        "xp_s": ([D, T], BF16), "g0_s": ([D, T], BF16),
        "gates_s": ([8192, T], F32), "h0_s": ([D, T], BF16), "s0_s": ([D, T], BF16),
        "x1_s": ([D, T], F32), "xn1_s": ([D, T], BF16),
        "hconv_s": ([4096, T], BF16), "sg_s": ([4096, T], BF16),
        "delta_s": ([4096, T], F32), "bc_s": ([32, T], BF16),
        "y1_s": ([4096, T], BF16), "acc1_s": ([D, T], F32), "s1_s": ([D, T], BF16),
        "wfm_s": ([8, T], BF16), "whs_s": ([10 * 1408, T], BF16),
    }.items():
        SC[k] = nc.dram_tensor(k, s, d, kind="Internal").ap()
    OUT = nc.dram_tensor("y_fm", [D, T], F32, kind="ExternalOutput").ap()

    bld = Bld(nc, cfg, ctx)
    bld.IN, bld.SC, bld.OUT = IN, SC, OUT

    phases = [ph_consts, ph_ln1, ph_win0, ph_ffn0, ph_gates, ph_scan0, ph_wout0,
              ph_comb1, ph_win1, ph_ssmp, ph_moescan, ph_wout1, ph_comb2]
    if _PHASE_LIMIT is not None:
        phases = phases[:_PHASE_LIMIT]
    bld.phases = phases
    _run_phases(nc, bld, phases, ctx)
    return nc, cfg


def build_program_upto(nphases, Bc=2, L=2048):
    global _PHASE_LIMIT
    _PHASE_LIMIT = nphases
    try:
        return build_program(Bc, L)
    finally:
        _PHASE_LIMIT = None


_PHASE_LIMIT = None



_CONSTS = [
    ("bgate_t", [128, 64], F32), ("rmsw_t", [128, 4], F32),
    ("wr_t", [128, 128], F32), ("br_row", [1, 8], F32),
    ("bd_col", [128, 1], F32),
    ("bu_t", [8, 2048], BF16), ("r16_t", [8, 128], BF16),
    ("convw_t", [128, 128], F32), ("convb_t", [128, 32], F32),
    ("wx_bc", [128, 1024], BF16),
    ("sel32", [32, 2], F32), ("sel2", [2, 32], F32), ("lnw32", [32, 1], F32),
    ("dtlnw_t", [128, 1], F32),
    ("dtb_t", [128, 32], F32), ("a_t", [128, 512], F32), ("d_t", [128, 32], F32),
    ("gatew_t", [128, 128], F32),
    ("eps_t", [128, 2], F32),
    ("ones_col", [128, 1], F32), ("ones_row", [1, 128], F32),
    ("ones_row_bf", [1, 128], BF16), ("ident_bf", [128, 128], BF16),
]


def ph_consts(bld, es):
    nc = bld.nc
    if not hasattr(bld, "K"):
        bld.K = {k: bld.ctx.enter_context(nc.sbuf_tensor(k + "_sb", shape, dt))
                 for k, shape, dt in _CONSTS}
    K = bld.K

    def gpsimd(E):
        for k, shape, dt in _CONSTS:
            E.dma(K[k][:, :], bld.IN[k][:, :], sem="gd0")
        last = bld.mark("consts_done", "gd0")
        E.wait_ge("gd0", last)

    return {"gpsimd": gpsimd}



def ph_ln1(bld, es):
    return _ln_bodies(bld, es, src=bld.IN["x_fm"], dst=bld.SC["xn_s"], tag="ln1",
                      dst32=bld.SC["xn32_s"])


def _ln_bodies(bld, es, src, dst, tag, eps=1e-5, dst32=None):
    cfg = bld.cfg
    NS, D = cfg["NS"], cfg["D"]
    nc = bld.nc
    K = bld.K
    ps = bld.ps
    xs = bld.sb(es, "ln_x", [128, 2 * 16 * 512], F32)
    sq = bld.sb(es, "ln_sq", [128, 512], F32)
    row = bld.sb(es, "ln_row", [1, 6 * 512], F32)
    xn = bld.sb(es, "ln_xn", [128, 2 * 16 * 512], BF16)
    tmp = bld.sb(es, "ln_tmp", [128, 512], F32)

    def xsv(par, dt):
        return xs[:, (par * 16 + dt) * 512:(par * 16 + dt + 1) * 512]

    def xnv(par, dt):
        return xn[:, (par * 16 + dt) * 512:(par * 16 + dt + 1) * 512]

    R = lambda i: row[0:1, i * 512:(i + 1) * 512]

    def sync(E):
        for s in range(NS):
            par = s % 2
            if s >= 2:
                E.wait_ge("ve", bld.at(f"{tag}_xnd_{s-2}"))
                E.wait_ge(f"gd{par}", bld.at(f"{tag}_st_{s-2}"))
            for g4 in range(4):
                E.dma(xs[:, (par * 16 + g4 * 4) * 512:(par * 16 + g4 * 4 + 4) * 512]
                      .rearrange("p (dt c) -> p dt c", c=512),
                      src[g4 * 512:(g4 + 1) * 512, s * 512:(s + 1) * 512]
                      .rearrange("(dt p) c -> p dt c", p=128), sem=f"ld{par}")
            bld.mark(f"{tag}_ld_{s}", f"ld{par}")

    def scalar(E):
        for s in range(NS):
            par = s % 2
            E.wait_ge(f"ld{par}", bld.at(f"{tag}_ld_{s}"))
            for dt in range(16):
                E.wait_ge("pe", bld.at(f"{tag}_sqfree_{s}_{dt}"))
                E.activation(out=sq[:, :], in_=xsv(par, dt), func=AF.Square).inc()
                bld.mark(f"{tag}_sq_{s}_{dt}", "se")
            E.wait_ge("ve", bld.at(f"{tag}_var_{s}"))
            E.activation(out=R(4), in_=R(3), func=AF.Ln,
                         bias=K["eps_t"][0:1, 0:1] if eps == 1e-5 else K["eps_t"][0:1, 1:2]).inc()
            E.fence()
            E.activation(out=R(5), in_=R(4), func=AF.Exp, scale=-0.5).inc()
            bld.mark(f"{tag}_inv_{s}", "se")

    def tensor(E):
        for s in range(NS):
            par = s % 2
            E.wait_ge(f"ld{par}", bld.at(f"{tag}_ld_{s}"))
            if s >= 1:
                E.wait_ge("ve", bld.at(f"{tag}_statsread_{s-1}"))
            for dt in range(16):
                E.matmul(ps[0][0:1, :], lhsT=K["ones_col"][:, :], rhs=xsv(par, dt),
                         start=(dt == 0), stop=(dt == 15))
            for dt in range(16):
                bld.mark(f"{tag}_sqfree_{s}_{dt}", "pe")
                E.wait_ge("se", bld.at(f"{tag}_sq_{s}_{dt}"))
                E.matmul(ps[1][0:1, :], lhsT=K["ones_col"][:, :], rhs=sq[:, :],
                         start=(dt == 0), stop=(dt == 15)).then_inc("pe", 1)
            bld.mark(f"{tag}_stats_{s}", "pe")
            E.wait_ge("ve", bld.at(f"{tag}_minv_{s}"))
            if s >= 1:
                E.wait_ge("ve", bld.at(f"{tag}_xnd_{s-1}"))
            E.matmul(ps[2][:, :], lhsT=K["ones_row"][:, :], rhs=R(5), start=True, stop=True)
            E.matmul(ps[3][:, :], lhsT=K["ones_row"][:, :], rhs=R(2), start=True, stop=True).then_inc("pe", 1)
            bld.mark(f"{tag}_bcast_{s}", "pe")

    def vector(E):
        for s in range(NS):
            par = s % 2
            E.wait_ge("pe", bld.at(f"{tag}_stats_{s}"))
            E.tensor_scalar(out=R(0), in0=ps[0][0:1, :], scalar1=1.0 / D,
                            scalar2=None, op0=ALU.mult).inc()
            E.tensor_scalar(out=R(1), in0=ps[1][0:1, :], scalar1=1.0 / D,
                            scalar2=None, op0=ALU.mult).inc()
            bld.mark(f"{tag}_statsread_{s}", "ve")
            E.fence()
            E.tensor_tensor(out=tmp[0:1, :], in0=R(0), in1=R(0), op=ALU.mult).inc()
            E.fence()
            E.tensor_tensor(out=R(3), in0=R(1), in1=tmp[0:1, :], op=ALU.subtract).inc()
            bld.mark(f"{tag}_var_{s}", "ve")
            E.wait_ge("se", bld.at(f"{tag}_inv_{s}"))
            E.tensor_tensor(out=R(2), in0=R(0), in1=R(5), op=ALU.mult).inc()
            bld.mark(f"{tag}_minv_{s}", "ve")
            E.wait_ge("pe", bld.at(f"{tag}_bcast_{s}"))
            if s >= 2:
                E.wait_ge(f"gd{par}", bld.at(f"{tag}_st_{s-2}"))
            E.fence()
            for dt in range(16):
                E.tensor_tensor(out=tmp[:, :], in0=xsv(par, dt), in1=ps[2][:, :],
                                op=ALU.mult).inc()
                E.fence()
                E.tensor_tensor(out=xsv(par, dt), in0=tmp[:, :], in1=ps[3][:, :],
                                op=ALU.subtract).inc()
                E.fence()
                E.tensor_copy(out=xnv(par, dt), in_=xsv(par, dt)).inc()
                E.fence()
            bld.mark(f"{tag}_xnd_{s}", "ve")

    def gpsimd(E):
        for s in range(NS):
            par = s % 2
            E.wait_ge("ve", bld.at(f"{tag}_xnd_{s}"))
            for g4 in range(4):
                E.dma(dst[g4 * 512:(g4 + 1) * 512, s * 512:(s + 1) * 512]
                      .rearrange("(dt p) c -> p dt c", p=128),
                      xn[:, (par * 16 + g4 * 4) * 512:(par * 16 + g4 * 4 + 4) * 512]
                      .rearrange("p (dt c) -> p dt c", c=512), sem=f"gd{par}")
            if dst32 is not None:
                for g4 in range(4):
                    E.dma(dst32[g4 * 512:(g4 + 1) * 512, s * 512:(s + 1) * 512]
                          .rearrange("(dt p) c -> p dt c", p=128),
                          xs[:, (par * 16 + g4 * 4) * 512:(par * 16 + g4 * 4 + 4) * 512]
                          .rearrange("p (dt c) -> p dt c", c=512), sem=f"gd{par}")
            bld.mark(f"{tag}_st_{s}", f"gd{par}")
        for p2 in range(min(2, NS)):
            E.wait_ge(f"gd{p2}", bld.C[f"gd{p2}"])

    return {"sync": sync, "scalar": scalar, "tensor": tensor, "vector": vector,
            "gpsimd": gpsimd}



def _mm_wres(bld, es, tag, w_dram, n_kt, n_mt, src, dst, func=AF.Copy, out_dt=BF16,
             bias_col=None):
    cfg = bld.cfg
    NS = cfg["NS"]
    T = cfg["T"]
    ps = bld.ps
    NB = 4
    wsb = bld.sb(es, f"{tag}_w", [128, n_mt * n_kt * 128], BF16)
    xb = bld.sb(es, f"{tag}_x", [128, 2 * n_kt * 512], BF16)
    stage = bld.sb(es, f"{tag}_st", [128, NB * 512], out_dt)

    def wv(mt, kt):
        return wsb[:, (mt * n_kt + kt) * 128:(mt * n_kt + kt + 1) * 128]

    def xv(par, kt):
        return xb[:, (par * n_kt + kt) * 512:(par * n_kt + kt + 1) * 512]

    def gpsimd(E):
        for j in range(n_mt):
            E.dma(wsb[:, j * n_kt * 128:(j + 1) * n_kt * 128],
                  w_dram[j * 128:(j + 1) * 128, :], sem="wl0")
        bld.mark(f"{tag}_wld", "wl0")

    def sync(E):
        for s in range(NS):
            par = s % 2
            if s >= 2:
                E.wait_ge("pe", bld.at(f"{tag}_mmd_{s-2}"))
            for h in range(2):
                E.dma(xb[:, (par * n_kt + h * n_kt // 2) * 512:
                          (par * n_kt + (h + 1) * n_kt // 2) * 512]
                      .rearrange("p (kt c) -> p kt c", c=512),
                      src[h * (n_kt // 2) * 128:(h + 1) * (n_kt // 2) * 128,
                          s * 512:(s + 1) * 512]
                      .rearrange("(kt p) c -> p kt c", p=128), sem=f"ld{par}")
            bld.mark(f"{tag}_rhs_{s}", f"ld{par}")

    def tensor(E):
        E.wait_ge("wl0", bld.at(f"{tag}_wld"))
        for s in range(NS):
            par = s % 2
            E.wait_ge(f"ld{par}", bld.at(f"{tag}_rhs_{s}"))
            for mt in range(n_mt):
                g = s * n_mt + mt
                bank = g % NB
                if g >= NB:
                    E.wait_ge("se", bld.at(f"{tag}_ev_{g-NB}"))
                for kt in range(n_kt):
                    mm = E.matmul(ps[bank][:, :], lhsT=wv(mt, kt), rhs=xv(par, kt),
                                  start=(kt == 0), stop=(kt == n_kt - 1))
                mm.inc()
                bld.mark(f"{tag}_mm_{g}", "pe")
            bld.mark(f"{tag}_mmd_{s}", "pe")

    def scalar(E):
        for s in range(NS):
            for mt in range(n_mt):
                g = s * n_mt + mt
                bank = g % NB
                E.wait_ge("pe", bld.at(f"{tag}_mm_{g}"))
                if g >= NB:
                    E.wait_ge(f"st{bank}", bld.at(f"{tag}_sst_{g-NB}"))
                kw = {}
                fn = func
                if bias_col is not None:
                    kw["bias"] = bias_col(mt)
                    if fn == AF.Copy:
                        fn = AF.Identity
                E.activation(out=stage[:, bank * 512:(bank + 1) * 512],
                             in_=ps[bank][:, :], func=fn, **kw).inc()
                bld.mark(f"{tag}_ev_{g}", "se")
                E.fence("se")
                E.dma(dst[mt * 128:(mt + 1) * 128, s * 512:(s + 1) * 512],
                      stage[:, bank * 512:(bank + 1) * 512], sem=f"st{bank}")
                bld.mark(f"{tag}_sst_{g}", f"st{bank}")
        for b2 in range(NB):
            E.wait_ge(f"st{b2}", bld.C[f"st{b2}"])

    return {"gpsimd": gpsimd, "sync": sync, "tensor": tensor, "scalar": scalar}


def _mm_rhsres(bld, es, tag, w_dram, n_kt, n_mt, src, dst, windows=None,
               func=AF.Copy, out_dt=BF16, bias_col=None):
    cfg = bld.cfg
    if windows is None:
        windows = [(0, cfg["T"])]
    WMAX = max(c1 - c0 for c0, c1 in windows)
    ps = bld.ps
    NB = 4
    xr = bld.sb(es, f"{tag}_x", [128, n_kt * WMAX], BF16)
    wsl = bld.sb(es, f"{tag}_w", [128, 2 * n_kt * 128], BF16)
    stage = bld.sb(es, f"{tag}_st", [128, NB * 512], out_dt)

    def sync(E):
        for w, (c0, c1) in enumerate(windows):
            if w >= 1:
                E.wait_ge("pe", bld.at(f"{tag}_alldone_{w-1}"))
            for h in range(n_kt):
                E.dma(xr[:, h * WMAX:h * WMAX + (c1 - c0)],
                      src[h * 128:(h + 1) * 128, c0:c1], sem="ld0")
            bld.mark(f"{tag}_rhs_{w}", "ld0")

    def gpsimd(E):
        for w, _ in enumerate(windows):
            for mt in range(n_mt):
                if (w, mt) != (0, 0):
                    pw, pmt = (w, mt - 2) if mt >= 2 else (w - 1, n_mt + mt - 2)
                    if pmt >= 0:
                        E.wait_ge("pe", bld.at(f"{tag}_wfree_{pw}_{pmt}"))
                E.dma(wsl[:, (mt % 2) * n_kt * 128:((mt % 2) + 1) * n_kt * 128],
                      w_dram[mt * 128:(mt + 1) * 128, :], sem=f"wl{mt % 2}")
                bld.mark(f"{tag}_wld_{w}_{mt}", f"wl{mt % 2}")

    def tensor(E):
        g = 0
        for w, (c0, c1) in enumerate(windows):
            NSL = (c1 - c0) // 512
            E.wait_ge("ld0", bld.at(f"{tag}_rhs_{w}"))
            for mt in range(n_mt):
                par = mt % 2
                E.wait_ge(f"wl{par}", bld.at(f"{tag}_wld_{w}_{mt}"))
                for s in range(NSL):
                    bank = g % NB
                    if g >= NB:
                        E.wait_ge("se", bld.at(f"{tag}_ev_{g-NB}"))
                    for kt in range(n_kt):
                        mm = E.matmul(ps[bank][:, :],
                                      lhsT=wsl[:, (par * n_kt + kt) * 128:(par * n_kt + kt + 1) * 128],
                                      rhs=xr[:, kt * WMAX + s * 512:kt * WMAX + (s + 1) * 512],
                                      start=(kt == 0), stop=(kt == n_kt - 1))
                    mm.inc()
                    bld.mark(f"{tag}_mm_{g}", "pe")
                    g += 1
                bld.mark(f"{tag}_wfree_{w}_{mt}", "pe")
            bld.mark(f"{tag}_alldone_{w}", "pe")

    def scalar(E):
        g = 0
        for w, (c0, c1) in enumerate(windows):
            NSL = (c1 - c0) // 512
            for mt in range(n_mt):
                for s in range(NSL):
                    bank = g % NB
                    E.wait_ge("pe", bld.at(f"{tag}_mm_{g}"))
                    if g >= NB:
                        E.wait_ge(f"st{bank}", bld.at(f"{tag}_sst_{g-NB}"))
                    kw = {}
                    fn = func
                    if bias_col is not None:
                        kw["bias"] = bias_col(mt)
                        if fn == AF.Copy:
                            fn = AF.Identity
                    E.activation(out=stage[:, bank * 512:(bank + 1) * 512],
                                 in_=ps[bank][:, :], func=fn, **kw).inc()
                    bld.mark(f"{tag}_ev_{g}", "se")
                    E.fence("se")
                    E.dma(dst[mt * 128:(mt + 1) * 128, c0 + s * 512:c0 + (s + 1) * 512],
                          stage[:, bank * 512:(bank + 1) * 512], sem=f"st{bank}")
                    bld.mark(f"{tag}_sst_{g}", f"st{bank}")
                    g += 1
        for b2 in range(NB):
            E.wait_ge(f"st{b2}", bld.C[f"st{b2}"])

    return {"gpsimd": gpsimd, "sync": sync, "tensor": tensor, "scalar": scalar}



def ph_win0(bld, es):
    return _mm_wres(bld, es, "win0", bld.IN["w_in0"], 16, 16,
                    bld.SC["xn_s"], bld.SC["xp_s"])


def ph_gates(bld, es):
    return _mm_rhsres(bld, es, "gat", bld.IN["w_gate0"], 16, 64,
                      bld.SC["xp_s"], bld.SC["gates_s"],
                      out_dt=F32, bias_col=lambda mt: bld.K["bgate_t"][:, mt:mt + 1])


def ph_ffn0(bld, es):
    cfg = bld.cfg
    NS, T = cfg["NS"], cfg["T"]
    nc = bld.nc
    K = bld.K
    ps, psbf = bld.ps, bld.psbf
    xb = bld.sb(es, "f0_x", [128, 2 * 16 * 512], BF16)
    xb32 = bld.sb(es, "f0_x32", [128, 2 * 16 * 512], F32)
    wdl = bld.sb(es, "f0_wd", [128, 2048], BF16)
    wul = bld.sb(es, "f0_wu", [128, 2048], BF16)
    gex = bld.sb(es, "f0_gex", [128, 32], F32)
    gsum = bld.sb(es, "f0_gsum", [128, 4], F32)
    gsm = bld.sb(es, "f0_gsm", [128, 32], F32)
    mx = bld.sb(es, "f0_mx", [128, 32], F32)
    m1 = bld.sb(es, "f0_m1", [128, 32], F32)
    m2 = bld.sb(es, "f0_m2", [128, 32], F32)
    t1 = bld.sb(es, "f0_t1", [128, 32], F32)
    t2 = bld.sb(es, "f0_t2", [128, 32], F32)
    rs = bld.sb(es, "f0_rs", [128, 12], F32)
    wtm = bld.sb(es, "f0_wtm", [128, 32], BF16)
    wfm = bld.sb(es, "f0_wfm", [8, T], BF16)
    hsb = bld.sb(es, "f0_hsb", [128, 512], F32)
    whb = bld.sb(es, "f0_whb", [128, 512], BF16)
    stage = bld.sb(es, "f0_stg", [128, 4 * 512], BF16)

    def xv(par, kt):
        return xb[:, (par * 16 + kt) * 512:(par * 16 + kt + 1) * 512]

    def c8(buf, tt):
        return buf[:, tt * 8:(tt + 1) * 8]

    def gpsimd(E):
        E.dma(wdl[:, :], bld.IN["wd_b"][:, :], sem="wl0")
        E.dma(wul.ap().rearrange("p (bk m) -> p bk m", m=128),
              bld.IN["wu_b"].rearrange("(bk p) m -> p bk m", p=128), sem="wl0")
        bld.mark("f0_wld", "wl0")

    def sync(E):
        for s in range(NS):
            par = s % 2
            if s >= 2:
                E.wait_ge("pe", bld.at(f"f0_done_{s-2}"))
            for h in range(2):
                E.dma(xb[:, (par * 16 + h * 8) * 512:(par * 16 + (h + 1) * 8) * 512]
                      .rearrange("p (kt c) -> p kt c", c=512),
                      bld.SC["xn_s"][h * 8 * 128:(h + 1) * 8 * 128, s * 512:(s + 1) * 512]
                      .rearrange("(kt p) c -> p kt c", p=128), sem=f"ld{par}")
            for h in range(4):
                E.dma(xb32[:, (par * 16 + h * 4) * 512:(par * 16 + (h + 1) * 4) * 512]
                      .rearrange("p (kt c) -> p kt c", c=512),
                      bld.SC["xn32_s"][h * 4 * 128:(h + 1) * 4 * 128, s * 512:(s + 1) * 512]
                      .rearrange("(kt p) c -> p kt c", p=128), sem=f"ld{par}")
            bld.mark(f"f0_rhs_{s}", f"ld{par}")

    def tensor(E):
        E.wait_ge("wl0", bld.at("f0_wld"))
        for s in range(NS):
            par = s % 2
            E.wait_ge(f"ld{par}", bld.at(f"f0_rhs_{s}"))
            for tt in range(4):
                if s >= 1:
                    E.wait_ge("se", bld.at(f"f0_allev_{s-1}"))
                for kt in range(16):
                    E.matmul(ps[tt][:, 0:8],
                             lhsT=xb32[:, (par * 16 + kt) * 512 + tt * 128:
                                       (par * 16 + kt) * 512 + (tt + 1) * 128],
                             rhs=K["wr_t"][:, kt * 8:(kt + 1) * 8],
                             start=(kt == 0), stop=False)
                E.matmul(ps[tt][:, 0:8], lhsT=K["ones_row"][0:1, :],
                         rhs=K["br_row"][0:1, :], start=False, stop=True).inc()
                bld.mark(f"f0_rmm_{s}_{tt}", "pe")
            if s >= 1:
                E.wait_ge("se", bld.at(f"f0_wfm_{s-1}"))
            for tt in range(4):
                E.wait_ge("ve", bld.at(f"f0_wtm_{s}_{tt}"))
                E.transpose(psbf[0:8, tt * 128:(tt + 1) * 128], in_=c8(wtm, tt),
                            identity=K["ident_bf"][:, :]).inc()
                bld.mark(f"f0_tr_{s}_{tt}", "pe")
            if s >= 1:
                E.wait_ge("se", bld.at(f"f0_hev_{s-1}"))
            for kt in range(16):
                mm = E.matmul(ps[4][:, :], lhsT=wdl[:, kt * 128:(kt + 1) * 128],
                              rhs=xv(par, kt), start=(kt == 0), stop=(kt == 15))
            mm.inc()
            bld.mark(f"f0_hmm_{s}", "pe")
            E.wait_ge("se", bld.at(f"f0_wfm_{s}"))
            if s >= 1:
                E.wait_ge("ve", bld.at(f"f0_whb_{s-1}"))
            E.matmul(ps[5][:, :], lhsT=K["r16_t"][:, :],
                     rhs=wfm[0:8, s * 512:(s + 1) * 512], start=True, stop=True).inc()
            bld.mark(f"f0_we_{s}", "pe")
            E.wait_ge("ve", bld.at(f"f0_whb_{s}"))
            for dt in range(16):
                bank = dt % 4
                E.wait_ge("se", bld.at(f"f0_rexp_{s}_{bank}"))
                if dt >= 4:
                    E.wait_ge("se", bld.at(f"f0_g0ev_{s}_{dt-4}"))
                E.matmul(ps[bank][:, :], lhsT=wul[:, dt * 128:(dt + 1) * 128],
                         rhs=whb[:, :], start=True, stop=False)
                E.matmul(ps[bank][:, :], lhsT=K["bu_t"][0:8, dt * 128:(dt + 1) * 128],
                         rhs=wfm[0:8, s * 512:(s + 1) * 512],
                         start=False, stop=True).inc()
                bld.mark(f"f0_dmm_{s}_{dt}", "pe")
            bld.mark(f"f0_done_{s}", "pe")

    def scalar(E):
        for s in range(NS):
            if s >= 1:
                E.wait_ge("ve", bld.at(f"f0_wtm_{s-1}_3"))
            for tt in range(4):
                E.wait_ge("pe", bld.at(f"f0_rmm_{s}_{tt}"))
                E.activation(out=c8(gex, tt), in_=ps[tt][:, 0:8], func=AF.Exp,
                             accum_out=gsum[:, tt:tt + 1]).inc()
                bld.mark(f"f0_rexp_{s}_{tt}", "se")
            for tt in range(4):
                E.wait_ge("pe", bld.at(f"f0_tr_{s}_{tt}"))
                E.activation(out=wfm[0:8, s * 512 + tt * 128:s * 512 + (tt + 1) * 128],
                             in_=psbf[0:8, tt * 128:(tt + 1) * 128], func=AF.Copy).inc()
            bld.mark(f"f0_wfm_{s}", "se")
            E.wait_ge("pe", bld.at(f"f0_hmm_{s}"))
            if s >= 1:
                E.wait_ge("ve", bld.at(f"f0_whb_{s-1}"))
            E.activation(out=hsb[:, :], in_=ps[4][:, :], func=AF.Relu,
                         bias=K["bd_col"][:, :]).inc()
            bld.mark(f"f0_hev_{s}", "se")
            for dt in range(16):
                bank = dt % 4
                g = s * 16 + dt
                E.wait_ge("pe", bld.at(f"f0_dmm_{s}_{dt}"))
                if g >= 4:
                    E.wait_ge(f"st{bank}", bld.at(f"f0_g0st_{g-4}"))
                E.activation(out=stage[:, bank * 512:(bank + 1) * 512],
                             in_=ps[bank][:, :], func=AF.Silu).inc()
                bld.mark(f"f0_g0ev_{s}_{dt}", "se")
                E.fence("se")
                E.dma(bld.SC["g0_s"][dt * 128:(dt + 1) * 128, s * 512:(s + 1) * 512],
                      stage[:, bank * 512:(bank + 1) * 512], sem=f"st{bank}")
                bld.mark(f"f0_g0st_{g}", f"st{bank}")
            bld.mark(f"f0_allev_{s}", "se")
        for b2 in range(4):
            E.wait_ge(f"st{b2}", bld.C[f"st{b2}"])

    def vector(E):
        for s in range(NS):
            for tt in range(4):
                E.wait_ge("se", bld.at(f"f0_rexp_{s}_{tt}"))
                if s >= 1:
                    E.wait_ge("pe", bld.at(f"f0_tr_{s-1}_{tt}"))
                E.reciprocal(rs[:, tt:tt + 1], gsum[:, tt:tt + 1]).inc()
                E.fence()
                E.tensor_scalar(out=c8(gsm, tt), in0=c8(gex, tt),
                                scalar1=rs[:, tt:tt + 1], scalar2=None,
                                op0=ALU.mult).inc()
                E.fence()
                E.max(out=c8(mx, tt), in_=c8(gsm, tt)).inc()
                E.fence()
                E.tensor_tensor(out=rs[:, 4 + tt:5 + tt], in0=mx[:, tt * 8:tt * 8 + 1],
                                in1=mx[:, tt * 8 + 1:tt * 8 + 2], op=ALU.add).inc()
                E.fence()
                E.reciprocal(rs[:, 8 + tt:9 + tt], rs[:, 4 + tt:5 + tt]).inc()
                E.tensor_scalar(out=c8(m1, tt), in0=c8(gsm, tt),
                                scalar1=mx[:, tt * 8:tt * 8 + 1], scalar2=None,
                                op0=ALU.is_equal).inc()
                E.tensor_scalar(out=c8(m2, tt), in0=c8(gsm, tt),
                                scalar1=mx[:, tt * 8 + 1:tt * 8 + 2], scalar2=None,
                                op0=ALU.is_equal).inc()
                E.fence()
                E.tensor_scalar(out=c8(t1, tt), in0=c8(m1, tt),
                                scalar1=mx[:, tt * 8:tt * 8 + 1],
                                scalar2=rs[:, 8 + tt:9 + tt],
                                op0=ALU.mult, op1=ALU.mult).inc()
                E.tensor_scalar(out=c8(t2, tt), in0=c8(m2, tt),
                                scalar1=mx[:, tt * 8 + 1:tt * 8 + 2],
                                scalar2=rs[:, 8 + tt:9 + tt],
                                op0=ALU.mult, op1=ALU.mult).inc()
                E.fence()
                E.tensor_tensor(out=c8(wtm, tt), in0=c8(t1, tt), in1=c8(t2, tt),
                                op=ALU.add).inc()
                bld.mark(f"f0_wtm_{s}_{tt}", "ve")
            E.wait_ge("pe", bld.at(f"f0_we_{s}"))
            E.wait_ge("se", bld.at(f"f0_hev_{s}"))
            if s >= 1:
                E.wait_ge("pe", bld.at(f"f0_done_{s-1}"))
            E.tensor_tensor(out=whb[:, :], in0=hsb[:, :], in1=ps[5][:, :],
                            op=ALU.mult).inc()
            bld.mark(f"f0_whb_{s}", "ve")

    return {"sync": sync, "gpsimd": gpsimd, "tensor": tensor, "scalar": scalar, "vector": vector}

def ph_scan0(bld, es):
    cfg = bld.cfg
    Bc, L, NCL = cfg["Bc"], cfg["L"], cfg["NCL"]
    K = bld.K
    ps = bld.ps
    GS = bld.SC["gates_s"]
    trip = bld.sb(es, "s0_trip", [128, 3 * L], F32)
    oin = bld.sb(es, "s0_oin", [128, 4 * L], F32)
    ctr = bld.sb(es, "s0_ctr", [128, 4 * L], F32)
    us = bld.sb(es, "s0_us", [128, L], F32)
    sq = bld.sb(es, "s0_sq", [128, L], F32)
    cn = bld.sb(es, "s0_cn", [128, L], F32)
    th = bld.sb(es, "s0_th", [128, L], F32)
    invr = bld.sb(es, "s0_inv", [1, L], F32)
    hst = bld.sb(es, "s0_h", [128, 4 * L], BF16)
    NBH = Bc * 4

    def rows(gt, h, q):
        r0 = gt * 2048 + (h * 4 + q) * 128
        return (r0, r0 + 128)

    def sync(E):
        for bh in range(NBH):
            b, h = divmod(bh, 4)
            for q in range(4):
                if bh > 0 or q > 0:
                    pb, pq = (bh, q - 1) if q > 0 else (bh - 1, 3)
                    E.wait_ge("ve", bld.at(f"s0_tripfree_{pb}_{pq}"))
                for slot, gt in ((0, 0), (1, 1), (2, 3)):
                    r0, r1 = rows(gt, h, q)
                    E.dma(trip[:, slot * L:(slot + 1) * L],
                          GS[r0:r1, b * L:(b + 1) * L], sem="ld0")
                bld.mark(f"s0_trip_{bh}_{q}", "ld0")
            if bh > 0:
                E.wait_ge("ve", bld.at(f"s0_hdone_{bh-1}"))
            for q in range(4):
                r0, r1 = rows(2, h, q)
                E.dma(oin[:, q * L:(q + 1) * L], GS[r0:r1, b * L:(b + 1) * L], sem="ld1")
            bld.mark(f"s0_oin_{bh}", "ld1")

    def scalar(E):
        for bh in range(NBH):
            for q in range(4):
                E.wait_ge("ld0", bld.at(f"s0_trip_{bh}_{q}"))
                E.activation(out=trip[:, L:2 * L], in_=trip[:, L:2 * L],
                             func=AF.Sigmoid).inc()
                E.activation(out=trip[:, 2 * L:3 * L], in_=trip[:, 2 * L:3 * L],
                             func=AF.Tanh).inc()
                E.activation(out=trip[:, 0:L], in_=trip[:, 0:L], func=AF.Exp).inc()
                bld.mark(f"s0_acts_{bh}_{q}", "se")
                E.wait_ge("ve", bld.at(f"s0_scan_{bh}_{q}"))
                if q > 0:
                    E.wait_ge("pe", bld.at(f"s0_sqmm_{bh}_{q-1}"))
                elif bh > 0:
                    E.wait_ge("pe", bld.at(f"s0_sqmm_{bh-1}_3"))
                E.activation(out=sq[:, :], in_=ctr[:, q * L:(q + 1) * L],
                             func=AF.Square).inc()
                bld.mark(f"s0_csq_{bh}_{q}", "se")
            E.wait_ge("ld1", bld.at(f"s0_oin_{bh}"))
            E.activation(out=oin[:, :], in_=oin[:, :], func=AF.Sigmoid).inc()
            bld.mark(f"s0_osig_{bh}", "se")
            E.wait_ge("pe", bld.at(f"s0_stats_{bh}"))
            for lc in range(NCL):
                E.activation(out=invr[0:1, lc * 512:(lc + 1) * 512],
                             in_=ps[lc][0:1, :], func=AF.Ln, scale=1.0 / 512,
                             bias=K["eps_t"][0:1, 1:2]).inc()
            bld.mark(f"s0_lnread_{bh}", "se")
            E.fence("se")
            E.activation(out=invr[0:1, :], in_=invr[0:1, :], func=AF.Exp,
                         scale=-0.5).inc()
            bld.mark(f"s0_inv_{bh}", "se")
            for q in range(4):
                E.wait_ge("ve", bld.at(f"s0_cn_{bh}_{q}"))
                if q > 0:
                    E.wait_ge("ve", bld.at(f"s0_hq_{bh}_{q-1}"))
                elif bh > 0:
                    E.wait_ge("ve", bld.at(f"s0_hq_{bh-1}_3"))
                E.activation(out=th[:, :], in_=cn[:, :], func=AF.Tanh,
                             scale=K["rmsw_t"][:, q:q + 1]).inc()
                bld.mark(f"s0_th_{bh}_{q}", "se")

    def vector(E):
        for bh in range(NBH):
            for q in range(4):
                E.wait_ge("se", bld.at(f"s0_acts_{bh}_{q}"))
                if bh > 0:
                    E.wait_ge("se", bld.at(f"s0_csq_{bh-1}_{q}"))
                E.fence()
                E.tensor_tensor(out=us[:, :], in0=trip[:, 0:L], in1=trip[:, 2 * L:3 * L],
                                op=ALU.mult).inc()
                E.fence()
                E.tensor_tensor_scan(out=ctr[:, q * L:(q + 1) * L],
                                     data0=trip[:, L:2 * L], data1=us[:, :],
                                     initial=0.0, op0=ALU.mult, op1=ALU.add).inc()
                bld.mark(f"s0_scan_{bh}_{q}", "ve")
                bld.mark(f"s0_tripfree_{bh}_{q}", "ve")
            for q in range(4):
                E.wait_ge("pe", bld.at(f"s0_bcast_{bh}"))
                E.fence()
                for lc in range(NCL):
                    E.tensor_tensor(out=cn[:, lc * 512:(lc + 1) * 512],
                                    in0=ctr[:, q * L + lc * 512:q * L + (lc + 1) * 512],
                                    in1=ps[lc][:, :], op=ALU.mult).inc()
                bld.mark(f"s0_cn_{bh}_{q}", "ve")
                E.wait_ge("se", bld.at(f"s0_th_{bh}_{q}"))
                E.wait_ge("se", bld.at(f"s0_osig_{bh}"))
                if q == 0 and bh > 0:
                    E.wait_ge("gd0", bld.at(f"s0_hst_{bh-1}"))
                E.fence()
                E.tensor_tensor(out=hst[:, q * L:(q + 1) * L],
                                in0=oin[:, q * L:(q + 1) * L], in1=th[:, :],
                                op=ALU.mult).inc()
                bld.mark(f"s0_hq_{bh}_{q}", "ve")
            bld.mark(f"s0_hdone_{bh}", "ve")

    def tensor(E):
        for bh in range(NBH):
            for q in range(4):
                E.wait_ge("se", bld.at(f"s0_csq_{bh}_{q}"))
                if q == 0 and bh > 0:
                    E.wait_ge("ve", bld.at(f"s0_cn_{bh-1}_3"))
                for lc in range(NCL):
                    mm = E.matmul(ps[lc][0:1, :], lhsT=K["ones_col"][:, :],
                                  rhs=sq[:, lc * 512:(lc + 1) * 512],
                                  start=(q == 0), stop=(q == 3))
                mm.inc()
                bld.mark(f"s0_sqmm_{bh}_{q}", "pe")
            bld.mark(f"s0_stats_{bh}", "pe")
            E.wait_ge("se", bld.at(f"s0_inv_{bh}"))
            for lc in range(NCL):
                mm = E.matmul(ps[lc][:, :], lhsT=K["ones_row"][:, :],
                              rhs=invr[0:1, lc * 512:(lc + 1) * 512],
                              start=True, stop=True)
            mm.inc()
            bld.mark(f"s0_bcast_{bh}", "pe")

    def gpsimd(E):
        for bh in range(NBH):
            b, h = divmod(bh, 4)
            for q in range(4):
                E.wait_ge("ve", bld.at(f"s0_hq_{bh}_{q}"))
                E.dma(bld.SC["h0_s"][(h * 4 + q) * 128:(h * 4 + q + 1) * 128,
                                     b * L:(b + 1) * L],
                      hst[:, q * L:(q + 1) * L], sem="gd0")
            bld.mark(f"s0_hst_{bh}", "gd0")
        E.wait_ge("gd0", bld.C["gd0"])

    return {"sync": sync, "scalar": scalar, "vector": vector, "tensor": tensor,
            "gpsimd": gpsimd}

def ph_wout0(bld, es):
    return _mm_wres(bld, es, "wo0", bld.IN["w_out0"], 16, 16,
                    bld.SC["h0_s"], bld.SC["s0_s"])

def ph_comb1(bld, es):
    cfg = bld.cfg
    NS, D = cfg["NS"], cfg["D"]
    K = bld.K
    ps = bld.ps
    tag = "cb1"
    eps = 1e-5
    xs = bld.sb(es, "cb_x", [128, 2 * 16 * 512], F32)
    s0b = bld.sb(es, "cb_s0", [128, 16 * 512], BF16)
    g0b = bld.sb(es, "cb_g0", [128, 16 * 512], BF16)
    xf = bld.sb(es, "cb_xf", [128, 16 * 512], F32)
    sq = bld.sb(es, "cb_sq", [128, 512], F32)
    row = bld.sb(es, "cb_row", [1, 6 * 512], F32)
    xn = bld.sb(es, "cb_xn", [128, 16 * 512], BF16)
    tmp = bld.sb(es, "cb_tmp", [128, 512], F32)

    def xsv(par, dt):
        return xs[:, (par * 16 + dt) * 512:(par * 16 + dt + 1) * 512]

    def xnv(par, dt):
        return xn[:, dt * 512:(dt + 1) * 512]

    def v16(buf, dt):
        return buf[:, dt * 512:(dt + 1) * 512]

    R = lambda i: row[0:1, i * 512:(i + 1) * 512]
    src, s0s, g0s = bld.IN["x_fm"], bld.SC["s0_s"], bld.SC["g0_s"]
    dx1, dxn, dxn32 = bld.SC["x1_s"], bld.SC["xn1_s"], bld.SC["xn132_s"]
    gex = bld.sb(es, "cb_gex", [128, 32], F32)
    gsum = bld.sb(es, "cb_gsum", [128, 4], F32)
    gsm = bld.sb(es, "cb_gsm", [128, 32], F32)
    mx = bld.sb(es, "cb_mx", [128, 32], F32)
    m1 = bld.sb(es, "cb_m1", [128, 32], F32)
    m2 = bld.sb(es, "cb_m2", [128, 32], F32)
    t1 = bld.sb(es, "cb_t1", [128, 32], F32)
    t2 = bld.sb(es, "cb_t2", [128, 32], F32)
    rs = bld.sb(es, "cb_rs", [128, 4], F32)
    wtm = bld.sb(es, "cb_wtm", [128, 32], BF16)
    wstg = bld.sb(es, "cb_wst", [8, 512], BF16)
    psbf = bld.psbf

    def c8(buf, tt):
        return buf[:, tt * 8:(tt + 1) * 8]

    def sync(E):
        for s in range(NS):
            par = s % 2
            if s >= 2:
                E.wait_ge("ve", bld.at(f"{tag}_xnd_{s-2}"))
                E.wait_ge(f"gd{par}", bld.at(f"{tag}_stx_{s-2}"))
            for g4 in range(4):
                E.dma(xs[:, (par * 16 + g4 * 4) * 512:(par * 16 + g4 * 4 + 4) * 512]
                      .rearrange("p (dt c) -> p dt c", c=512),
                      src[g4 * 512:(g4 + 1) * 512, s * 512:(s + 1) * 512]
                      .rearrange("(dt p) c -> p dt c", p=128), sem=f"ld{par}")
            bld.mark(f"{tag}_ldx_{s}", f"ld{par}")
            if s >= 1:
                E.wait_ge("ve", bld.at(f"{tag}_x1d_{s-1}"))
            for g8, dst_b, src_b in ((0, s0b, s0s), (1, g0b, g0s)):
                for h in range(2):
                    E.dma(dst_b[:, h * 8 * 512:(h + 1) * 8 * 512]
                          .rearrange("p (dt c) -> p dt c", c=512),
                          src_b[h * 8 * 128:(h + 1) * 8 * 128, s * 512:(s + 1) * 512]
                          .rearrange("(dt p) c -> p dt c", p=128), sem="ld2")
            bld.mark(f"{tag}_ldsg_{s}", "ld2")

    def scalar(E):
        for s in range(NS):
            par = s % 2
            for dt in range(16):
                E.wait_ge("ve", bld.at(f"{tag}_x1_{s}_{dt}"))
                E.wait_ge("pe", bld.at(f"{tag}_sqfree_{s}_{dt}"))
                E.activation(out=sq[:, :], in_=xsv(par, dt), func=AF.Square).inc()
                bld.mark(f"{tag}_sq_{s}_{dt}", "se")
            E.wait_ge("ve", bld.at(f"{tag}_var_{s}"))
            E.activation(out=R(4), in_=R(3), func=AF.Ln,
                         bias=K["eps_t"][0:1, 0:1]).inc()
            E.fence()
            E.activation(out=R(5), in_=R(4), func=AF.Exp, scale=-0.5).inc()
            bld.mark(f"{tag}_inv_{s}", "se")
            if s >= 1:
                E.wait_ge("ve", bld.at(f"{tag}_wtm_{s-1}_3"))
            for tt in range(4):
                E.wait_ge("pe", bld.at(f"{tag}_rmm_{s}_{tt}"))
                E.activation(out=c8(gex, tt), in_=ps[4 + tt % 2][:, 0:8], func=AF.Exp,
                             accum_out=gsum[:, tt:tt + 1]).inc()
                bld.mark(f"{tag}_rexp_{s}_{tt}", "se")
            for tt in range(4):
                E.wait_ge("pe", bld.at(f"{tag}_tr_{s}_{tt}"))
                if s >= 1 and tt == 0:
                    E.wait_ge("sd0", bld.at(f"{tag}_wst_{s-1}"))
                E.activation(out=wstg[0:8, tt * 128:(tt + 1) * 128],
                             in_=psbf[0:8, tt * 128:(tt + 1) * 128], func=AF.Copy).inc()
                bld.mark(f"{tag}_wcp_{s}_{tt}", "se")
            E.fence("se")
            E.dma(bld.SC["wfm_s"][0:8, s * 512:(s + 1) * 512], wstg[0:8, :], sem="sd0")
            bld.mark(f"{tag}_wst_{s}", "sd0")

    def tensor(E):
        for s in range(NS):
            par = s % 2
            if s >= 1:
                E.wait_ge("ve", bld.at(f"{tag}_statsread_{s-1}"))
            for dt in range(16):
                E.wait_ge("ve", bld.at(f"{tag}_x1_{s}_{dt}"))
                E.matmul(ps[0][0:1, :], lhsT=K["ones_col"][:, :], rhs=xsv(par, dt),
                         start=(dt == 0), stop=(dt == 15))
            for dt in range(16):
                bld.mark(f"{tag}_sqfree_{s}_{dt}", "pe")
                E.wait_ge("se", bld.at(f"{tag}_sq_{s}_{dt}"))
                E.matmul(ps[1][0:1, :], lhsT=K["ones_col"][:, :], rhs=sq[:, :],
                         start=(dt == 0), stop=(dt == 15)).then_inc("pe", 1)
            bld.mark(f"{tag}_stats_{s}", "pe")
            E.wait_ge("ve", bld.at(f"{tag}_minv_{s}"))
            if s >= 1:
                E.wait_ge("ve", bld.at(f"{tag}_xnd_{s-1}"))
            E.matmul(ps[2][:, :], lhsT=K["ones_row"][:, :], rhs=R(5), start=True, stop=True)
            E.matmul(ps[3][:, :], lhsT=K["ones_row"][:, :], rhs=R(2), start=True,
                     stop=True).then_inc("pe", 1)
            bld.mark(f"{tag}_bcast_{s}", "pe")
            E.wait_ge("ve", bld.at(f"{tag}_xnd_{s}"))
            for tt in range(4):
                if tt >= 2:
                    E.wait_ge("se", bld.at(f"{tag}_rexp_{s}_{tt-2}"))
                elif s >= 1:
                    E.wait_ge("se", bld.at(f"{tag}_rexp_{s-1}_{tt+2}"))
                for kt in range(16):
                    mm = E.matmul(ps[4 + tt % 2][:, 0:8],
                                  lhsT=xf[:, kt * 512 + tt * 128:kt * 512 + (tt + 1) * 128],
                                  rhs=K["gatew_t"][:, kt * 8:(kt + 1) * 8],
                                  start=(kt == 0), stop=(kt == 15))
                mm.inc()
                bld.mark(f"{tag}_rmm_{s}_{tt}", "pe")
            if s >= 1:
                E.wait_ge("se", bld.at(f"{tag}_wcp_{s-1}_3"))
            for tt in range(4):
                E.wait_ge("ve", bld.at(f"{tag}_wtm_{s}_{tt}"))
                E.transpose(psbf[0:8, tt * 128:(tt + 1) * 128], in_=c8(wtm, tt),
                            identity=K["ident_bf"][:, :]).inc()
                bld.mark(f"{tag}_tr_{s}_{tt}", "pe")

    def vector(E):
        for s in range(NS):
            par = s % 2
            E.wait_ge(f"ld{par}", bld.at(f"{tag}_ldx_{s}"))
            E.wait_ge("ld2", bld.at(f"{tag}_ldsg_{s}"))
            for dt in range(16):
                E.fence()
                E.tensor_tensor(out=tmp[:, :], in0=v16(s0b, dt), in1=v16(g0b, dt),
                                op=ALU.mult).inc()
                E.fence()
                E.tensor_tensor(out=xsv(par, dt), in0=xsv(par, dt), in1=tmp[:, :],
                                op=ALU.add).inc()
                bld.mark(f"{tag}_x1_{s}_{dt}", "ve")
            bld.mark(f"{tag}_x1d_{s}", "ve")
            E.wait_ge("pe", bld.at(f"{tag}_stats_{s}"))
            E.tensor_scalar(out=R(0), in0=ps[0][0:1, :], scalar1=1.0 / D,
                            scalar2=None, op0=ALU.mult).inc()
            E.tensor_scalar(out=R(1), in0=ps[1][0:1, :], scalar1=1.0 / D,
                            scalar2=None, op0=ALU.mult).inc()
            bld.mark(f"{tag}_statsread_{s}", "ve")
            E.fence()
            E.tensor_tensor(out=tmp[0:1, :], in0=R(0), in1=R(0), op=ALU.mult).inc()
            E.fence()
            E.tensor_tensor(out=R(3), in0=R(1), in1=tmp[0:1, :], op=ALU.subtract).inc()
            bld.mark(f"{tag}_var_{s}", "ve")
            E.wait_ge("se", bld.at(f"{tag}_inv_{s}"))
            E.tensor_tensor(out=R(2), in0=R(0), in1=R(5), op=ALU.mult).inc()
            bld.mark(f"{tag}_minv_{s}", "ve")
            E.wait_ge("pe", bld.at(f"{tag}_bcast_{s}"))
            if s >= 1:
                E.wait_ge(f"gd{1-par}", bld.at(f"{tag}_stn_{s-1}"))
            if s >= 2:
                E.wait_ge(f"gd{par}", bld.at(f"{tag}_stn_{s-2}"))
            E.fence()
            for dt in range(16):
                E.tensor_tensor(out=tmp[:, :], in0=xsv(par, dt), in1=ps[2][:, :],
                                op=ALU.mult).inc()
                E.fence()
                E.tensor_tensor(out=v16(xf, dt), in0=tmp[:, :], in1=ps[3][:, :],
                                op=ALU.subtract).inc()
                E.fence()
                E.tensor_copy(out=xnv(par, dt), in_=v16(xf, dt)).inc()
                E.fence()
            bld.mark(f"{tag}_xnd_{s}", "ve")
            for tt in range(4):
                E.wait_ge("se", bld.at(f"{tag}_rexp_{s}_{tt}"))
                if s >= 1:
                    E.wait_ge("pe", bld.at(f"{tag}_tr_{s-1}_{tt}"))
                E.fence()
                E.reciprocal(rs[:, tt:tt + 1], gsum[:, tt:tt + 1]).inc()
                E.fence()
                E.tensor_scalar(out=c8(gsm, tt), in0=c8(gex, tt),
                                scalar1=rs[:, tt:tt + 1], scalar2=None,
                                op0=ALU.mult).inc()
                E.fence()
                E.max(out=c8(mx, tt), in_=c8(gsm, tt)).inc()
                E.fence()
                E.tensor_scalar(out=c8(m1, tt), in0=c8(gsm, tt),
                                scalar1=mx[:, tt * 8:tt * 8 + 1], scalar2=None,
                                op0=ALU.is_equal).inc()
                E.tensor_scalar(out=c8(m2, tt), in0=c8(gsm, tt),
                                scalar1=mx[:, tt * 8 + 1:tt * 8 + 2], scalar2=None,
                                op0=ALU.is_equal).inc()
                E.fence()
                E.tensor_scalar(out=c8(t1, tt), in0=c8(m1, tt),
                                scalar1=mx[:, tt * 8:tt * 8 + 1], scalar2=None,
                                op0=ALU.mult).inc()
                E.tensor_scalar(out=c8(t2, tt), in0=c8(m2, tt),
                                scalar1=mx[:, tt * 8 + 1:tt * 8 + 2], scalar2=None,
                                op0=ALU.mult).inc()
                E.fence()
                E.tensor_tensor(out=c8(wtm, tt), in0=c8(t1, tt), in1=c8(t2, tt),
                                op=ALU.add).inc()
                bld.mark(f"{tag}_wtm_{s}_{tt}", "ve")

    def gpsimd(E):
        for s in range(NS):
            par = s % 2
            E.wait_ge("ve", bld.at(f"{tag}_x1d_{s}"))
            for g4 in range(4):
                E.dma(dx1[g4 * 512:(g4 + 1) * 512, s * 512:(s + 1) * 512]
                      .rearrange("(dt p) c -> p dt c", p=128),
                      xs[:, (par * 16 + g4 * 4) * 512:(par * 16 + g4 * 4 + 4) * 512]
                      .rearrange("p (dt c) -> p dt c", c=512), sem=f"gd{par}")
            bld.mark(f"{tag}_stx_{s}", f"gd{par}")
            E.wait_ge("ve", bld.at(f"{tag}_xnd_{s}"))
            for g4 in range(4):
                E.dma(dxn[g4 * 512:(g4 + 1) * 512, s * 512:(s + 1) * 512]
                      .rearrange("(dt p) c -> p dt c", p=128),
                      xn[:, g4 * 4 * 512:(g4 * 4 + 4) * 512]
                      .rearrange("p (dt c) -> p dt c", c=512), sem=f"gd{par}")
                E.dma(dxn32[g4 * 512:(g4 + 1) * 512, s * 512:(s + 1) * 512]
                      .rearrange("(dt p) c -> p dt c", p=128),
                      xf[:, g4 * 4 * 512:(g4 * 4 + 4) * 512]
                      .rearrange("p (dt c) -> p dt c", c=512), sem=f"gd{par}")
            bld.mark(f"{tag}_stn_{s}", f"gd{par}")
        for p2 in range(min(2, NS)):
            E.wait_ge(f"gd{p2}", bld.C[f"gd{p2}"])

    return {"sync": sync, "scalar": scalar, "tensor": tensor, "vector": vector,
            "gpsimd": gpsimd}

def ph_win1(bld, es):
    cfg = bld.cfg
    Bc, L, NCL, T = cfg["Bc"], cfg["L"], cfg["NCL"], cfg["T"]
    K = bld.K
    ps = bld.ps
    n_kt, n_mt = 16, 64
    xr = bld.sb(es, "w1_x", [128, n_kt * T], BF16)
    wsl = bld.sb(es, "w1_w", [128, 2 * n_kt * 128], BF16)
    cvb = bld.sb(es, "w1_cv", [128, 3 + L], F32)
    acc = bld.sb(es, "w1_acc", [128, L], F32)
    hstg = bld.sb(es, "w1_hst", [128, 2 * L], BF16)
    gstg = bld.sb(es, "w1_gst", [128, 4 * 512], BF16)
    src = bld.SC["xn1_s"]
    dsth, dstg = bld.SC["hconv_s"], bld.SC["sg_s"]

    def sync(E):
        for h in range(n_kt):
            E.dma(xr[:, h * T:(h + 1) * T], src[h * 128:(h + 1) * 128, :], sem="ld0")
        bld.mark("w1_rhs", "ld0")

    def gpsimd(E):
        for mt in range(n_mt):
            if mt >= 2:
                E.wait_ge("pe", bld.at(f"w1_wfree_{mt-2}"))
            E.dma(wsl[:, (mt % 2) * n_kt * 128:((mt % 2) + 1) * n_kt * 128],
                  bld.IN["w_in1"][mt * 128:(mt + 1) * 128, :], sem=f"wl{mt % 2}")
            bld.mark(f"w1_wld_{mt}", f"wl{mt % 2}")

    def tensor(E):
        E.wait_ge("ld0", bld.at("w1_rhs"))
        g = 0
        for mt in range(n_mt):
            par = mt % 2
            E.wait_ge(f"wl{par}", bld.at(f"w1_wld_{mt}"))
            for b in range(Bc):
                for lc in range(NCL):
                    bank = g % 4
                    if g >= 4:
                        E.wait_ge("se", bld.at(f"w1_ev_{g-4}"))
                    for kt in range(n_kt):
                        mm = E.matmul(ps[bank][:, :],
                                      lhsT=wsl[:, (par * n_kt + kt) * 128:
                                              (par * n_kt + kt + 1) * 128],
                                      rhs=xr[:, kt * T + b * L + lc * 512:
                                             kt * T + b * L + (lc + 1) * 512],
                                      start=(kt == 0), stop=(kt == n_kt - 1))
                    mm.inc()
                    bld.mark(f"w1_mm_{g}", "pe")
                    g += 1
            bld.mark(f"w1_wfree_{mt}", "pe")

    def scalar(E):
        g = 0
        for mt in range(n_mt):
            for b in range(Bc):
                ob = (mt * Bc + b) % 2
                for lc in range(NCL):
                    bank = g % 4
                    E.wait_ge("pe", bld.at(f"w1_mm_{g}"))
                    if mt < 32:
                        if mt + b > 0:
                            E.wait_ge("ve", bld.at(f"w1_acc_{(mt * Bc + b) - 1}"))
                        E.activation(out=cvb[:, 3 + lc * 512:3 + (lc + 1) * 512],
                                     in_=ps[bank][:, :], func=AF.Copy).inc()
                    else:
                        gg = (mt - 32) * Bc * NCL + b * NCL + lc
                        if gg >= 4:
                            E.wait_ge(f"st{bank}", bld.at(f"w1_gst_{gg-4}"))
                        E.activation(out=gstg[:, bank * 512:(bank + 1) * 512],
                                     in_=ps[bank][:, :], func=AF.Silu).inc()
                        E.fence("se")
                        E.dma(dstg[(mt - 32) * 128:(mt - 31) * 128,
                                   b * L + lc * 512:b * L + (lc + 1) * 512],
                              gstg[:, bank * 512:(bank + 1) * 512], sem=f"st{bank}")
                        bld.mark(f"w1_gst_{gg}", f"st{bank}")
                    bld.mark(f"w1_ev_{g}", "se")
                    g += 1
                if mt < 32:
                    bld.mark(f"w1_cvb_{mt * Bc + b}", "se")
                    E.wait_ge("ve", bld.at(f"w1_conv_{mt * Bc + b}"))
                    if mt * Bc + b >= 2:
                        E.wait_ge(f"sd{ob}", bld.at(f"w1_hstd_{mt * Bc + b - 2}"))
                    E.activation(out=hstg[:, ob * L:(ob + 1) * L], in_=acc[:, :],
                                 func=AF.Silu, bias=K["convb_t"][:, mt:mt + 1]).inc()
                    bld.mark(f"w1_hsig_{mt * Bc + b}", "se")
                    E.fence("se")
                    E.dma(dsth[mt * 128:(mt + 1) * 128, b * L:(b + 1) * L],
                          hstg[:, ob * L:(ob + 1) * L], sem=f"sd{ob}")
                    bld.mark(f"w1_hstd_{mt * Bc + b}", f"sd{ob}")
        for b2 in range(4):
            E.wait_ge(f"st{b2}", bld.C[f"st{b2}"])
        for b2 in range(2):
            E.wait_ge(f"sd{b2}", bld.C[f"sd{b2}"])

    def vector(E):
        E.memset(cvb[:, 0:3], 0.0).inc()
        for ob in range(32 * Bc):
            mt = ob // Bc
            E.wait_ge("se", bld.at(f"w1_cvb_{ob}"))
            if ob >= 1:
                E.wait_ge("se", bld.at(f"w1_hsig_{ob-1}"))
            E.fence()
            E.tensor_scalar(out=acc[:, :], in0=cvb[:, 0:L],
                            scalar1=K["convw_t"][:, mt * 4:mt * 4 + 1],
                            scalar2=None, op0=ALU.mult).inc()
            for k in range(1, 4):
                E.fence()
                E.scalar_tensor_tensor(out=acc[:, :], in0=cvb[:, k:k + L],
                                       scalar=K["convw_t"][:, mt * 4 + k:mt * 4 + k + 1],
                                       in1=acc[:, :], op0=ALU.mult, op1=ALU.add).inc()
            bld.mark(f"w1_conv_{ob}", "ve")

    return {"sync": sync, "gpsimd": gpsimd, "tensor": tensor, "scalar": scalar,
            "vector": vector}

def ph_ssmp(bld, es):
    cfg = bld.cfg
    Bc, L, NCL = cfg["Bc"], cfg["L"], cfg["NCL"]
    K = bld.K
    ps = bld.ps
    rhsb = bld.sb(es, "sp_x", [128, 2 * 32 * 512], BF16)
    wxl = bld.sb(es, "sp_wx", [128, 4096], BF16)
    wdtl = bld.sb(es, "sp_wdt", [128, 4096], BF16)
    tssq = bld.sb(es, "sp_tssq", [128, 512], F32)
    ts_sb = bld.sb(es, "sp_ts", [128, 512], F32)
    tsn = bld.sb(es, "sp_tsn", [128, 512], BF16)
    bcsq = bld.sb(es, "sp_bcsq", [32, 512], F32)
    bc_sb = bld.sb(es, "sp_bc", [32, 512], F32)
    bcn = bld.sb(es, "sp_bcn", [32, 512], BF16)
    tmp2 = bld.sb(es, "sp_tmp2", [32, 512], F32)
    invr = bld.sb(es, "sp_inv", [1, 512], F32)
    inv2 = bld.sb(es, "sp_inv2", [2, 512], F32)
    spa = bld.sb(es, "sp_a", [128, 512], F32)
    spr = bld.sb(es, "sp_r", [128, 512], F32)
    dstage = bld.sb(es, "sp_dst", [128, 4 * 512], F32)
    tmp = bld.sb(es, "sp_tmp", [128, 512], F32)
    src = bld.SC["hconv_s"]
    NCH = Bc * NCL

    def gpsimd(E):
        E.dma(wxl[:, :], bld.IN["wx_ts"][:, :], sem="wl0")
        E.dma(wdtl.ap().rearrange("p (bk m) -> p bk m", m=128),
              bld.IN["wdt_b"].rearrange("(bk p) m -> p bk m", p=128), sem="wl0")
        bld.mark("sp_wld", "wl0")
        for c in range(NCH):
            b, lc = divmod(c, NCL)
            E.wait_ge("ve", bld.at(f"sp_bcn_{c}"))
            E.dma(bld.SC["bc_s"][0:32, b * L + lc * 512:b * L + (lc + 1) * 512],
                  bcn[0:32, :], sem="gd0")
            bld.mark(f"sp_bcst_{c}", "gd0")
        E.wait_ge("gd0", bld.C["gd0"])

    def rv(par, kt):
        return rhsb[:, (par * 32 + kt) * 512:(par * 32 + kt + 1) * 512]

    def tensor(E):
        E.wait_ge("wl0", bld.at("sp_wld"))
        for c in range(NCH):
            par = c % 2
            E.wait_ge(f"ld{par}", bld.at(f"sp_rhs_{c}"))
            if c >= 1:
                E.wait_ge("se", bld.at(f"sp_psfree_{c-1}"))
            for kt in range(32):
                mm = E.matmul(ps[0][:, :], lhsT=wxl[:, kt * 128:(kt + 1) * 128],
                              rhs=rv(par, kt), start=(kt == 0), stop=(kt == 31))
            mm.inc()
            bld.mark(f"sp_tsmm_{c}", "pe")
            for kt in range(32):
                mm = E.matmul(ps[1][0:32, :], lhsT=K["wx_bc"][:, kt * 32:(kt + 1) * 32],
                              rhs=rv(par, kt), start=(kt == 0), stop=(kt == 31))
            mm.inc()
            bld.mark(f"sp_bcmm_{c}", "pe")
            E.wait_ge("se", bld.at(f"sp_tssq_{c}"))
            E.matmul(ps[2][0:1, :], lhsT=K["ones_col"][:, :], rhs=tssq[:, :],
                     start=True, stop=True).inc()
            bld.mark(f"sp_st1_{c}", "pe")
            E.wait_ge("se", bld.at(f"sp_bcsq_{c}"))
            if c >= 1:
                E.wait_ge("se", bld.at(f"sp_spl_{c-1}_31"))
            E.matmul(ps[6][0:2, :], lhsT=K["sel32"][0:32, :], rhs=bcsq[0:32, :],
                     start=True, stop=True).inc()
            bld.mark(f"sp_st2_{c}", "pe")
            E.wait_ge("se", bld.at(f"sp_inv_{c}"))
            if c >= 1:
                E.wait_ge("ve", bld.at(f"sp_tsn_{c-1}"))
            E.matmul(ps[3][:, :], lhsT=K["ones_row"][:, :], rhs=invr[0:1, :],
                     start=True, stop=True).inc()
            bld.mark(f"sp_bc1_{c}", "pe")
            E.wait_ge("se", bld.at(f"sp_inv2_{c}"))
            if c >= 1:
                E.wait_ge("ve", bld.at(f"sp_bcn_{c-1}"))
            E.matmul(ps[4][0:32, :], lhsT=K["sel2"][0:2, :], rhs=inv2[0:2, :],
                     start=True, stop=True).inc()
            bld.mark(f"sp_bc2_{c}", "pe")
            E.wait_ge("ve", bld.at(f"sp_tsn_{c}"))
            E.wait_ge("se", bld.at(f"sp_inv2r_{c}"))
            for it in range(32):
                bank = 5 + it % 2
                if it >= 2:
                    E.wait_ge("se", bld.at(f"sp_spl_{c}_{it-2}"))
                elif c >= 1:
                    E.wait_ge("se", bld.at(f"sp_spl_{c-1}_{30 + it}"))
                E.matmul(ps[bank][:, :], lhsT=wdtl[:, it * 128:(it + 1) * 128],
                         rhs=tsn[:, :], start=True, stop=True).inc()
                bld.mark(f"sp_dt_{c}_{it}", "pe")
            bld.mark(f"sp_done_{c}", "pe")

    def scalar(E):
        for c in range(NCH):
            E.wait_ge("pe", bld.at(f"sp_tsmm_{c}"))
            if c >= 1:
                E.wait_ge("pe", bld.at(f"sp_st1_{c-1}"))
            E.activation(out=tssq[:, :], in_=ps[0][:, :], func=AF.Square).inc()
            bld.mark(f"sp_tssq_{c}", "se")
            E.wait_ge("ve", bld.at(f"sp_tsn1_{c-1}") if c >= 1 else 0)
            E.activation(out=ts_sb[:, :], in_=ps[0][:, :], func=AF.Copy).inc()
            bld.mark(f"sp_tscp_{c}", "se")
            E.wait_ge("pe", bld.at(f"sp_bcmm_{c}"))
            if c >= 1:
                E.wait_ge("pe", bld.at(f"sp_st2_{c-1}"))
            E.activation(out=bcsq[0:32, :], in_=ps[1][0:32, :], func=AF.Square).inc()
            bld.mark(f"sp_bcsq_{c}", "se")
            E.wait_ge("ve", bld.at(f"sp_bcn1_{c-1}") if c >= 1 else 0)
            E.activation(out=bc_sb[0:32, :], in_=ps[1][0:32, :], func=AF.Copy).inc()
            bld.mark(f"sp_bccp_{c}", "se")
            bld.mark(f"sp_psfree_{c}", "se")
            E.wait_ge("pe", bld.at(f"sp_st1_{c}"))
            if c >= 1:
                E.wait_ge("pe", bld.at(f"sp_bc1_{c-1}"))
            E.activation(out=invr[0:1, :], in_=ps[2][0:1, :], func=AF.Ln,
                         scale=1.0 / 128, bias=K["eps_t"][0:1, 0:1]).inc()
            E.fence("se")
            E.activation(out=invr[0:1, :], in_=invr[0:1, :], func=AF.Exp,
                         scale=-0.5).inc()
            bld.mark(f"sp_inv_{c}", "se")
            E.wait_ge("pe", bld.at(f"sp_st2_{c}"))
            if c >= 1:
                E.wait_ge("pe", bld.at(f"sp_bc2_{c-1}"))
            E.activation(out=inv2[0:2, :], in_=ps[6][0:2, :], func=AF.Ln,
                         scale=1.0 / 16, bias=K["eps_t"][0:2, 0:1]).inc()
            E.fence("se")
            E.activation(out=inv2[0:2, :], in_=inv2[0:2, :], func=AF.Exp,
                         scale=-0.5).inc()
            bld.mark(f"sp_inv2_{c}", "se")
            bld.mark(f"sp_inv2r_{c}", "se")
            for it in range(32):
                bank = 5 + it % 2
                E.wait_ge("pe", bld.at(f"sp_dt_{c}_{it}"))
                if it >= 1:
                    E.wait_ge("ve", bld.at(f"sp_del_{c}_{it-1}"))
                elif c >= 1:
                    E.wait_ge("ve", bld.at(f"sp_del_{c-1}_31"))
                E.activation(out=spa[:, :], in_=ps[bank][:, :], func=AF.Abs,
                             bias=K["dtb_t"][:, it:it + 1]).inc()
                E.activation(out=spr[:, :], in_=ps[bank][:, :], func=AF.Relu,
                             bias=K["dtb_t"][:, it:it + 1]).inc()
                bld.mark(f"sp_spl0_{c}_{it}", "se")
                E.fence("se")
                E.activation(out=spa[:, :], in_=spa[:, :], func=AF.Exp,
                             scale=-1.0).inc()
                E.fence("se")
                E.activation(out=spa[:, :], in_=spa[:, :], func=AF.Ln,
                             bias=1.0).inc()
                bld.mark(f"sp_spl_{c}_{it}", "se")

    def vector(E):
        for c in range(NCH):
            b, lc = divmod(c, NCL)
            E.wait_ge("pe", bld.at(f"sp_bc1_{c}"))
            E.wait_ge("se", bld.at(f"sp_tscp_{c}"))
            E.tensor_tensor(out=tmp[:, :], in0=ts_sb[:, :], in1=ps[3][:, :],
                            op=ALU.mult).inc()
            bld.mark(f"sp_tsn1_{c}", "ve")
            E.fence()
            E.tensor_scalar(out=tsn[:, :], in0=tmp[:, :], scalar1=K["dtlnw_t"][:, :],
                            scalar2=None, op0=ALU.mult).inc()
            bld.mark(f"sp_tsn_{c}", "ve")
            E.wait_ge("pe", bld.at(f"sp_bc2_{c}"))
            E.wait_ge("se", bld.at(f"sp_bccp_{c}"))
            if c >= 1:
                E.wait_ge("gd0", bld.at(f"sp_bcst_{c-1}"))
            E.fence()
            E.tensor_tensor(out=tmp2[0:32, :], in0=bc_sb[0:32, :], in1=ps[4][0:32, :],
                            op=ALU.mult).inc()
            bld.mark(f"sp_bcn1_{c}", "ve")
            E.fence()
            E.tensor_scalar(out=bcn[0:32, :], in0=tmp2[0:32, :],
                            scalar1=K["lnw32"][0:32, :], scalar2=None,
                            op0=ALU.mult).inc()
            bld.mark(f"sp_bcn_{c}", "ve")
            for it in range(32):
                slot = it % 4
                E.wait_ge("se", bld.at(f"sp_spl_{c}_{it}"))
                if it % 4 == 0:
                    it4p = it // 4 - 1
                    if it4p >= 0:
                        E.wait_ge(f"st{it4p % 2}", bld.at(f"sp_dst_{c}_{it4p}"))
                    elif c >= 1:
                        E.wait_ge("st1", bld.at(f"sp_dst_{c-1}_7"))
                E.fence()
                E.tensor_tensor(out=dstage[:, slot * 512:(slot + 1) * 512],
                                in0=spa[:, :], in1=spr[:, :], op=ALU.add).inc()
                bld.mark(f"sp_del_{c}_{it}", "ve")

    def sync_all(E):
        for c in range(NCH + 1):
            if c < NCH:
                b, lc = divmod(c, NCL)
                par = c % 2
                if c >= 2:
                    E.wait_ge("pe", bld.at(f"sp_done_{c-2}"))
                for h in range(4):
                    E.dma(rhsb[:, (par * 32 + h * 8) * 512:(par * 32 + (h + 1) * 8) * 512]
                          .rearrange("p (kt x) -> p kt x", x=512),
                          src[h * 8 * 128:(h + 1) * 8 * 128,
                              b * L + lc * 512:b * L + (lc + 1) * 512]
                          .rearrange("(kt p) x -> p kt x", p=128), sem=f"ld{par}")
                bld.mark(f"sp_rhs_{c}", f"ld{par}")
            if c >= 1:
                cc = c - 1
                b, lc = divmod(cc, NCL)
                for it4 in range(8):
                    E.wait_ge("ve", bld.at(f"sp_del_{cc}_{it4 * 4 + 3}"))
                    E.dma(bld.SC["delta_s"][it4 * 4 * 128:(it4 + 1) * 4 * 128,
                                            b * L + lc * 512:b * L + (lc + 1) * 512]
                          .rearrange("(q p) x -> p q x", p=128),
                          dstage.ap().rearrange("p (q x) -> p q x", x=512),
                          sem=f"st{it4 % 2}")
                    bld.mark(f"sp_dst_{cc}_{it4}", f"st{it4 % 2}")
        for b2 in range(2):
            E.wait_ge(f"st{b2}", bld.C[f"st{b2}"])

    return {"sync": sync_all, "gpsimd": gpsimd, "tensor": tensor, "scalar": scalar,
            "vector": vector}

def ph_scan1(bld, es):
    cfg = bld.cfg
    Bc, L, NCL = cfg["Bc"], cfg["L"], cfg["NCL"]
    K = bld.K
    ps = bld.ps
    G = 2
    NIT = 32 // G
    delta2 = bld.sb(es, "s1_d", [128, G * L], F32)
    u2 = bld.sb(es, "s1_u", [128, G * L], BF16)
    sg2 = bld.sb(es, "s1_sg", [128, G * L], BF16)
    du2 = bld.sb(es, "s1_du", [128, G * L], BF16)
    dA2 = bld.sb(es, "s1_dA", [128, G * L], F32)
    dbu_v = bld.sb(es, "s1_dbv", [128, L], BF16)
    dbu_g = bld.sb(es, "s1_dbg", [128, L], BF16)
    s_v = bld.sb(es, "s1_sv", [128, L], BF16)
    s_g = bld.sb(es, "s1_sgg", [128, L], BF16)
    t_v = bld.sb(es, "s1_tv", [128, L], BF16)
    t_g = bld.sb(es, "s1_tg", [128, L], BF16)
    yac = bld.sb(es, "s1_y", [128, G * L], F32)
    bcrow = bld.sb(es, "s1_bcr", [1, 2 * 2 * L], BF16)
    Bb = bld.sb(es, "s1_Bb", [128, 2 * L], BF16)
    Cb = bld.sb(es, "s1_Cb", [128, 2 * L], BF16)
    ystg = du2
    NBG = Bc * NIT

    def dv(i):
        return delta2[:, i * L:(i + 1) * L]

    def uv(i):
        return u2[:, i * L:(i + 1) * L]

    def duv(i):
        return du2[:, i * L:(i + 1) * L]

    def yv(i):
        return yac[:, i * L:(i + 1) * L]

    def Bv(n):
        return Bb[:, (n % 2) * L:(n % 2 + 1) * L]

    def Cv(n):
        return Cb[:, (n % 2) * L:(n % 2 + 1) * L]

    def sync(E):
        for bg in range(NBG):
            b, g = divmod(bg, NIT)
            if bg >= 1:
                E.wait_ge("ve", bld.at(f"s1_grpdone_{bg-1}"))
                E.wait_ge("ge", bld.at(f"s1_gdone_{bg-1}"))
                E.wait_ge("gd0", bld.at(f"s1_yst_{bg-1}"))
            r0 = g * G * 128
            E.dma(delta2.ap().rearrange("p (q x) -> p q x", x=L),
                  bld.SC["delta_s"][r0:r0 + G * 128, b * L:(b + 1) * L]
                  .rearrange("(q p) x -> p q x", p=128), sem="ld0")
            E.dma(u2.ap().rearrange("p (q x) -> p q x", x=L),
                  bld.SC["hconv_s"][r0:r0 + G * 128, b * L:(b + 1) * L]
                  .rearrange("(q p) x -> p q x", p=128), sem="ld0")
            E.dma(sg2.ap().rearrange("p (q x) -> p q x", x=L),
                  bld.SC["sg_s"][r0:r0 + G * 128, b * L:(b + 1) * L]
                  .rearrange("(q p) x -> p q x", p=128), sem="ld0")
            bld.mark(f"s1_ld_{bg}", "ld0")
            for n in range(16):
                par = n % 2
                if bg + n > 0:
                    pk = (f"s1_bcmm_{bg}_{n-2}" if n >= 2 else
                          (f"s1_bcmm_{bg}_{n-1}" if n == 1 else f"s1_bcmm_{bg-1}_15"))
                    if n >= 2:
                        E.wait_ge("pe", bld.at(f"s1_bcmm_{bg}_{n-2}"))
                    elif bg > 0:
                        E.wait_ge("pe", bld.at(f"s1_bcmm_{bg-1}_{14 + n}"))
                sem = "ld1" if par == 0 else "ld3"
                E.dma(bcrow[0:1, par * 2 * L:par * 2 * L + L],
                      bld.SC["bc_s"][n:n + 1, b * L:(b + 1) * L], sem=sem)
                E.dma(bcrow[0:1, par * 2 * L + L:(par + 1) * 2 * L],
                      bld.SC["bc_s"][16 + n:17 + n, b * L:(b + 1) * L], sem=sem)
                bld.mark(f"s1_bcld_{bg}_{n}", sem)

    def tensor(E):
        for bg in range(NBG):
            for n in range(16):
                par = n % 2
                E.wait_ge("ld1" if par == 0 else "ld3", bld.at(f"s1_bcld_{bg}_{n}"))
                if n >= 2:
                    E.wait_ge("ve", bld.at(f"s1_p_{bg}_{n-2}_0"))
                    E.wait_ge("ge", bld.at(f"s1_gp_{bg}_{n-2}"))
                elif bg > 0:
                    E.wait_ge("ve", bld.at(f"s1_p_{bg-1}_{14 + n}_0"))
                    E.wait_ge("ge", bld.at(f"s1_gp_{bg-1}_{14 + n}"))
                for j in range(2):
                    for lc in range(NCL):
                        u = j * NCL + lc
                        bank = 4 + u % 2
                        if u >= 2:
                            E.wait_ge("se", bld.at(f"s1_bcc_{bg}_{n}_{u-2}"))
                        elif bg + n > 0:
                            pu = 2 * NCL - 2 + u
                            E.wait_ge("se", bld.at(f"s1_bcc_{bg}_{n-1}_{pu}" if n > 0
                                                   else f"s1_bcc_{bg-1}_15_{pu}"))
                        E.matmul(ps[bank][:, :512],
                                 lhsT=K["ones_row_bf"][:, :],
                                 rhs=bcrow[0:1, par * 2 * L + j * L + lc * 512:
                                           par * 2 * L + j * L + (lc + 1) * 512],
                                 start=True, stop=True).inc()
                        bld.mark(f"s1_bcb_{bg}_{n}_{u}", "pe")
                bld.mark(f"s1_bcmm_{bg}_{n}", "pe")

    def scalar(E):
        for bg in range(NBG):
            b, g = divmod(bg, NIT)
            E.wait_ge("ld0", bld.at(f"s1_ld_{bg}"))
            for n in range(16):
                for j in range(2):
                    buf = Bb if j == 0 else Cb
                    for lc in range(NCL):
                        u = j * NCL + lc
                        bank = 4 + u % 2
                        E.wait_ge("pe", bld.at(f"s1_bcb_{bg}_{n}_{u}"))
                        E.activation(out=buf[:, (n % 2) * L + lc * 512:
                                             (n % 2) * L + (lc + 1) * 512],
                                     in_=ps[bank][:, :512], func=AF.Copy).inc()
                        bld.mark(f"s1_bcc_{bg}_{n}_{u}", "se")
                for i in range(G):
                    it = g * G + i
                    if n > 0:
                        E.wait_ge("ve" if i == 0 else "ve",
                                  bld.at(f"s1_scan_{bg}_{n-1}_{i}"))
                    elif bg > 0:
                        E.wait_ge("ve", bld.at(f"s1_scan_{bg-1}_15_{i}"))
                    E.activation(out=dA2[:, i * L:(i + 1) * L], in_=dv(i),
                                 func=AF.Exp,
                                 scale=K["a_t"][:, it * 16 + n:it * 16 + n + 1]).inc()
                    bld.mark(f"s1_dA_{bg}_{n}_{i}", "se")

    def vector(E):
        for bg in range(NBG):
            b, g = divmod(bg, NIT)
            E.wait_ge("ld0", bld.at(f"s1_ld_{bg}"))
            if bg >= 1:
                E.wait_ge("ge", bld.at(f"s1_gdone_{bg-1}"))
                E.wait_ge("gd0", bld.at(f"s1_yst_{bg-1}"))
            E.fence()
            for i in range(G):
                E.tensor_tensor(out=duv(i), in0=dv(i), in1=uv(i), op=ALU.mult).inc()
            bld.mark(f"s1_dud_{bg}", "ve")
            E.fence()
            for n in range(16):
                E.wait_ge("se", bld.at(f"s1_bcc_{bg}_{n}_{NCL-1}"))
                E.fence()
                E.tensor_tensor(out=dbu_v[:, :], in0=duv(0), in1=Bv(n),
                                op=ALU.mult).inc()
                bld.mark(f"s1_dbu_{bg}_{n}_0", "ve")
                E.wait_ge("se", bld.at(f"s1_dA_{bg}_{n}_0"))
                E.fence()
                E.tensor_tensor_scan(out=s_v[:, :], data0=dA2[:, 0:L],
                                     data1=dbu_v[:, :], initial=0.0,
                                     op0=ALU.mult, op1=ALU.add).inc()
                bld.mark(f"s1_scan_{bg}_{n}_0", "ve")
                E.wait_ge("se", bld.at(f"s1_bcc_{bg}_{n}_{2*NCL-1}"))
                E.fence()
                if n == 0:
                    E.tensor_tensor(out=yv(0), in0=s_v[:, :], in1=Cv(n),
                                    op=ALU.mult).inc()
                else:
                    E.tensor_tensor(out=t_v[:, :], in0=s_v[:, :], in1=Cv(n),
                                    op=ALU.mult).inc()
                    E.fence()
                    E.tensor_tensor(out=yv(0), in0=yv(0), in1=t_v[:, :],
                                    op=ALU.add).inc()
                bld.mark(f"s1_p_{bg}_{n}_0", "ve")
                E.wait_ge("ge", bld.at(f"s1_gdbu_{bg}_{n}"))
                E.wait_ge("se", bld.at(f"s1_dA_{bg}_{n}_1"))
                if n > 0:
                    E.wait_ge("ge", bld.at(f"s1_gp_{bg}_{n-1}"))
                elif bg > 0:
                    E.wait_ge("ge", bld.at(f"s1_gp_{bg-1}_15"))
                E.fence()
                E.tensor_tensor_scan(out=s_g[:, :], data0=dA2[:, L:2 * L],
                                     data1=dbu_g[:, :], initial=0.0,
                                     op0=ALU.mult, op1=ALU.add).inc()
                bld.mark(f"s1_scan_{bg}_{n}_1", "ve")
            E.wait_ge("ge", bld.at(f"s1_gnd_{bg}"))
            for i in range(G):
                it = g * G + i
                E.fence()
                E.scalar_tensor_tensor(out=t_v[:, :], in0=uv(i),
                                       scalar=K["d_t"][:, it:it + 1],
                                       in1=yv(i), op0=ALU.mult, op1=ALU.add).inc()
                E.fence()
                E.tensor_tensor(out=ystg[:, i * L:(i + 1) * L], in0=t_v[:, :],
                                in1=sg2[:, i * L:(i + 1) * L], op=ALU.mult).inc()
            bld.mark(f"s1_tail_{bg}", "ve")
            bld.mark(f"s1_grpdone_{bg}", "ve")

    def gpsimd(E):
        for bg in range(NBG):
            b, g = divmod(bg, NIT)
            E.wait_ge("ve", bld.at(f"s1_dud_{bg}"))
            for n in range(16):
                E.wait_ge("se", bld.at(f"s1_bcc_{bg}_{n}_{NCL-1}"))
                E.fence()
                E.tensor_tensor(out=dbu_g[:, :], in0=duv(1), in1=Bv(n),
                                op=ALU.mult).inc()
                bld.mark(f"s1_gdbu_{bg}_{n}", "ge")
                E.wait_ge("ve", bld.at(f"s1_scan_{bg}_{n}_1"))
                E.wait_ge("se", bld.at(f"s1_bcc_{bg}_{n}_{2*NCL-1}"))
                E.fence()
                if n == 0:
                    E.tensor_tensor(out=yv(1), in0=s_g[:, :], in1=Cv(n),
                                    op=ALU.mult).inc()
                else:
                    E.tensor_tensor(out=t_g[:, :], in0=s_g[:, :], in1=Cv(n),
                                    op=ALU.mult).inc()
                    E.fence()
                    E.tensor_tensor(out=yv(1), in0=yv(1), in1=t_g[:, :],
                                    op=ALU.add).inc()
                bld.mark(f"s1_gp_{bg}_{n}", "ge")
            bld.mark(f"s1_gnd_{bg}", "ge")
            bld.mark(f"s1_gdone_{bg}", "ge")
            E.wait_ge("ve", bld.at(f"s1_tail_{bg}"))
            E.dma(bld.SC["y1_s"][g * G * 128:(g + 1) * G * 128, b * L:(b + 1) * L]
                  .rearrange("(q p) x -> p q x", p=128),
                  ystg.ap().rearrange("p (q x) -> p q x", x=L), sem="gd0")
            bld.mark(f"s1_yst_{bg}", "gd0")
        E.wait_ge("gd0", bld.C["gd0"])

    return {"sync": sync, "tensor": tensor, "scalar": scalar, "vector": vector,
            "gpsimd": gpsimd}


def ph_moe(bld, es):
    cfg = bld.cfg
    NS, T = cfg["NS"], cfg["T"]
    K = bld.K
    ps = bld.ps
    NMT = 11
    wgr = bld.sb(es, "mo_wg", [128, NMT * 2048], BF16)
    wur = bld.sb(es, "mo_wu", [128, NMT * 2048], BF16)
    wdr = bld.sb(es, "mo_wd", [128, 16 * NMT * 128], BF16)
    xb = bld.sb(es, "mo_x", [128, 16 * 512], BF16)
    wrow = bld.sb(es, "mo_wr", [1, T], BF16)
    gtmp = bld.sb(es, "mo_gt", [128, 512], F32)
    wh = bld.sb(es, "mo_wh", [128, NMT * 512], BF16)
    dstage = bld.sb(es, "mo_ds", [128, 4 * 512], F32)
    tmp = bld.sb(es, "mo_tmp", [128, 512], F32)
    acc = bld.SC["acc1_s"]

    def wsrc(e):
        if e < 8:
            return (bld.IN["wg_b"][e * 1408:(e + 1) * 1408, :],
                    bld.IN["wu_b1"][e * 1408:(e + 1) * 1408, :],
                    lambda j: bld.IN["wdn_b"][e * 2048 + j * 128:e * 2048 + (j + 1) * 128, :])
        half = e - 8
        return (bld.IN["swg_b"][half * 1408:(half + 1) * 1408, :],
                bld.IN["swu_b"][half * 1408:(half + 1) * 1408, :],
                lambda j: bld.IN["swd_b"][j * 128:(j + 1) * 128,
                                          half * 1408:(half + 1) * 1408])

    def gpsimd(E):
        for e in range(10):
            gd, ud, ddf = wsrc(e)
            if e >= 1:
                E.wait_ge("pe", bld.at(f"mo_edone_{e-1}"))
            for j in range(NMT):
                E.dma(wgr[:, j * 2048:(j + 1) * 2048], gd[j * 128:(j + 1) * 128, :],
                      sem="wl0")
                E.dma(wur[:, j * 2048:(j + 1) * 2048], ud[j * 128:(j + 1) * 128, :],
                      sem="wl0")
            for j in range(16):
                E.dma(wdr[:, j * NMT * 128:(j + 1) * NMT * 128], ddf(j), sem="wl0")
            bld.mark(f"mo_wld_{e}", "wl0")
            for s in range(NS):
                for d4 in range(4):
                    E.wait_ge("se", bld.at(f"mo_dcp_{e}_{s}_{d4 * 4 + 3}"))
                    if e == 0:
                        E.dma(acc[d4 * 4 * 128:(d4 + 1) * 4 * 128,
                                  s * 512:(s + 1) * 512]
                              .rearrange("(q p) x -> p q x", p=128),
                              dstage.ap().rearrange("p (q x) -> p q x", x=512),
                              sem="gd0")
                    else:
                        bld.C["gd0"] += 16
                        if not bld.pass1:
                            E._raw.dma_start(
                                out=acc[d4 * 4 * 128:(d4 + 1) * 4 * 128,
                                        s * 512:(s + 1) * 512]
                                .rearrange("(q p) x -> p q x", p=128),
                                in_=dstage.ap().rearrange("p (q x) -> p q x", x=512),
                                accum_op=ALU.add,
                            ).then_inc(bld.sems["gd0"], 16)
                    bld.mark(f"mo_dst_{e}_{s}_{d4}", "gd0")
        E.wait_ge("gd0", bld.C["gd0"])

    def sync(E):
        for e in range(10):
            if e < 8:
                if e >= 1:
                    E.wait_ge("pe", bld.at(f"mo_cb_{e-1}_{bld.cfg['NS']-1}"))
                E.dma(wrow[0:1, :], bld.SC["wfm_s"][e:e + 1, :], sem="ld2")
                bld.mark(f"mo_wrow_{e}", "ld2")
            for s in range(NS):
                if e + s > 0:
                    E.wait_ge("pe", bld.at(f"mo_gu_{e}_{s-1}" if s > 0
                                           else f"mo_gu_{e-1}_{NS-1}"))
                for h in range(2):
                    E.dma(xb[:, h * 8 * 512:(h + 1) * 8 * 512]
                          .rearrange("p (kt x) -> p kt x", x=512),
                          bld.SC["xn1_s"][h * 8 * 128:(h + 1) * 8 * 128,
                                          s * 512:(s + 1) * 512]
                          .rearrange("(kt p) x -> p kt x", p=128), sem="ld0")
                bld.mark(f"mo_rhs_{e}_{s}", "ld0")

    def tensor(E):
        for e in range(10):
            E.wait_ge("wl0", bld.at(f"mo_wld_{e}"))
            if e < 8:
                E.wait_ge("ld2", bld.at(f"mo_wrow_{e}"))
            for s in range(NS):
                E.wait_ge("ld0", bld.at(f"mo_rhs_{e}_{s}"))
                if e < 8:
                    if e + s > 0:
                        E.wait_ge("ve", bld.at(f"mo_wh_{e}_{s-1}_{NMT-1}" if s > 0
                                               else f"mo_wh_{e-1}_{NS-1}_{NMT-1}"))
                    E.matmul(ps[4][:, :], lhsT=K["ones_row_bf"][0:1, :],
                             rhs=wrow[0:1, s * 512:(s + 1) * 512],
                             start=True, stop=True).inc()
                    bld.mark(f"mo_cb_{e}_{s}", "pe")
                for mt in range(NMT):
                    bg, bu = mt % 2, 2 + mt % 2
                    if mt >= 2:
                        E.wait_ge("se", bld.at(f"mo_sil_{e}_{s}_{mt-2}"))
                        E.wait_ge("ve", bld.at(f"mo_wh1_{e}_{s}_{mt-2}"))
                    elif e + s > 0:
                        pk = (f"{e}_{s-1}" if s > 0 else f"{e-1}_{NS-1}")
                        E.wait_ge("se", bld.at(f"mo_sil_{pk}_{NMT + mt - 2}"))
                        E.wait_ge("ve", bld.at(f"mo_wh1_{pk}_{NMT + mt - 2}"))
                    for kt in range(16):
                        mm = E.matmul(ps[bg][:, :],
                                      lhsT=wgr[:, mt * 2048 + kt * 128:mt * 2048 + (kt + 1) * 128],
                                      rhs=xb[:, kt * 512:(kt + 1) * 512],
                                      start=(kt == 0), stop=(kt == 15))
                    mm.inc()
                    bld.mark(f"mo_gmm_{e}_{s}_{mt}", "pe")
                    for kt in range(16):
                        mm = E.matmul(ps[bu][:, :],
                                      lhsT=wur[:, mt * 2048 + kt * 128:mt * 2048 + (kt + 1) * 128],
                                      rhs=xb[:, kt * 512:(kt + 1) * 512],
                                      start=(kt == 0), stop=(kt == 15))
                    mm.inc()
                    bld.mark(f"mo_umm_{e}_{s}_{mt}", "pe")
                bld.mark(f"mo_gu_{e}_{s}", "pe")
                E.wait_ge("ve", bld.at(f"mo_wh_{e}_{s}_{NMT-1}"))
                for dt in range(16):
                    bank = 5 + dt % 2
                    if dt >= 2:
                        E.wait_ge("se", bld.at(f"mo_dcp_{e}_{s}_{dt-2}"))
                    elif e + s > 0:
                        pk = (f"{e}_{s-1}" if s > 0 else f"{e-1}_{NS-1}")
                        E.wait_ge("se", bld.at(f"mo_dcp_{pk}_{14 + dt}"))
                    for j in range(NMT):
                        mm = E.matmul(ps[bank][:, :],
                                      lhsT=wdr[:, (dt * NMT + j) * 128:(dt * NMT + j + 1) * 128],
                                      rhs=wh[:, j * 512:(j + 1) * 512],
                                      start=(j == 0), stop=(j == NMT - 1))
                    mm.inc()
                    bld.mark(f"mo_dmm_{e}_{s}_{dt}", "pe")
            bld.mark(f"mo_edone_{e}", "pe")

    def scalar(E):
        for e in range(10):
            for s in range(NS):
                for mt in range(NMT):
                    bg = mt % 2
                    E.wait_ge("pe", bld.at(f"mo_gmm_{e}_{s}_{mt}"))
                    idx = (e * NS + s) * NMT + mt
                    if idx > 0:
                        pidx = idx - 1
                        pe_, ps_, pm_ = pidx // (NS * NMT), (pidx // NMT) % NS, pidx % NMT
                        E.wait_ge("ve", bld.at(f"mo_wh1_{pe_}_{ps_}_{pm_}"))
                    E.activation(out=gtmp[:, :], in_=ps[bg][:, :], func=AF.Silu).inc()
                    bld.mark(f"mo_sil_{e}_{s}_{mt}", "se")
                for dt in range(16):
                    bank = 5 + dt % 2
                    E.wait_ge("pe", bld.at(f"mo_dmm_{e}_{s}_{dt}"))
                    if dt % 4 == 0:
                        d4p = dt // 4 - 1
                        if d4p >= 0:
                            E.wait_ge("gd0", bld.at(f"mo_dst_{e}_{s}_{d4p}"))
                        elif e + s > 0:
                            pk = (f"{e}_{s-1}" if s > 0 else f"{e-1}_{NS-1}")
                            E.wait_ge("gd0", bld.at(f"mo_dst_{pk}_3"))
                    E.activation(out=dstage[:, (dt % 4) * 512:(dt % 4 + 1) * 512],
                                 in_=ps[bank][:, :], func=AF.Copy).inc()
                    bld.mark(f"mo_dcp_{e}_{s}_{dt}", "se")

    def vector(E):
        for e in range(10):
            for s in range(NS):
                for mt in range(NMT):
                    bu = 2 + mt % 2
                    E.wait_ge("se", bld.at(f"mo_sil_{e}_{s}_{mt}"))
                    E.wait_ge("pe", bld.at(f"mo_umm_{e}_{s}_{mt}"))
                    if s == 0 and mt == 0 and e > 0:
                        E.wait_ge("pe", bld.at(f"mo_edone_{e-1}"))
                    elif mt == 0 and s > 0:
                        E.wait_ge("pe", bld.at(f"mo_dmm_{e}_{s-1}_15"))
                    E.fence()
                    if e < 8:
                        E.wait_ge("pe", bld.at(f"mo_cb_{e}_{s}"))
                        E.tensor_tensor(out=tmp[:, :], in0=gtmp[:, :],
                                        in1=ps[bu][:, :], op=ALU.mult).inc()
                        bld.mark(f"mo_wh1_{e}_{s}_{mt}", "ve")
                        E.fence()
                        E.tensor_tensor(out=wh[:, mt * 512:(mt + 1) * 512],
                                        in0=tmp[:, :], in1=ps[4][:, :],
                                        op=ALU.mult).inc()
                    else:
                        E.tensor_tensor(out=wh[:, mt * 512:(mt + 1) * 512],
                                        in0=gtmp[:, :], in1=ps[bu][:, :],
                                        op=ALU.mult).inc()
                        bld.mark(f"mo_wh1_{e}_{s}_{mt}", "ve")
                    bld.mark(f"mo_wh_{e}_{s}_{mt}", "ve")

    return {"gpsimd": gpsimd, "sync": sync, "tensor": tensor, "scalar": scalar,
            "vector": vector}

MOESCAN = r"""Merged MoE + selective-scan phase: round-robin unit schedule so
TensorE runs the MoE while VectorE runs the scan. GpSimd does the MoE
elementwise (from SBUF evacuations; Pool cannot touch PSUM)."""


def ph_moescan(bld, es):
    cfg = bld.cfg
    Bc, L, NCL, NS, T = cfg["Bc"], cfg["L"], cfg["NCL"], cfg["NS"], cfg["T"]
    K = bld.K
    ps = bld.ps
    NIT = 32
    NBG = Bc * NIT
    delta1 = bld.sb(es, "ms_d", [128, L], F32)
    u1 = bld.sb(es, "ms_u", [128, L], BF16)
    du1 = bld.sb(es, "ms_du", [128, L], BF16)
    dA2 = bld.sb(es, "ms_dA", [128, 3 * L], F32)
    dbu_v = bld.sb(es, "ms_dbv", [128, L], BF16)
    s_v = bld.sb(es, "ms_sv", [128, L], F32)
    t_v = bld.sb(es, "ms_tv", [128, L], BF16)
    yac = bld.sb(es, "ms_y", [128, L], F32)
    Bb = bld.sb(es, "ms_Bb", [128, 2 * L], BF16)
    Cb = bld.sb(es, "ms_Cb", [128, 2 * L], BF16)
    ystg = du1
    NMT = 11
    arena = bld.sb(es, "ms_ar", [128, 24576], BF16)
    xb = bld.sb(es, "ms_x", [128, 2 * 16 * 512], BF16)
    whl = bld.sb(es, "ms_whl", [128, NMT * 512], BF16)
    gtmp = bld.sb(es, "ms_gt", [128, 2 * 512], F32)
    usb = bld.sb(es, "ms_us", [128, 2 * 512], F32)
    comb_sb = bld.sb(es, "ms_cmb", [128, 2 * 512], BF16)
    whst = bld.sb(es, "ms_whst", [128, 2 * 512], BF16)
    dstage = bld.sb(es, "ms_ds", [128, 2 * 512], F32)
    acc = bld.SC["acc1_s"]
    whs = bld.SC["whs_s"]

    def Bv(n):
        return Bb[:, (n % 2) * L:(n % 2 + 1) * L]

    def Cv(n):
        return Cb[:, (n % 2) * L:(n % 2 + 1) * L]

    def dAv(n):
        return dA2[:, (n % 3) * L:(n % 3 + 1) * L]


    def su_sync(E, bg, n):
        b, it = divmod(bg, NIT)
        if n == 0:
            if bg >= 1:
                E.wait_ge("ve", bld.at(f"ms_grpdone_{bg-1}"))
                E.wait_ge("se", bld.at(f"ms_dA_{bg-1}_15"))
                E.wait_ge("gd1", bld.at(f"ms_yst_{bg-1}"))
            r0 = it * 128
            E.dma(delta1[:, :],
                  bld.SC["delta_s"][r0:r0 + 128, b * L:(b + 1) * L], sem="ld2")
            E.dma(u1[:, :],
                  bld.SC["hconv_s"][r0:r0 + 128, b * L:(b + 1) * L], sem="ld2")
            bld.mark(f"ms_ld_{bg}", "ld2")
        if n >= 2:
            E.wait_ge("ve", bld.at(f"ms_dbu_{bg}_{n-2}"))
        elif bg > 0:
            E.wait_ge("ve", bld.at(f"ms_dbu_{bg-1}_{14+n}"))
        E.dma(Bv(n), bld.SC["bc_s"][n:n + 1, b * L:(b + 1) * L]
              .partition_broadcast(128), sem="xb0")
        bld.mark(f"ms_bcldB_{bg}_{n}", "xb0")
        if n >= 2:
            E.wait_ge("ve", bld.at(f"ms_p_{bg}_{n-2}"))
        elif bg > 0:
            E.wait_ge("ve", bld.at(f"ms_p_{bg-1}_{14+n}"))
        E.dma(Cv(n), bld.SC["bc_s"][16 + n:17 + n, b * L:(b + 1) * L]
              .partition_broadcast(128), sem="xb1")
        bld.mark(f"ms_bcldC_{bg}_{n}", "xb1")

    def su_sync_tail(E, bg):
        b, it = divmod(bg, NIT)
        E.wait_ge("ve", bld.at(f"ms_scan_{bg}_15"))
        E.dma(dbu_v[:, :], bld.SC["sg_s"][it * 128:(it + 1) * 128,
                                          b * L:(b + 1) * L], sem="xa1")
        bld.mark(f"ms_sgld_{bg}", "xa1")

    def su_scalar(E, bg, n):
        b, it = divmod(bg, NIT)
        if n == 0:
            E.wait_ge("ld2", bld.at(f"ms_ld_{bg}"))
        if n >= 3:
            E.wait_ge("ve", bld.at(f"ms_scan_{bg}_{n-3}"))
        elif bg > 0:
            E.wait_ge("ve", bld.at(f"ms_scan_{bg-1}_{13+n}"))
        E.activation(out=dAv(n), in_=delta1[:, :], func=AF.Exp,
                     scale=K["a_t"][:, it * 16 + n:it * 16 + n + 1]).inc()
        bld.mark(f"ms_dA_{bg}_{n}", "se")

    def su_vector(E, bg, n):
        b, it = divmod(bg, NIT)
        if n == 0:
            E.wait_ge("ld2", bld.at(f"ms_ld_{bg}"))
            if bg >= 1:
                E.wait_ge("gd1", bld.at(f"ms_yst_{bg-1}"))
            E.fence()
            E.tensor_tensor(out=du1[:, :], in0=delta1[:, :], in1=u1[:, :],
                            op=ALU.mult).inc()
            bld.mark(f"ms_dud_{bg}", "ve")
        E.wait_ge("xb0", bld.at(f"ms_bcldB_{bg}_{n}"))
        E.fence()
        E.tensor_tensor(out=dbu_v[:, :], in0=du1[:, :], in1=Bv(n),
                        op=ALU.mult).inc()
        bld.mark(f"ms_dbu_{bg}_{n}", "ve")
        E.wait_ge("se", bld.at(f"ms_dA_{bg}_{n}"))
        E.fence()
        E.tensor_tensor_scan(out=s_v[:, :], data0=dAv(n), data1=dbu_v[:, :],
                             initial=0.0, op0=ALU.mult, op1=ALU.add).inc()
        bld.mark(f"ms_scan_{bg}_{n}", "ve")
        E.wait_ge("xb1", bld.at(f"ms_bcldC_{bg}_{n}"))
        E.fence()
        if n == 0:
            E.tensor_tensor(out=yac[:, :], in0=s_v[:, :], in1=Cv(n),
                            op=ALU.mult).inc()
        else:
            E.tensor_tensor(out=t_v[:, :], in0=s_v[:, :], in1=Cv(n),
                            op=ALU.mult).inc()
            E.fence()
            E.tensor_tensor(out=yac[:, :], in0=yac[:, :], in1=t_v[:, :],
                            op=ALU.add).inc()
        bld.mark(f"ms_p_{bg}_{n}", "ve")
        if n == 15:
            E.wait_ge("xa1", bld.at(f"ms_sgld_{bg}"))
            E.fence()
            E.scalar_tensor_tensor(out=t_v[:, :], in0=u1[:, :],
                                   scalar=K["d_t"][:, it:it + 1],
                                   in1=yac[:, :], op0=ALU.mult, op1=ALU.add).inc()
            E.fence()
            E.tensor_tensor(out=ystg[:, :], in0=t_v[:, :], in1=dbu_v[:, :],
                            op=ALU.mult).inc()
            bld.mark(f"ms_tail_{bg}", "ve")
            bld.mark(f"ms_grpdone_{bg}", "ve")

    def su_gpsimd(E, bg):
        b, it = divmod(bg, NIT)
        E.wait_ge("ve", bld.at(f"ms_grpdone_{bg}"))
        E.dma(bld.SC["y1_s"][it * 128:(it + 1) * 128, b * L:(b + 1) * L],
              ystg[:, :], sem="gd1")
        bld.mark(f"ms_yst_{bg}", "gd1")

    passes = []
    for e in range(10):
        for h in range(2):
            mts = list(range(h * 6, min(NMT, h * 6 + 6)))
            passes.append((e, h, mts))

    def wsrc(e):
        if e < 8:
            return (bld.IN["wg_b"][e * 1408:(e + 1) * 1408, :],
                    bld.IN["wu_b1"][e * 1408:(e + 1) * 1408, :],
                    lambda j: bld.IN["wdn_b"][e * 2048 + j * 128:e * 2048 + (j + 1) * 128, :])
        half = e - 8
        return (bld.IN["swg_b"][half * 1408:(half + 1) * 1408, :],
                bld.IN["swu_b"][half * 1408:(half + 1) * 1408, :],
                lambda j: bld.IN["swd_b"][j * 128:(j + 1) * 128,
                                          half * 1408:(half + 1) * 1408])

    kkbase = {}
    _acc = 0
    for _pi, (_e, _h, _mts) in enumerate(passes):
        for _s in range(NS):
            kkbase[(_pi, _s)] = _acc
            _acc += len(_mts)

    def xv(g, kt):
        return xb[:, ((g % 2) * 16 + kt) * 512:((g % 2) * 16 + kt + 1) * 512]

    def up_gpsimd(E, pi, s):
        e, h, mts = passes[pi]
        if s == 0:
            gd, ud, _ = wsrc(e)
            if pi >= 1:
                E.wait_ge("pe", bld.at(f"ms_passdone_{pi-1}"))
            if h == 0 and e > 0:
                E.wait_ge("pe", bld.at(f"ms_dmm_{(e - 1) * NS + NS - 1}_15"))
            for k, mt in enumerate(mts):
                E.dma(arena[:, k * 2048:(k + 1) * 2048],
                      gd[mt * 128:(mt + 1) * 128, :], sem="wl0")
                E.dma(arena[:, (6 + k) * 2048:(7 + k) * 2048],
                      ud[mt * 128:(mt + 1) * 128, :], sem="wl0")
            bld.mark(f"ms_wld_{pi}", "wl0")
        if e < 8:
            E.dma(comb_sb[:, (s % 2) * 512:(s % 2 + 1) * 512],
                  bld.SC["wfm_s"][e:e + 1, s * 512:(s + 1) * 512]
                  .partition_broadcast(128), sem="xa0")
            bld.mark(f"ms_comb_{pi}_{s}", "xa0")

    def up_sync(E, pi, s):
        g = pi * NS + s
        if g >= 2:
            E.wait_ge("pe", bld.at(f"ms_gu_{g-2}"))
        for hh in range(2):
            E.dma(xb[:, ((g % 2) * 16 + hh * 8) * 512:
                     ((g % 2) * 16 + (hh + 1) * 8) * 512]
                  .rearrange("p (kt x) -> p kt x", x=512),
                  bld.SC["xn1_s"][hh * 8 * 128:(hh + 1) * 8 * 128,
                                  s * 512:(s + 1) * 512]
                  .rearrange("(kt p) x -> p kt x", p=128), sem="ld0")
        bld.mark(f"ms_rhs_{g}", "ld0")

    def up_tensor(E, pi, s):
        e, h, mts = passes[pi]
        g = pi * NS + s
        E.wait_ge("wl0", bld.at(f"ms_wld_{pi}"))
        E.wait_ge("ld0", bld.at(f"ms_rhs_{g}"))
        for k, mt in enumerate(mts):
            bg_, bu_ = k % 2, 2 + k % 2
            kk = kkbase[(pi, s)] + k
            if kk >= 2:
                E.wait_ge("se", bld.at(f"ms_sil_{kk-2}"))
            for kt in range(16):
                mm = E.matmul(ps[bg_][:, :], lhsT=arena[:, k * 2048 + kt * 128:
                                                        k * 2048 + (kt + 1) * 128],
                              rhs=xv(g, kt), start=(kt == 0), stop=(kt == 15))
            mm.inc()
            bld.mark(f"ms_gmm_{kk}", "pe")
            if kk >= 2:
                E.wait_ge("se", bld.at(f"ms_ucp_{kk-2}"))
            for kt in range(16):
                mm = E.matmul(ps[bu_][:, :], lhsT=arena[:, (6 + k) * 2048 + kt * 128:
                                                        (6 + k) * 2048 + (kt + 1) * 128],
                              rhs=xv(g, kt), start=(kt == 0), stop=(kt == 15))
            mm.inc()
            bld.mark(f"ms_umm_{kk}", "pe")
        bld.mark(f"ms_gu_{g}", "pe")

    def up_scalar(E, pi, s):
        e, h, mts = passes[pi]
        for k, mt in enumerate(mts):
            bg_, bu_ = k % 2, 2 + k % 2
            kk = kkbase[(pi, s)] + k
            sl = kk % 2
            E.wait_ge("pe", bld.at(f"ms_gmm_{kk}"))
            if kk >= 2:
                E.wait_ge("ge", bld.at(f"ms_wht_{kk-2}"))
            E.activation(out=gtmp[:, sl * 512:(sl + 1) * 512],
                         in_=ps[bg_][:, :], func=AF.Silu).inc()
            bld.mark(f"ms_sil_{kk}", "se")
            E.wait_ge("pe", bld.at(f"ms_umm_{kk}"))
            E.activation(out=usb[:, sl * 512:(sl + 1) * 512],
                         in_=ps[bu_][:, :], func=AF.Copy).inc()
            bld.mark(f"ms_ucp_{kk}", "se")
            E.wait_ge("ge", bld.at(f"ms_wht_{kk}"))
            E.fence("se")
            E.dma(whs[(e * 1408 if e < 8 else (8 * 1408 + (e - 8) * 1408)) +
                      mt * 128:(e * 1408 if e < 8 else (8 * 1408 + (e - 8) * 1408)) +
                      (mt + 1) * 128, s * 512:(s + 1) * 512],
                  whst[:, (kk % 2) * 512:(kk % 2 + 1) * 512], sem=f"sd{kk % 2}")
            bld.mark(f"ms_whstd_{kk}", f"sd{kk % 2}")

    def up_gpsimd2(E, pi, s):
        e, h, mts = passes[pi]
        for k, mt in enumerate(mts):
            kk = kkbase[(pi, s)] + k
            sl = kk % 2
            E.wait_ge("se", bld.at(f"ms_sil_{kk}"))
            E.wait_ge("se", bld.at(f"ms_ucp_{kk}"))
            if kk >= 2:
                E.wait_ge(f"sd{kk % 2}", bld.at(f"ms_whstd_{kk-2}"))
            if e < 8:
                E.wait_ge("xa0", bld.at(f"ms_comb_{pi}_{s}"))
            E.fence()
            if e < 8:
                E.tensor_tensor(out=usb[:, sl * 512:(sl + 1) * 512],
                                in0=gtmp[:, sl * 512:(sl + 1) * 512],
                                in1=usb[:, sl * 512:(sl + 1) * 512],
                                op=ALU.mult).inc()
                E.fence()
                E.tensor_tensor(out=whst[:, (kk % 2) * 512:(kk % 2 + 1) * 512],
                                in0=usb[:, sl * 512:(sl + 1) * 512],
                                in1=comb_sb[:, (s % 2) * 512:(s % 2 + 1) * 512],
                                op=ALU.mult).inc()
            else:
                E.tensor_tensor(out=whst[:, (kk % 2) * 512:(kk % 2 + 1) * 512],
                                in0=gtmp[:, sl * 512:(sl + 1) * 512],
                                in1=usb[:, sl * 512:(sl + 1) * 512],
                                op=ALU.mult).inc()
            bld.mark(f"ms_wht_{kk}", "ge")

    def dn_sync(E, e, s):
        d = e * NS + s
        if d >= 1:
            E.wait_ge("pe", bld.at(f"ms_dmm_{d-1}_15"))
        for b2 in range(2):
            E.wait_ge(f"sd{b2}", bld.at(f"ms_upallst_{e}_{b2}"))
        base = e * 1408 if e < 8 else 8 * 1408 + (e - 8) * 1408
        E.dma(whl.ap().rearrange("p (q x) -> p q x", x=512),
              whs[base:base + 1408, s * 512:(s + 1) * 512]
              .rearrange("(q p) x -> p q x", p=128), sem="ld3")
        bld.mark(f"ms_whl_{d}", "ld3")

    def dn_tensor(E, e, s):
        d = e * NS + s
        E.wait_ge("ld3", bld.at(f"ms_whl_{d}"))
        if s == 0:
            E.wait_ge("wl1", bld.at(f"ms_dwld_{e}"))
        for dt in range(16):
            idx = d * 16 + dt
            bank = 4 + idx % 3
            if idx >= 3:
                E.wait_ge("se", bld.at(f"ms_dcp_i{idx-3}"))
            for j in range(NMT):
                mm = E.matmul(ps[bank][:, :], lhsT=arena[:, (dt * NMT + j) * 128:
                                                        (dt * NMT + j + 1) * 128],
                              rhs=whl[:, j * 512:(j + 1) * 512],
                              start=(j == 0), stop=(j == NMT - 1))
            mm.inc()
            bld.mark(f"ms_dmm_{d}_{dt}", "pe")

    def dn_gpsimd(E, e, s):
        d = e * NS + s
        if s == 0:
            _, _, ddf = wsrc(e)
            E.wait_ge("pe", bld.at(f"ms_gu_{(2 * e + 2) * NS - 1}"))
            for j in range(16):
                E.dma(arena[:, j * 1408:(j + 1) * 1408], ddf(j), sem="wl1")
            bld.mark(f"ms_dwld_{e}", "wl1")
        for d4 in range(8):
            E.wait_ge("se", bld.at(f"ms_dcp_{d}_{d4 * 2 + 1}"))
            if e == 0:
                E.dma(acc[d4 * 2 * 128:(d4 + 1) * 2 * 128, s * 512:(s + 1) * 512]
                      .rearrange("(q p) x -> p q x", p=128),
                      dstage.ap().rearrange("p (q x) -> p q x", x=512), sem="gd0")
            else:
                bld.C["gd0"] += 16
                if not bld.pass1:
                    E._raw.dma_start(
                        out=acc[d4 * 2 * 128:(d4 + 1) * 2 * 128,
                                s * 512:(s + 1) * 512]
                        .rearrange("(q p) x -> p q x", p=128),
                        in_=dstage.ap().rearrange("p (q x) -> p q x", x=512),
                        accum_op=ALU.add,
                    ).then_inc(bld.sems["gd0"], 16)
            bld.mark(f"ms_dst_{d}_{d4}", "gd0")

    def dn_scalar(E, e, s):
        d = e * NS + s
        for dt in range(16):
            idx = d * 16 + dt
            bank = 4 + idx % 3
            E.wait_ge("pe", bld.at(f"ms_dmm_{d}_{dt}"))
            if dt % 2 == 0:
                d4p = dt // 2 - 1
                if d4p >= 0:
                    E.wait_ge("gd0", bld.at(f"ms_dst_{d}_{d4p}"))
                elif d >= 1:
                    E.wait_ge("gd0", bld.at(f"ms_dst_{d-1}_7"))
            E.activation(out=dstage[:, (dt % 2) * 512:(dt % 2 + 1) * 512],
                         in_=ps[bank][:, :], func=AF.Copy).inc()
            bld.mark(f"ms_dcp_i{idx}", "se")
            bld.mark(f"ms_dcp_{d}_{dt}", "se")

    scan_units = [("s", bg, n) for bg in range(NBG) for n in range(16)]
    moe_units = []
    for e in range(10):
        for h in range(2):
            pi = e * 2 + h
            for s in range(NS):
                moe_units.append(("u", pi, s))
        for s in range(NS):
            moe_units.append(("d", e, s))
    sched = []
    mi = 0
    for k, su in enumerate(scan_units):
        sched.append(su)
        want = (k + 1) * len(moe_units) // len(scan_units)
        while mi < want:
            sched.append(moe_units[mi])
            mi += 1
    while mi < len(moe_units):
        sched.append(moe_units[mi])
        mi += 1

    def mark_upall(E, e):
        for b2 in range(2):
            bld.mark(f"ms_upallst_{e}_{b2}", f"sd{b2}")

    def body(engine):
        def run(E):
            for unit in sched:
                if unit[0] == "s":
                    _, bg, n = unit
                    if engine == "sync":
                        su_sync(E, bg, n)
                        if n == 15:
                            su_sync_tail(E, bg)
                    elif engine == "scalar":
                        su_scalar(E, bg, n)
                    elif engine == "vector":
                        su_vector(E, bg, n)
                    elif engine == "gpsimd" and n == 15:
                        su_gpsimd(E, bg)
                elif unit[0] == "u":
                    _, pi, s = unit
                    e, h, mts = passes[pi]
                    if engine == "sync":
                        up_sync(E, pi, s)
                    elif engine == "gpsimd":
                        up_gpsimd(E, pi, s)
                        up_gpsimd2(E, pi, s)
                    elif engine == "tensor":
                        up_tensor(E, pi, s)
                        if s == NS - 1:
                            bld.mark(f"ms_passdone_{pi}", "pe")
                    elif engine == "scalar":
                        up_scalar(E, pi, s)
                        if s == NS - 1 and h == 1:
                            mark_upall(E, e)
                else:
                    _, e, s = unit
                    if engine == "sync":
                        dn_sync(E, e, s)
                    elif engine == "tensor":
                        dn_tensor(E, e, s)
                    elif engine == "gpsimd":
                        dn_gpsimd(E, e, s)
                    elif engine == "scalar":
                        dn_scalar(E, e, s)
            if engine == "gpsimd":
                E.wait_ge("gd0", bld.C["gd0"])
                E.wait_ge("gd1", bld.C["gd1"])
        return run

    return {e: body(e) for e in ("sync", "gpsimd", "tensor", "scalar", "vector")}


def ph_wout1(bld, es):
    cfg = bld.cfg
    wins = [(b * cfg["L"], (b + 1) * cfg["L"]) for b in range(cfg["Bc"])]
    return _mm_rhsres(bld, es, "wo1", bld.IN["w_out1"], 32, 16,
                      bld.SC["y1_s"], bld.SC["s1_s"], windows=wins)

def ph_comb2(bld, es):
    cfg = bld.cfg
    NS = cfg["NS"]
    ps = bld.ps
    x1b = bld.sb(es, "c2_x", [128, 2 * 16 * 512], F32)
    a1b = bld.sb(es, "c2_a", [128, 2 * 16 * 512], F32)
    s1b = bld.sb(es, "c2_s", [128, 2 * 16 * 512], BF16)
    g1t = bld.sb(es, "c2_g", [128, 2 * 512], BF16)
    tmp = bld.sb(es, "c2_t", [128, 512], F32)
    ystg = bld.sb(es, "c2_y", [128, 16 * 512], F32)
    x1s, a1s, s1s = bld.SC["x1_s"], bld.SC["acc1_s"], bld.SC["s1_s"]
    OUT = bld.OUT

    def sync(E):
        for s in range(NS):
            par = s % 2
            if s >= 2:
                E.wait_ge("ve", bld.at(f"c2_yd_{s-2}"))
            for g4 in range(4):
                E.dma(x1b[:, (par * 16 + g4 * 4) * 512:(par * 16 + g4 * 4 + 4) * 512]
                      .rearrange("p (dt c) -> p dt c", c=512),
                      x1s[g4 * 512:(g4 + 1) * 512, s * 512:(s + 1) * 512]
                      .rearrange("(dt p) c -> p dt c", p=128), sem=f"ld{par}")
                E.dma(a1b[:, (par * 16 + g4 * 4) * 512:(par * 16 + g4 * 4 + 4) * 512]
                      .rearrange("p (dt c) -> p dt c", c=512),
                      a1s[g4 * 512:(g4 + 1) * 512, s * 512:(s + 1) * 512]
                      .rearrange("(dt p) c -> p dt c", p=128), sem=f"ld{par + 2}")
            bld.mark(f"c2_lda_{s}", f"ld{par + 2}")
            for h in range(2):
                E.dma(s1b[:, (par * 16 + h * 8) * 512:(par * 16 + (h + 1) * 8) * 512]
                      .rearrange("p (dt c) -> p dt c", c=512),
                      s1s[h * 8 * 128:(h + 1) * 8 * 128, s * 512:(s + 1) * 512]
                      .rearrange("(dt p) c -> p dt c", p=128), sem=f"xa{par}")
            bld.mark(f"c2_lds_{s}", f"xa{par}")
            bld.mark(f"c2_ld_{s}", f"ld{par}")

    def scalar(E):
        for s in range(NS):
            par = s % 2
            E.wait_ge(f"ld{par + 2}", bld.at(f"c2_lda_{s}"))
            for dt in range(16):
                slot = dt % 2
                if s + dt > 0:
                    pk = f"{s}_{dt-2}" if dt >= 2 else f"{s-1}_{14 + dt}"
                    E.wait_ge("ve", bld.at(f"c2_t_{pk}"))
                E.activation(out=g1t[:, slot * 512:(slot + 1) * 512],
                             in_=a1b[:, (par * 16 + dt) * 512:(par * 16 + dt + 1) * 512],
                             func=AF.Silu).inc()
                bld.mark(f"c2_g_{s}_{dt}", "se")

    def vector(E):
        for s in range(NS):
            par = s % 2
            E.wait_ge(f"ld{par}", bld.at(f"c2_ld_{s}"))
            E.wait_ge(f"xa{par}", bld.at(f"c2_lds_{s}"))
            if s >= 1:
                E.wait_ge(f"gd{1 - par}", bld.at(f"c2_st_{s-1}"))
            for dt in range(16):
                slot = dt % 2
                E.wait_ge("se", bld.at(f"c2_g_{s}_{dt}"))
                E.fence()
                E.tensor_tensor(out=tmp[:, :],
                                in0=s1b[:, (par * 16 + dt) * 512:(par * 16 + dt + 1) * 512],
                                in1=g1t[:, slot * 512:(slot + 1) * 512],
                                op=ALU.mult).inc()
                bld.mark(f"c2_t_{s}_{dt}", "ve")
                E.fence()
                E.tensor_tensor(out=ystg[:, dt * 512:(dt + 1) * 512],
                                in0=x1b[:, (par * 16 + dt) * 512:(par * 16 + dt + 1) * 512],
                                in1=tmp[:, :], op=ALU.add).inc()
            bld.mark(f"c2_yd_{s}", "ve")

    def gpsimd(E):
        for s in range(NS):
            par = s % 2
            E.wait_ge("ve", bld.at(f"c2_yd_{s}"))
            for g4 in range(4):
                E.dma(OUT[g4 * 512:(g4 + 1) * 512, s * 512:(s + 1) * 512]
                      .rearrange("(dt p) c -> p dt c", p=128),
                      ystg[:, g4 * 4 * 512:(g4 * 4 + 4) * 512]
                      .rearrange("p (dt c) -> p dt c", c=512), sem=f"gd{par}")
            bld.mark(f"c2_st_{s}", f"gd{par}")
        for p2 in range(min(2, NS)):
            E.wait_ge(f"gd{p2}", bld.C[f"gd{p2}"])

    return {"sync": sync, "scalar": scalar, "vector": vector, "gpsimd": gpsimd}



def kernel(**inputs):
    Bc, L = 2, 2048
    in_maps = prep_inputs(inputs, N_CORES, Bc, L)
    nc, cfg = build_program(Bc, L)
    res = run_bass_kernel_spmd(nc, in_maps, core_ids=list(range(N_CORES)))
    D, T = cfg["D"], cfg["T"]
    outs = []
    for c in range(N_CORES):
        y = np.asarray(res.results[c]["y_fm"]).reshape(D, Bc, L)
        outs.append(y.transpose(1, 2, 0))
    return np.ascontiguousarray(np.concatenate(outs, 0), dtype=np.float32)
